# version 03
# baseline (speedup 1.0000x reference)
import numpy as np
from contextlib import ExitStack
import ml_dtypes
import concourse.bass as bass
import concourse.mybir as mybir
from concourse.bass_utils import run_bass_kernel_spmd

F32 = mybir.dt.float32
BF16 = mybir.dt.bfloat16
I32 = mybir.dt.int32
AF = mybir.ActivationFunctionType
ALU = mybir.AluOpType
AX = mybir.AxisListType

T = 2048
D = 1024
NT = T // 128
ND = D // 128
NC = T // 512
QB = 1024
NQB = T // QB

ENGS = ("gpsimd", "tensor", "scalar", "vector", "sync")



def _t(w):
    return np.ascontiguousarray(np.asarray(w).T.astype(np.float32, copy=False))


def _colcat(*vs):
    cols = []
    for v in vs:
        v = np.asarray(v, np.float32).reshape(ND, 128).T
        cols.append(v)
    return np.ascontiguousarray(np.concatenate(cols, axis=1))


def _rope_perm():
    p = np.zeros(D, np.int64)
    for h in range(16):
        for r in range(32):
            p[h * 64 + r] = h * 64 + 2 * r
            p[h * 64 + 32 + r] = h * 64 + 2 * r + 1
    return p


def _rope_tables_l0():
    base = 10000.0
    inv = 1.0 / base ** (np.arange(0, 64, 2, dtype=np.float32) / 64)
    t = np.arange(T, dtype=np.float32)
    fr = np.outer(t, inv)
    cos = np.concatenate([np.cos(fr), np.cos(fr)], -1)[:, ::2]
    sin = np.concatenate([np.sin(fr), np.sin(fr)], -1)[:, ::2]
    CC = np.zeros((128, T), np.float32)
    SS = np.zeros((128, T), np.float32)
    for rep in range(2):
        b = rep * 64
        CC[b:b + 32] = cos.T
        CC[b + 32:b + 64] = cos.T
        SS[b:b + 32] = -sin.T
        SS[b + 32:b + 64] = sin.T
    return CC, SS


def _consts():
    ident = np.eye(128, dtype=np.float32)
    pswap = np.zeros((128, 128), np.float32)
    for b in range(0, 128, 64):
        for i in range(32):
            pswap[b + i, b + 32 + i] = 1.0
            pswap[b + 32 + i, b + i] = 1.0
    ones = np.ones((128, 128), np.float32)
    return np.concatenate([ident, pswap, ones], axis=1)


def stage_shared(inputs):
    g = {}
    rp = _rope_perm()
    g["l0_wdq"] = _t(inputs["L0_attn_w_down_q"])
    g["l0_wdkv"] = _t(inputs["L0_attn_w_down_kv"])
    g["l0_wupq"] = np.ascontiguousarray(_t(inputs["L0_attn_w_up_q"])[:, rp])
    g["l0_wupk"] = np.ascontiguousarray(_t(inputs["L0_attn_w_up_k"])[:, rp])
    g["l0_wupv"] = _t(inputs["L0_attn_w_up_v"])
    wo = _t(inputs["L0_attn_w_out"])
    g["l0_wout_h0"] = np.ascontiguousarray(wo[:, 0:512])
    g["l0_wout_h1"] = np.ascontiguousarray(wo[:, 512:1024])
    g["l0_cols"] = _colcat(inputs["L0_attn_ln_scale"], inputs["L0_attn_ln_bias"])
    CC, SS = _rope_tables_l0()
    g["ropeC"] = CC
    g["ropeS"] = SS
    g["consts"] = _consts()
    return g


def stage_x(inputs):
    x = np.asarray(inputs["L0_attn_x"], np.float32)
    return [np.ascontiguousarray(x[b].T.reshape(ND, 128, T)) for b in range(8)]



class Sem:
    def __init__(self, es, nc, name):
        self.h = es.enter_context(nc.semaphore(name))
        self.n = 0

    def pre(self, k=1):
        self.n += k
        return self.n


class Ctx:
    def __init__(self, nc, es):
        self.nc = nc
        self.es = es
        self.streams = {e: [] for e in ENGS}
        self.pe = Sem(es, nc, "c_pe")
        self.act = Sem(es, nc, "c_act")
        self.ve = Sem(es, nc, "c_ve")
        self.po = Sem(es, nc, "c_po")
        self.dma_sems = {}
        self.ps = es.enter_context(nc.psum_tensor("ps", [128, 3584], F32))
        self.psb = es.enter_context(nc.psum_tensor("psb", [128, 1024], BF16))
        self.bank_guard = [None] * 7
        self.RA = es.enter_context(nc.sbuf_tensor("RA", [128, 16384], F32))
        self.RB = es.enter_context(nc.sbuf_tensor("RB", [128, 16384], BF16))
        self.RC = es.enter_context(nc.sbuf_tensor("RC", [128, 16384], BF16))
        self.RW = es.enter_context(nc.sbuf_tensor("RW", [128, 12288], BF16))
        self.RX = es.enter_context(nc.sbuf_tensor("RX", [128, 8192], BF16))
        self.RY = es.enter_context(nc.sbuf_tensor("RY", [128, 4096], F32))
        self.RM = es.enter_context(nc.sbuf_tensor("RM", [128, 2048], F32))
        self.ROPE = es.enter_context(nc.sbuf_tensor("ROPE", [128, 4096], BF16))
        self.CONST = es.enter_context(nc.sbuf_tensor("CONST", [128, 384], BF16))
        self.RCOL = es.enter_context(nc.sbuf_tensor("RCOL", [128, 64], F32))
        self.RCOL2 = es.enter_context(nc.sbuf_tensor("RCOL2", [128, 128], F32))
        self.RCOL2b = es.enter_context(nc.sbuf_tensor("RCOL2b", [128, 16], F32))
        self.ONEF = es.enter_context(nc.sbuf_tensor("ONEF", [128, 1], F32))
        self.RAb = self.RA[:].bitcast(BF16)
        self.const_vals = {}
        self.s_const = Sem(es, nc, "s_const")

    def constf(self, v):
        v = float(v)
        if v not in self.const_vals:
            col = 48 + len(self.const_vals)
            assert col < 64
            self.const_vals[v] = col
        return self.RCOL[:, self.const_vals[v]:self.const_vals[v] + 1]

    def sem(self, name):
        if name not in self.dma_sems:
            self.dma_sems[name] = Sem(self.es, self.nc, "s_" + name)
        return self.dma_sems[name]

    def emit(self, eng, fn):
        self.streams[eng].append(fn)

    def bank(self, i, w=512, rows=128, r0=0):
        return self.ps[r0:r0 + rows, i * 512: i * 512 + w]

    def guard_bank(self, i, sem, count):
        self.bank_guard[i] = (sem, count)

    def snap_guards(self, idxs):
        return [self.bank_guard[i] for i in idxs]

    @property
    def ident(self):
        return self.CONST[:, 0:128]

    @property
    def pswap(self):
        return self.CONST[:, 128:256]

    @property
    def ones_bf(self):
        return self.CONST[:, 256:384]

    def finalize(self, block):
        streams = self.streams
        cv = dict(self.const_vals)
        RCOL = self.RCOL
        ONEF = self.ONEF
        sc = self.s_const
        def run(name):
            def _f(eng):
                if name == "gpsimd":
                    last = eng.memset(ONEF[:], 1.0)
                    for v, col in cv.items():
                        last = eng.memset(RCOL[:, col:col + 1], v)
                    last.then_inc(sc.h, 1)
                else:
                    eng.wait_ge(sc.h, 1)
                for fn in streams[name]:
                    fn(eng)
            return _f
        block.gpsimd(run("gpsimd"))
        block.tensor(run("tensor"))
        block.scalar(run("scalar"))
        block.vector(run("vector"))
        block.sync(run("sync"))


def wg(e, gd):
    if gd is not None:
        e.wait_ge(gd[0].h, gd[1])



def e_dma(C, eng, dst, src, sem, waits=()):
    tgt = sem.pre(16)
    def f(e, dst=dst, src=src, sem=sem, allw=tuple(waits)):
        for w in allw:
            wg(e, w)
        e.dma_start(dst, src).then_inc(sem.h, 16)
    C.emit(eng, f)
    return tgt


def load_wmat(C, region, dst_cols, src_hbm, kt, out_w, sem, waits=()):
    dst = region[:, dst_cols:dst_cols + kt * out_w]\
        .rearrange("p (k o) -> p k o", k=kt)
    src = src_hbm.rearrange("(k p) o -> p k o", p=128)
    return e_dma(C, "gpsimd", dst, src, sem, waits=waits)


def mm_chain(C, lhsT_fn, rhs_fn, evac_fn, m_tiles, k_list, n_chunks,
             banks=(0, 1, 2, 3), pre_waits=(), m_rows=128):
    first = True
    bi = 0
    for m in range(m_tiles):
        for c in range(n_chunks):
            b = banks[bi % len(banks)]
            bi += 1
            gd = C.bank_guard[b]
            pw = tuple(pre_waits) if first else ()
            first = False
            pe_t = C.pe.pre(1)
            kl = list(k_list)
            def fmm(t, m=m, c=c, b=b, gd=gd, pw=pw, kl=kl):
                for w in pw:
                    wg(t, w)
                wg(t, gd)
                last = None
                for ki, k in enumerate(kl):
                    r = rhs_fn(k=k, c=c)
                    last = t.matmul(C.bank(b, w=r.shape[-1], rows=m_rows),
                                    lhsT_fn(k=k, m=m),
                                    r, start=(ki == 0), stop=(ki == len(kl) - 1))
                last.then_inc(C.pe.h, 1)
            C.emit("tensor", fmm)
            C.guard_bank(b, *evac_fn(m, c, b, pe_t))


def evac_copy(C, dst_fn, eng="scalar", func=None, scale=1.0, bias_fn=None):
    def evac(m, c, b, pe_t):
        dst = dst_fn(m, c)
        w = dst.shape[-1]
        rows = dst.shape[0]
        if eng == "vector":
            cnt = C.ve.pre(1)
            def f(v, dst=dst, b=b, pe_t=pe_t, w=w, rows=rows):
                v.wait_ge(C.pe.h, pe_t)
                v.tensor_copy(dst, C.bank(b, w=w, rows=rows))\
                    .then_inc(C.ve.h, 1)
            C.emit("vector", f)
            return (C.ve, cnt)
        cnt = C.act.pre(1)
        fn = AF.Copy if func is None else func
        bias = bias_fn(m) if bias_fn is not None else None
        def f(s, dst=dst, b=b, pe_t=pe_t, w=w, fn=fn, scale=scale, bias=bias,
              rows=rows):
            s.wait_ge(C.pe.h, pe_t)
            kw = {"scale": scale}
            if bias is not None:
                kw["bias"] = bias
            s.activation(dst, C.bank(b, w=w, rows=rows), fn, **kw)\
                .then_inc(C.act.h, 1)
        C.emit("scalar", f)
        return (C.act, cnt)
    return evac


def evac_resid(C, dst_fn, res_fn):
    def evac(m, c, b, pe_t):
        cnt = C.ve.pre(1)
        def f(v, m=m, c=c, b=b, pe_t=pe_t):
            v.wait_ge(C.pe.h, pe_t)
            dst = dst_fn(m, c)
            v.scalar_tensor_tensor(dst, C.bank(b, w=dst.shape[-1]), 1.0,
                                   res_fn(m, c), ALU.mult, ALU.add)\
                .then_inc(C.ve.h, 1)
        C.emit("vector", f)
        return (C.ve, cnt)
    return evac



def emit_ln(C, x_tile_fn, dst_fn, nd=ND, eps=1e-5, rms=False, gain_col=None,
            bias_col=None, pre_waits=(), sq_region=None, in_bf16=False):
    invD = 1.0 / (nd * 128)
    SQ = sq_region if sq_region is not None else C.RC
    for c in range(NC):
        for k in range(nd):
            C.act.pre(1)
            pw = tuple(pre_waits) if (c == 0 and k == 0) else ()
            def fsq(s, k=k, c=c, pw=pw):
                for w in pw:
                    wg(s, w)
                s.square(SQ[:, k * 512:(k + 1) * 512],
                         x_tile_fn(k)[:, c * 512:(c + 1) * 512])\
                    .then_inc(C.act.h, 1)
            C.emit("scalar", fsq)
        sq_last = C.act.n
        gd01 = C.snap_guards([0, 1])
        pe_t = C.pe.pre(1)
        ones1 = C.ones_bf[:, 0:1] if in_bf16 else C.ONEF[:, 0:1]
        def fmm(t, c=c, gd01=gd01, sq_last=sq_last, rms=rms, nd=nd, ones1=ones1):
            t.wait_ge(C.act.h, sq_last)
            wg(t, gd01[0])
            if not rms:
                wg(t, gd01[1])
            last = None
            for k in range(nd):
                last = t.matmul(C.ps[0:1, 0:512], C.ones_bf[:, 0:1],
                                SQ[:, k * 512:(k + 1) * 512],
                                start=(k == 0), stop=(k == nd - 1))
            if not rms:
                for k in range(nd):
                    last = t.matmul(C.ps[0:1, 512:1024], ones1,
                                    x_tile_fn(k)[:, c * 512:(c + 1) * 512],
                                    start=(k == 0), stop=(k == nd - 1))
            last.then_inc(C.pe.h, 1)
        C.emit("tensor", fmm)
        if not rms:
            a1 = C.act.pre(1)
            def frow(s, pe_t=pe_t, invD=invD):
                s.wait_ge(C.pe.h, pe_t)
                s.activation(C.RM[0:1, 0:512], C.ps[0:1, 512:1024], AF.Copy,
                             scale=invD)
                s.square(C.RM[0:1, 512:1024], C.RM[0:1, 0:512])\
                    .then_inc(C.act.h, 1)
            C.emit("scalar", frow)
        else:
            a1 = None
        v1 = C.ve.pre(1)
        def frow2(v, pe_t=pe_t, rms=rms, invD=invD, a1=a1):
            v.wait_ge(C.pe.h, pe_t)
            if rms:
                v.tensor_scalar(C.RM[0:1, 1024:1536], C.ps[0:1, 0:512], invD,
                                None, ALU.mult).then_inc(C.ve.h, 1)
            else:
                v.wait_ge(C.act.h, a1)
                v.scalar_tensor_tensor(C.RM[0:1, 1024:1536], C.ps[0:1, 0:512],
                                       invD, C.RM[0:1, 512:1024], ALU.mult,
                                       ALU.subtract).then_inc(C.ve.h, 1)
        C.emit("vector", frow2)
        a2 = C.act.pre(1)
        eps_ap = C.constf(eps)[0:1, :]
        def frow3(s, v1=v1, eps_ap=eps_ap):
            s.wait_ge(C.ve.h, v1)
            s.activation(C.RM[0:1, 1536:2048], C.RM[0:1, 1024:1536], AF.Ln,
                         bias=eps_ap)
            s.activation(C.RM[0:1, 1536:2048], C.RM[0:1, 1536:2048], AF.Exp,
                         scale=-0.5).then_inc(C.act.h, 1)
        C.emit("scalar", frow3)
        v2 = C.ve.pre(1)
        def fhilo(v, a2=a2, rms=rms):
            v.wait_ge(C.act.h, a2)
            if not rms:
                v.tensor_copy(C.RX[0:1, 4096:4608], C.RM[0:1, 0:512])
                v.tensor_tensor(C.RX[0:1, 4608:5120], C.RM[0:1, 0:512],
                                C.RX[0:1, 4096:4608], ALU.subtract)
            v.tensor_copy(C.RX[0:1, 5120:5632], C.RM[0:1, 1536:2048])
            v.tensor_tensor(C.RX[0:1, 5632:6144], C.RM[0:1, 1536:2048],
                            C.RX[0:1, 5120:5632], ALU.subtract)\
                .then_inc(C.ve.h, 1)
        C.emit("vector", fhilo)
        gd23 = C.snap_guards([2, 3])
        pe_b = C.pe.pre(1)
        def fbc(t, v2=v2, gd23=gd23, rms=rms):
            t.wait_ge(C.ve.h, v2)
            wg(t, gd23[0])
            wg(t, gd23[1])
            if not rms:
                t.matmul(C.ps[:, 1024:1536], C.ones_bf[0:1, :],
                         C.RX[0:1, 4096:4608], start=True, stop=False)
                t.matmul(C.ps[:, 1024:1536], C.ones_bf[0:1, :],
                         C.RX[0:1, 4608:5120], start=False, stop=True)
            t.matmul(C.ps[:, 1536:2048], C.ones_bf[0:1, :],
                     C.RX[0:1, 5120:5632], start=True, stop=False)
            t.matmul(C.ps[:, 1536:2048], C.ones_bf[0:1, :],
                     C.RX[0:1, 5632:6144], start=False, stop=True)\
                .then_inc(C.pe.h, 1)
        C.emit("tensor", fbc)
        for k in range(nd):
            C.ve.pre(1)
            gc = gain_col(k) if gain_col is not None else None
            bc = bias_col(k) if bias_col is not None else None
            def fnorm(v, k=k, c=c, pe_b=pe_b, rms=rms, gc=gc, bc=bc):
                v.wait_ge(C.pe.h, pe_b)
                xs = x_tile_fn(k)[:, c * 512:(c + 1) * 512]
                dst = dst_fn(k)[:, c * 512:(c + 1) * 512]
                ry = C.RY[:, 0:512]
                if rms:
                    if gc is None:
                        v.tensor_tensor(dst, xs, C.ps[:, 1536:2048], ALU.mult)\
                            .then_inc(C.ve.h, 1)
                    else:
                        v.scalar_tensor_tensor(dst, xs, gc, C.ps[:, 1536:2048],
                                               ALU.mult, ALU.mult)\
                            .then_inc(C.ve.h, 1)
                else:
                    v.tensor_tensor(ry, xs, C.ps[:, 1024:1536], ALU.subtract)
                    if gc is None:
                        v.tensor_tensor(dst, ry, C.ps[:, 1536:2048], ALU.mult)\
                            .then_inc(C.ve.h, 1)
                    elif bc is None:
                        v.scalar_tensor_tensor(dst, ry, gc, C.ps[:, 1536:2048],
                                               ALU.mult, ALU.mult)\
                            .then_inc(C.ve.h, 1)
                    else:
                        v.scalar_tensor_tensor(ry, ry, gc, C.ps[:, 1536:2048],
                                               ALU.mult, ALU.mult)
                        v.tensor_scalar(dst, ry, bc, None, ALU.add)\
                            .then_inc(C.ve.h, 1)
            C.emit("vector", fnorm)
        last_v = C.ve.n
        C.guard_bank(0, C.ve, v1)
        if not rms:
            C.guard_bank(1, C.ve, v1)
        C.guard_bank(2, C.ve, last_v)
        C.guard_bank(3, C.ve, last_v)
    return (C.ve, C.ve.n)



def emit_attn64(C, heads, qT_fn, kT_fn, v_lhsT_fn, o_dst_fn,
                shift="passA", kt_hi_fn=None, mask_fn=None, pmask_fn=None,
                ry_off=0, bias_blk=None, den_eps=None):
    nqt = QB // 128
    for h in heads:
        base = 64 * (h % 2)
        for qb in range(NQB):
            kt_hi = kt_hi_fn(qb) if kt_hi_fn is not None else NT
            if shift == "passA":
                for qt in range(nqt):
                    qt_g = qb * nqt + qt
                    kwa = kt_hi * 128 if kt_hi_fn is None \
                        else min(kt_hi, qt_g + 1) * 128
                    nkc = (kwa + 511) // 512
                    gds = C.snap_guards(list(range(nkc)))
                    pe_t = C.pe.pre(1)
                    def fpa(t, h=h, qb=qb, qt=qt, nkc=nkc, kwa=kwa, gds=gds):
                        q = qT_fn(h)
                        k_ = kT_fn(h)
                        last = None
                        for ci in range(nkc):
                            wg(t, gds[ci])
                            w = min(512, kwa - ci * 512)
                            last = t.matmul(
                                C.bank(ci, w=w),
                                q[:, qb * QB + qt * 128: qb * QB + qt * 128 + 128],
                                k_[:, ci * 512: ci * 512 + w],
                                start=True, stop=True)
                        last.then_inc(C.pe.h, 1)
                    C.emit("tensor", fpa)
                    C.ve.pre(1)
                    def fmax(v, qt=qt, pe_t=pe_t, kwa=kwa):
                        v.wait_ge(C.pe.h, pe_t)
                        v.reduce_max(C.RM[:, qt:qt + 1], C.ps[:, 0:kwa], AX.X)\
                            .then_inc(C.ve.h, 1)
                    C.emit("vector", fmax)
                    for ci in range(nkc):
                        C.guard_bank(ci, C.ve, C.ve.n)
                vbk = C.ve.pre(1)
                v_t = C.ve.pre(1)
                def fblk(v, nqt=nqt, vbk=vbk):
                    v.reduce_max(C.RM[:, 8:9], C.RM[:, 0:nqt], AX.X)\
                        .then_inc(C.ve.h, 1)
                    v.wait_ge(C.ve.h, vbk)
                    v.tensor_copy(C.RX[:, 6144:6145], C.RM[:, 8:9])\
                        .then_inc(C.ve.h, 1)
                C.emit("vector", fblk)
                gd6 = C.snap_guards([6])
                pe_r = C.pe.pre(1)
                def frow(t, v_t=v_t, gd6=gd6):
                    t.wait_ge(C.ve.h, v_t)
                    wg(t, gd6[0])
                    t.matmul(C.ps[0:1, 3072:3200], C.RX[:, 6144:6145], C.ident,
                             start=True, stop=True).then_inc(C.pe.h, 1)
                C.emit("tensor", frow)
                vck = C.ve.pre(1)
                v_r = C.ve.pre(1)
                def fcm(v, pe_r=pe_r, vck=vck):
                    v.wait_ge(C.pe.h, pe_r)
                    v.reduce_max(C.RM[0:1, 9:10], C.ps[0:1, 3072:3200], AX.X)\
                        .then_inc(C.ve.h, 1)
                    v.wait_ge(C.ve.h, vck)
                    v.tensor_copy(C.RX[0:1, 6146:6147], C.RM[0:1, 9:10])\
                        .then_inc(C.ve.h, 1)
                C.emit("vector", fcm)
                C.guard_bank(6, C.ve, C.ve.n)
                gd6 = C.snap_guards([6])
                pe_c = C.pe.pre(1)
                def fbcc(t, v_r=v_r, gd6=gd6):
                    t.wait_ge(C.ve.h, v_r)
                    wg(t, gd6[0])
                    t.matmul(C.ps[:, 3072:3073], C.ones_bf[0:1, :],
                             C.RX[0:1, 6146:6147], start=True, stop=True)\
                        .then_inc(C.pe.h, 1)
                C.emit("tensor", fbcc)
                a_c = C.act.pre(1)
                def fnegc(s, pe_c=pe_c):
                    s.wait_ge(C.pe.h, pe_c)
                    s.activation(C.RM[:, 16:17], C.ps[:, 3072:3073], AF.Copy,
                                 scale=-1.0).then_inc(C.act.h, 1)
                C.emit("scalar", fnegc)
                C.guard_bank(6, C.act, a_c)
                bias_ready = (C.act, a_c)
            else:
                bias_ready = None
            if bias_blk is not None:
                bias_fn, bias_ready = bias_blk(h, qb, bias_ready)
            else:
                bias_fn = None

            av_started = False
            for kt in range(kt_hi):
                sb = 4 if (kt % 2 == 0) else 0
                gds = C.snap_guards([sb, sb + 1])
                pe_s = C.pe.pre(1)
                def fst(t, h=h, qb=qb, kt=kt, sb=sb, gds=gds):
                    wg(t, gds[0])
                    wg(t, gds[1])
                    q = qT_fn(h)
                    k_ = kT_fn(h)
                    last = None
                    for half in range(2):
                        last = t.matmul(
                            C.bank(sb + half, w=512),
                            k_[:, kt * 128: kt * 128 + 128],
                            q[:, qb * QB + half * 512: qb * QB + half * 512 + 512],
                            start=True, stop=True)
                    last.then_inc(C.pe.h, 1)
                C.emit("tensor", fst)
                pe_after_mask = (C.pe, pe_s)
                if mask_fn is not None:
                    pe_after_mask = mask_fn(h, qb, kt, sb, pe_s)
                pcols = (kt % 2) * 2048
                a_e = C.act.pre(1)
                bias_ap = bias_fn(kt) if bias_fn is not None else (
                    C.RM[:, 16:17] if shift == "passA"
                    else C.constf(float(shift)))
                def fexp(s, sb=sb, pcols=pcols, pam=pe_after_mask,
                         br=bias_ready, kt=kt, bias_ap=bias_ap):
                    wg(s, pam)
                    if br is not None and kt == 0:
                        wg(s, br)
                    s.activation(C.RX[:, pcols:pcols + 1024],
                                 C.ps[:, sb * 512: sb * 512 + 1024],
                                 AF.Exp, bias=bias_ap).then_inc(C.act.h, 1)
                C.emit("scalar", fexp)
                p_ready = (C.act, a_e)
                if pmask_fn is not None:
                    p_ready = pmask_fn(h, qb, kt, pcols, p_ready)
                gd23 = C.snap_guards([2, 3]) if not av_started else [None, None]
                C.pe.pre(1)
                def fav(t, h=h, kt=kt, pcols=pcols, p_ready=p_ready,
                        gd23=gd23, avs=av_started, last=(kt == kt_hi - 1)):
                    wg(t, p_ready)
                    if not avs:
                        wg(t, gd23[0])
                        wg(t, gd23[1])
                    li = None
                    for half in range(2):
                        li = t.matmul(
                            C.ps[0:65, 1024 + half * 512: 1536 + half * 512],
                            v_lhsT_fn(h, kt),
                            C.RX[:, pcols + half * 512: pcols + half * 512 + 512],
                            start=(not avs), stop=last)
                    li.then_inc(C.pe.h, 1)
                C.emit("tensor", fav)
                av_started = True
                C.guard_bank(sb, C.act, a_e)
                C.guard_bank(sb + 1, C.act, a_e)
            pe_av_done = C.pe.n
            v_n = C.ve.pre(1)
            def fnorm1(v, pe_av_done=pe_av_done):
                v.wait_ge(C.pe.h, pe_av_done)
                if den_eps is not None:
                    v.tensor_scalar(C.RM[64:65, 1024:2048],
                                    C.ps[64:65, 1024:2048], den_eps, None,
                                    ALU.add)
                    v.reciprocal(C.RM[64:65, 1024:2048],
                                 C.RM[64:65, 1024:2048])
                else:
                    v.reciprocal(C.RM[64:65, 1024:2048],
                                 C.ps[64:65, 1024:2048])
                v.tensor_copy(C.RX[64:65, 4096:5120], C.RM[64:65, 1024:2048])
                v.tensor_tensor(C.RX[64:65, 5120:6144], C.RM[64:65, 1024:2048],
                                C.RX[64:65, 4096:5120], ALU.subtract)\
                    .then_inc(C.ve.h, 1)
            C.emit("vector", fnorm1)
            for half in range(2):
                gd6 = C.snap_guards([6])
                pe_n = C.pe.pre(1)
                def fnorm2(t, v_n=v_n, gd6=gd6, half=half):
                    t.wait_ge(C.ve.h, v_n)
                    wg(t, gd6[0])
                    t.matmul(C.ps[0:64, 3072:3584], C.ones_bf[64:65, 0:64],
                             C.RX[64:65, 4096 + half * 512: 4608 + half * 512],
                             start=True, stop=False)
                    t.matmul(C.ps[0:64, 3072:3584], C.ones_bf[64:65, 0:64],
                             C.RX[64:65, 5120 + half * 512: 5632 + half * 512],
                             start=False, stop=True).then_inc(C.pe.h, 1)
                C.emit("tensor", fnorm2)
                C.ve.pre(2)
                def fnorm3(v, pe_n=pe_n, h=h, qb=qb, base=base, half=half):
                    v.wait_ge(C.pe.h, pe_n)
                    v.tensor_copy(C.RY[0:64, ry_off:ry_off + 512],
                                  C.ps[0:64, 3072:3584])
                    dst = o_dst_fn(h, qb)[:, half * 512:half * 512 + 512] \
                        if base == 0 else \
                        C.RX[0:64, 6656 + half * 512: 7168 + half * 512]
                    v.tensor_tensor(
                        dst, C.ps[0:64, 1024 + half * 512: 1536 + half * 512],
                        C.RY[0:64, ry_off:ry_off + 512], ALU.mult)\
                        .then_inc(C.ve.h, 2)
                C.emit("vector", fnorm3)
                C.guard_bank(6, C.ve, C.ve.n)
            C.guard_bank(2, C.ve, C.ve.n)
            C.guard_bank(3, C.ve, C.ve.n)
            if base == 64:
                v_m = C.ve.n
                gds = C.snap_guards([4, 5])
                pe_m = C.pe.pre(1)
                def fmove(t, v_m=v_m, gds=gds):
                    t.wait_ge(C.ve.h, v_m)
                    wg(t, gds[0])
                    wg(t, gds[1])
                    last = None
                    for half in range(2):
                        last = t.matmul(
                            C.ps[64:128, 2048 + half * 512: 2560 + half * 512],
                            C.ident[0:64, 0:64],
                            C.RX[0:64, 6656 + half * 512: 7168 + half * 512],
                            start=True, stop=True)
                    last.then_inc(C.pe.h, 1)
                C.emit("tensor", fmove)
                C.ve.pre(1)
                def fmv2(v, pe_m=pe_m, h=h, qb=qb):
                    v.wait_ge(C.pe.h, pe_m)
                    v.tensor_copy(o_dst_fn(h, qb), C.ps[64:128, 2048:3072])\
                        .then_inc(C.ve.h, 1)
                C.emit("vector", fmv2)
                C.guard_bank(4, C.ve, C.ve.n)
                C.guard_bank(5, C.ve, C.ve.n)
    return (C.ve, C.ve.n)



def emit_l0a(C, W, stop=None):
    RAb = C.RAb
    s_wa = C.sem("l0wa")
    s_wb = C.sem("l0wb")
    cq = lambda k: RAb[:, k * 2048: (k + 1) * 2048]
    ckv = lambda k: RAb[:, 8192 + k * 2048: 8192 + (k + 1) * 2048]
    h_t = lambda k, c: C.RC[:, k * 2048 + c * 512: k * 2048 + c * 512 + 512]

    t_wdq = load_wmat(C, C.RW, 0, W["l0_wdq"], 8, 512, s_wa)
    t_wdkv = load_wmat(C, C.RW, 4096, W["l0_wdkv"], 8, 512, s_wb)
    inner_done = (C.ve, C.ve.n)

    mm_chain(C,
             lambda k, m: C.RW[:, k * 512 + m * 128: k * 512 + m * 128 + 128],
             h_t,
             evac_copy(C, lambda m, c: cq(m)[:, c * 512:(c + 1) * 512]),
             m_tiles=4, k_list=range(8), n_chunks=4,
             pre_waits=((s_wa, t_wdq), inner_done))
    pe_cq = (C.pe, C.pe.n)
    mm_chain(C,
             lambda k, m: C.RW[:, 4096 + k * 512 + m * 128:
                               4096 + k * 512 + m * 128 + 128],
             h_t,
             evac_copy(C, lambda m, c: ckv(m)[:, c * 512:(c + 1) * 512]),
             m_tiles=4, k_list=range(8), n_chunks=4,
             pre_waits=((s_wb, t_wdkv),))
    pe_ckv = (C.pe, C.pe.n)

    if stop == "cq":
        return
    t_upv = load_wmat(C, C.RW, 0, W["l0_wupv"], 4, 1024, s_wa, waits=(pe_cq,))
    t_upq = load_wmat(C, C.RW, 4096, W["l0_wupq"], 4, 1024, s_wb,
                      waits=(pe_ckv,))
    t_upk = load_wmat(C, C.RW, 8192, W["l0_wupk"], 4, 1024, s_wb)

    def v_chain(vhalf, extra_pre):
        po_t = C.po.pre(1)
        def fones(g, extra_pre=extra_pre):
            for w in extra_pre:
                wg(g, w)
            g.memset(RAb[:, 16384:24704], 1.0).then_inc(C.po.h, 1)
        C.emit("gpsimd", fones)
        def v_evac(tt, c, b, pe_t):
            cnt = C.ve.pre(1)
            def f(v, tt=tt, b=b, pe_t=pe_t, po_t=po_t):
                v.wait_ge(C.pe.h, pe_t)
                v.wait_ge(C.po.h, po_t)
                dst = RAb[:, 16384 + tt * 520: 16384 + (tt + 1) * 520]\
                    .rearrange("p (h e) -> p h e", h=8)[:, :, 0:64]
                v.tensor_copy(dst, C.bank(b).rearrange("p (h e) -> p h e", h=8))\
                    .then_inc(C.ve.h, 1)
            C.emit("vector", f)
            return (C.ve, cnt)
        mm_chain(C,
                 lambda k, m: ckv(k)[:, m * 128: m * 128 + 128],
                 lambda k, c: C.RW[:, k * 1024 + vhalf * 512:
                                   k * 1024 + vhalf * 512 + 512],
                 v_evac, m_tiles=16, k_list=range(4), n_chunks=1,
                 pre_waits=((s_wa, t_upv),) + tuple(extra_pre),
                 banks=(0, 1, 2, 3))

    def v_lhsT(h, kt):
        hh = h % 8
        return RAb[:, 16384 + kt * 520 + hh * 65:
                   16384 + kt * 520 + hh * 65 + 65]

    def o_dst(h, qb):
        b = 64 * (h % 2)
        return C.RC[b:b + 64,
                    (h // 2) * 2048 + qb * QB: (h // 2) * 2048 + qb * QB + QB]

    def qT_fn(h):
        b = 64 * (h % 2)
        return RAb[b:b + 64, 24704:24704 + 2048]

    def kT_fn(h):
        b = 64 * (h % 2)
        return RAb[b:b + 64, 26752:26752 + 2048]

    for half in range(2):
        if half == 0:
            v_chain(0, ())
        else:
            v_chain(1, ((C.pe, C.pe.n),))
        for grp in range(4):
            gh = half * 8 + grp * 2
            dtile = gh // 2
            mm_chain(C,
                     lambda k, m, dtile=dtile:
                     C.RW[:, 4096 + k * 1024 + dtile * 128:
                          4096 + k * 1024 + dtile * 128 + 128],
                     lambda k, c: cq(k)[:, c * 512: c * 512 + 512],
                     evac_copy(C, lambda m, c:
                               RAb[:, 24704 + c * 512: 24704 + c * 512 + 512],
                               scale=0.125),
                     m_tiles=1, k_list=range(4), n_chunks=4,
                     pre_waits=((s_wb, t_upk),), banks=(4, 5))
            mm_chain(C,
                     lambda k, m, dtile=dtile:
                     C.RW[:, 8192 + k * 1024 + dtile * 128:
                          8192 + k * 1024 + dtile * 128 + 128],
                     lambda k, c: ckv(k)[:, c * 512: c * 512 + 512],
                     evac_copy(C, lambda m, c:
                               RAb[:, 26752 + c * 512: 26752 + c * 512 + 512]),
                     m_tiles=1, k_list=range(4), n_chunks=4, banks=(4, 5))
            for slot in (24704, 26752):
                a_prev = (C.act, C.act.n)
                gds = C.snap_guards([2, 3, 4, 5])
                pe_sw = C.pe.pre(1)
                def f_sw(t, slot=slot, gds=gds, a_prev=a_prev):
                    wg(t, a_prev)
                    for c4 in range(4):
                        wg(t, gds[c4])
                    last = None
                    for c4 in range(4):
                        last = t.matmul(C.bank(2 + c4), C.pswap,
                                        RAb[:, slot + c4 * 512:
                                            slot + c4 * 512 + 512],
                                        start=True, stop=True)
                    last.then_inc(C.pe.h, 1)
                C.emit("tensor", f_sw)
                C.ve.pre(1)
                def f_t1(v, slot=slot, pe_sw=pe_sw):
                    v.wait_ge(C.pe.h, pe_sw)
                    v.tensor_tensor(RAb[:, slot:slot + 2048],
                                    RAb[:, slot:slot + 2048],
                                    C.ROPE[:, 0:2048], ALU.mult)\
                        .then_inc(C.ve.h, 1)
                C.emit("vector", f_t1)
                for c4 in range(4):
                    C.ve.pre(2)
                    def f_t2(v, slot=slot, c4=c4):
                        v.tensor_tensor(C.RY[:, 0:512], C.bank(2 + c4),
                                        C.ROPE[:, 2048 + c4 * 512:
                                               2048 + c4 * 512 + 512],
                                        ALU.mult)
                        v.tensor_tensor(RAb[:, slot + c4 * 512:
                                            slot + c4 * 512 + 512],
                                        RAb[:, slot + c4 * 512:
                                            slot + c4 * 512 + 512],
                                        C.RY[:, 0:512], ALU.add)\
                            .then_inc(C.ve.h, 2)
                    C.emit("vector", f_t2)
                    C.guard_bank(2 + c4, C.ve, C.ve.n)
            rope_done = (C.ve, C.ve.n)
            C.guard_bank(0, *rope_done)
            C.guard_bank(1, *rope_done)
            if stop == "qkv":
                return
            emit_attn64(C, [gh, gh + 1], qT_fn, kT_fn, v_lhsT, o_dst,
                        shift="passA")
            if stop == "attn1":
                return
    if stop == "attn":
        return

    attn_pe = (C.pe, C.pe.n)
    attn_ve = (C.ve, C.ve.n)
    for halfw in range(2):
        t_wo = load_wmat(C, C.RW, halfw * 4096, W["l0_wout_h%d" % halfw],
                         8, 512, s_wa if halfw == 0 else s_wb,
                         waits=(attn_pe,))
        mm_chain(C,
                 lambda k, m, halfw=halfw:
                 C.RW[:, halfw * 4096 + k * 512 + m * 128:
                      halfw * 4096 + k * 512 + m * 128 + 128],
                 lambda k, c: C.RC[:, k * 2048 + c * 512:
                                   k * 2048 + c * 512 + 512],
                 evac_resid(C,
                            lambda m, c, halfw=halfw:
                            RAb[:, (halfw * 4 + m) * 2048 + c * 512:
                                (halfw * 4 + m) * 2048 + c * 512 + 512],
                            lambda m, c, halfw=halfw:
                            C.RB[:, (halfw * 4 + m) * 2048 + c * 512:
                                 (halfw * 4 + m) * 2048 + c * 512 + 512]),
                 m_tiles=4, k_list=range(8), n_chunks=4,
                 pre_waits=((s_wa if halfw == 0 else s_wb, t_wo), attn_ve),
                 banks=(0, 1, 2, 3))
    return (C.ve, C.ve.n)



def build_l0a_test(stage="full", grps=None):
    nc = bass.Bass("TRN2", target_bir_lowering=False)
    xin = nc.dram_tensor("xin", [ND, 128, T], F32, kind="ExternalInput").ap()
    H = {}
    for nm, shp in [("l0_wdq", [D, 512]), ("l0_wdkv", [D, 512]),
                    ("l0_wupq", [512, D]), ("l0_wupk", [512, D]),
                    ("l0_wupv", [512, D]),
                    ("l0_wout_h0", [D, 512]), ("l0_wout_h1", [D, 512]),
                    ("l0_cols", [128, 16]),
                    ("ropeC", [128, T]), ("ropeS", [128, T]),
                    ("consts", [128, 384])]:
        H[nm] = nc.dram_tensor(nm, shp, F32, kind="ExternalInput").ap()
    s_out = nc.dram_tensor("s_out", [ND, 128, T], F32, kind="ExternalOutput").ap()
    s2_out = nc.dram_tensor("s2_out", [ND, 128, T], F32,
                            kind="ExternalOutput").ap()

    es = ExitStack()
    with es:
        C = Ctx(nc, es)
        blk = es.enter_context(nc.Block())
        s_x = C.sem("x")
        s_k = C.sem("k")
        s_o = C.sem("out")
        for k in range(ND):
            e_dma(C, "sync", C.RA[:, k * 2048:(k + 1) * 2048], xin[k], s_x)
        e_dma(C, "gpsimd", C.CONST[:], H["consts"], s_k)
        e_dma(C, "gpsimd", C.ROPE[:, 0:2048], H["ropeC"], s_k)
        e_dma(C, "gpsimd", C.ROPE[:, 2048:4096], H["ropeS"], s_k)
        e_dma(C, "gpsimd", C.RCOL[:, 0:16], H["l0_cols"], s_k)
        konst_ready = [(s_k, s_k.n), (s_x, s_x.n)]
        ln1 = emit_ln(C, lambda k: C.RA[:, k * 2048:(k + 1) * 2048],
                      lambda k: C.RB[:, k * 2048:(k + 1) * 2048],
                      pre_waits=tuple(konst_ready))
        ln2 = emit_ln(C, lambda k: C.RB[:, k * 2048:(k + 1) * 2048],
                      lambda k: C.RC[:, k * 2048:(k + 1) * 2048],
                      gain_col=lambda k: C.RCOL[:, k:k + 1],
                      bias_col=lambda k: C.RCOL[:, 8 + k:8 + k + 1],
                      pre_waits=(ln1,), sq_region=C.RAb[:, 0:4096],
                      in_bf16=True)
        if stage == "ln2":
            for k in range(ND):
                e_dma(C, "gpsimd", s_out[k], C.RB[:, k * 2048:(k + 1) * 2048],
                      s_o, waits=(ln1,))
            for k in range(ND):
                e_dma(C, "gpsimd", s2_out[k], C.RC[:, k * 2048:(k + 1) * 2048],
                      s_o, waits=(ln2,))
        elif stage == "cq":
            emit_l0a(C, H, stop="cq")
            done = (C.act, C.act.n)
            for k in range(ND):
                e_dma(C, "gpsimd", s_out[k],
                      C.RAb[:, k * 2048:(k + 1) * 2048], s_o, waits=(done,))
        elif stage == "qkv":
            emit_l0a(C, H, stop="qkv")
            done = (C.ve, C.ve.n)
            e_dma(C, "gpsimd", s_out[0], C.RAb[:, 24704:26752], s_o,
                  waits=(done,))
            e_dma(C, "gpsimd", s_out[1], C.RAb[:, 26752:28800], s_o)
            e_dma(C, "gpsimd", s_out[2][:, 0:520],
                  C.RAb[:, 16384:16904], s_o)
            e_dma(C, "gpsimd", s_out[3][:, 0:520],
                  C.RAb[:, 16384 + 520:16384 + 1040], s_o)
        elif stage == "attn1":
            emit_l0a(C, H, stop="attn1")
            done = (C.ve, C.ve.n)
            e_dma(C, "gpsimd", s_out[0], C.RC[:, 0:2048], s_o, waits=(done,))
            e_dma(C, "gpsimd", s_out[1][:, 0:17], C.RM[:, 0:17], s_o)
        elif stage == "attn":
            emit_l0a(C, H, stop="attn")
            done = (C.ve, C.ve.n)
            for k in range(ND):
                e_dma(C, "gpsimd", s_out[k],
                      C.RC[:, k * 2048:(k + 1) * 2048], s_o, waits=(done,))
        else:
            s_done = emit_l0a(C, H)
            for k in range(ND):
                e_dma(C, "gpsimd", s_out[k],
                      C.RAb[:, k * 2048:(k + 1) * 2048], s_o, waits=(s_done,))
        def ffin(g, n=s_o.n):
            g.wait_ge(s_o.h, n)
        C.emit("gpsimd", ffin)
        C.finalize(blk)
    return nc


_BUILD_CACHE = {}


def kernel(**inputs):
    G = stage_shared(inputs)
    stage_l0f(inputs, G)
    stage_l1(inputs, G)
    stage_l2(inputs, G)
    stage_l3(inputs, G)
    xs = stage_x(inputs)
    if "prog" not in _BUILD_CACHE:
        _BUILD_CACHE["prog"] = build_all()
    nc = _BUILD_CACHE["prog"]
    in_maps = []
    for b in range(8):
        m = dict(G, xin=xs[b])
        m["l3_pos"] = stage_l3_pos(inputs, b)
        in_maps.append(m)
    res = run_bass_kernel_spmd(nc, in_maps, core_ids=list(range(8)))
    cur = [np.asarray(res.results[b]["xout"]) for b in range(8)]
    out = np.stack([cur[b].reshape(1024, 2048).T for b in range(8)], 0)
    return np.ascontiguousarray(out.astype(np.float32))



def emit_attn256(C, heads, qT_fn, kT_fn, vc_lhsT, o_dst_fn,
                 pre_waits=()):
    QBS = 512
    nqb = T // QBS
    nqt = QBS // 128
    for h in heads:
        for qb in range(nqb):
            for qt in range(nqt):
                gds = C.snap_guards([0, 1, 2, 3])
                pw = tuple(pre_waits) if (qb == 0 and qt == 0) else ()
                pe_t = C.pe.pre(1)
                def fpa(t, h=h, qb=qb, qt=qt, gds=gds, pw=pw):
                    for w_ in pw:
                        wg(t, w_)
                    last = None
                    for ci in range(4):
                        wg(t, gds[ci])
                        for k2 in range(2):
                            last = t.matmul(
                                C.bank(ci),
                                qT_fn(h, k2)[:, qb * QBS + qt * 128:
                                             qb * QBS + qt * 128 + 128],
                                kT_fn(k2)[:, ci * 512: ci * 512 + 512],
                                start=(k2 == 0), stop=(k2 == 1))
                    last.then_inc(C.pe.h, 1)
                C.emit("tensor", fpa)
                C.ve.pre(1)
                def fmax(v, qt=qt, pe_t=pe_t):
                    v.wait_ge(C.pe.h, pe_t)
                    v.reduce_max(C.RM[:, qt:qt + 1], C.ps[:, 0:2048], AX.X)\
                        .then_inc(C.ve.h, 1)
                C.emit("vector", fmax)
                for ci in range(4):
                    C.guard_bank(ci, C.ve, C.ve.n)
            vbk = C.ve.pre(1)
            v_t = C.ve.pre(1)
            def fblk(v, nqt=nqt, vbk=vbk):
                v.reduce_max(C.RM[:, 8:9], C.RM[:, 0:nqt], AX.X)\
                    .then_inc(C.ve.h, 1)
                v.wait_ge(C.ve.h, vbk)
                v.tensor_copy(C.RX[:, 6144:6145], C.RM[:, 8:9])\
                    .then_inc(C.ve.h, 1)
            C.emit("vector", fblk)
            gd6 = C.snap_guards([6])
            pe_r = C.pe.pre(1)
            def frow(t, v_t=v_t, gd6=gd6):
                t.wait_ge(C.ve.h, v_t)
                wg(t, gd6[0])
                t.matmul(C.ps[0:1, 3072:3200], C.RX[:, 6144:6145], C.ident,
                         start=True, stop=True).then_inc(C.pe.h, 1)
            C.emit("tensor", frow)
            vck = C.ve.pre(1)
            v_r = C.ve.pre(1)
            def fcm(v, pe_r=pe_r, vck=vck):
                v.wait_ge(C.pe.h, pe_r)
                v.reduce_max(C.RM[0:1, 9:10], C.ps[0:1, 3072:3200], AX.X)\
                    .then_inc(C.ve.h, 1)
                v.wait_ge(C.ve.h, vck)
                v.tensor_copy(C.RX[0:1, 6146:6147], C.RM[0:1, 9:10])\
                    .then_inc(C.ve.h, 1)
            C.emit("vector", fcm)
            C.guard_bank(6, C.ve, C.ve.n)
            gd6 = C.snap_guards([6])
            pe_c = C.pe.pre(1)
            def fbcc(t, v_r=v_r, gd6=gd6):
                t.wait_ge(C.ve.h, v_r)
                wg(t, gd6[0])
                t.matmul(C.ps[:, 3072:3073], C.ones_bf[0:1, :],
                         C.RX[0:1, 6146:6147], start=True, stop=True)\
                    .then_inc(C.pe.h, 1)
            C.emit("tensor", fbcc)
            a_c = C.act.pre(1)
            def fnegc(s, pe_c=pe_c):
                s.wait_ge(C.pe.h, pe_c)
                s.activation(C.RM[:, 16:17], C.ps[:, 3072:3073], AF.Copy,
                             scale=-1.0).then_inc(C.act.h, 1)
            C.emit("scalar", fnegc)
            C.guard_bank(6, C.act, a_c)
            bias_ready = (C.act, a_c)

            av_started = False
            for kt in range(NT):
                sb = 4 if kt % 2 == 0 else 5
                gds = C.snap_guards([sb])
                pe_s = C.pe.pre(1)
                def fst(t, h=h, qb=qb, kt=kt, sb=sb, gds=gds):
                    wg(t, gds[0])
                    last = None
                    for k2 in range(2):
                        last = t.matmul(
                            C.bank(sb),
                            kT_fn(k2)[:, kt * 128: kt * 128 + 128],
                            qT_fn(h, k2)[:, qb * QBS: qb * QBS + QBS],
                            start=(k2 == 0), stop=(k2 == 1))
                    last.then_inc(C.pe.h, 1)
                C.emit("tensor", fst)
                pcols = (kt % 2) * 512
                a_e = C.act.pre(1)
                def fexp(s, sb=sb, pcols=pcols, pe_s=pe_s, br=bias_ready,
                         kt=kt):
                    s.wait_ge(C.pe.h, pe_s)
                    if kt == 0:
                        wg(s, br)
                    s.activation(C.RX[:, pcols:pcols + 512], C.bank(sb),
                                 AF.Exp, bias=C.RM[:, 16:17])\
                        .then_inc(C.act.h, 1)
                C.emit("scalar", fexp)
                gdav = C.snap_guards([2, 3, 6]) if not av_started \
                    else [None, None, None]
                C.pe.pre(1)
                def fav(t, kt=kt, pcols=pcols, a_e=a_e, gdav=gdav,
                        avs=av_started, last=(kt == NT - 1)):
                    t.wait_ge(C.act.h, a_e)
                    if not avs:
                        for g_ in gdav:
                            wg(t, g_)
                    p = C.RX[:, pcols:pcols + 512]
                    for mc in range(2):
                        t.matmul(C.ps[0:128, (2 + mc) * 512:(3 + mc) * 512],
                                 vc_lhsT(kt, mc), p,
                                 start=(not avs), stop=last)
                    t.matmul(C.ps[0:1, 3072:3584], vc_lhsT(kt, 2), p,
                             start=(not avs), stop=last).then_inc(C.pe.h, 1)
                C.emit("tensor", fav)
                av_started = True
                C.guard_bank(sb, C.act, a_e)
            pe_av = C.pe.n
            v_n = C.ve.pre(1)
            def fnorm1(v, pe_av=pe_av):
                v.wait_ge(C.pe.h, pe_av)
                v.reciprocal(C.RM[0:1, 1024:1536], C.ps[0:1, 3072:3584])
                v.tensor_copy(C.RX[0:1, 4096:4608], C.RM[0:1, 1024:1536])
                v.tensor_tensor(C.RX[0:1, 4608:5120], C.RM[0:1, 1024:1536],
                                C.RX[0:1, 4096:4608], ALU.subtract)\
                    .then_inc(C.ve.h, 1)
            C.emit("vector", fnorm1)
            gds = C.snap_guards([4, 5])
            pe_n = C.pe.pre(1)
            def fnorm2(t, v_n=v_n, gds=gds):
                t.wait_ge(C.ve.h, v_n)
                wg(t, gds[0])
                t.matmul(C.ps[:, 2048:2560], C.ones_bf[0:1, :],
                         C.RX[0:1, 4096:4608], start=True, stop=False)
                t.matmul(C.ps[:, 2048:2560], C.ones_bf[0:1, :],
                         C.RX[0:1, 4608:5120], start=False, stop=True)\
                    .then_inc(C.pe.h, 1)
            C.emit("tensor", fnorm2)
            v_m = C.ve.pre(1)
            def fnorm3(v, pe_n=pe_n, h=h, qb=qb):
                v.wait_ge(C.pe.h, pe_n)
                v.tensor_copy(C.RY[:, 0:512], C.ps[:, 2048:2560])
                v.tensor_tensor(o_dst_fn(h, 0, qb),
                                C.ps[0:128, 1024:1536],
                                C.RY[:, 0:512], ALU.mult)
                v.tensor_tensor(o_dst_fn(h, 1, qb),
                                C.ps[0:128, 1536:2048],
                                C.RY[:, 0:512], ALU.mult)\
                    .then_inc(C.ve.h, 1)
            C.emit("vector", fnorm3)
            C.guard_bank(2, C.ve, C.ve.n)
            C.guard_bank(3, C.ve, C.ve.n)
            C.guard_bank(4, C.ve, C.ve.n)
            C.guard_bank(6, C.ve, C.ve.n)
    return (C.ve, C.ve.n)



def emit_l0f(C, W, spill_done=None, stop=None):
    RAb = C.RAb
    s_wa = C.sem("fwa")
    s_wb = C.sem("fwb")
    s_wc = C.sem("fwc")
    nx1_t = lambda k: C.RC[:, k * 2048:(k + 1) * 2048]
    rms1 = emit_ln(C, lambda k: C.RB[:, k * 2048:(k + 1) * 2048],
                   nx1_t, rms=True, eps=1e-6,
                   gain_col=lambda k: C.RCOL[:, 16 + k:17 + k],
                   sq_region=RAb[:, 0:4096], in_bf16=True,
                   pre_waits=(spill_done,) if spill_done else ())
    pe0 = (C.pe, C.pe.n)
    t_kc = load_wmat(C, C.RW, 0, W["l0f_wkc"], 8, 256, s_wa, waits=(pe0,))
    t_vc = load_wmat(C, C.RW, 2048, W["l0f_wvc"], 8, 256, s_wa)
    mm_chain(C,
             lambda k, m: C.RW[:, k * 256 + m * 128: k * 256 + m * 128 + 128],
             lambda k, c: C.RC[:, k * 2048 + c * 512: k * 2048 + c * 512 + 512],
             evac_copy(C, lambda m, c:
                       RAb[:, m * 2048 + c * 512: m * 2048 + c * 512 + 512]),
             m_tiles=2, k_list=range(8), n_chunks=4,
             pre_waits=((s_wa, t_kc), rms1), banks=(0, 1))
    po_t = C.po.pre(1)
    def fonesv(g, spill_done=spill_done):
        if spill_done is not None:
            wg(g, spill_done)
        g.memset(RAb[:, 4096:8224], 1.0).then_inc(C.po.h, 1)
    C.emit("gpsimd", fonesv)
    def vc_evac(tt, c, b, pe_t):
        cnt = C.ve.pre(1)
        def f(v, tt=tt, b=b, pe_t=pe_t, po_t=po_t):
            v.wait_ge(C.pe.h, pe_t)
            v.wait_ge(C.po.h, po_t)
            v.tensor_copy(RAb[:, 4096 + tt * 258: 4096 + tt * 258 + 256],
                          C.bank(b, w=256)).then_inc(C.ve.h, 1)
        C.emit("vector", f)
        return (C.ve, cnt)
    mm_chain(C,
             lambda k, m: C.RC[:, k * 2048 + m * 128: k * 2048 + m * 128 + 128],
             lambda k, c: C.RW[:, 2048 + k * 256: 2048 + k * 256 + 256],
             vc_evac, m_tiles=16, k_list=range(8), n_chunks=1,
             pre_waits=((s_wa, t_vc),), banks=(2, 3))
    if stop == "kcvc":
        return None

    def kT_fn(k2):
        return RAb[:, k2 * 2048: (k2 + 1) * 2048]

    def qT_fn(h, k2):
        return RAb[:, 8224 + k2 * 2048: 8224 + (k2 + 1) * 2048]

    def vc_lhsT(kt, mc):
        base = 4096 + kt * 258
        if mc == 2:
            return RAb[:, base + 256: base + 257]
        return RAb[:, base + mc * 128: base + (mc + 1) * 128]

    def o_dst(h, mc, qb):
        tile = 2 * h + mc
        return RAb[:, 12320 + tile * 2048 + qb * 512:
                   12320 + tile * 2048 + qb * 512 + 512]

    wq_guard = (pe0,)
    for h in range(4):
        t_wq = load_wmat(C, C.RW, 4096, W["l0f_wq"][:, 256 * h:256 * h + 256],
                         8, 256, s_wb, waits=wq_guard)
        mm_chain(C,
                 lambda k, m: C.RW[:, 4096 + k * 256 + m * 128:
                                   4096 + k * 256 + m * 128 + 128],
                 lambda k, c: C.RC[:, k * 2048 + c * 512:
                                   k * 2048 + c * 512 + 512],
                 evac_copy(C, lambda m, c:
                           RAb[:, 8224 + m * 2048 + c * 512:
                               8224 + m * 2048 + c * 512 + 512],
                           scale=1.0 / 16.0),
                 m_tiles=2, k_list=range(8), n_chunks=4,
                 pre_waits=((s_wb, t_wq),), banks=(4, 5))
        wq_guard = ((C.pe, C.pe.n),)
        emit_attn256(C, [h], qT_fn, kT_fn, vc_lhsT, o_dst,
                     pre_waits=((C.act, C.act.n),))
    if stop == "attn":
        return None

    attn_pe = (C.pe, C.pe.n)
    attn_ve = (C.ve, C.ve.n)
    for halfw in range(2):
        t_wo = load_wmat(C, C.RW, 8192 if halfw == 0 else 0,
                         W["l0f_wo_h%d" % halfw], 8, 512,
                         s_wc if halfw == 0 else s_wa, waits=(attn_pe,))
        base = 8192 if halfw == 0 else 0
        mm_chain(C,
                 lambda k, m, base=base:
                 C.RW[:, base + k * 512 + m * 128: base + k * 512 + m * 128 + 128],
                 lambda k, c: RAb[:, 12320 + k * 2048 + c * 512:
                                  12320 + k * 2048 + c * 512 + 512],
                 evac_resid(C,
                            lambda m, c, halfw=halfw:
                            C.RC[:, (halfw * 4 + m) * 2048 + c * 512:
                                 (halfw * 4 + m) * 2048 + c * 512 + 512],
                            lambda m, c, halfw=halfw:
                            C.RB[:, (halfw * 4 + m) * 2048 + c * 512:
                                 (halfw * 4 + m) * 2048 + c * 512 + 512]),
                 m_tiles=4, k_list=range(8), n_chunks=4,
                 pre_waits=((s_wc if halfw == 0 else s_wa, t_wo), attn_ve),
                 banks=(0, 1, 2, 3))
    x2_done = (C.ve, C.ve.n)
    if stop == "x2":
        return x2_done

    rms2 = emit_ln(C, lambda k: C.RC[:, k * 2048:(k + 1) * 2048],
                   lambda k: C.RB[:, k * 2048:(k + 1) * 2048],
                   rms=True, eps=1e-6,
                   gain_col=lambda k: C.RCOL[:, 24 + k:25 + k],
                   sq_region=RAb[:, 0:4096], in_bf16=True,
                   pre_waits=(x2_done,))
    if stop == "nx2":
        return rms2
    x2sp = C.nc.dram_tensor("x2sp", [ND, 128, T], BF16, kind="Internal").ap()
    C.x2sp = x2sp
    s_sp = C.sem("x2sp")
    for k in range(ND):
        e_dma(C, "sync", x2sp[k], C.RC[:, k * 2048:(k + 1) * 2048], s_sp,
              waits=(rms2,) if k == 0 else ())
    x2_spilled = (s_sp, s_sp.n)

    t_wr = load_wmat(C, C.RW, 12160, W["l0f_wr"], 8, 16, s_wb,
                     waits=((C.pe, C.pe.n),))
    def router_evac(m, c, b, pe_t):
        cnt0 = C.ve.pre(1)
        cnt = C.ve.pre(1)
        def f(v, c=c, b=b, pe_t=pe_t, cnt0=cnt0):
            v.wait_ge(C.pe.h, pe_t)
            v.tensor_scalar(C.RY[0:8, c * 512:(c + 1) * 512],
                            C.bank(b, rows=8), C.RCOL[0:8, 32:33], None,
                            ALU.add)
            v.tensor_copy(RAb[0:8, c * 512:(c + 1) * 512],
                          C.RY[0:8, c * 512:(c + 1) * 512]).then_inc(C.ve.h, 1)
            v.wait_ge(C.ve.h, cnt0)
            v.tensor_tensor(RAb[0:8, 2048 + c * 512: 2048 + (c + 1) * 512],
                            C.RY[0:8, c * 512:(c + 1) * 512],
                            RAb[0:8, c * 512:(c + 1) * 512], ALU.subtract)\
                .then_inc(C.ve.h, 1)
        C.emit("vector", f)
        return (C.ve, cnt)
    mm_chain(C,
             lambda k, m: C.RW[:, 12160 + (k % 8) * 16 + (k // 8) * 8:
                               12160 + (k % 8) * 16 + (k // 8) * 8 + 8],
             lambda k, c: C.RB[:, (k % 8) * 2048 + c * 512:
                               (k % 8) * 2048 + c * 512 + 512],
             router_evac,
             m_tiles=1, k_list=range(16), n_chunks=4,
             pre_waits=((s_wb, t_wr), rms2, x2_spilled), banks=(4, 5),
             m_rows=8)
    logits_done = (C.ve, C.ve.n)

    for tt in range(NT):
        gd = C.snap_guards([0])
        pe_t = C.pe.pre(1)
        def ftr(t, tt=tt, gd=gd, logits_done=logits_done):
            wg(t, logits_done)
            wg(t, gd[0])
            t.matmul(C.bank(0, w=8), RAb[0:8, tt * 128:(tt + 1) * 128],
                     C.ident[0:8, 0:8], start=True, stop=False)
            t.matmul(C.bank(0, w=8), RAb[0:8, 2048 + tt * 128:
                                         2048 + (tt + 1) * 128],
                     C.ident[0:8, 0:8], start=False, stop=True)\
                .then_inc(C.pe.h, 1)
        C.emit("tensor", ftr)
        v_t = C.ve.pre(1)
        def fsm1(v, pe_t=pe_t):
            v.wait_ge(C.pe.h, pe_t)
            v.reduce_max(C.RM[:, 20:21], C.bank(0, w=8), AX.X)\
                .then_inc(C.ve.h, 1)
        C.emit("vector", fsm1)
        a_m = C.act.pre(1)
        a_t = C.act.pre(1)
        def fsm2(s, v_t=v_t, a_m=a_m):
            s.wait_ge(C.ve.h, v_t)
            s.activation(C.RM[:, 21:22], C.RM[:, 20:21], AF.Copy, scale=-1.0)\
                .then_inc(C.act.h, 1)
            s.wait_ge(C.act.h, a_m)
            s.activation(C.RY[:, 0:8], C.bank(0, w=8), AF.Exp,
                         bias=C.RM[:, 21:22], accum_out=C.RM[:, 22:23])\
                .then_inc(C.act.h, 1)
        C.emit("scalar", fsm2)
        vs = [C.ve.pre(1) for _ in range(6)]
        v_u = vs[-1]
        def fsm3(v, a_t=a_t, vs=vs):
            v.wait_ge(C.act.h, a_t)
            v.reciprocal(C.RM[:, 23:24], C.RM[:, 22:23]).then_inc(C.ve.h, 1)
            v.wait_ge(C.ve.h, vs[0])
            v.tensor_scalar(C.RY[:, 8:16], C.RY[:, 0:8], C.RM[:, 23:24], None,
                            ALU.mult).then_inc(C.ve.h, 1)
            v.wait_ge(C.ve.h, vs[1])
            v.reduce_max(C.RM[:, 24:25], C.RY[:, 8:16], AX.X)\
                .then_inc(C.ve.h, 1)
            v.wait_ge(C.ve.h, vs[2])
            v.tensor_scalar(C.RY[:, 16:24], C.RY[:, 8:16], C.RM[:, 24:25],
                            None, ALU.is_ge).then_inc(C.ve.h, 1)
            v.wait_ge(C.ve.h, vs[3])
            v.scalar_tensor_tensor(C.RY[:, 24:32], C.RY[:, 16:24], -2.0,
                                   C.RY[:, 8:16], ALU.mult, ALU.add)\
                .then_inc(C.ve.h, 1)
            v.wait_ge(C.ve.h, vs[4])
            v.reduce_max(C.RM[:, 25:26], C.RY[:, 24:32], AX.X)\
                .then_inc(C.ve.h, 1)
        C.emit("vector", fsm3)
        a_u = C.act.pre(1)
        def fsm4(s, v_u=v_u):
            s.wait_ge(C.ve.h, v_u)
            s.activation(C.RM[:, 26:27], C.RM[:, 24:25], AF.Exp)
            s.activation(C.RM[:, 27:28], C.RM[:, 25:26], AF.Exp)\
                .then_inc(C.act.h, 1)
        C.emit("scalar", fsm4)
        vs5 = [C.ve.pre(1) for _ in range(5)]
        def fsm5(v, a_u=a_u, tt=tt, vs5=vs5):
            v.wait_ge(C.act.h, a_u)
            v.tensor_tensor(C.RM[:, 28:29], C.RM[:, 26:27], C.RM[:, 27:28],
                            ALU.add).then_inc(C.ve.h, 1)
            v.wait_ge(C.ve.h, vs5[0])
            v.reciprocal(C.RM[:, 29:30], C.RM[:, 28:29]).then_inc(C.ve.h, 1)
            v.wait_ge(C.ve.h, vs5[1])
            v.tensor_tensor(C.RM[:, 30:31], C.RM[:, 26:27], C.RM[:, 29:30],
                            ALU.mult)
            v.tensor_tensor(C.RM[:, 31:32], C.RM[:, 27:28], C.RM[:, 29:30],
                            ALU.mult).then_inc(C.ve.h, 1)
            v.wait_ge(C.ve.h, vs5[2])
            v.tensor_scalar(C.RY[:, 32:40], C.RY[:, 24:32], C.RM[:, 25:26],
                            None, ALU.is_ge)
            v.tensor_scalar(C.RY[:, 40:48], C.RY[:, 16:24], C.RM[:, 30:31],
                            None, ALU.mult).then_inc(C.ve.h, 1)
            v.wait_ge(C.ve.h, vs5[3])
            v.scalar_tensor_tensor(C.RX[:, 6656 + tt * 8: 6664 + tt * 8],
                                   C.RY[:, 32:40], C.RM[:, 31:32],
                                   C.RY[:, 40:48], ALU.mult, ALU.add)\
                .then_inc(C.ve.h, 1)
        C.emit("vector", fsm5)
        C.guard_bank(0, C.ve, C.ve.n)
    comb_done = (C.ve, C.ve.n)
    if stop == "router":
        return comb_done
    return _emit_l0f_moe(C, W, s_wa, s_wb, s_wc, comb_done)


def _emit_l0f_moe(C, W, s_wa, s_wb, s_wc, comb_done):
    RAb = C.RAb
    RAf = C.RA
    slotA, slotB, slotC = 0, 4096, 8192
    guardA = guardB = guardC = ()
    gate_hist = {}
    git = [0]
    for ei, ename in enumerate(["s"] + list(range(8))):
        shared = (ename == "s")
        if not shared:
            e = ename
            gd = C.snap_guards([0, 1, 2, 3])
            pe_t = C.pe.pre(1)
            def ftrk(t, e=e, gd=gd, comb_done=comb_done):
                wg(t, comb_done)
                last = None
                for bk in range(4):
                    wg(t, gd[bk])
                    for j in range(4):
                        tt = bk * 4 + j
                        last = t.matmul(
                            C.ps[0:1, bk * 512 + j * 128: bk * 512 + j * 128 + 128],
                            C.RX[:, 6656 + tt * 8 + e: 6657 + tt * 8 + e],
                            C.ident, start=True, stop=True)
                last.then_inc(C.pe.h, 1)
            C.emit("tensor", ftrk)
            a_t = C.act.pre(1)
            def frowe(s, pe_t=pe_t):
                s.wait_ge(C.pe.h, pe_t)
                s.activation(C.RX[0:1, 4096:4608], C.ps[0:1, 0:512], AF.Copy)
                s.activation(C.RX[0:1, 4608:5120], C.ps[0:1, 512:1024], AF.Copy)
                s.activation(C.RX[0:1, 5120:5632], C.ps[0:1, 1024:1536], AF.Copy)
                s.activation(C.RX[0:1, 5632:6144], C.ps[0:1, 1536:2048], AF.Copy)\
                    .then_inc(C.act.h, 1)
            C.emit("scalar", frowe)
            for bk in range(4):
                C.guard_bank(bk, C.act, a_t)
            for c in range(4):
                gd = C.snap_guards([6])
                pe_b = C.pe.pre(1)
                def fbce(t, c=c, gd=gd, a_t=a_t):
                    t.wait_ge(C.act.h, a_t)
                    wg(t, gd[0])
                    t.matmul(C.ps[:, 3072:3584], C.ones_bf[0:1, :],
                             C.RX[0:1, 4096 + c * 512: 4608 + c * 512],
                             start=True, stop=True).then_inc(C.pe.h, 1)
                C.emit("tensor", fbce)
                C.ve.pre(1)
                def fcpy(v, c=c, pe_b=pe_b):
                    v.wait_ge(C.pe.h, pe_b)
                    v.tensor_copy(C.RY[:, 2048 + c * 512: 2560 + c * 512],
                                  C.ps[:, 3072:3584]).then_inc(C.ve.h, 1)
                C.emit("vector", fcpy)
                C.guard_bank(6, C.ve, C.ve.n)
            bc_ready = (C.ve, C.ve.n)
        else:
            bc_ready = comb_done
        wkey1 = "l0f_sw1" if shared else "l0f_ew1_%d" % ename
        wkey3 = "l0f_sw3" if shared else "l0f_ew3_%d" % ename
        wkey2 = "l0f_sw2" if shared else "l0f_ew2_%d" % ename
        for hm in range(2):
            t1 = load_wmat(C, C.RW, slotA, W[wkey1][:, hm * 512:(hm + 1) * 512],
                           8, 512, s_wa, waits=guardA)
            t3 = load_wmat(C, C.RW, slotB, W[wkey3][:, hm * 512:(hm + 1) * 512],
                           8, 512, s_wb, waits=guardB)
            for mt in range(4):
                for c in range(4):
                    bA = (mt * 4 + c) % 2
                    bB = 2 + (mt * 4 + c) % 2
                    gdA = C.bank_guard[bA]
                    gdB = C.bank_guard[bB]
                    pe_t = C.pe.pre(1)
                    pwa = ((s_wa, t1), (s_wb, t3), bc_ready) \
                        if (mt == 0 and c == 0) else ()
                    def fhmm(t, mt=mt, c=c, bA=bA, bB=bB, gdA=gdA, gdB=gdB,
                             pwa=pwa, slotA=slotA, slotB=slotB):
                        for w_ in pwa:
                            wg(t, w_)
                        wg(t, gdA)
                        wg(t, gdB)
                        last = None
                        for k in range(8):
                            t.matmul(C.bank(bA),
                                     C.RW[:, slotA + k * 512 + mt * 128:
                                          slotA + k * 512 + mt * 128 + 128],
                                     C.RB[:, k * 2048 + c * 512:
                                          k * 2048 + c * 512 + 512],
                                     start=(k == 0), stop=(k == 7))
                        for k in range(8):
                            last = t.matmul(
                                C.bank(bB),
                                C.RW[:, slotB + k * 512 + mt * 128:
                                     slotB + k * 512 + mt * 128 + 128],
                                C.RB[:, k * 2048 + c * 512:
                                     k * 2048 + c * 512 + 512],
                                start=(k == 0), stop=(k == 7))
                        last.then_inc(C.pe.h, 1)
                    C.emit("tensor", fhmm)
                    it = git[0]
                    git[0] += 1
                    sslot = (it % 2) * 512
                    gprev = gate_hist.get(it - 2)
                    a_s = C.act.pre(1)
                    def fsilu(s, bA=bA, pe_t=pe_t, sslot=sslot, gprev=gprev):
                        s.wait_ge(C.pe.h, pe_t)
                        if gprev is not None:
                            s.wait_ge(C.ve.h, gprev)
                        s.activation(C.RX[:, sslot:sslot + 512], C.bank(bA),
                                     AF.Silu).then_inc(C.act.h, 1)
                    C.emit("scalar", fsilu)
                    g_cnt = C.ve.pre(1)
                    def fgate(v, bB=bB, a_s=a_s, mt=mt, c=c, hm=hm,
                              shared=shared, sslot=sslot):
                        v.wait_ge(C.act.h, a_s)
                        htile = hm * 4 + mt
                        dst = C.RC[:, htile * 2048 + c * 512:
                                   htile * 2048 + c * 512 + 512]
                        if shared:
                            v.tensor_tensor(dst, C.RX[:, sslot:sslot + 512],
                                            C.bank(bB), ALU.mult)\
                                .then_inc(C.ve.h, 1)
                        else:
                            v.tensor_tensor(C.RY[:, sslot:sslot + 512],
                                            C.RX[:, sslot:sslot + 512],
                                            C.bank(bB), ALU.mult)
                            v.tensor_tensor(dst, C.RY[:, sslot:sslot + 512],
                                            C.RY[:, 2048 + c * 512:
                                                 2560 + c * 512], ALU.mult)\
                                .then_inc(C.ve.h, 1)
                    C.emit("vector", fgate)
                    gate_hist[it] = g_cnt
                    C.guard_bank(bA, C.act, a_s)
                    C.guard_bank(bB, C.ve, C.ve.n)
            guardA = ((C.pe, C.pe.n),)
            guardB = ((C.pe, C.pe.n),)
        h_done = (C.ve, C.ve.n)
        for hw in range(2):
            t2 = load_wmat(C, C.RW, slotC, W[wkey2][:, hw * 512:(hw + 1) * 512],
                           8, 512, s_wc, waits=guardC)
            def w2_evac(m, c, b, pe_t, hw=hw, init=shared):
                cnt = C.ve.pre(1)
                def f(v, m=m, c=c, b=b, pe_t=pe_t, hw=hw, init=init):
                    v.wait_ge(C.pe.h, pe_t)
                    dtile = hw * 4 + m
                    dst = C.RA[:, dtile * 2048 + c * 512:
                               dtile * 2048 + c * 512 + 512]
                    if init:
                        v.tensor_copy(dst, C.bank(b)).then_inc(C.ve.h, 1)
                    else:
                        v.tensor_tensor(dst, dst, C.bank(b), ALU.add)\
                            .then_inc(C.ve.h, 1)
                C.emit("vector", f)
                return (C.ve, cnt)
            mm_chain(C,
                     lambda k, m: C.RW[:, slotC + k * 512 + m * 128:
                                       slotC + k * 512 + m * 128 + 128],
                     lambda k, c: C.RC[:, k * 2048 + c * 512:
                                       k * 2048 + c * 512 + 512],
                     w2_evac, m_tiles=4, k_list=range(8), n_chunks=4,
                     pre_waits=((s_wc, t2), h_done), banks=(4, 5, 6))
            guardC = ((C.pe, C.pe.n),)
    return (C.ve, C.ve.n)


def emit_l0f_finish(C, x2sp, moe_done):
    s_r = C.sem("x2rl")
    act0 = (C.act, C.act.n)
    tt_hist = {}
    g_done = None
    for k in range(ND):
        waits = [moe_done, act0] if k < 2 else [(C.ve, tt_hist[k - 2])]
        slot = (k % 2) * 2048
        t_l = e_dma(C, "sync", C.RX[:, slot:slot + 2048], x2sp[k], s_r,
                    waits=waits)
        v_t = C.ve.pre(1)
        def ftt(v, k=k, slot=slot, t_l=t_l, moe_done=moe_done):
            v.wait_ge(s_r.h, t_l)
            wg(v, moe_done)
            v.tensor_tensor(C.RY[:, (k % 2) * 2048:(k % 2) * 2048 + 2048],
                            C.RA[:, k * 2048:(k + 1) * 2048],
                            C.RX[:, slot:slot + 2048], ALU.add)\
                .then_inc(C.ve.h, 1)
        C.emit("vector", ftt)
        tt_hist[k] = v_t
        C.act.pre(1)
        def fsil(s, k=k, v_t=v_t):
            s.wait_ge(C.ve.h, v_t)
            s.activation(C.RB[:, k * 2048:(k + 1) * 2048],
                         C.RY[:, (k % 2) * 2048:(k % 2) * 2048 + 2048],
                         AF.Silu).then_inc(C.act.h, 1)
        C.emit("scalar", fsil)
    return (C.act, C.act.n)


def emit_gate(C, x_src, s_sp, xout, g_done):
    s_x = C.sem("gx")
    s_s = C.sem("gs")
    s_o = C.sem("gout")
    prev_tt = {}
    for k in range(ND):
        wx = [g_done] if k == 0 else [(C.ve, prev_tt[k - 1])]
        t_x = e_dma(C, "sync", C.RY[:, 0:2048], x_src[k], s_x, waits=wx)
        slot = 4096 + (k % 2) * 2048
        ws = [g_done] if k < 2 else [(C.ve, prev_tt[k - 2])]
        t_s = e_dma(C, "gpsimd", C.RX[:, slot:slot + 2048], s_sp[k], s_s,
                    waits=ws)
        v_t = C.ve.pre(1)
        def ftt(v, k=k, slot=slot, t_x=t_x, t_s=t_s):
            v.wait_ge(s_x.h, t_x)
            v.wait_ge(s_s.h, t_s)
            v.tensor_tensor(C.RY[:, 2048:4096], C.RX[:, slot:slot + 2048],
                            C.RB[:, k * 2048:(k + 1) * 2048], ALU.mult)
            v.tensor_tensor(C.RY[:, 0:2048], C.RY[:, 0:2048],
                            C.RY[:, 2048:4096], ALU.add).then_inc(C.ve.h, 1)
        C.emit("vector", ftt)
        prev_tt[k] = v_t
        e_dma(C, "sync", xout[k], C.RY[:, 0:2048], s_o, waits=[(C.ve, v_t)])
    def ffin(g, n=s_o.n):
        g.wait_ge(s_o.h, n)
        g.memset(C.RX[0:1, 0:1], 0.0)
    C.emit("gpsimd", ffin)
    return s_o


def stage_l0f(inputs, g):
    g["l0f_wq"] = _t(inputs["L0_ffn_Wq"])
    g["l0f_wkc"] = _t(inputs["L0_ffn_Wkc"])
    g["l0f_wvc"] = _t(inputs["L0_ffn_Wvc"])
    wo = _t(inputs["L0_ffn_Wo"])
    g["l0f_wo_h0"] = np.ascontiguousarray(wo[:, 0:512])
    g["l0f_wo_h1"] = np.ascontiguousarray(wo[:, 512:1024])
    wrt = _t(inputs["L0_ffn_Wr"])
    wr_hi = wrt.astype(ml_dtypes.bfloat16).astype(np.float32)
    g["l0f_wr"] = np.ascontiguousarray(
        np.concatenate([wr_hi, wrt - wr_hi], axis=1))
    g["l0f_sw1"] = _t(inputs["L0_ffn_sW1"])
    g["l0f_sw2"] = _t(inputs["L0_ffn_sW2"])
    g["l0f_sw3"] = _t(inputs["L0_ffn_sW3"])
    for e in range(8):
        g["l0f_ew1_%d" % e] = _t(inputs["L0_ffn_eW1"][e])
        g["l0f_ew2_%d" % e] = _t(inputs["L0_ffn_eW2"][e])
        g["l0f_ew3_%d" % e] = _t(inputs["L0_ffn_eW3"][e])
    cols = np.zeros((128, 17), np.float32)
    cols[:, 0:8] = np.asarray(inputs["L0_ffn_g1"], np.float32).reshape(8, 128).T
    cols[:, 8:16] = np.asarray(inputs["L0_ffn_g2"], np.float32).reshape(8, 128).T
    cols[0:8, 16] = np.asarray(inputs["L0_ffn_expert_bias"], np.float32)
    g["l0f_cols"] = np.ascontiguousarray(cols)
    return g


L0F_SHAPES = [("l0f_wq", [D, D]), ("l0f_wkc", [D, 256]), ("l0f_wvc", [D, 256]),
              ("l0f_wo_h0", [D, 512]), ("l0f_wo_h1", [D, 512]),
              ("l0f_wr", [D, 16]), ("l0f_sw1", [D, D]), ("l0f_sw2", [D, D]),
              ("l0f_sw3", [D, D])] + \
    [("l0f_ew%d_%d" % (j, e), [D, D]) for e in range(8) for j in (1, 2, 3)] + \
    [("l0f_cols", [128, 17])]


def build_l0(stop=None):
    nc = bass.Bass("TRN2", target_bir_lowering=False)
    xin = nc.dram_tensor("xin", [ND, 128, T], F32, kind="ExternalInput").ap()
    H = {}
    for nm, shp in [("l0_wdq", [D, 512]), ("l0_wdkv", [D, 512]),
                    ("l0_wupq", [512, D]), ("l0_wupk", [512, D]),
                    ("l0_wupv", [512, D]),
                    ("l0_wout_h0", [D, 512]), ("l0_wout_h1", [D, 512]),
                    ("l0_cols", [128, 16]),
                    ("ropeC", [128, T]), ("ropeS", [128, T]),
                    ("consts", [128, 384])] + L0F_SHAPES:
        H[nm] = nc.dram_tensor(nm, shp, F32, kind="ExternalInput").ap()
    xout = nc.dram_tensor("xout", [ND, 128, T], F32, kind="ExternalOutput").ap()
    dbg = nc.dram_tensor("dbg", [ND, 128, T], F32, kind="ExternalOutput").ap()
    dbg2 = nc.dram_tensor("dbg2", [ND, 128, T], F32, kind="ExternalOutput").ap()
    dbg3 = nc.dram_tensor("dbg3", [ND, 128, T], F32, kind="ExternalOutput").ap()

    es = ExitStack()
    with es:
        C = Ctx(nc, es)
        blk = es.enter_context(nc.Block())
        s_x = C.sem("x")
        s_k = C.sem("k")
        for k in range(ND):
            e_dma(C, "sync", C.RA[:, k * 2048:(k + 1) * 2048], xin[k], s_x)
        e_dma(C, "gpsimd", C.CONST[:], H["consts"], s_k)
        e_dma(C, "gpsimd", C.ROPE[:, 0:2048], H["ropeC"], s_k)
        e_dma(C, "gpsimd", C.ROPE[:, 2048:4096], H["ropeS"], s_k)
        e_dma(C, "gpsimd", C.RCOL[:, 0:16], H["l0_cols"], s_k)
        e_dma(C, "gpsimd", C.RCOL[:, 16:33], H["l0f_cols"], s_k)
        konst_ready = [(s_k, s_k.n), (s_x, s_x.n)]
        ln1 = emit_ln(C, lambda k: C.RA[:, k * 2048:(k + 1) * 2048],
                      lambda k: C.RB[:, k * 2048:(k + 1) * 2048],
                      pre_waits=tuple(konst_ready))
        ln2 = emit_ln(C, lambda k: C.RB[:, k * 2048:(k + 1) * 2048],
                      lambda k: C.RC[:, k * 2048:(k + 1) * 2048],
                      gain_col=lambda k: C.RCOL[:, k:k + 1],
                      bias_col=lambda k: C.RCOL[:, 8 + k:8 + k + 1],
                      pre_waits=(ln1,), sq_region=C.RAb[:, 0:4096],
                      in_bf16=True)
        s_done = emit_l0a(C, H)
        ssp = nc.dram_tensor("ssp", [ND, 128, T], BF16, kind="Internal").ap()
        s_sp = C.sem("ssp")
        for k in range(ND):
            e_dma(C, "sync", ssp[k], C.RAb[:, k * 2048:(k + 1) * 2048], s_sp,
                  waits=(s_done,) if k == 0 else ())
        spill_done = (s_sp, s_sp.n)
        if stop == "s":
            for k in range(ND):
                e_dma(C, "gpsimd", dbg[k], C.RAb[:, k * 2048:(k + 1) * 2048],
                      C.sem("dbgo"), waits=(s_done,))
            sd = C.dma_sems["dbgo"]
            def ffin(g, n=sd.n):
                g.wait_ge(sd.h, n)
            C.emit("gpsimd", ffin)
            C.finalize(blk)
            return nc
        r = emit_l0f(C, H, spill_done,
                     stop=stop if stop != "macc" else None)
        if stop == "nx2":
            sd = C.sem("dbgo")
            for k in range(ND):
                e_dma(C, "gpsimd", dbg[k], C.RC[:, k * 2048:(k + 1) * 2048],
                      sd, waits=(r,))
                e_dma(C, "gpsimd", dbg2[k], C.RB[:, k * 2048:(k + 1) * 2048],
                      sd)
            def ffin(g, n=sd.n):
                g.wait_ge(sd.h, n)
            C.emit("gpsimd", ffin)
            C.finalize(blk)
            return nc
        if stop in ("kcvc", "attn", "x2", "router"):
            sd = C.sem("dbgo")
            done = r if r is not None else (C.ve, C.ve.n)
            if stop == "x2":
                for k in range(ND):
                    e_dma(C, "gpsimd", dbg[k],
                          C.RC[:, k * 2048:(k + 1) * 2048], sd, waits=(done,))
            else:
                for k in range(ND):
                    if k == 1 and stop == "router":
                        e_dma(C, "gpsimd", dbg[k], C.RX[:, 6144:8192], sd,
                              waits=(done,))
                        continue
                    if k == 2 and stop == "router":
                        e_dma(C, "gpsimd", dbg[k], C.RY[:, 0:2048], sd,
                              waits=(done,))
                        continue
                    if k == 3 and stop == "router":
                        e_dma(C, "gpsimd", dbg[k], C.RM[:, 0:2048], sd,
                              waits=(done,))
                        continue
                    e_dma(C, "gpsimd", dbg[k],
                          C.RAb[:, k * 2048:(k + 1) * 2048], sd, waits=(done,))
            def ffin(g, n=sd.n):
                g.wait_ge(sd.h, n)
            C.emit("gpsimd", ffin)
            C.finalize(blk)
            return nc
        moe_done = r
        if stop == "macc":
            sd = C.sem("dbgo")
            for k in range(ND):
                e_dma(C, "gpsimd", dbg[k], C.RA[:, k * 2048:(k + 1) * 2048],
                      sd, waits=(moe_done,))
            for k in range(ND):
                e_dma(C, "gpsimd", dbg2[k], C.RC[:, k * 2048:(k + 1) * 2048],
                      sd, waits=(moe_done,))
                e_dma(C, "gpsimd", dbg3[k], C.RB[:, k * 2048:(k + 1) * 2048],
                      sd, waits=(moe_done,))
            def ffin(g, n=sd.n):
                g.wait_ge(sd.h, n)
            C.emit("gpsimd", ffin)
            C.finalize(blk)
            return nc
        x2sp = C.x2sp
        g_done = emit_l0f_finish(C, x2sp, moe_done)
        if stop == "g":
            sd = C.sem("dbgo")
            for k in range(ND):
                e_dma(C, "gpsimd", dbg[k], C.RB[:, k * 2048:(k + 1) * 2048],
                      sd, waits=(g_done,))
            def ffin(g, n=sd.n):
                g.wait_ge(sd.h, n)
            C.emit("gpsimd", ffin)
            C.finalize(blk)
            return nc
        emit_gate(C, xin, ssp, xout, g_done)
        C.finalize(blk)
    return nc



def causal_pmask(C, qbs=QB):
    def pmask(h, qb, kt, pcols, p_ready):
        if (kt + 1) * 128 <= qb * qbs:
            return p_ready
        cnt = C.po.pre(1)
        def f(g, kt=kt, qb=qb, pcols=pcols, p_ready=p_ready):
            wg(g, p_ready)
            g.affine_select(C.RX[:, pcols:pcols + qbs],
                            C.RX[:, pcols:pcols + qbs],
                            [[1, qbs]], ALU.is_ge, 0.0,
                            base=qb * qbs - kt * 128, channel_multiplier=-1)\
                .then_inc(C.po.h, 1)
        C.emit("gpsimd", f)
        return (C.po, cnt)
    return pmask



def stage_l1(inputs, g):
    g["l1_w1"] = _t(inputs["L1_ffn_w1_w"])
    g["l1_w2"] = _t(inputs["L1_ffn_w2_w"])
    g["l1_h1"] = _t(inputs["L1_ffn_h1_w"])
    g["l1_h2"] = _t(inputs["L1_ffn_h2_w"])
    g["l1_w3"] = _t(inputs["L1_ffn_w3_w"])
    c2 = np.zeros((128, 128), np.float32)
    c2[:, 0:32] = np.asarray(inputs["L1_ffn_w1_s"], np.float32).reshape(32, 128).T
    c2[:, 32:64] = np.asarray(inputs["L1_ffn_w2_s"], np.float32).reshape(32, 128).T
    c2[:, 64:96] = np.asarray(inputs["L1_ffn_h1_s"], np.float32).reshape(32, 128).T
    c2[:, 96:128] = np.asarray(inputs["L1_ffn_h2_s"], np.float32).reshape(32, 128).T
    g["l1_cols2"] = np.ascontiguousarray(c2)
    c3 = np.zeros((128, 16), np.float32)
    c3[:, 0:8] = np.asarray(inputs["L1_ffn_w3_s"], np.float32).reshape(8, 128).T
    g["l1_cols3"] = np.ascontiguousarray(c3)
    g["l1_wcd"] = np.ascontiguousarray(np.asarray(inputs["L1_attn_Wc_down"],
                                                  np.float32))
    wkv = np.asarray(inputs["L1_attn_Wkv_up"], np.float32)
    g["l1_wkvk"] = np.ascontiguousarray(
        np.concatenate([wkv[:, 0:64], wkv[:, 0:64]], axis=1))
    g["l1_wkvv"] = np.ascontiguousarray(wkv[:, 64:128])
    g["l1_wqu"] = np.ascontiguousarray(np.asarray(inputs["L1_attn_Wq_up"],
                                                  np.float32))
    wp = np.ascontiguousarray(np.asarray(inputs["L1_attn_Wc_proj"], np.float32))
    g["l1_wcp_h0"] = np.ascontiguousarray(wp[:, 0:512])
    g["l1_wcp_h1"] = np.ascontiguousarray(wp[:, 512:1024])
    return g


L1_SHAPES = [("l1_w1", [D, 4096]), ("l1_w2", [D, 4096]),
             ("l1_h1", [4096, 4096]), ("l1_h2", [4096, 4096]),
             ("l1_w3", [4096, D]), ("l1_cols2", [128, 128]),
             ("l1_cols3", [128, 16]),
             ("l1_wcd", [D, 512]), ("l1_wkvk", [512, 128]),
             ("l1_wkvv", [512, 64]), ("l1_wqu", [512, D]),
             ("l1_wcp_h0", [D, 512]), ("l1_wcp_h1", [D, 512])]


def emit_l1(C, W, gsp, stop=None):
    RAb = C.RAb
    s_wa = C.sem("1wa")
    s_wb = C.sem("1wb")
    s_gsp = C.sem("1gsp")
    xnsp = C.nc.dram_tensor("xnsp1", [ND, 128, T], BF16, kind="Internal").ap()
    s_xn = C.sem("1xn")
    xn0 = (C.ve, C.ve.n)
    for k in range(ND):
        e_dma(C, "sync", xnsp[k], C.RB[:, k * 2048:(k + 1) * 2048], s_xn,
              waits=(xn0,) if k == 0 else ())
    guardA = guardB = ()
    gate_hist = {}
    git = [0]
    for tc in range(2):
        for p8 in range(8):
            t1 = load_wmat(C, C.RW, 0, W["l1_w1"][:, p8 * 512:(p8 + 1) * 512],
                           8, 512, s_wa, waits=guardA)
            t2 = load_wmat(C, C.RW, 4096,
                           W["l1_w2"][:, p8 * 512:(p8 + 1) * 512],
                           8, 512, s_wb, waits=guardB)
            for mt in range(4):
                for c2 in range(2):
                    bA = git[0] % 2
                    bB = 2 + git[0] % 2
                    it = git[0]
                    git[0] += 1
                    gdA = C.bank_guard[bA]
                    gdB = C.bank_guard[bB]
                    pe_t = C.pe.pre(1)
                    pwa = ((s_wa, t1), (s_wb, t2)) if (mt == 0 and c2 == 0) \
                        else ()
                    def fmm(t, mt=mt, c2=c2, tc=tc, bA=bA, bB=bB, gdA=gdA,
                            gdB=gdB, pwa=pwa):
                        for w_ in pwa:
                            wg(t, w_)
                        wg(t, gdA)
                        wg(t, gdB)
                        cl = tc * 1024 + c2 * 512
                        last = None
                        for k in range(8):
                            t.matmul(C.bank(bA),
                                     C.RW[:, k * 512 + mt * 128:
                                          k * 512 + mt * 128 + 128],
                                     C.RB[:, k * 2048 + cl: k * 2048 + cl + 512],
                                     start=(k == 0), stop=(k == 7))
                        for k in range(8):
                            last = t.matmul(
                                C.bank(bB),
                                C.RW[:, 4096 + k * 512 + mt * 128:
                                     4096 + k * 512 + mt * 128 + 128],
                                C.RB[:, k * 2048 + cl: k * 2048 + cl + 512],
                                start=(k == 0), stop=(k == 7))
                        last.then_inc(C.pe.h, 1)
                    C.emit("tensor", fmm)
                    sslot = (it % 2) * 512
                    gprev = gate_hist.get(it - 2)
                    a_s = C.act.pre(1)
                    htile = p8 * 4 + mt
                    def fsilu(s, bA=bA, pe_t=pe_t, sslot=sslot, gprev=gprev,
                              htile=htile):
                        s.wait_ge(C.pe.h, pe_t)
                        if gprev is not None:
                            s.wait_ge(C.ve.h, gprev)
                        s.activation(C.RX[:, sslot:sslot + 512], C.bank(bA),
                                     AF.Silu, scale=C.RCOL2[:, htile:htile + 1])\
                            .then_inc(C.act.h, 1)
                    C.emit("scalar", fsilu)
                    g_cnt = C.ve.pre(1)
                    def fgate(v, bB=bB, a_s=a_s, htile=htile, c2=c2,
                              sslot=sslot):
                        v.wait_ge(C.act.h, a_s)
                        dst = RAb[:, htile * 1024 + c2 * 512:
                                  htile * 1024 + c2 * 512 + 512]
                        v.scalar_tensor_tensor(dst, C.bank(bB),
                                               C.RCOL2[:, 32 + htile:33 + htile],
                                               C.RX[:, sslot:sslot + 512],
                                               ALU.mult, ALU.mult)\
                            .then_inc(C.ve.h, 1)
                    C.emit("vector", fgate)
                    gate_hist[it] = g_cnt
                    C.guard_bank(bA, C.act, a_s)
                    C.guard_bank(bB, C.ve, C.ve.n)
            guardA = ((C.pe, C.pe.n),)
            guardB = ((C.pe, C.pe.n),)
        h_done = (C.ve, C.ve.n)
        def big_stage(wkey, scol_base, src_tile, dst_tile, act_fn, sem,
                      pre, guard0):
            guard = guard0
            out = []
            for sl in range(16):
                tl = load_wmat(C, C.RW, 0, W[wkey][:, sl * 256:(sl + 1) * 256],
                               32, 256, sem, waits=guard)
                for mt in range(2):
                    for c2 in range(2):
                        b = (sl * 4 + mt * 2 + c2) % 4
                        gd = C.bank_guard[b]
                        pe_t = C.pe.pre(1)
                        pwa = ((sem, tl),) + (tuple(pre) if sl == 0 and
                                              mt == 0 and c2 == 0 else ())
                        def fmm(t, sl=sl, mt=mt, c2=c2, b=b, gd=gd, pwa=pwa):
                            for w_ in pwa:
                                wg(t, w_)
                            wg(t, gd)
                            last = None
                            for k in range(32):
                                last = t.matmul(
                                    C.bank(b),
                                    C.RW[:, k * 256 + mt * 128:
                                         k * 256 + mt * 128 + 128],
                                    src_tile(k)[:, c2 * 512: c2 * 512 + 512],
                                    start=(k == 0), stop=(k == 31))
                            last.then_inc(C.pe.h, 1)
                        C.emit("tensor", fmm)
                        otile = sl * 2 + mt
                        a_t = C.act.pre(1)
                        def fev(s, b=b, pe_t=pe_t, otile=otile, c2=c2,
                                act_fn=act_fn, scol_base=scol_base):
                            s.wait_ge(C.pe.h, pe_t)
                            s.activation(dst_tile(otile)[:, c2 * 512:
                                                         c2 * 512 + 512],
                                         C.bank(b), act_fn,
                                         scale=C.RCOL2[:, scol_base + otile:
                                                       scol_base + otile + 1])\
                                .then_inc(C.act.h, 1)
                        C.emit("scalar", fev)
                        C.guard_bank(b, C.act, a_t)
                guard = ((C.pe, C.pe.n),)
            return (C.act, C.act.n)
        def h_src(k):
            return RAb[:, k * 1024:(k + 1) * 1024]
        def h2_dst(i):
            if i < 16:
                return C.RB[:, i * 1024:(i + 1) * 1024]
            return C.RC[:, (i - 16) * 1024:(i - 15) * 1024]
        pre2 = [h_done, (s_xn, s_xn.n)]
        d2 = big_stage("l1_h1", 64, h_src, h2_dst, AF.Gelu, s_wa, pre2,
                       ((C.pe, C.pe.n),))
        def h2_src(k):
            return h2_dst(k)
        def h3_dst(i):
            return RAb[:, i * 1024:(i + 1) * 1024]
        d3 = big_stage("l1_h2", 96, h2_src, h3_dst, AF.Gelu, s_wb, [d2],
                       ((C.pe, C.pe.n),))
        guard4 = ((C.pe, C.pe.n),)
        for sl in range(4):
            tl = load_wmat(C, C.RW, 0, W["l1_w3"][:, sl * 256:(sl + 1) * 256],
                           32, 256, s_wa, waits=guard4)
            for mt in range(2):
                for c2 in range(2):
                    b = (sl * 4 + mt * 2 + c2) % 4
                    gd = C.bank_guard[b]
                    pe_t = C.pe.pre(1)
                    pwa = ((s_wa, tl),) + ((d3,) if sl == 0 and mt == 0 and
                                           c2 == 0 else ())
                    def fmm(t, sl=sl, mt=mt, c2=c2, b=b, gd=gd, pwa=pwa):
                        for w_ in pwa:
                            wg(t, w_)
                        wg(t, gd)
                        last = None
                        for k in range(32):
                            last = t.matmul(
                                C.bank(b),
                                C.RW[:, k * 256 + mt * 128:
                                     k * 256 + mt * 128 + 128],
                                RAb[:, k * 1024 + c2 * 512:
                                    k * 1024 + c2 * 512 + 512],
                                start=(k == 0), stop=(k == 31))
                        last.then_inc(C.pe.h, 1)
                    C.emit("tensor", fmm)
                    otile = sl * 2 + mt
                    a_t = C.act.pre(1)
                    def fev(s, b=b, pe_t=pe_t, otile=otile, c2=c2):
                        s.wait_ge(C.pe.h, pe_t)
                        s.activation(C.RX[:, otile * 1024 + c2 * 512:
                                          otile * 1024 + c2 * 512 + 512],
                                     C.bank(b), AF.Silu,
                                     scale=C.RCOL2b[:, otile:otile + 1])\
                            .then_inc(C.act.h, 1)
                    C.emit("scalar", fev)
                    C.guard_bank(b, C.act, a_t)
            guard4 = ((C.pe, C.pe.n),)
        g_done = (C.act, C.act.n)
        for dtile in range(8):
            e_dma(C, "sync", gsp[dtile][:, tc * 1024:(tc + 1) * 1024],
                  C.RX[:, dtile * 1024:(dtile + 1) * 1024], s_gsp,
                  waits=(g_done,) if dtile == 0 else ())
        if tc == 0:
            s_x2 = C.sem("1xr")
            pe_now = (C.pe, C.pe.n)
            for k in range(ND):
                e_dma(C, "gpsimd", C.RB[:, k * 2048:(k + 1) * 2048], xnsp[k],
                      s_x2, waits=(pe_now,) if k == 0 else ())
            guardA = guardA + ((s_x2, s_x2.n),)
            guardB = guardB + ((s_gsp, s_gsp.n),)
    if stop == "ffn":
        return (s_gsp, s_gsp.n)
    s_x3 = C.sem("1xr2")
    pe_now = (C.pe, C.pe.n)
    ve_now = (C.ve, C.ve.n)
    for k in range(ND):
        e_dma(C, "gpsimd", C.RB[:, k * 2048:(k + 1) * 2048], xnsp[k], s_x3,
              waits=(pe_now, (s_gsp, s_gsp.n)) if k == 0 else ())
    xn_rl = (s_x3, s_x3.n)
    t_cd = load_wmat(C, C.RW, 0, W["l1_wcd"], 8, 512, s_wa,
                     waits=((C.pe, C.pe.n),))
    mm_chain(C,
             lambda k, m: C.RW[:, k * 512 + m * 128: k * 512 + m * 128 + 128],
             lambda k, c: C.RB[:, k * 2048 + c * 512: k * 2048 + c * 512 + 512],
             evac_copy(C, lambda m, c:
                       RAb[:, m * 2048 + c * 512: m * 2048 + c * 512 + 512]),
             m_tiles=4, k_list=range(8), n_chunks=4,
             pre_waits=((s_wa, t_cd), xn_rl), banks=(0, 1, 2, 3))
    lat_pe = (C.pe, C.pe.n)
    lat = lambda k: RAb[:, k * 2048:(k + 1) * 2048]
    t_kd = load_wmat(C, C.RW, 4096, W["l1_wkvk"], 4, 128, s_wb,
                     waits=(lat_pe,))
    t_vd = load_wmat(C, C.RW, 4608, W["l1_wkvv"], 4, 64, s_wb)
    mm_chain(C,
             lambda k, m: C.RW[:, 4096 + k * 128: 4096 + k * 128 + 128],
             lambda k, c: lat(k)[:, c * 512: c * 512 + 512],
             evac_copy(C, lambda m, c:
                       RAb[:, 8192 + c * 512: 8192 + c * 512 + 512]),
             m_tiles=1, k_list=range(4), n_chunks=4,
             pre_waits=((s_wb, t_kd),), banks=(4, 5))
    po_t = C.po.pre(1)
    ve_b4 = (C.ve, C.ve.n)
    def fones(g, ve_b4=ve_b4):
        wg(g, ve_b4)
        g.memset(RAb[:, 10240:11280], 1.0).then_inc(C.po.h, 1)
    C.emit("gpsimd", fones)
    def v_evac(tt, c, b, pe_t):
        cnt = C.ve.pre(1)
        def f(v, tt=tt, b=b, pe_t=pe_t, po_t=po_t):
            v.wait_ge(C.pe.h, pe_t)
            v.wait_ge(C.po.h, po_t)
            v.tensor_copy(RAb[:, 10240 + tt * 65: 10240 + tt * 65 + 64],
                          C.bank(b, w=64)).then_inc(C.ve.h, 1)
        C.emit("vector", f)
        return (C.ve, cnt)
    mm_chain(C,
             lambda k, m: lat(k)[:, m * 128: m * 128 + 128],
             lambda k, c: C.RW[:, 4608 + k * 64: 4608 + k * 64 + 64],
             v_evac, m_tiles=16, k_list=range(4), n_chunks=1,
             pre_waits=((s_wb, t_vd),), banks=(0, 1, 2, 3))
    t_qu = load_wmat(C, C.RW, 6144, W["l1_wqu"], 4, 1024, s_wa)

    def qT_fn(h):
        b = 64 * (h % 2)
        return RAb[b:b + 64, 11280:11280 + 2048]

    def kT_fn(h):
        b = 64 * (h % 2)
        return RAb[b:b + 64, 8192:8192 + 2048]

    def v_lhsT(h, kt):
        return RAb[:, 10240 + kt * 65: 10240 + kt * 65 + 65]

    def o_dst(h, qb):
        b = 64 * (h % 2)
        return C.RC[b:b + 64,
                    (h // 2) * 2048 + qb * QB: (h // 2) * 2048 + qb * QB + QB]

    kt_hi = lambda qb: (qb + 1) * (QB // 128)
    pmask = causal_pmask(C)
    for grp in range(8):
        gh = grp * 2
        mm_chain(C,
                 lambda k, m, gh=gh: C.RW[:, 6144 + k * 1024 + (gh // 2) * 128:
                                          6144 + k * 1024 + (gh // 2) * 128 + 128],
                 lambda k, c: lat(k)[:, c * 512: c * 512 + 512],
                 evac_copy(C, lambda m, c:
                           RAb[:, 11280 + c * 512: 11280 + c * 512 + 512],
                           scale=0.125),
                 m_tiles=1, k_list=range(4), n_chunks=4,
                 pre_waits=((s_wa, t_qu),), banks=(4, 5))
        qdone = (C.act, C.act.n)
        C.guard_bank(0, *qdone)
        C.guard_bank(1, *qdone)
        emit_attn64(C, [gh, gh + 1], qT_fn, kT_fn, v_lhsT, o_dst,
                    shift="passA", kt_hi_fn=kt_hi, pmask_fn=pmask)
    attn_pe = (C.pe, C.pe.n)
    attn_ve = (C.ve, C.ve.n)
    for halfw in range(2):
        t_wo = load_wmat(C, C.RW, halfw * 4096, W["l1_wcp_h%d" % halfw],
                         8, 512, s_wb if halfw == 0 else s_wa,
                         waits=(attn_pe,))
        mm_chain(C,
                 lambda k, m, halfw=halfw:
                 C.RW[:, halfw * 4096 + k * 512 + m * 128:
                      halfw * 4096 + k * 512 + m * 128 + 128],
                 lambda k, c: C.RC[:, k * 2048 + c * 512:
                                   k * 2048 + c * 512 + 512],
                 evac_copy(C, lambda m, c, halfw=halfw:
                           RAb[:, 16384 + (halfw * 4 + m) * 2048 + c * 512:
                               16384 + (halfw * 4 + m) * 2048 + c * 512 + 512],
                           eng="vector"),
                 m_tiles=4, k_list=range(8), n_chunks=4,
                 pre_waits=((s_wb if halfw == 0 else s_wa, t_wo), attn_ve),
                 banks=(0, 1, 2, 3))
    return (C.ve, C.ve.n)


def build_l1(stop=None):
    nc = bass.Bass("TRN2", target_bir_lowering=False)
    xin = nc.dram_tensor("xin", [ND, 128, T], F32, kind="ExternalInput").ap()
    H = {}
    for nm, shp in [("consts", [128, 384])] + L1_SHAPES:
        H[nm] = nc.dram_tensor(nm, shp, F32, kind="ExternalInput").ap()
    xout = nc.dram_tensor("xout", [ND, 128, T], F32, kind="ExternalOutput").ap()
    dbg = nc.dram_tensor("dbg", [ND, 128, T], F32, kind="ExternalOutput").ap()
    es = ExitStack()
    with es:
        C = Ctx(nc, es)
        blk = es.enter_context(nc.Block())
        s_x = C.sem("x")
        s_k = C.sem("k")
        for k in range(ND):
            e_dma(C, "sync", C.RA[:, k * 2048:(k + 1) * 2048], xin[k], s_x)
        e_dma(C, "gpsimd", C.CONST[:], H["consts"], s_k)
        e_dma(C, "gpsimd", C.RCOL2[:], H["l1_cols2"], s_k)
        e_dma(C, "gpsimd", C.RCOL2b[:], H["l1_cols3"], s_k)
        konst_ready = [(s_k, s_k.n), (s_x, s_x.n)]
        ln1 = emit_ln(C, lambda k: C.RA[:, k * 2048:(k + 1) * 2048],
                      lambda k: C.RB[:, k * 2048:(k + 1) * 2048],
                      pre_waits=tuple(konst_ready))
        gsp = nc.dram_tensor("gsp", [ND, 128, T], BF16, kind="Internal").ap()
        r = emit_l1(C, H, gsp, stop=stop)
        if stop == "ffn":
            sd = C.sem("dbgo")
            s_gr = C.sem("grl")
            for k in range(ND):
                e_dma(C, "gpsimd", C.RB[:, k * 2048:(k + 1) * 2048], gsp[k],
                      s_gr, waits=(r,) if k == 0 else ())
            for k in range(ND):
                e_dma(C, "gpsimd", dbg[k], C.RB[:, k * 2048:(k + 1) * 2048],
                      sd, waits=((s_gr, s_gr.n),))
            def ffin(g, n=sd.n):
                g.wait_ge(sd.h, n)
            C.emit("gpsimd", ffin)
            C.finalize(blk)
            return nc
        s_done = r
        s_xg = C.sem("gx")
        s_g = C.sem("gg")
        s_o = C.sem("gout")
        prev_tt = {}
        for k in range(ND):
            wx = [s_done] if k == 0 else [(C.ve, prev_tt[k - 1])]
            t_x = e_dma(C, "sync", C.RY[:, 0:2048], xin[k], s_xg, waits=wx)
            slot = (k % 2) * 2048
            ws = [s_done] if k < 2 else [(C.ve, prev_tt[k - 2])]
            t_g = e_dma(C, "gpsimd", C.RX[:, slot:slot + 2048], gsp[k], s_g,
                        waits=ws)
            v_t = C.ve.pre(1)
            def ftt(v, k=k, slot=slot, t_x=t_x, t_g=t_g):
                v.wait_ge(s_xg.h, t_x)
                v.wait_ge(s_g.h, t_g)
                v.tensor_tensor(C.RY[:, 2048:4096],
                                C.RAb[:, 16384 + k * 2048:
                                      16384 + (k + 1) * 2048],
                                C.RX[:, slot:slot + 2048], ALU.mult)
                v.tensor_tensor(C.RY[:, 0:2048], C.RY[:, 0:2048],
                                C.RY[:, 2048:4096], ALU.add)\
                    .then_inc(C.ve.h, 1)
            C.emit("vector", ftt)
            prev_tt[k] = v_t
            e_dma(C, "sync", xout[k], C.RY[:, 0:2048], s_o,
                  waits=[(C.ve, v_t)])
        def ffin(g, n=s_o.n):
            g.wait_ge(s_o.h, n)
            g.memset(C.RX[0:1, 0:1], 0.0)
        C.emit("gpsimd", ffin)
        C.finalize(blk)
    return nc



def stage_l2(inputs, g):
    Wx = np.asarray(inputs["L2_attn_Wx"], np.float32)
    g["l2_w2x"] = np.ascontiguousarray(
        np.concatenate([Wx[h] for h in range(16)], axis=1))
    bx = np.asarray(inputs["L2_attn_bx"], np.float32).reshape(-1)
    g["l2_bxrow"] = np.ascontiguousarray(bx.reshape(1, 1024))
    wp = np.ascontiguousarray(np.asarray(inputs["L2_attn_Wp"], np.float32))
    g["l2_wp_h0"] = np.ascontiguousarray(wp[:, 0:512])
    g["l2_wp_h1"] = np.ascontiguousarray(wp[:, 512:1024])
    for j in range(6):
        g["l2_qw_%d" % j] = np.ascontiguousarray(
            np.asarray(inputs["L2_ffn_qweights"][j], np.int32).T)
        g["l2_sc_%d" % j] = np.ascontiguousarray(
            np.repeat(np.asarray(inputs["L2_ffn_scales"][j], np.float32),
                      16, axis=1).T)
        g["l2_la_%d" % j] = _t(inputs["L2_ffn_lora_a"][j])
        g["l2_lb_%d" % j] = _t(inputs["L2_ffn_lora_b"][j])
    c2 = np.zeros((128, 128), np.float32)
    for j in range(6):
        c2[:, j * 8:(j + 1) * 8] = np.asarray(
            inputs["L2_ffn_biases"][j], np.float32).reshape(8, 128).T
    for b in range(3):
        c2[:, 48 + b * 8:56 + b * 8] = np.asarray(
            inputs["L2_ffn_ln_gamma"][b], np.float32).reshape(8, 128).T
        c2[:, 72 + b * 8:80 + b * 8] = np.asarray(
            inputs["L2_ffn_ln_beta"][b], np.float32).reshape(8, 128).T
    c2[:, 96:104] = bx.reshape(8, 128).T
    c2[:, 104:112] = np.asarray(inputs["L2_attn_bp"], np.float32)\
        .reshape(8, 128).T
    g["l2_cols"] = np.ascontiguousarray(c2)
    return g


L2_SHAPES = [("l2_w2x", [D, D]), ("l2_bxrow", [1, D]),
             ("l2_wp_h0", [D, 512]), ("l2_wp_h1", [D, 512]),
             ("l2_cols", [128, 128])] + \
    [("l2_qw_%d" % j, [D, D]) for j in range(6)] + \
    [("l2_sc_%d" % j, [D, D]) for j in range(6)] + \
    [("l2_la_%d" % j, [D, 32]) for j in range(6)] + \
    [("l2_lb_%d" % j, [32, D]) for j in range(6)]


def evac_ts_bias(C, dst_fn, bias_fn):
    def evac(m, c, b, pe_t):
        cnt = C.ve.pre(1)
        def f(v, m=m, c=c, b=b, pe_t=pe_t):
            v.wait_ge(C.pe.h, pe_t)
            dst = dst_fn(m, c)
            v.tensor_scalar(dst, C.bank(b, w=dst.shape[-1],
                                        rows=dst.shape[0]),
                            bias_fn(m), None, ALU.add).then_inc(C.ve.h, 1)
        C.emit("vector", f)
        return (C.ve, cnt)
    return evac


def emit_l2(C, W, stop=None):
    RAb = C.RAb
    s_wa = C.sem("2wa")
    s_wb = C.sem("2wb")
    pe0 = (C.pe, C.pe.n)
    t_wx = load_wmat(C, C.RW, 0, W["l2_w2x"], 8, 1024, s_wa, waits=(pe0,))
    t_bx = e_dma(C, "gpsimd", C.RX[0:1, 6656:7680], W["l2_bxrow"], s_wa)
    mm_chain(C,
             lambda k, m: C.RW[:, k * 1024 + m * 128: k * 1024 + m * 128 + 128],
             lambda k, c: C.RB[:, k * 2048 + c * 512: k * 2048 + c * 512 + 512],
             evac_ts_bias(C, lambda m, c:
                          RAb[:, m * 2048 + c * 512: m * 2048 + c * 512 + 512],
                          lambda m: C.RCOL2[:, 96 + m:97 + m]),
             m_tiles=8, k_list=range(8), n_chunks=4,
             pre_waits=((s_wa, t_wx),), banks=(0, 1, 2, 3))
    po_t = C.po.pre(1)
    ve_now = (C.ve, C.ve.n)
    def fones(g, ve_now=ve_now):
        wg(g, ve_now)
        g.memset(RAb[:, 16384:31984], 1.0)
        g.memset(C.RW[:, 8192:9232].bitcast(BF16)[:, 0:1040] if False else
                 C.RW[:, 8192:9232], 1.0).then_inc(C.po.h, 1)
    C.emit("gpsimd", fones)
    def htm_slice(tt):
        if tt < 15:
            return RAb[:, 16384 + tt * 1040: 16384 + (tt + 1) * 1040]
        return C.RW[:, 8192:9232]
    def htm_evac(tt, c, b, pe_t):
        cnt = C.ve.pre(1)
        def f(v, tt=tt, c=c, b=b, pe_t=pe_t, po_t=po_t):
            v.wait_ge(C.pe.h, pe_t)
            v.wait_ge(C.po.h, po_t)
            dst = htm_slice(tt).rearrange("p (h e) -> p h e", h=16)[:, c * 8:
                                                                   c * 8 + 8,
                                                                   0:64]
            v.tensor_copy(dst, C.bank(b).rearrange("p (h e) -> p h e", h=8))\
                .then_inc(C.ve.h, 1)
        C.emit("vector", f)
        return (C.ve, cnt)
    def htm_lhsT(k, m):
        return C.RB[:, k * 2048 + m * 128: k * 2048 + m * 128 + 128]
    for tt in range(16):
        for c in range(2):
            b = (tt * 2 + c) % 4
            gd = C.bank_guard[b]
            pe_t = C.pe.pre(1)
            def fmm(t, tt=tt, c=c, b=b, gd=gd):
                wg(t, gd)
                for k in range(8):
                    t.matmul(C.bank(b), htm_lhsT(k, tt),
                             C.RW[:, k * 1024 + c * 512: k * 1024 + c * 512 + 512],
                             start=(k == 0), stop=False)
                t.matmul(C.bank(b), C.ones_bf[0:1, 0:128],
                         C.RX[0:1, 6656 + c * 512: 6656 + c * 512 + 512],
                         start=False, stop=True).then_inc(C.pe.h, 1)
            C.emit("tensor", fmm)
            C.guard_bank(b, *htm_evac(tt, c, b, pe_t))
    htm_done = (C.ve, C.ve.n)
    if stop == "h":
        return htm_done
    a0 = C.act.pre(0)
    for tt in range(16):
        for h in range(16):
            C.act.pre(1)
            def fsq(s, tt=tt, h=h, htm_done=htm_done):
                if tt == 0 and h == 0:
                    wg(s, htm_done)
                sl = htm_slice(tt)
                s.activation(C.RY[:, 0:64],
                             sl[:, h * 65: h * 65 + 64], AF.Square,
                             accum_out=C.RM[:, 1024 + tt * 16 + h:
                                            1025 + tt * 16 + h])\
                    .then_inc(C.act.h, 1)
            C.emit("scalar", fsq)
    a_n = C.act.pre(1)
    def frn(s):
        s.activation(C.RM[:, 1280:1536], C.RM[:, 1024:1280], AF.Ln)
        s.activation(C.RM[:, 1280:1536], C.RM[:, 1280:1536], AF.Exp,
                     scale=-0.5).then_inc(C.act.h, 1)
    C.emit("scalar", frn)
    v_bf = C.ve.pre(1)
    def fbf(v, a_n=a_n):
        v.wait_ge(C.act.h, a_n)
        v.tensor_copy(C.RX[:, 6144:6400], C.RM[:, 1280:1536])\
            .then_inc(C.ve.h, 1)
    C.emit("vector", fbf)
    for h in range(16):
        gds = C.snap_guards([0, 1, 2, 3])
        pe_t = C.pe.pre(1)
        def ftr(t, h=h, gds=gds, v_bf=v_bf):
            t.wait_ge(C.ve.h, v_bf)
            last = None
            for bk in range(4):
                wg(t, gds[bk])
                for j in range(4):
                    tt = bk * 4 + j
                    last = t.matmul(
                        C.ps[0:1, bk * 512 + j * 128: bk * 512 + j * 128 + 128],
                        C.RX[:, 6144 + tt * 16 + h: 6145 + tt * 16 + h],
                        C.ident, start=True, stop=True)
            last.then_inc(C.pe.h, 1)
        C.emit("tensor", ftr)
        a_r = C.act.pre(1)
        def frow(s, pe_t=pe_t):
            s.wait_ge(C.pe.h, pe_t)
            s.activation(C.RX[0:1, 4096:4608], C.ps[0:1, 0:512], AF.Copy)
            s.activation(C.RX[0:1, 4608:5120], C.ps[0:1, 512:1024], AF.Copy)
            s.activation(C.RX[0:1, 5120:5632], C.ps[0:1, 1024:1536], AF.Copy)
            s.activation(C.RX[0:1, 5632:6144], C.ps[0:1, 1536:2048], AF.Copy)\
                .then_inc(C.act.h, 1)
        C.emit("scalar", frow)
        for bk in range(4):
            C.guard_bank(bk, C.act, a_r)
        base = 64 * (h % 2)
        for c in range(4):
            gd = C.snap_guards([4 + c % 2])
            pe_b = C.pe.pre(1)
            def fbc(t, h=h, c=c, gd=gd, a_r=a_r, base=base):
                t.wait_ge(C.act.h, a_r)
                wg(t, gd[0])
                bnk = 4 + c % 2
                t.matmul(C.ps[base:base + 64, bnk * 512:(bnk + 1) * 512],
                         C.ones_bf[0:1, 0:64],
                         C.RX[0:1, 4096 + c * 512: 4608 + c * 512],
                         start=True, stop=True).then_inc(C.pe.h, 1)
            C.emit("tensor", fbc)
            C.ve.pre(1)
            def fnm(v, h=h, c=c, pe_b=pe_b, base=base):
                v.wait_ge(C.pe.h, pe_b)
                bnk = 4 + c % 2
                sl = RAb[base:base + 64,
                         (h // 2) * 2048 + c * 512: (h // 2) * 2048 + c * 512 + 512]
                v.tensor_tensor(sl, sl,
                                C.ps[base:base + 64, bnk * 512:(bnk + 1) * 512],
                                ALU.mult).then_inc(C.ve.h, 1)
            C.emit("vector", fnm)
            C.guard_bank(4 + c % 2, C.ve, C.ve.n)
    norm_done = (C.ve, C.ve.n)
    if stop == "norm":
        return norm_done

    def qT_fn(h):
        b = 64 * (h % 2)
        return RAb[b:b + 64, (h // 2) * 2048: (h // 2) * 2048 + 2048]

    def v_lhsT(h, kt):
        return htm_slice(kt)[:, h * 65: h * 65 + 65]

    def o_dst(h, qb):
        b = 64 * (h % 2)
        return C.RC[b:b + 64,
                    (h // 2) * 2048 + qb * QB: (h // 2) * 2048 + qb * QB + QB]

    C.guard_bank(0, *norm_done)
    C.guard_bank(1, *norm_done)
    C.guard_bank(4, *norm_done)
    C.guard_bank(5, *norm_done)
    emit_attn64(C, list(range(16)), qT_fn, qT_fn, v_lhsT, o_dst, shift=-1.0)
    attn_pe = (C.pe, C.pe.n)
    attn_ve = (C.ve, C.ve.n)
    for halfw in range(2):
        t_wo = load_wmat(C, C.RW, halfw * 4096, W["l2_wp_h%d" % halfw],
                         8, 512, s_wb if halfw == 0 else s_wa,
                         waits=(attn_pe,))
        mm_chain(C,
                 lambda k, m, halfw=halfw:
                 C.RW[:, halfw * 4096 + k * 512 + m * 128:
                      halfw * 4096 + k * 512 + m * 128 + 128],
                 lambda k, c: C.RC[:, k * 2048 + c * 512:
                                   k * 2048 + c * 512 + 512],
                 evac_ts_bias(C, lambda m, c, halfw=halfw:
                              RAb[:, (halfw * 4 + m) * 2048 + c * 512:
                                  (halfw * 4 + m) * 2048 + c * 512 + 512],
                              lambda m, halfw=halfw:
                              C.RCOL2[:, 104 + halfw * 4 + m:
                                      105 + halfw * 4 + m]),
                 m_tiles=4, k_list=range(8), n_chunks=4,
                 pre_waits=((s_wb if halfw == 0 else s_wa, t_wo), attn_ve),
                 banks=(0, 1, 2, 3))
    s_done = (C.ve, C.ve.n)
    if stop == "attn":
        return s_done
    return emit_l2f(C, W, s_done, stop=stop)


def emit_l2f(C, W, s_done, stop=None):
    RAb = C.RAb
    s_wa = C.sem("2fa")
    s_wb = C.sem("2fb")
    s_wc = C.sem("2fc")
    pe00 = ((C.pe, C.pe.n),)
    guards = {"q": pe00, "s": pe00, "l": pe00}
    for blk_i in range(3):
        for jj in range(2):
            j = 2 * blk_i + jj
            src_t = (lambda k: C.RB[:, k * 2048:(k + 1) * 2048]) if jj == 0 \
                else (lambda k: C.RC[:, k * 2048:(k + 1) * 2048])
            t_la = e_dma(C, "gpsimd",
                         C.RX[:, 1024:1280].rearrange("p (k o) -> p k o", k=8),
                         W["l2_la_%d" % j].rearrange("(k p) o -> p k o", p=128),
                         s_wc, waits=guards["l"])
            t_lb = e_dma(C, "gpsimd", C.RX[0:32, 0:1024],
                         W["l2_lb_%d" % j], s_wc)
            def mid_evac(m, c, b, pe_t):
                cnt = C.ve.pre(1)
                def f(v, c=c, b=b, pe_t=pe_t):
                    v.wait_ge(C.pe.h, pe_t)
                    v.tensor_copy(C.RX[0:32, 6144 + c * 512: 6656 + c * 512],
                                  C.bank(b, rows=32)).then_inc(C.ve.h, 1)
                C.emit("vector", f)
                return (C.ve, cnt)
            mm_chain(C,
                     lambda k, m: C.RX[:, 1024 + k * 32: 1024 + k * 32 + 32],
                     lambda k, c: src_t(k)[:, c * 512: c * 512 + 512],
                     mid_evac, m_tiles=1, k_list=range(8), n_chunks=4,
                     pre_waits=((s_wc, t_la), (s_wc, t_lb), s_done),
                     banks=(4, 5), m_rows=32)
            mid_done = (C.ve, C.ve.n)
            if stop == "mid%d" % j:
                return mid_done
            for hf in range(2):
                t_q = load_wmat(C, C.RW, 0,
                                W["l2_qw_%d" % j][:, hf * 512:(hf + 1) * 512],
                                8, 512, s_wa, waits=guards["q"])
                t_s = load_wmat(C, C.RW, 4096,
                                W["l2_sc_%d" % j][:, hf * 512:(hf + 1) * 512],
                                8, 512, s_wb, waits=guards["s"])
                pe_prev = (C.pe, C.pe.n)
                if stop == "dq1l" and j == 0 and hf == 1:
                    return (s_wb, t_s)
                vq = C.ve.pre(2)
                def fdq(v, t_q=t_q, t_s=t_s, pe_prev=pe_prev, vq=vq):
                    v.wait_ge(s_wa.h, t_q)
                    v.wait_ge(s_wb.h, t_s)
                    wg(v, pe_prev)
                    v.tensor_tensor(C.RW[:, 8192:12288], C.RW[:, 0:4096],
                                    C.RW[:, 4096:8192], ALU.mult)\
                        .then_inc(C.ve.h, 1)
                    v.wait_ge(C.ve.h, vq - 1)
                    v.scalar_tensor_tensor(C.RW[:, 8192:12288],
                                           C.RW[:, 8192:12288], 1.0 / 7.5,
                                           C.RW[:, 4096:8192], ALU.mult,
                                           ALU.subtract).then_inc(C.ve.h, 1)
                C.emit("vector", fdq)
                dq_done = (C.ve, vq)
                if stop == "dq" and j == 0 and hf == 0:
                    return dq_done
                if stop == "dqx" and j == 1 and hf == 0:
                    return dq_done
                if stop == "dq1" and j == 0 and hf == 1:
                    return dq_done
                def ql_evac(m, c, b, bl, pe_t, hf=hf, jj=jj, j=j):
                    c1 = C.ve.pre(1)
                    c2 = C.ve.pre(1)
                    cnt = C.ve.pre(1)
                    def f(v, m=m, c=c, b=b, bl=bl, pe_t=pe_t, hf=hf, jj=jj,
                          j=j, c1=c1, c2=c2):
                        v.wait_ge(C.pe.h, pe_t)
                        mg = hf * 4 + m
                        v.tensor_scalar(C.RY[:, 0:512], C.bank(bl),
                                        C.RCOL2[:, j * 8 + mg:
                                                j * 8 + mg + 1],
                                        None, ALU.add).then_inc(C.ve.h, 1)
                        v.wait_ge(C.ve.h, c1)
                        if jj == 0:
                            dst = C.RC[:, mg * 2048 + c * 512:
                                       mg * 2048 + c * 512 + 512]
                            v.tensor_tensor(dst, C.RY[:, 0:512], C.bank(b),
                                            ALU.add).then_inc(C.ve.h, 2)
                        else:
                            dst = C.RB[:, mg * 2048 + c * 512:
                                       mg * 2048 + c * 512 + 512]
                            v.tensor_tensor(C.RY[:, 512:1024], C.RY[:, 0:512],
                                            C.bank(b), ALU.add)\
                                .then_inc(C.ve.h, 1)
                            v.wait_ge(C.ve.h, c2)
                            v.tensor_tensor(dst, dst, C.RY[:, 512:1024],
                                            ALU.add).then_inc(C.ve.h, 1)
                    C.emit("vector", f)
                    return (C.ve, cnt)
                def ql_lhsT(k, m, hf=hf):
                    return C.RW[:, 8192 + k * 512 + m * 128:
                                8192 + k * 512 + m * 128 + 128]
                first = True
                bi = 0
                for m in range(4):
                    for c in range(4):
                        b = (0, 1, 2, 3)[bi % 4]
                        bl = 4 + bi % 2
                        bi += 1
                        gd = C.bank_guard[b]
                        gdl = C.bank_guard[bl]
                        pw = ((dq_done, mid_done) if first else ())
                        first = False
                        pe_t = C.pe.pre(1)
                        mg = hf * 4 + m
                        def fmm(t, m=m, c=c, b=b, bl=bl, gd=gd, gdl=gdl,
                                pw=pw, mg=mg, src_t=src_t, hf=hf):
                            for w_ in pw:
                                wg(t, w_)
                            wg(t, gd)
                            wg(t, gdl)
                            for k in range(8):
                                t.matmul(C.bank(b), ql_lhsT(k, m, hf),
                                         src_t(k)[:, c * 512: c * 512 + 512],
                                         start=(k == 0), stop=(k == 7))
                            t.matmul(C.bank(bl),
                                     C.RX[0:32, mg * 128: mg * 128 + 128],
                                     C.RX[0:32, 6144 + c * 512: 6656 + c * 512],
                                     start=True, stop=True)\
                                .then_inc(C.pe.h, 1)
                        C.emit("tensor", fmm)
                        gd_ev = ql_evac(m, c, b, bl, pe_t)
                        C.guard_bank(b, *gd_ev)
                        C.guard_bank(bl, *gd_ev)
                guards["q"] = ((C.pe, C.pe.n),)
                guards["s"] = ((C.pe, C.pe.n),)
                if stop == "qlh" and j == 0 and hf == 0:
                    return (C.ve, C.ve.n)
            guards["l"] = ((C.pe, C.pe.n),)
            if stop == "ql%d" % j:
                return (C.ve, C.ve.n)
            if jj == 0:
                lnr = emit_ln(C, lambda k: C.RC[:, k * 2048:(k + 1) * 2048],
                              lambda k: C.RC[:, k * 2048:(k + 1) * 2048],
                              eps=1e-5, in_bf16=True,
                              gain_col=lambda k, b=blk_i:
                              C.RCOL2[:, 48 + b * 8 + k:49 + b * 8 + k],
                              bias_col=lambda k, b=blk_i:
                              C.RCOL2[:, 72 + b * 8 + k:73 + b * 8 + k],
                              pre_waits=((C.ve, C.ve.n),),
                              sq_region=RAb[:, 16384:20480])
                for k in range(ND):
                    C.act.pre(1)
                    def fgl(s, k=k, lnr=lnr):
                        if k == 0:
                            wg(s, lnr)
                        s.activation(C.RC[:, k * 2048:(k + 1) * 2048],
                                     C.RC[:, k * 2048:(k + 1) * 2048],
                                     AF.Gelu).then_inc(C.act.h, 1)
                    C.emit("scalar", fgl)
                s_done = (C.act, C.act.n)
                if stop == "lng" and blk_i == 0:
                    return s_done
            else:
                s_done = (C.ve, C.ve.n)
    ve_fin = (C.ve, C.ve.n)
    for k in range(ND):
        C.act.pre(1)
        def fsl(s, k=k, ve_fin=ve_fin):
            if k == 0:
                wg(s, ve_fin)
            s.activation(C.RB[:, k * 2048:(k + 1) * 2048],
                         C.RB[:, k * 2048:(k + 1) * 2048], AF.Silu)\
                .then_inc(C.act.h, 1)
        C.emit("scalar", fsl)
    return (C.act, C.act.n)


def build_l2(stop=None):
    nc = bass.Bass("TRN2", target_bir_lowering=False)
    xin = nc.dram_tensor("xin", [ND, 128, T], F32, kind="ExternalInput").ap()
    H = {}
    for nm, shp in [("consts", [128, 384])] + L2_SHAPES:
        kind = I32 if nm.startswith("l2_qw") else F32
        H[nm] = nc.dram_tensor(nm, shp, kind, kind="ExternalInput").ap()
    xout = nc.dram_tensor("xout", [ND, 128, T], F32, kind="ExternalOutput").ap()
    dbg = nc.dram_tensor("dbg", [ND, 128, T], F32, kind="ExternalOutput").ap()
    es = ExitStack()
    with es:
        C = Ctx(nc, es)
        blk = es.enter_context(nc.Block())
        s_x = C.sem("x")
        s_k = C.sem("k")
        for k in range(ND):
            e_dma(C, "sync", C.RA[:, k * 2048:(k + 1) * 2048], xin[k], s_x)
        e_dma(C, "gpsimd", C.CONST[:], H["consts"], s_k)
        e_dma(C, "gpsimd", C.RCOL2[:], H["l2_cols"], s_k)
        konst_ready = [(s_k, s_k.n), (s_x, s_x.n)]
        ln1 = emit_ln(C, lambda k: C.RA[:, k * 2048:(k + 1) * 2048],
                      lambda k: C.RB[:, k * 2048:(k + 1) * 2048],
                      pre_waits=tuple(konst_ready))
        r = emit_l2(C, H, stop=stop)
        if stop == "lng":
            sd = C.sem("dbgo")
            for k in range(ND):
                e_dma(C, "gpsimd", dbg[k], C.RC[:, k * 2048:(k + 1) * 2048],
                      sd, waits=(r,))
            def ffin(g, n=sd.n):
                g.wait_ge(sd.h, n)
            C.emit("gpsimd", ffin)
            C.finalize(blk)
            return nc
        if stop == "dq1l":
            sd = C.sem("dbgo")
            e_dma(C, "gpsimd", dbg[0], C.RW[:, 0:2048], sd, waits=(r,))
            def ffin(g, n=sd.n):
                g.wait_ge(sd.h, n)
            C.emit("gpsimd", ffin)
            C.finalize(blk)
            return nc
        if stop == "dq1":
            sd = C.sem("dbgo")
            e_dma(C, "gpsimd", dbg[0], C.RW[:, 8192:10240], sd, waits=(r,))
            def ffin(g, n=sd.n):
                g.wait_ge(sd.h, n)
            C.emit("gpsimd", ffin)
            C.finalize(blk)
            return nc
        if stop == "qlh":
            sd = C.sem("dbgo")
            for k in range(4):
                e_dma(C, "gpsimd", dbg[k], C.RC[:, k * 2048:(k + 1) * 2048],
                      sd, waits=(r,))
            def ffin(g, n=sd.n):
                g.wait_ge(sd.h, n)
            C.emit("gpsimd", ffin)
            C.finalize(blk)
            return nc
        if stop in ("dq", "dqx"):
            sd = C.sem("dbgo")
            for k in range(2):
                e_dma(C, "gpsimd", dbg[k], C.RW[:, 8192 + k * 2048:
                                                8192 + (k + 1) * 2048],
                      sd, waits=(r,))
            def ffin(g, n=sd.n):
                g.wait_ge(sd.h, n)
            C.emit("gpsimd", ffin)
            C.finalize(blk)
            return nc
        if stop is not None and stop.startswith("mid"):
            sd = C.sem("dbgo")
            e_dma(C, "gpsimd", dbg[0][0:32, :], C.RX[0:32, 5120:7168],
                  sd, waits=(r,))
            def ffin(g, n=sd.n):
                g.wait_ge(sd.h, n)
            C.emit("gpsimd", ffin)
            C.finalize(blk)
            return nc
        if stop is not None and stop.startswith("ql"):
            sd = C.sem("dbgo")
            reg = C.RC if stop in ("ql0", "ql2", "ql4") else C.RB
            for k in range(ND):
                e_dma(C, "gpsimd", dbg[k], reg[:, k * 2048:(k + 1) * 2048],
                      sd, waits=(r,))
            reg2 = C.RB if stop in ("ql0", "ql2", "ql4") else C.RC
            for k in range(ND):
                e_dma(C, "gpsimd", xout[k], reg2[:, k * 2048:(k + 1) * 2048],
                      sd, waits=(r,))
            def ffin(g, n=sd.n):
                g.wait_ge(sd.h, n)
            C.emit("gpsimd", ffin)
            C.finalize(blk)
            return nc
        if stop in ("h", "norm", "attn"):
            sd = C.sem("dbgo")
            src_reg = C.RAb if stop != "attn" else C.RAb
            for k in range(ND):
                e_dma(C, "gpsimd", dbg[k],
                      C.RAb[:, k * 2048:(k + 1) * 2048], sd, waits=(r,))
            def ffin(g, n=sd.n):
                g.wait_ge(sd.h, n)
            C.emit("gpsimd", ffin)
            C.finalize(blk)
            return nc
        g_done = r
        s_xg = C.sem("gx")
        s_o = C.sem("gout")
        prev_tt = {}
        for k in range(ND):
            wx = [g_done] if k == 0 else [(C.ve, prev_tt[k - 1])]
            t_x = e_dma(C, "sync", C.RY[:, 0:2048], xin[k], s_xg, waits=wx)
            v_t = C.ve.pre(1)
            def ftt(v, k=k, t_x=t_x, g_done=g_done):
                v.wait_ge(s_xg.h, t_x)
                wg(v, g_done)
                v.tensor_tensor(C.RY[:, 2048:4096],
                                C.RAb[:, k * 2048:(k + 1) * 2048],
                                C.RB[:, k * 2048:(k + 1) * 2048], ALU.mult)
                v.tensor_tensor(C.RY[:, 0:2048], C.RY[:, 0:2048],
                                C.RY[:, 2048:4096], ALU.add)\
                    .then_inc(C.ve.h, 1)
            C.emit("vector", ftt)
            prev_tt[k] = v_t
            e_dma(C, "sync", xout[k], C.RY[:, 0:2048], s_o,
                  waits=[(C.ve, v_t)])
        def ffin(g, n=s_o.n):
            g.wait_ge(s_o.h, n)
            g.memset(C.RX[0:1, 0:1], 0.0)
        C.emit("gpsimd", ffin)
        C.finalize(blk)
    return nc



def stage_l3(inputs, g):
    Wa = np.asarray(inputs["L3_attn_W_attn"], np.float32)
    g["l3_wq"] = np.ascontiguousarray(Wa[:, 0:1024])
    g["l3_wk"] = np.ascontiguousarray(Wa[:, 1024:2048])
    g["l3_wv"] = np.ascontiguousarray(Wa[:, 2048:3072])
    ba = np.asarray(inputs["L3_attn_b_attn"], np.float32)
    g["l3_bvrow"] = np.ascontiguousarray(ba[2048:3072].reshape(1, 1024))
    wp = np.ascontiguousarray(np.asarray(inputs["L3_attn_W_proj"], np.float32))
    g["l3_wp_h0"] = np.ascontiguousarray(wp[:, 0:512])
    g["l3_wp_h1"] = np.ascontiguousarray(wp[:, 512:1024])
    rwt = np.asarray(inputs["L3_ffn_router_w"],
                     np.float32).reshape(8, 128).T
    rw_hi = rwt.astype(ml_dtypes.bfloat16).astype(np.float32)
    g["l3f_rw"] = np.ascontiguousarray(
        np.concatenate([rw_hi, rwt - rw_hi], axis=1))
    g["l3f_wq"] = np.ascontiguousarray(np.asarray(inputs["L3_ffn_wq"],
                                                  np.float32))
    g["l3f_wk"] = np.ascontiguousarray(np.asarray(inputs["L3_ffn_wk"],
                                                  np.float32))
    g["l3f_wv"] = np.ascontiguousarray(np.asarray(inputs["L3_ffn_wv"],
                                                  np.float32))
    wo = np.ascontiguousarray(np.asarray(inputs["L3_ffn_wo"], np.float32))
    g["l3f_wo_h0"] = np.ascontiguousarray(wo[:, 0:512])
    g["l3f_wo_h1"] = np.ascontiguousarray(wo[:, 512:1024])
    g["l3f_w1"] = np.ascontiguousarray(np.asarray(inputs["L3_ffn_w1"],
                                                  np.float32))
    g["l3f_w2"] = np.ascontiguousarray(np.asarray(inputs["L3_ffn_w2"],
                                                  np.float32))
    cols = np.zeros((128, 64), np.float32)
    cols[:, 0:8] = ba[0:1024].reshape(8, 128).T
    cols[:, 8:16] = ba[1024:2048].reshape(8, 128).T
    cols[:, 16:24] = np.asarray(inputs["L3_attn_b_proj"], np.float32)\
        .reshape(8, 128).T
    cols[:, 24:32] = np.asarray(inputs["L3_ffn_ln1"], np.float32)\
        .reshape(8, 128).T
    cols[:, 32:40] = np.asarray(inputs["L3_ffn_ln2"], np.float32)\
        .reshape(8, 128).T
    cols[0, 40] = float(np.asarray(inputs["L3_ffn_router_b"]).reshape(-1)[0])
    cols[:, 41] = np.where((np.arange(128) % 64) < 32, -1.0, 1.0)
    inv = np.float32(1.0) / np.float32(10000.0) ** (
        np.arange(0, 64, 2, dtype=np.float32) / np.float32(64.0))
    cols[:, 42] = np.tile(inv, 4)
    g["l3_cols"] = np.ascontiguousarray(cols)
    return g


def stage_l3_pos(inputs, b):
    pid = np.asarray(inputs["L3_ffn_position_ids"])
    return np.ascontiguousarray(
        pid[min(b, 3), 0:T].astype(np.float32).reshape(1, T))


L3_SHAPES = [("l3_wq", [D, D]), ("l3_wk", [D, D]), ("l3_wv", [D, D]),
             ("l3_bvrow", [1, D]), ("l3_wp_h0", [D, 512]),
             ("l3_wp_h1", [D, 512]), ("l3f_rw", [128, 16]),
             ("l3f_wq", [D, D]), ("l3f_wk", [D, D]), ("l3f_wv", [D, D]),
             ("l3f_wo_h0", [D, 512]), ("l3f_wo_h1", [D, 512]),
             ("l3f_w1", [D, 4096]), ("l3f_w2", [4096, D]),
             ("l3_cols", [128, 64])]


def emit_l3a(C, W, spill_done=None, stop=None):
    RAb = C.RAb
    s_wa = C.sem("3wa")
    s_wb = C.sem("3wb")
    pe0 = (C.pe, C.pe.n)
    po_t = C.po.pre(1)
    ve_in = (C.ve, C.ve.n)
    def fmast(g, ve_in=ve_in):
        wg(g, ve_in)
        g.iota(C.RA[:, 0:2944].bitcast(I32), [[-1, 2944]], base=896,
               channel_multiplier=1).then_inc(C.po.h, 1)
    C.emit("gpsimd", fmast)
    v_mc = C.ve.pre(1)
    def fmcv(v, po_t=po_t):
        v.wait_ge(C.po.h, po_t)
        v.tensor_copy(C.RY[:, 0:2944], C.RA[:, 0:2944].bitcast(I32))\
            .then_inc(C.ve.h, 1)
    C.emit("vector", fmcv)
    po_t2 = C.po.pre(1)
    def fmast2(g, v_mc=v_mc):
        g.wait_ge(C.ve.h, v_mc)
        g.affine_select(C.RY[:, 0:2944], C.RY[:, 0:2944], [[1, 2944]],
                        ALU.is_ge, -1e9, base=-896, channel_multiplier=-1)\
            .then_inc(C.po.h, 1)
    C.emit("gpsimd", fmast2)
    mast_ready = (C.po, C.po.n)
    t_wv = load_wmat(C, C.RW, 0, W["l3_wv"], 8, 1024, s_wa, waits=(pe0,))
    t_bv = e_dma(C, "gpsimd", C.RX[0:1, 6656:7680], W["l3_bvrow"], s_wa)
    po_v = C.po.pre(1)
    ve0 = (C.ve, C.ve.n)
    def fonesv(g, ve0=ve0):
        wg(g, ve0)
        g.memset(C.RC[:, 0:15600], 1.0)
        g.memset(C.RW[:, 8192:9232], 1.0).then_inc(C.po.h, 1)
    C.emit("gpsimd", fonesv)
    def vtm_slice(tt):
        if tt < 15:
            return C.RC[:, tt * 1040:(tt + 1) * 1040]
        return C.RW[:, 8192:9232]
    for tt in range(16):
        for c in range(2):
            b = (tt * 2 + c) % 4
            gd = C.bank_guard[b]
            pe_t = C.pe.pre(1)
            pwv = ((s_wa, t_wv), (s_wa, t_bv), spill_done) \
                if tt == 0 and c == 0 else ()
            def fmm(t, tt=tt, c=c, b=b, gd=gd, pwv=pwv):
                for w_ in pwv:
                    wg(t, w_)
                wg(t, gd)
                for k in range(8):
                    t.matmul(C.bank(b),
                             C.RB[:, k * 2048 + tt * 128: k * 2048 + tt * 128 + 128],
                             C.RW[:, k * 1024 + c * 512: k * 1024 + c * 512 + 512],
                             start=(k == 0), stop=False)
                t.matmul(C.bank(b), C.ones_bf[0:1, 0:128],
                         C.RX[0:1, 6656 + c * 512: 6656 + c * 512 + 512],
                         start=False, stop=True).then_inc(C.pe.h, 1)
            C.emit("tensor", fmm)
            cnt = C.ve.pre(1)
            def fev(v, tt=tt, c=c, b=b, pe_t=pe_t, po_v=po_v):
                v.wait_ge(C.pe.h, pe_t)
                v.wait_ge(C.po.h, po_v)
                dst = vtm_slice(tt).rearrange("p (h e) -> p h e", h=16)[:, c * 8:
                                                                       c * 8 + 8,
                                                                       0:64]
                v.tensor_copy(dst, C.bank(b).rearrange("p (h e) -> p h e", h=8))\
                    .then_inc(C.ve.h, 1)
            C.emit("vector", fev)
            C.guard_bank(b, C.ve, cnt)
    vtm_done = (C.ve, C.ve.n)
    pe_v = (C.pe, C.pe.n)
    t_wq = load_wmat(C, C.RW, 0, W["l3_wq"], 8, 1024, s_wb, waits=(pe_v,))
    def q_evac(m, c, b, pe_t):
        cnt = C.ve.pre(1)
        def f(v, m=m, c=c, b=b, pe_t=pe_t):
            v.wait_ge(C.pe.h, pe_t)
            v.tensor_scalar(RAb[:, m * 2048 + c * 512: m * 2048 + c * 512 + 512],
                            C.bank(b), C.RCOL[0:128, 0 + m:1 + m], 0.125,
                            ALU.add, ALU.mult).then_inc(C.ve.h, 1)
        C.emit("vector", f)
        return (C.ve, cnt)
    mm_chain(C,
             lambda k, m: C.RW[:, k * 1024 + m * 128: k * 1024 + m * 128 + 128],
             lambda k, c: C.RB[:, k * 2048 + c * 512: k * 2048 + c * 512 + 512],
             q_evac, m_tiles=8, k_list=range(8), n_chunks=4,
             pre_waits=((s_wb, t_wq),), banks=(0, 1, 2, 3))
    pe_q = (C.pe, C.pe.n)
    t_wk = load_wmat(C, C.RW, 0, W["l3_wk"], 8, 1024, s_wa, waits=(pe_q,))
    mm_chain(C,
             lambda k, m: C.RW[:, k * 1024 + m * 128: k * 1024 + m * 128 + 128],
             lambda k, c: C.RB[:, k * 2048 + c * 512: k * 2048 + c * 512 + 512],
             evac_ts_bias(C, lambda m, c:
                          RAb[:, 16384 + m * 2048 + c * 512:
                              16384 + m * 2048 + c * 512 + 512],
                          lambda m: C.RCOL[:, 8 + m:9 + m]),
             m_tiles=8, k_list=range(8), n_chunks=4,
             pre_waits=((s_wa, t_wk),), banks=(0, 1, 2, 3))
    qk_done = (C.ve, C.ve.n)
    if stop == "qkv":
        return qk_done

    def qT_fn(h):
        b = 64 * (h % 2)
        return RAb[b:b + 64, (h // 2) * 2048: (h // 2) * 2048 + 2048]

    def kT_fn(h):
        b = 64 * (h % 2)
        return RAb[b:b + 64, 16384 + (h // 2) * 2048:
                   16384 + (h // 2) * 2048 + 2048]

    def v_lhsT(h, kt):
        return vtm_slice(kt)[:, h * 65: h * 65 + 65]

    def o_dst(h, qb):
        b = 64 * (h % 2)
        return C.RB[b:b + 64,
                    (h // 2) * 2048 + qb * QB: (h // 2) * 2048 + qb * QB + QB]

    def alibi_mask(h, qb, kt, sb, pe_s):
        slope = 1.0 / (2.0 ** (h + 1))
        off = 896 - (kt * 128 - qb * QB)
        cnt = C.ve.pre(1)
        def f(v, sb=sb, off=off, slope=slope, pe_s=pe_s,
              mast_ready=mast_ready):
            v.wait_ge(C.pe.h, pe_s)
            wg(v, mast_ready)
            v.scalar_tensor_tensor(C.ps[:, sb * 512: sb * 512 + 1024],
                                   C.RY[:, off:off + 1024], slope,
                                   C.ps[:, sb * 512: sb * 512 + 1024],
                                   ALU.mult, ALU.add).then_inc(C.ve.h, 1)
        C.emit("vector", f)
        return (C.ve, cnt)

    kt_hi = lambda qb: (qb + 1) * (QB // 128)
    emit_attn64(C, list(range(16)), qT_fn, kT_fn, v_lhsT, o_dst,
                shift="passA", kt_hi_fn=kt_hi, mask_fn=alibi_mask,
                ry_off=3072)
    attn_pe = (C.pe, C.pe.n)
    attn_ve = (C.ve, C.ve.n)
    for halfw in range(2):
        t_wo = load_wmat(C, C.RW, halfw * 4096, W["l3_wp_h%d" % halfw],
                         8, 512, s_wb if halfw == 0 else s_wa,
                         waits=(attn_pe,))
        mm_chain(C,
                 lambda k, m, halfw=halfw:
                 C.RW[:, halfw * 4096 + k * 512 + m * 128:
                      halfw * 4096 + k * 512 + m * 128 + 128],
                 lambda k, c: C.RB[:, k * 2048 + c * 512:
                                   k * 2048 + c * 512 + 512],
                 evac_ts_bias(C, lambda m, c, halfw=halfw:
                              RAb[:, (halfw * 4 + m) * 2048 + c * 512:
                                  (halfw * 4 + m) * 2048 + c * 512 + 512],
                              lambda m, halfw=halfw:
                              C.RCOL[:, 16 + halfw * 4 + m:
                                     17 + halfw * 4 + m]),
                 m_tiles=4, k_list=range(8), n_chunks=4,
                 pre_waits=((s_wb if halfw == 0 else s_wa, t_wo), attn_ve),
                 banks=(0, 1, 2, 3))
    return (C.ve, C.ve.n)


def emit_l3f(C, W, xnsp, s_spill_done, stop=None):
    RAb = C.RAb
    s_wa = C.sem("3fa")
    s_wb = C.sem("3fb")
    s_xr = C.sem("3xr")
    pe0 = (C.pe, C.pe.n)
    ve0 = (C.ve, C.ve.n)
    for k in range(ND):
        e_dma(C, "gpsimd", C.RB[:, k * 2048:(k + 1) * 2048], xnsp[k], s_xr,
              waits=(pe0, ve0) if k == 0 else ())
    xn_rl = (s_xr, s_xr.n)
    t_rw = e_dma(C, "gpsimd", C.RW[:, 9232:9248], W["l3f_rw"], s_wa,
                 waits=(pe0,))
    v_w = None
    for rr in range(2):
        gd45 = C.snap_guards([4, 5])
        pe_r = C.pe.pre(1)
        def frout(t, rr=rr, t_rw=t_rw, gd45=gd45, xn_rl=xn_rl):
            if rr == 0:
                wg(t, (s_wa, t_rw))
                wg(t, xn_rl)
            wg(t, gd45[0])
            wg(t, gd45[1])
            last = None
            for ch in range(2):
                c = rr * 2 + ch
                for kk in range(16):
                    k = kk % 8
                    col = 9232 + (kk // 8) * 8 + k
                    last = t.matmul(
                        C.ps[0:1, 2048 + ch * 512: 2048 + ch * 512 + 512],
                        C.RW[:, col: col + 1],
                        C.RB[:, k * 2048 + c * 512: k * 2048 + c * 512 + 512],
                        start=(kk == 0), stop=(kk == 15))
            last.then_inc(C.pe.h, 1)
        C.emit("tensor", frout)
        v_w = C.ve.pre(1)
        def fwev(v, rr=rr, pe_r=pe_r):
            v.wait_ge(C.pe.h, pe_r)
            v.tensor_scalar(C.RY[0:1, rr * 1024: rr * 1024 + 1024],
                            C.ps[0:1, 2048:3072], C.RCOL[0:1, 40:41], None,
                            ALU.add).then_inc(C.ve.h, 1)
        C.emit("vector", fwev)
        C.guard_bank(4, C.ve, C.ve.n)
        C.guard_bank(5, C.ve, C.ve.n)
    vwh = C.ve.pre(2)
    def fwhl(v, v_w=v_w, vwh=vwh):
        v.wait_ge(C.ve.h, v_w)
        v.tensor_copy(C.RX[0:1, 2048:4096], C.RY[0:1, 0:2048])\
            .then_inc(C.ve.h, 1)
        v.wait_ge(C.ve.h, vwh - 1)
        v.tensor_tensor(C.RX[0:1, 4096:6144], C.RY[0:1, 0:2048],
                        C.RX[0:1, 2048:4096], ALU.subtract)\
            .then_inc(C.ve.h, 1)
    C.emit("vector", fwhl)
    gd6w = C.snap_guards([6])
    pe_tr = C.pe.pre(1)
    def ftr16(t, vwh=vwh, gd6w=gd6w):
        t.wait_ge(C.ve.h, vwh)
        wg(t, gd6w[0])
        last = None
        for tt in range(16):
            t.matmul(C.ps[:, 3072 + tt: 3073 + tt],
                     C.RX[0:1, 2048 + tt * 128: 2048 + tt * 128 + 128],
                     C.ones_bf[0:1, 0:1], start=True, stop=False)
            last = t.matmul(C.ps[:, 3072 + tt: 3073 + tt],
                            C.RX[0:1, 4096 + tt * 128: 4096 + tt * 128 + 128],
                            C.ones_bf[0:1, 0:1], start=False, stop=True)
        last.then_inc(C.pe.h, 1)
    C.emit("tensor", ftr16)
    v_16 = C.ve.pre(1)
    def fw16(v, pe_tr=pe_tr):
        v.wait_ge(C.pe.h, pe_tr)
        v.tensor_copy(C.RY[:, 2048:2064], C.ps[:, 3072:3088])\
            .then_inc(C.ve.h, 1)
    C.emit("vector", fw16)
    C.guard_bank(6, C.ve, v_16)
    w16_ready = (C.ve, v_16)
    if stop == "w16":
        return w16_ready
    vb = C.ve.pre(1)
    c_lo = C.constf(-10000.0)
    c_hi = C.constf(10000.0)
    def finit(v, w16_ready=w16_ready, c_lo=c_lo, c_hi=c_hi):
        wg(v, w16_ready)
        v.tensor_copy(C.RY[:, 2064:2065], c_lo)
        v.tensor_copy(C.RY[:, 2065:2066], c_hi).then_inc(C.ve.h, 1)
    C.emit("vector", finit)
    for it in range(48):
        prev_it = C.ve.n
        v1 = C.ve.pre(1)
        v1b = C.ve.pre(1)
        v2 = C.ve.pre(1)
        def fbm(v, v1=v1, v1b=v1b, prev_it=prev_it):
            v.wait_ge(C.ve.h, prev_it)
            v.tensor_scalar(C.RY[:, 2066:2067], C.RY[:, 2064:2065],
                            C.RY[:, 2065:2066], 0.5, ALU.add, ALU.mult)\
                .then_inc(C.ve.h, 1)
            v.wait_ge(C.ve.h, v1)
            v.tensor_scalar(C.RY[:, 2080:2096], C.RY[:, 2048:2064],
                            C.RY[:, 2066:2067], None, ALU.is_gt)\
                .then_inc(C.ve.h, 1)
            v.wait_ge(C.ve.h, v1b)
            v.tensor_scalar(C.RX[:, 2048:2064], C.RY[:, 2080:2096], 1.0,
                            0.0, ALU.mult, ALU.add,
                            accum_out=C.RY[:, 2067:2068]).then_inc(C.ve.h, 1)
        C.emit("vector", fbm)
        v3 = C.ve.pre(1)
        def fcb(v, v2=v2):
            v.wait_ge(C.ve.h, v2)
            v.tensor_copy(C.RX[:, 2064:2065], C.RY[:, 2067:2068])\
                .then_inc(C.ve.h, 1)
        C.emit("vector", fcb)
        gd6 = C.snap_guards([6])
        pe_c = C.pe.pre(1)
        def fct(t, v3=v3, gd6=gd6):
            t.wait_ge(C.ve.h, v3)
            wg(t, gd6[0])
            t.matmul(C.ps[:, 3072:3073], C.ones_bf, C.RX[:, 2064:2065],
                     start=True, stop=True).then_inc(C.pe.h, 1)
        C.emit("tensor", fct)
        v4 = C.ve.pre(1)
        v5 = C.ve.pre(1)
        v6 = C.ve.pre(1)
        v7 = C.ve.pre(1)
        def fup(v, pe_c=pe_c, v4=v4, v5=v5, v6=v6, it=it):
            v.wait_ge(C.pe.h, pe_c)
            if it < 16:
                v.tensor_copy(C.RY[:, 2128 + it: 2129 + it],
                              C.ps[:, 3072:3073])
            v.tensor_scalar(C.RY[:, 2068:2069], C.ps[:, 3072:3073], 255.5,
                            None, ALU.is_ge).then_inc(C.ve.h, 1)
            v.wait_ge(C.ve.h, v4)
            v.tensor_tensor(C.RY[:, 2069:2070], C.RY[:, 2066:2067],
                            C.RY[:, 2064:2065], ALU.subtract)
            v.tensor_tensor(C.RY[:, 2070:2071], C.RY[:, 2066:2067],
                            C.RY[:, 2065:2066], ALU.subtract)
            v.tensor_scalar(C.RY[:, 2071:2072], C.RY[:, 2068:2069], -1.0, 1.0,
                            ALU.mult, ALU.add).then_inc(C.ve.h, 1)
            v.wait_ge(C.ve.h, v5)
            v.tensor_tensor(C.RY[:, 2069:2070], C.RY[:, 2069:2070],
                            C.RY[:, 2068:2069], ALU.mult)
            v.tensor_tensor(C.RY[:, 2070:2071], C.RY[:, 2070:2071],
                            C.RY[:, 2071:2072], ALU.mult).then_inc(C.ve.h, 1)
            v.wait_ge(C.ve.h, v6)
            v.tensor_tensor(C.RY[:, 2064:2065], C.RY[:, 2064:2065],
                            C.RY[:, 2069:2070], ALU.add)
            v.tensor_tensor(C.RY[:, 2065:2066], C.RY[:, 2065:2066],
                            C.RY[:, 2070:2071], ALU.add).then_inc(C.ve.h, 1)
        C.emit("vector", fup)
        C.guard_bank(6, C.ve, C.ve.n)
    prev_n = C.ve.n
    if stop == "bis":
        return (C.ve, prev_n)
    thr_done = C.ve.pre(1)
    def fsel(v, prev_n=prev_n):
        v.wait_ge(C.ve.h, prev_n)
        v.tensor_scalar(C.RY[:, 2080:2096], C.RY[:, 2048:2064],
                        C.RY[:, 2065:2066], None, ALU.is_gt)\
            .then_inc(C.ve.h, 1)
    C.emit("vector", fsel)
    v6 = C.ve.pre(1)
    def fkm(v, thr_done=thr_done):
        v.wait_ge(C.ve.h, thr_done)
        v.tensor_scalar(C.RM[:, 64:80], C.RY[:, 2080:2096], 1e9, -1e9,
                        ALU.mult, ALU.add)
        v.tensor_tensor(C.RY[:, 2096:2112], C.RY[:, 2048:2064],
                        C.RY[:, 2080:2096], ALU.mult).then_inc(C.ve.h, 1)
    C.emit("vector", fkm)
    v7a = C.ve.pre(1)
    v7 = C.ve.pre(1)
    def fbf2(v, v6=v6, v7a=v7a):
        v.wait_ge(C.ve.h, v6)
        v.tensor_copy(C.RX[:, 2064:2080], C.RY[:, 2096:2112])\
            .then_inc(C.ve.h, 1)
        v.wait_ge(C.ve.h, v7a)
        v.tensor_tensor(C.RY[:, 2112:2128], C.RY[:, 2096:2112],
                        C.RX[:, 2064:2080], ALU.subtract).then_inc(C.ve.h, 1)
    C.emit("vector", fbf2)
    v8 = C.ve.pre(1)
    def fbf3(v, v7=v7):
        v.wait_ge(C.ve.h, v7)
        v.tensor_copy(C.RX[:, 2080:2096], C.RY[:, 2112:2128])
        v.tensor_scalar(C.RX[:, 2096:2112], C.RY[:, 2080:2096], -1.0, 1.0,
                        ALU.mult, ALU.add).then_inc(C.ve.h, 1)
    C.emit("vector", fbf3)
    for nm in ("wh", "iv"):
        gds = C.snap_guards([0, 1, 2, 3])
        pe_t = C.pe.pre(1)
        def ftr(t, nm=nm, gds=gds, v8=v8):
            t.wait_ge(C.ve.h, v8)
            last = None
            for bk in range(4):
                wg(t, gds[bk])
                for j in range(4):
                    tt = bk * 4 + j
                    dst = C.ps[0:1, bk * 512 + j * 128: bk * 512 + j * 128 + 128]
                    if nm == "wh":
                        t.matmul(dst, C.RX[:, 2064 + tt: 2065 + tt], C.ident,
                                 start=True, stop=False)
                        last = t.matmul(dst, C.RX[:, 2080 + tt: 2081 + tt],
                                        C.ident, start=False, stop=True)
                    else:
                        last = t.matmul(dst, C.RX[:, 2096 + tt: 2097 + tt],
                                        C.ident, start=True, stop=True)
            last.then_inc(C.pe.h, 1)
        C.emit("tensor", ftr)
        a_t = C.act.pre(1)
        def frow(s, pe_t=pe_t, nm=nm):
            s.wait_ge(C.pe.h, pe_t)
            last = None
            for bk in range(4):
                dst = (C.RY[0:1, bk * 512: bk * 512 + 512] if nm == "wh"
                       else C.RY[0:1, 2048 + bk * 512: 2048 + bk * 512 + 512])
                last = s.activation(dst, C.ps[0:1, bk * 512: bk * 512 + 512],
                                    AF.Copy)
            last.then_inc(C.act.h, 1)
        C.emit("scalar", frow)
        for bk in range(4):
            C.guard_bank(bk, C.act, a_t)
    rows_act = (C.act, C.act.n)
    rsp = C.nc.dram_tensor("rsp", [2, T], F32, kind="Internal").ap()
    C.rsp = rsp
    s_rs = C.sem("3rs")
    t_r1 = e_dma(C, "sync", rsp[0:1], C.RY[0:1, 0:2048], s_rs,
                 waits=[rows_act])
    t_r2 = e_dma(C, "sync", rsp[1:2], C.RY[0:1, 2048:4096], s_rs,
                 waits=[(s_rs, t_r1)])
    rows_done = (s_rs, t_r2)
    if stop == "router":
        return rows_done
    return emit_l3f_attn(C, W, xnsp, rows_done, rsp, s_spill_done, stop)


def emit_l3f_attn(C, W, xnsp, rows_done, rsp, s_spill_done, stop=None):
    RAb = C.RAb
    s_wa = C.sem("3ga")
    s_wb = C.sem("3gb")
    s_ps = C.sem("3ps")
    t_p = e_dma(C, "gpsimd", C.RY[0:1, 2048:4096], W["l3_pos"], s_ps,
                waits=[rows_done])
    vp = C.ve.pre(2)
    def fph(v, t_p=t_p, vp=vp, rows_done=rows_done):
        v.wait_ge(s_ps.h, t_p)
        wg(v, rows_done)
        v.tensor_copy(C.RX[0:1, 2048:4096], C.RY[0:1, 2048:4096])\
            .then_inc(C.ve.h, 1)
        v.wait_ge(C.ve.h, vp - 1)
        v.tensor_tensor(C.RX[0:1, 4096:6144], C.RY[0:1, 2048:4096],
                        C.RX[0:1, 2048:4096], ALU.subtract).then_inc(C.ve.h, 1)
    C.emit("vector", fph)
    for c in range(4):
        gd6 = C.snap_guards([6])
        pe_p = C.pe.pre(1)
        def fpbc(t, c=c, vp=vp, gd6=gd6):
            t.wait_ge(C.ve.h, vp)
            wg(t, gd6[0])
            t.matmul(C.ps[:, 3072:3584], C.ones_bf[0:1, :],
                     C.RX[0:1, 2048 + c * 512: 2048 + c * 512 + 512],
                     start=True, stop=False)
            t.matmul(C.ps[:, 3072:3584], C.ones_bf[0:1, :],
                     C.RX[0:1, 4096 + c * 512: 4096 + c * 512 + 512],
                     start=False, stop=True).then_inc(C.pe.h, 1)
        C.emit("tensor", fpbc)
        vs = [C.ve.pre(1) for _ in range(4)]
        def fth(v, c=c, pe_p=pe_p, vs=vs):
            v.wait_ge(C.pe.h, pe_p)
            v.tensor_scalar(C.RY[:, 3584:4096], C.ps[:, 3072:3584],
                            C.RCOL[:, 42:43], None, ALU.mult)\
                .then_inc(C.ve.h, 1)
            v.wait_ge(C.ve.h, vs[0])
            v.tensor_scalar(C.RY[:, 3072:3584], C.RY[:, 3584:4096],
                            0.15915494309189535, None, ALU.mult)\
                .then_inc(C.ve.h, 1)
            v.wait_ge(C.ve.h, vs[1])
            v.tensor_copy(C.RY[:, 2048:2560].bitcast(I32),
                          C.RY[:, 3072:3584]).then_inc(C.ve.h, 1)
            v.wait_ge(C.ve.h, vs[2])
            v.tensor_copy(C.RY[:, 2560:3072],
                          C.RY[:, 2048:2560].bitcast(I32)).then_inc(C.ve.h, 1)
            v.wait_ge(C.ve.h, vs[3])
            v.scalar_tensor_tensor(C.RY[:, 3072:3584], C.RY[:, 2560:3072],
                                   -6.283185307179586, C.RY[:, 3584:4096],
                                   ALU.mult, ALU.add).then_inc(C.ve.h, 1)
        C.ve.pre(1)
        C.emit("vector", fth)
        v_r = C.ve.n
        acs = [C.act.pre(1) for _ in range(3)]
        c_hpi = C.constf(1.5707963267948966)
        def fsin(s, c=c, v_r=v_r, acs=acs, c_hpi=c_hpi):
            s.wait_ge(C.ve.h, v_r)
            s.activation(C.RY[:, 2048:2560], C.RY[:, 3072:3584], AF.Sin)\
                .then_inc(C.act.h, 1)
            s.activation(C.RY[:, 2560:3072], C.RY[:, 3072:3584], AF.Abs)\
                .then_inc(C.act.h, 1)
            s.wait_ge(C.act.h, acs[1])
            s.activation(C.RY[:, 2560:3072], C.RY[:, 2560:3072], AF.Sin,
                         scale=-1.0, bias=c_hpi).then_inc(C.act.h, 1)
        C.emit("scalar", fsin)
        C.ve.pre(1)
        def ftab(v, c=c, a3=acs[2]):
            v.wait_ge(C.act.h, a3)
            v.tensor_copy(C.ROPE[:, c * 512: c * 512 + 512],
                          C.RY[:, 2560:3072])
            v.tensor_scalar(C.ROPE[:, 2048 + c * 512: 2048 + c * 512 + 512],
                            C.RY[:, 2048:2560], C.RCOL[:, 41:42], None,
                            ALU.mult).then_inc(C.ve.h, 1)
        C.emit("vector", ftab)
        C.guard_bank(6, C.ve, C.ve.n)
    tab_done = (C.ve, C.ve.n)
    if stop == "tab":
        return tab_done
    rms1 = emit_ln(C, lambda k: C.RB[:, k * 2048:(k + 1) * 2048],
                   lambda k: C.RB[:, k * 2048:(k + 1) * 2048],
                   rms=True, eps=1e-6, in_bf16=True,
                   gain_col=lambda k: C.RCOL[:, 24 + k:25 + k],
                   pre_waits=(rows_done, tab_done), sq_region=C.RC[:, 0:4096])
    pe0 = (C.pe, C.pe.n)
    t_wv = load_wmat(C, C.RW, 0, W["l3f_wv"], 8, 1024, s_wa, waits=(pe0,))
    po_v = C.po.pre(1)
    def fonesv(g, rms1=rms1):
        wg(g, rms1)
        g.memset(C.RC[:, 0:15600], 1.0)
        g.memset(C.RW[:, 8192:9232], 1.0).then_inc(C.po.h, 1)
    C.emit("gpsimd", fonesv)
    def vtm_slice(tt):
        if tt < 15:
            return C.RC[:, tt * 1040:(tt + 1) * 1040]
        return C.RW[:, 8192:9232]
    for tt in range(16):
        for c in range(2):
            b = (tt * 2 + c) % 4
            gd = C.bank_guard[b]
            pe_t = C.pe.pre(1)
            pwv = ((s_wa, t_wv), rms1) if tt == 0 and c == 0 else ()
            def fmm(t, tt=tt, c=c, b=b, gd=gd, pwv=pwv):
                for w_ in pwv:
                    wg(t, w_)
                wg(t, gd)
                last = None
                for k in range(8):
                    last = t.matmul(
                        C.bank(b),
                        C.RB[:, k * 2048 + tt * 128: k * 2048 + tt * 128 + 128],
                        C.RW[:, k * 1024 + c * 512: k * 1024 + c * 512 + 512],
                        start=(k == 0), stop=(k == 7))
                last.then_inc(C.pe.h, 1)
            C.emit("tensor", fmm)
            cnt = C.ve.pre(1)
            def fev(v, tt=tt, c=c, b=b, pe_t=pe_t, po_v=po_v):
                v.wait_ge(C.pe.h, pe_t)
                v.wait_ge(C.po.h, po_v)
                dst = vtm_slice(tt).rearrange("p (h e) -> p h e", h=16)[:, c * 8:
                                                                       c * 8 + 8,
                                                                       0:64]
                v.tensor_copy(dst, C.bank(b).rearrange("p (h e) -> p h e", h=8))\
                    .then_inc(C.ve.h, 1)
            C.emit("vector", fev)
            C.guard_bank(b, C.ve, cnt)
    pe_v = (C.pe, C.pe.n)
    t_wq = load_wmat(C, C.RW, 0, W["l3f_wq"], 8, 1024, s_wb, waits=(pe_v,))
    mm_chain(C,
             lambda k, m: C.RW[:, k * 1024 + m * 128: k * 1024 + m * 128 + 128],
             lambda k, c: C.RB[:, k * 2048 + c * 512: k * 2048 + c * 512 + 512],
             evac_copy(C, lambda m, c:
                       RAb[:, m * 2048 + c * 512: m * 2048 + c * 512 + 512],
                       scale=0.125),
             m_tiles=8, k_list=range(8), n_chunks=4,
             pre_waits=((s_wb, t_wq), s_spill_done), banks=(0, 1, 2, 3))
    pe_q = (C.pe, C.pe.n)
    t_wk = load_wmat(C, C.RW, 0, W["l3f_wk"], 8, 1024, s_wa, waits=(pe_q,))
    mm_chain(C,
             lambda k, m: C.RW[:, k * 1024 + m * 128: k * 1024 + m * 128 + 128],
             lambda k, c: C.RB[:, k * 2048 + c * 512: k * 2048 + c * 512 + 512],
             evac_copy(C, lambda m, c:
                       RAb[:, 16384 + m * 2048 + c * 512:
                           16384 + m * 2048 + c * 512 + 512], eng="vector"),
             m_tiles=8, k_list=range(8), n_chunks=4,
             pre_waits=((s_wa, t_wk),), banks=(0, 1, 2, 3))
    qk_ve = (C.ve, C.ve.n)
    qk_act = (C.act, C.act.n)
    for reg_base in (0, 16384):
        for dt in range(8):
            slot = reg_base + dt * 2048
            gds = C.snap_guards([2, 3, 4, 5])
            pe_sw = C.pe.pre(1)
            def f_sw(t, slot=slot, gds=gds, qk_ve=qk_ve, qk_act=qk_act,
                     tab_done=tab_done, first=(reg_base == 0 and dt == 0)):
                if first:
                    wg(t, qk_ve)
                    wg(t, qk_act)
                    wg(t, tab_done)
                for c4 in range(4):
                    wg(t, gds[c4])
                last = None
                for c4 in range(4):
                    last = t.matmul(C.bank(2 + c4), C.pswap,
                                    RAb[:, slot + c4 * 512:
                                        slot + c4 * 512 + 512],
                                    start=True, stop=True)
                last.then_inc(C.pe.h, 1)
            C.emit("tensor", f_sw)
            C.ve.pre(1)
            def f_t1(v, slot=slot, pe_sw=pe_sw):
                v.wait_ge(C.pe.h, pe_sw)
                v.tensor_tensor(RAb[:, slot:slot + 2048],
                                RAb[:, slot:slot + 2048],
                                C.ROPE[:, 0:2048], ALU.mult)\
                    .then_inc(C.ve.h, 1)
            C.emit("vector", f_t1)
            for c4 in range(4):
                C.ve.pre(2)
                def f_t2(v, slot=slot, c4=c4):
                    v.tensor_tensor(C.RY[:, 3584:4096], C.bank(2 + c4),
                                    C.ROPE[:, 2048 + c4 * 512:
                                           2048 + c4 * 512 + 512],
                                    ALU.mult)
                    v.tensor_tensor(RAb[:, slot + c4 * 512:
                                        slot + c4 * 512 + 512],
                                    RAb[:, slot + c4 * 512:
                                        slot + c4 * 512 + 512],
                                    C.RY[:, 3584:4096], ALU.add)\
                        .then_inc(C.ve.h, 2)
                C.emit("vector", f_t2)
                C.guard_bank(2 + c4, C.ve, C.ve.n)
    rope_done = (C.ve, C.ve.n)
    if stop == "rope":
        return rope_done

    def qT_fn(h):
        b = 64 * (h % 2)
        return RAb[b:b + 64, (h // 2) * 2048: (h // 2) * 2048 + 2048]

    def kT_fn(h):
        b = 64 * (h % 2)
        return RAb[b:b + 64, 16384 + (h // 2) * 2048:
                   16384 + (h // 2) * 2048 + 2048]

    def v_lhsT(h, kt):
        return vtm_slice(kt)[:, h * 65: h * 65 + 65]

    def o_dst(h, qb):
        b = 64 * (h % 2)
        return C.RB[b:b + 64,
                    (h // 2) * 2048 + qb * QB: (h // 2) * 2048 + qb * QB + QB]

    def bias_blk(h, qb, bias_ready):
        cnt = C.ve.pre(1)
        def f(v, bias_ready=bias_ready):
            wg(v, bias_ready)
            v.tensor_scalar(C.RM[:, 48:64], C.RM[:, 64:80], C.RM[:, 16:17],
                            None, ALU.add).then_inc(C.ve.h, 1)
        C.emit("vector", f)
        return (lambda kt: C.RM[:, 48 + kt:49 + kt]), (C.ve, cnt)

    kt_hi = lambda qb: (qb + 1) * (QB // 128)
    C.guard_bank(0, *rope_done)
    C.guard_bank(1, *rope_done)
    emit_attn64(C, list(range(16)), qT_fn, kT_fn, v_lhsT, o_dst,
                shift="passA", kt_hi_fn=kt_hi, pmask_fn=causal_pmask(C),
                ry_off=3072, bias_blk=bias_blk, den_eps=1e-30)
    attn_pe = (C.pe, C.pe.n)
    attn_ve = (C.ve, C.ve.n)
    if stop == "attn":
        return attn_ve
    s_x2 = C.sem("3x2")
    for k in range(ND):
        e_dma(C, "gpsimd", RAb[:, 16384 + k * 2048: 16384 + (k + 1) * 2048],
              xnsp[k], s_x2, waits=(attn_pe, attn_ve) if k == 0 else ())
    xn2 = (s_x2, s_x2.n)
    for halfw in range(2):
        t_wo = load_wmat(C, C.RW, halfw * 4096, W["l3f_wo_h%d" % halfw],
                         8, 512, s_wb if halfw == 0 else s_wa,
                         waits=(attn_pe,))
        mm_chain(C,
                 lambda k, m, halfw=halfw:
                 C.RW[:, halfw * 4096 + k * 512 + m * 128:
                      halfw * 4096 + k * 512 + m * 128 + 128],
                 lambda k, c: C.RB[:, k * 2048 + c * 512:
                                   k * 2048 + c * 512 + 512],
                 evac_resid(C,
                            lambda m, c, halfw=halfw:
                            C.RC[:, (halfw * 4 + m) * 2048 + c * 512:
                                 (halfw * 4 + m) * 2048 + c * 512 + 512],
                            lambda m, c, halfw=halfw:
                            RAb[:, 16384 + (halfw * 4 + m) * 2048 + c * 512:
                                16384 + (halfw * 4 + m) * 2048 + c * 512 + 512]),
                 m_tiles=4, k_list=range(8), n_chunks=4,
                 pre_waits=((s_wb if halfw == 0 else s_wa, t_wo), attn_ve,
                            xn2),
                 banks=(0, 1, 2, 3))
    x1_done = (C.ve, C.ve.n)
    if stop == "x1":
        return x1_done
    rms2 = emit_ln(C, lambda k: C.RC[:, k * 2048:(k + 1) * 2048],
                   lambda k: C.RB[:, k * 2048:(k + 1) * 2048],
                   rms=True, eps=1e-6, in_bf16=True,
                   gain_col=lambda k: C.RCOL[:, 32 + k:33 + k],
                   pre_waits=(x1_done,), sq_region=RAb[:, 0:4096])
    guard1 = ((C.pe, C.pe.n),)
    prev_fg2 = None
    for tc in range(2):
        for sl in range(8):
            t1 = load_wmat(C, C.RW, 0, W["l3f_w1"][:, sl * 512:(sl + 1) * 512],
                           8, 512, s_wa, waits=guard1)
            for mt in range(4):
                for c2 in range(2):
                    b = (sl * 8 + mt * 2 + c2) % 4
                    gd = C.bank_guard[b]
                    pe_t = C.pe.pre(1)
                    pwa = ((s_wa, t1), rms2) if mt == 0 and c2 == 0 else ()
                    def fmm(t, sl=sl, mt=mt, c2=c2, tc=tc, b=b, gd=gd,
                            pwa=pwa):
                        for w_ in pwa:
                            wg(t, w_)
                        wg(t, gd)
                        cl = tc * 1024 + c2 * 512
                        last = None
                        for k in range(8):
                            last = t.matmul(
                                C.bank(b),
                                C.RW[:, k * 512 + mt * 128:
                                     k * 512 + mt * 128 + 128],
                                C.RB[:, k * 2048 + cl: k * 2048 + cl + 512],
                                start=(k == 0), stop=(k == 7))
                        last.then_inc(C.pe.h, 1)
                    C.emit("tensor", fmm)
                    htile = sl * 4 + mt
                    a_s = C.act.pre(1)
                    def fg1(s, b=b, pe_t=pe_t, prev_fg2=prev_fg2):
                        s.wait_ge(C.pe.h, pe_t)
                        if prev_fg2 is not None:
                            s.wait_ge(C.ve.h, prev_fg2)
                        s.square(C.RY[:, 2048:2560], C.bank(b))\
                            .then_inc(C.act.h, 1)
                    C.emit("scalar", fg1)
                    v_1 = C.ve.pre(1)
                    def fg2(v, b=b, a_s=a_s):
                        v.wait_ge(C.act.h, a_s)
                        v.tensor_scalar(C.RY[:, 2048:2560], C.RY[:, 2048:2560],
                                        0.044715, 1.0, ALU.mult, ALU.add)
                        v.tensor_tensor(C.RY[:, 2560:3072], C.RY[:, 2048:2560],
                                        C.bank(b), ALU.mult)\
                            .then_inc(C.ve.h, 1)
                    C.emit("vector", fg2)
                    prev_fg2 = v_1
                    a_s2 = C.act.pre(1)
                    def fg3(s, v_1=v_1):
                        s.wait_ge(C.ve.h, v_1)
                        s.activation(C.RY[:, 2560:3072], C.RY[:, 2560:3072],
                                     AF.Sigmoid, scale=1.5957691216057308)\
                            .then_inc(C.act.h, 1)
                    C.emit("scalar", fg3)
                    cnt = C.ve.pre(1)
                    def fg4(v, b=b, a_s2=a_s2, htile=htile, c2=c2):
                        v.wait_ge(C.act.h, a_s2)
                        v.tensor_tensor(RAb[:, htile * 1024 + c2 * 512:
                                            htile * 1024 + c2 * 512 + 512],
                                        C.RY[:, 2560:3072], C.bank(b),
                                        ALU.mult).then_inc(C.ve.h, 1)
                    C.emit("vector", fg4)
                    C.guard_bank(b, C.ve, cnt)
            guard1 = ((C.pe, C.pe.n),)
        h_done = (C.ve, C.ve.n)
        for sl in range(4):
            t2 = load_wmat(C, C.RW, 4096,
                           W["l3f_w2"][:, sl * 256:(sl + 1) * 256],
                           32, 256, s_wb, waits=guard1)
            for mt in range(2):
                for c2 in range(2):
                    b = 4 + (mt * 2 + c2) % 2
                    gd = C.bank_guard[b]
                    pe_t = C.pe.pre(1)
                    pwa = ((s_wb, t2), h_done) if mt == 0 and c2 == 0 else ()
                    def fmm2(t, sl=sl, mt=mt, c2=c2, b=b, gd=gd, pwa=pwa):
                        for w_ in pwa:
                            wg(t, w_)
                        wg(t, gd)
                        last = None
                        for k in range(32):
                            last = t.matmul(
                                C.bank(b),
                                C.RW[:, 4096 + k * 256 + mt * 128:
                                     4096 + k * 256 + mt * 128 + 128],
                                RAb[:, k * 1024 + c2 * 512:
                                    k * 1024 + c2 * 512 + 512],
                                start=(k == 0), stop=(k == 31))
                        last.then_inc(C.pe.h, 1)
                    C.emit("tensor", fmm2)
                    cnt = C.ve.pre(1)
                    def fev2(v, sl=sl, mt=mt, c2=c2, tc=tc, b=b, pe_t=pe_t):
                        v.wait_ge(C.pe.h, pe_t)
                        otile = sl * 2 + mt
                        dst = C.RC[:, otile * 2048 + tc * 1024 + c2 * 512:
                                   otile * 2048 + tc * 1024 + c2 * 512 + 512]
                        v.tensor_tensor(dst, dst, C.bank(b), ALU.add)\
                            .then_inc(C.ve.h, 1)
                    C.emit("vector", fev2)
                    C.guard_bank(b, C.ve, cnt)
            guard1 = ((C.pe, C.pe.n),)
    x2_done = (C.ve, C.ve.n)
    if stop == "x2":
        return x2_done
    s_rr = C.sem("3rr")
    t_w1r = e_dma(C, "gpsimd", C.RY[0:1, 2048:4096], rsp[0:1], s_rr,
                  waits=[x2_done])
    t_w2r = e_dma(C, "gpsimd", C.RX[0:1, 6144:8192], rsp[1:2], s_rr,
                  waits=[(s_rr, t_w1r)])
    v_hs = C.ve.pre(2)
    def fsplit(v, t_w1r=t_w1r, v_hs=v_hs):
        v.wait_ge(s_rr.h, t_w1r)
        v.tensor_copy(C.RX[0:1, 2048:4096], C.RY[0:1, 2048:4096])\
            .then_inc(C.ve.h, 1)
        v.wait_ge(C.ve.h, v_hs - 1)
        v.tensor_tensor(C.RX[0:1, 4096:6144], C.RY[0:1, 2048:4096],
                        C.RX[0:1, 2048:4096], ALU.subtract).then_inc(C.ve.h, 1)
    C.emit("vector", fsplit)
    split_done = v_hs
    for c in range(4):
        gd6 = C.snap_guards([6])
        pe_b = C.pe.pre(1)
        def fbc(t, c=c, split_done=split_done, gd6=gd6):
            t.wait_ge(C.ve.h, split_done)
            wg(t, gd6[0])
            t.matmul(C.ps[:, 3072:3584], C.ones_bf[0:1, :],
                     C.RX[0:1, 2048 + c * 512: 2048 + c * 512 + 512],
                     start=True, stop=False)
            t.matmul(C.ps[:, 3072:3584], C.ones_bf[0:1, :],
                     C.RX[0:1, 4096 + c * 512: 4096 + c * 512 + 512],
                     start=False, stop=True).then_inc(C.pe.h, 1)
        C.emit("tensor", fbc)
        C.ve.pre(1)
        def fcp(v, c=c, pe_b=pe_b):
            v.wait_ge(C.pe.h, pe_b)
            v.tensor_copy(C.RY[:, 2048 + c * 512: 2048 + c * 512 + 512],
                          C.ps[:, 3072:3584]).then_inc(C.ve.h, 1)
        C.emit("vector", fcp)
        C.guard_bank(6, C.ve, C.ve.n)
        gd6b = C.snap_guards([6])
        pe_b2 = C.pe.pre(1)
        def fbc2(t, c=c, t_w2r=t_w2r, gd6b=gd6b):
            t.wait_ge(s_rr.h, t_w2r)
            wg(t, gd6b[0])
            t.matmul(C.ps[:, 3072:3584], C.ones_bf[0:1, :],
                     C.RX[0:1, 6144 + c * 512: 6144 + c * 512 + 512],
                     start=True, stop=True).then_inc(C.pe.h, 1)
        C.emit("tensor", fbc2)
        C.ve.pre(1)
        def fcp2(v, c=c, pe_b2=pe_b2):
            v.wait_ge(C.pe.h, pe_b2)
            v.tensor_copy(C.RX[:, c * 512: c * 512 + 512], C.ps[:, 3072:3584])\
                .then_inc(C.ve.h, 1)
        C.emit("vector", fcp2)
        C.guard_bank(6, C.ve, C.ve.n)
    bc_done = (C.ve, C.ve.n)
    s_x3 = C.sem("3x3")
    prev = {}
    prev_sil = None
    for k in range(ND):
        slot = 4096 + (k % 2) * 2048
        ws = [bc_done] if k < 2 else [(C.ve, prev[k - 2])]
        t_x = e_dma(C, "gpsimd", C.RX[:, slot:slot + 2048], xnsp[k], s_x3,
                    waits=ws)
        v_t = C.ve.pre(1)
        def fcomb(v, k=k, slot=slot, t_x=t_x, bc_done=bc_done,
                  prev_sil=prev_sil):
            v.wait_ge(s_x3.h, t_x)
            wg(v, bc_done)
            if prev_sil is not None:
                v.wait_ge(C.act.h, prev_sil)
            v.tensor_tensor(C.RY[:, 0:2048], C.RC[:, k * 2048:(k + 1) * 2048],
                            C.RY[:, 2048:4096], ALU.mult)
            v.tensor_tensor(C.RX[:, slot:slot + 2048],
                            C.RX[:, slot:slot + 2048],
                            C.RX[:, 0:2048], ALU.mult)
            v.tensor_tensor(C.RY[:, 0:2048], C.RY[:, 0:2048],
                            C.RX[:, slot:slot + 2048], ALU.add)\
                .then_inc(C.ve.h, 1)
        C.emit("vector", fcomb)
        prev[k] = v_t
        a_sil = C.act.pre(1)
        def fsil(s, k=k, v_t=v_t):
            s.wait_ge(C.ve.h, v_t)
            s.activation(C.RB[:, k * 2048:(k + 1) * 2048], C.RY[:, 0:2048],
                         AF.Silu).then_inc(C.act.h, 1)
        C.emit("scalar", fsil)
        prev_sil = a_sil
    return (C.act, C.act.n)


def build_l3(stop=None):
    nc = bass.Bass("TRN2", target_bir_lowering=False)
    xin = nc.dram_tensor("xin", [ND, 128, T], F32, kind="ExternalInput").ap()
    H = {}
    for nm, shp in [("consts", [128, 384]), ("l3_pos", [1, T])] + L3_SHAPES:
        H[nm] = nc.dram_tensor(nm, shp, F32, kind="ExternalInput").ap()
    xout = nc.dram_tensor("xout", [ND, 128, T], F32, kind="ExternalOutput").ap()
    dbg = nc.dram_tensor("dbg", [ND, 128, T], F32, kind="ExternalOutput").ap()
    es = ExitStack()
    with es:
        C = Ctx(nc, es)
        blk = es.enter_context(nc.Block())
        s_x = C.sem("x")
        s_k = C.sem("k")
        for k in range(ND):
            e_dma(C, "sync", C.RA[:, k * 2048:(k + 1) * 2048], xin[k], s_x)
        e_dma(C, "gpsimd", C.CONST[:], H["consts"], s_k)
        e_dma(C, "gpsimd", C.RCOL[:, 0:48], H["l3_cols"][:, 0:48], s_k)
        konst_ready = [(s_k, s_k.n), (s_x, s_x.n)]
        ln1 = emit_ln(C, lambda k: C.RA[:, k * 2048:(k + 1) * 2048],
                      lambda k: C.RB[:, k * 2048:(k + 1) * 2048],
                      pre_waits=tuple(konst_ready))
        xnsp = nc.dram_tensor("xnsp", [ND, 128, T], BF16, kind="Internal").ap()
        s_sp = C.sem("xnsp")
        for k in range(ND):
            e_dma(C, "sync", xnsp[k], C.RB[:, k * 2048:(k + 1) * 2048], s_sp,
                  waits=(ln1,) if k == 0 else ())
        spill_done = (s_sp, s_sp.n)
        r_a = emit_l3a(C, H, spill_done=spill_done, stop=stop)

        def dump(tiles, wait, rows=128):
            sd = C.sem("dbgo")
            for k, src in enumerate(tiles):
                e_dma(C, "gpsimd", dbg[k][0:rows, :], src, sd,
                      waits=(wait,) if k == 0 else ())
            def ffin(g, n=sd.n):
                g.wait_ge(sd.h, n)
            C.emit("gpsimd", ffin)
            C.finalize(blk)
            return nc

        if stop in ("qkv", "s"):
            return dump([C.RAb[:, k * 2048:(k + 1) * 2048] for k in range(ND)],
                        r_a)
        ssp = nc.dram_tensor("ssp", [ND, 128, T], BF16, kind="Internal").ap()
        s_ss = C.sem("ssp")
        for k in range(ND):
            e_dma(C, "sync", ssp[k], C.RAb[:, k * 2048:(k + 1) * 2048], s_ss,
                  waits=(r_a,) if k == 0 else ())
        s_spd = (s_ss, s_ss.n)
        r = emit_l3f(C, H, xnsp, s_spd, stop=stop)
        if stop == "bis":
            sd = C.sem("dbgo")
            e_dma(C, "gpsimd", dbg[0][:, 0:64], C.RY[:, 2048:2112], sd,
                  waits=(r,))
            def ffin(g, n=sd.n):
                g.wait_ge(sd.h, n)
            C.emit("gpsimd", ffin)
            C.finalize(blk)
            return nc
        if stop == "w16":
            sd = C.sem("dbgo")
            e_dma(C, "gpsimd", dbg[0][:, 0:64], C.RY[:, 2048:2112], sd,
                  waits=(r,))
            e_dma(C, "gpsimd", dbg[1][0:1, :], C.RY[0:1, 0:2048], sd)
            def ffin(g, n=sd.n):
                g.wait_ge(sd.h, n)
            C.emit("gpsimd", ffin)
            C.finalize(blk)
            return nc
        if stop == "router":
            sd = C.sem("dbgo")
            e_dma(C, "gpsimd", dbg[0][0:1, :], C.RY[0:1, 0:2048], sd,
                  waits=(r,))
            e_dma(C, "gpsimd", dbg[1][0:1, :], C.RY[0:1, 2048:4096], sd)
            def ffin(g, n=sd.n):
                g.wait_ge(sd.h, n)
            C.emit("gpsimd", ffin)
            C.finalize(blk)
            return nc
        if stop in ("tab", "rope"):
            return dump([C.ROPE[:, 0:2048], C.ROPE[:, 2048:4096]], r)
        if stop == "qk":
            return dump([C.RAb[:, k * 2048:(k + 1) * 2048] for k in range(ND)],
                        r)
        if stop == "attn":
            return dump([C.RB[:, k * 2048:(k + 1) * 2048] for k in range(ND)],
                        r)
        if stop in ("x1", "x2"):
            return dump([C.RC[:, k * 2048:(k + 1) * 2048] for k in range(ND)],
                        r)
        if stop == "g":
            return dump([C.RB[:, k * 2048:(k + 1) * 2048] for k in range(ND)],
                        r)
        emit_gate(C, xin, ssp, xout, r)
        C.finalize(blk)
    return nc


def build_all():
    nc = bass.Bass("TRN2", target_bir_lowering=False)
    xin = nc.dram_tensor("xin", [ND, 128, T], F32, kind="ExternalInput").ap()
    H = {}
    all_shapes = ([("l0_wdq", [D, 512]), ("l0_wdkv", [D, 512]),
                   ("l0_wupq", [512, D]), ("l0_wupk", [512, D]),
                   ("l0_wupv", [512, D]),
                   ("l0_wout_h0", [D, 512]), ("l0_wout_h1", [D, 512]),
                   ("l0_cols", [128, 16]),
                   ("ropeC", [128, T]), ("ropeS", [128, T]),
                   ("consts", [128, 384]), ("l3_pos", [1, T])] +
                  L0F_SHAPES + L1_SHAPES + L2_SHAPES + L3_SHAPES)
    for nm, shp in all_shapes:
        kind = I32 if nm.startswith("l2_qw") else F32
        H[nm] = nc.dram_tensor(nm, shp, kind, kind="ExternalInput").ap()
    xout = nc.dram_tensor("xout", [ND, 128, T], F32, kind="ExternalOutput").ap()
    xc1 = nc.dram_tensor("xc1", [ND, 128, T], F32, kind="Internal").ap()
    xc2 = nc.dram_tensor("xc2", [ND, 128, T], F32, kind="Internal").ap()
    xc3 = nc.dram_tensor("xc3", [ND, 128, T], F32, kind="Internal").ap()
    es = ExitStack()
    with es:
        C = Ctx(nc, es)
        blk = es.enter_context(nc.Block())

        def barrier():
            return ((C.pe, C.pe.n), (C.ve, C.ve.n), (C.act, C.act.n),
                    (C.po, C.po.n))

        s_x = C.sem("x")
        s_k = C.sem("k")
        for k in range(ND):
            e_dma(C, "sync", C.RA[:, k * 2048:(k + 1) * 2048], xin[k], s_x)
        e_dma(C, "gpsimd", C.CONST[:], H["consts"], s_k)
        e_dma(C, "gpsimd", C.ROPE[:, 0:2048], H["ropeC"], s_k)
        e_dma(C, "gpsimd", C.ROPE[:, 2048:4096], H["ropeS"], s_k)
        e_dma(C, "gpsimd", C.RCOL[:, 0:16], H["l0_cols"], s_k)
        e_dma(C, "gpsimd", C.RCOL[:, 16:33], H["l0f_cols"], s_k)
        konst_ready = [(s_k, s_k.n), (s_x, s_x.n)]
        ln1 = emit_ln(C, lambda k: C.RA[:, k * 2048:(k + 1) * 2048],
                      lambda k: C.RB[:, k * 2048:(k + 1) * 2048],
                      pre_waits=tuple(konst_ready))
        ln2 = emit_ln(C, lambda k: C.RB[:, k * 2048:(k + 1) * 2048],
                      lambda k: C.RC[:, k * 2048:(k + 1) * 2048],
                      gain_col=lambda k: C.RCOL[:, k:k + 1],
                      bias_col=lambda k: C.RCOL[:, 8 + k:8 + k + 1],
                      pre_waits=(ln1,), sq_region=C.RAb[:, 0:4096],
                      in_bf16=True)
        s_done = emit_l0a(C, H)
        ssp0 = nc.dram_tensor("ssp0", [ND, 128, T], BF16, kind="Internal").ap()
        s_sp = C.sem("ssp")
        for k in range(ND):
            e_dma(C, "sync", ssp0[k], C.RAb[:, k * 2048:(k + 1) * 2048], s_sp,
                  waits=(s_done,) if k == 0 else ())
        spill_done = (s_sp, s_sp.n)
        moe_done = emit_l0f(C, H, spill_done)
        g_done = emit_l0f_finish(C, C.x2sp, moe_done)
        so0 = emit_gate(C, xin, ssp0, xc1, g_done)

        bar = barrier()
        s_x1 = C.sem("x")
        s_k1 = C.sem("k1")
        for k in range(ND):
            e_dma(C, "sync", C.RA[:, k * 2048:(k + 1) * 2048], xc1[k], s_x1,
                  waits=((so0, so0.n),) + bar if k == 0 else ())
        e_dma(C, "gpsimd", C.RCOL2[:], H["l1_cols2"], s_k1,
              waits=((so0, so0.n),))
        e_dma(C, "gpsimd", C.RCOL2b[:], H["l1_cols3"], s_k1)
        kr1 = [(s_k1, s_k1.n), (s_x1, s_x1.n)] + list(bar)
        ln1_1 = emit_ln(C, lambda k: C.RA[:, k * 2048:(k + 1) * 2048],
                        lambda k: C.RB[:, k * 2048:(k + 1) * 2048],
                        pre_waits=tuple(kr1))
        gsp = nc.dram_tensor("gsp", [ND, 128, T], BF16, kind="Internal").ap()
        r1 = emit_l1(C, H, gsp)
        s_xg = C.sem("gx")
        s_g = C.sem("gg")
        s_o1 = C.sem("gout")
        prev_tt = {}
        for k in range(ND):
            wx = [r1] if k == 0 else [(C.ve, prev_tt[k - 1])]
            t_x = e_dma(C, "sync", C.RY[:, 0:2048], xc1[k], s_xg, waits=wx)
            slot = (k % 2) * 2048
            ws = [r1] if k < 2 else [(C.ve, prev_tt[k - 2])]
            t_g = e_dma(C, "gpsimd", C.RX[:, slot:slot + 2048], gsp[k], s_g,
                        waits=ws)
            v_t = C.ve.pre(1)
            def ftt(v, k=k, slot=slot, t_x=t_x, t_g=t_g):
                v.wait_ge(s_xg.h, t_x)
                v.wait_ge(s_g.h, t_g)
                v.tensor_tensor(C.RY[:, 2048:4096],
                                C.RAb[:, 16384 + k * 2048:
                                      16384 + (k + 1) * 2048],
                                C.RX[:, slot:slot + 2048], ALU.mult)
                v.tensor_tensor(C.RY[:, 0:2048], C.RY[:, 0:2048],
                                C.RY[:, 2048:4096], ALU.add)\
                    .then_inc(C.ve.h, 1)
            C.emit("vector", ftt)
            prev_tt[k] = v_t
            e_dma(C, "sync", xc2[k], C.RY[:, 0:2048], s_o1,
                  waits=[(C.ve, v_t)])

        bar = barrier()
        s_x2 = C.sem("x")
        s_k2 = C.sem("k2")
        for k in range(ND):
            e_dma(C, "sync", C.RA[:, k * 2048:(k + 1) * 2048], xc2[k], s_x2,
                  waits=((s_o1, s_o1.n),) + bar if k == 0 else ())
        e_dma(C, "gpsimd", C.RCOL2[:], H["l2_cols"], s_k2,
              waits=((s_o1, s_o1.n),))
        kr2 = [(s_k2, s_k2.n), (s_x2, s_x2.n)] + list(bar)
        ln1_2 = emit_ln(C, lambda k: C.RA[:, k * 2048:(k + 1) * 2048],
                        lambda k: C.RB[:, k * 2048:(k + 1) * 2048],
                        pre_waits=tuple(kr2))
        r2 = emit_l2(C, H)
        s_xg2 = C.sem("gx")
        s_o2 = C.sem("gout")
        prev_tt = {}
        for k in range(ND):
            wx = [r2] if k == 0 else [(C.ve, prev_tt[k - 1])]
            t_x = e_dma(C, "sync", C.RY[:, 0:2048], xc2[k], s_xg2, waits=wx)
            v_t = C.ve.pre(1)
            def ftt2(v, k=k, t_x=t_x, r2=r2):
                v.wait_ge(s_xg2.h, t_x)
                wg(v, r2)
                v.tensor_tensor(C.RY[:, 2048:4096],
                                C.RAb[:, k * 2048:(k + 1) * 2048],
                                C.RB[:, k * 2048:(k + 1) * 2048], ALU.mult)
                v.tensor_tensor(C.RY[:, 0:2048], C.RY[:, 0:2048],
                                C.RY[:, 2048:4096], ALU.add)\
                    .then_inc(C.ve.h, 1)
            C.emit("vector", ftt2)
            prev_tt[k] = v_t
            e_dma(C, "sync", xc3[k], C.RY[:, 0:2048], s_o2,
                  waits=[(C.ve, v_t)])

        bar = barrier()
        s_x3 = C.sem("x")
        s_k3 = C.sem("k3")
        for k in range(ND):
            e_dma(C, "sync", C.RA[:, k * 2048:(k + 1) * 2048], xc3[k], s_x3,
                  waits=((s_o2, s_o2.n),) + bar if k == 0 else ())
        e_dma(C, "gpsimd", C.RCOL[:, 0:48], H["l3_cols"][:, 0:48], s_k3,
              waits=((s_o2, s_o2.n),))
        kr3 = [(s_k3, s_k3.n), (s_x3, s_x3.n)] + list(bar)
        ln1_3 = emit_ln(C, lambda k: C.RA[:, k * 2048:(k + 1) * 2048],
                        lambda k: C.RB[:, k * 2048:(k + 1) * 2048],
                        pre_waits=tuple(kr3))
        xnsp3 = nc.dram_tensor("xnsp3", [ND, 128, T], BF16,
                               kind="Internal").ap()
        s_sp3 = C.sem("xnsp")
        for k in range(ND):
            e_dma(C, "sync", xnsp3[k], C.RB[:, k * 2048:(k + 1) * 2048],
                  s_sp3, waits=(ln1_3,) if k == 0 else ())
        spill3 = (s_sp3, s_sp3.n)
        r_a = emit_l3a(C, H, spill_done=spill3)
        ssp3 = nc.dram_tensor("ssp3", [ND, 128, T], BF16,
                              kind="Internal").ap()
        s_ss3 = C.sem("ssp3")
        for k in range(ND):
            e_dma(C, "sync", ssp3[k], C.RAb[:, k * 2048:(k + 1) * 2048],
                  s_ss3, waits=(r_a,) if k == 0 else ())
        r3 = emit_l3f(C, H, xnsp3, (s_ss3, s_ss3.n))
        emit_gate(C, xc3, ssp3, xout, r3)
        C.finalize(blk)
    return nc


# version 05 (final)
# speedup vs baseline: 1.0367x; 1.0367x over previous
import numpy as np
from contextlib import ExitStack
import ml_dtypes
import concourse.bass as bass
import concourse.mybir as mybir
from concourse.bass_utils import run_bass_kernel_spmd

F32 = mybir.dt.float32
BF16 = mybir.dt.bfloat16
I32 = mybir.dt.int32
AF = mybir.ActivationFunctionType
ALU = mybir.AluOpType
AX = mybir.AxisListType

T = 2048
D = 1024
NT = T // 128
ND = D // 128
NC = T // 512
QB = 1024
NQB = T // QB

ENGS = ("gpsimd", "tensor", "scalar", "vector", "sync")



def _t(w):
    return np.ascontiguousarray(np.asarray(w).T.astype(np.float32, copy=False))


def _colcat(*vs):
    cols = []
    for v in vs:
        v = np.asarray(v, np.float32).reshape(ND, 128).T
        cols.append(v)
    return np.ascontiguousarray(np.concatenate(cols, axis=1))


def _rope_perm():
    p = np.zeros(D, np.int64)
    for h in range(16):
        for r in range(32):
            p[h * 64 + r] = h * 64 + 2 * r
            p[h * 64 + 32 + r] = h * 64 + 2 * r + 1
    return p


def _rope_tables_l0():
    base = 10000.0
    inv = 1.0 / base ** (np.arange(0, 64, 2, dtype=np.float32) / 64)
    t = np.arange(T, dtype=np.float32)
    fr = np.outer(t, inv)
    cos = np.concatenate([np.cos(fr), np.cos(fr)], -1)[:, ::2]
    sin = np.concatenate([np.sin(fr), np.sin(fr)], -1)[:, ::2]
    CC = np.zeros((128, T), np.float32)
    SS = np.zeros((128, T), np.float32)
    for rep in range(2):
        b = rep * 64
        CC[b:b + 32] = cos.T
        CC[b + 32:b + 64] = cos.T
        SS[b:b + 32] = -sin.T
        SS[b + 32:b + 64] = sin.T
    return CC, SS


def _consts():
    ident = np.eye(128, dtype=np.float32)
    pswap = np.zeros((128, 128), np.float32)
    for b in range(0, 128, 64):
        for i in range(32):
            pswap[b + i, b + 32 + i] = 1.0
            pswap[b + 32 + i, b + i] = 1.0
    ones = np.ones((128, 128), np.float32)
    return np.concatenate([ident, pswap, ones], axis=1)


def stage_shared(inputs):
    g = {}
    rp = _rope_perm()
    g["l0_wdq"] = _t(inputs["L0_attn_w_down_q"])
    g["l0_wdkv"] = _t(inputs["L0_attn_w_down_kv"])
    g["l0_wupq"] = np.ascontiguousarray(_t(inputs["L0_attn_w_up_q"])[:, rp])
    g["l0_wupk"] = np.ascontiguousarray(_t(inputs["L0_attn_w_up_k"])[:, rp])
    g["l0_wupv"] = _t(inputs["L0_attn_w_up_v"])
    wo = _t(inputs["L0_attn_w_out"])
    g["l0_wout_h0"] = np.ascontiguousarray(wo[:, 0:512])
    g["l0_wout_h1"] = np.ascontiguousarray(wo[:, 512:1024])
    g["l0_cols"] = _colcat(inputs["L0_attn_ln_scale"], inputs["L0_attn_ln_bias"])
    CC, SS = _rope_tables_l0()
    g["ropeC"] = CC
    g["ropeS"] = SS
    g["consts"] = _consts()
    return g


def stage_x(inputs):
    x = np.asarray(inputs["L0_attn_x"], np.float32)
    return [np.ascontiguousarray(x[b].T.reshape(ND, 128, T)) for b in range(8)]



class Sem:
    def __init__(self, es, nc, name):
        self.h = es.enter_context(nc.semaphore(name))
        self.n = 0

    def pre(self, k=1):
        self.n += k
        return self.n


class Ctx:
    def __init__(self, nc, es):
        self.nc = nc
        self.es = es
        self.streams = {e: [] for e in ENGS}
        self.pe = Sem(es, nc, "c_pe")
        self.act = Sem(es, nc, "c_act")
        self.ve = Sem(es, nc, "c_ve")
        self.po = Sem(es, nc, "c_po")
        self.dma_sems = {}
        self.ps = es.enter_context(nc.psum_tensor("ps", [128, 3584], F32))
        self.psb = es.enter_context(nc.psum_tensor("psb", [128, 1024], BF16))
        self.bank_guard = [None] * 7
        self.RA = es.enter_context(nc.sbuf_tensor("RA", [128, 16384], F32))
        self.RB = es.enter_context(nc.sbuf_tensor("RB", [128, 16384], BF16))
        self.RC = es.enter_context(nc.sbuf_tensor("RC", [128, 16384], BF16))
        self.RW = es.enter_context(nc.sbuf_tensor("RW", [128, 12288], BF16))
        self.RX = es.enter_context(nc.sbuf_tensor("RX", [128, 8192], BF16))
        self.RY = es.enter_context(nc.sbuf_tensor("RY", [128, 4096], F32))
        self.RM = es.enter_context(nc.sbuf_tensor("RM", [128, 2048], F32))
        self.ROPE = es.enter_context(nc.sbuf_tensor("ROPE", [128, 4096], BF16))
        self.CONST = es.enter_context(nc.sbuf_tensor("CONST", [128, 384], BF16))
        self.RCOL = es.enter_context(nc.sbuf_tensor("RCOL", [128, 64], F32))
        self.RCOL2 = es.enter_context(nc.sbuf_tensor("RCOL2", [128, 128], F32))
        self.RCOL2b = es.enter_context(nc.sbuf_tensor("RCOL2b", [128, 16], F32))
        self.ONEF = es.enter_context(nc.sbuf_tensor("ONEF", [128, 1], F32))
        self.RAb = self.RA[:].bitcast(BF16)
        self.const_vals = {}
        self.s_const = Sem(es, nc, "s_const")

    def constf(self, v):
        v = float(v)
        if v not in self.const_vals:
            col = 48 + len(self.const_vals)
            assert col < 64
            self.const_vals[v] = col
        return self.RCOL[:, self.const_vals[v]:self.const_vals[v] + 1]

    def sem(self, name):
        if name not in self.dma_sems:
            self.dma_sems[name] = Sem(self.es, self.nc, "s_" + name)
        return self.dma_sems[name]

    def emit(self, eng, fn):
        self.streams[eng].append(fn)

    def bank(self, i, w=512, rows=128, r0=0):
        return self.ps[r0:r0 + rows, i * 512: i * 512 + w]

    def guard_bank(self, i, sem, count):
        self.bank_guard[i] = (sem, count)

    def snap_guards(self, idxs):
        return [self.bank_guard[i] for i in idxs]

    @property
    def ident(self):
        return self.CONST[:, 0:128]

    @property
    def pswap(self):
        return self.CONST[:, 128:256]

    @property
    def ones_bf(self):
        return self.CONST[:, 256:384]

    def finalize(self, block):
        streams = self.streams
        cv = dict(self.const_vals)
        RCOL = self.RCOL
        ONEF = self.ONEF
        sc = self.s_const
        def run(name):
            def _f(eng):
                if name == "gpsimd":
                    last = eng.memset(ONEF[:], 1.0)
                    for v, col in cv.items():
                        last = eng.memset(RCOL[:, col:col + 1], v)
                    last.then_inc(sc.h, 1)
                else:
                    eng.wait_ge(sc.h, 1)
                for fn in streams[name]:
                    fn(eng)
            return _f
        block.gpsimd(run("gpsimd"))
        block.tensor(run("tensor"))
        block.scalar(run("scalar"))
        block.vector(run("vector"))
        block.sync(run("sync"))


def wg(e, gd):
    if gd is not None:
        e.wait_ge(gd[0].h, gd[1])



def e_dma(C, eng, dst, src, sem, waits=()):
    tgt = sem.pre(16)
    def f(e, dst=dst, src=src, sem=sem, allw=tuple(waits)):
        for w in allw:
            wg(e, w)
        e.dma_start(dst, src).then_inc(sem.h, 16)
    C.emit(eng, f)
    return tgt


def load_wmat(C, region, dst_cols, src_hbm, kt, out_w, sem, waits=()):
    dst = region[:, dst_cols:dst_cols + kt * out_w]\
        .rearrange("p (k o) -> p k o", k=kt)
    src = src_hbm.rearrange("(k p) o -> p k o", p=128)
    return e_dma(C, "gpsimd", dst, src, sem, waits=waits)


def mm_chain(C, lhsT_fn, rhs_fn, evac_fn, m_tiles, k_list, n_chunks,
             banks=(0, 1, 2, 3), pre_waits=(), m_rows=128):
    first = True
    bi = 0
    for m in range(m_tiles):
        for c in range(n_chunks):
            b = banks[bi % len(banks)]
            bi += 1
            gd = C.bank_guard[b]
            pw = tuple(pre_waits) if first else ()
            first = False
            pe_t = C.pe.pre(1)
            kl = list(k_list)
            def fmm(t, m=m, c=c, b=b, gd=gd, pw=pw, kl=kl):
                for w in pw:
                    wg(t, w)
                wg(t, gd)
                last = None
                for ki, k in enumerate(kl):
                    r = rhs_fn(k=k, c=c)
                    last = t.matmul(C.bank(b, w=r.shape[-1], rows=m_rows),
                                    lhsT_fn(k=k, m=m),
                                    r, start=(ki == 0), stop=(ki == len(kl) - 1))
                last.then_inc(C.pe.h, 1)
            C.emit("tensor", fmm)
            C.guard_bank(b, *evac_fn(m, c, b, pe_t))


def evac_copy(C, dst_fn, eng="scalar", func=None, scale=1.0, bias_fn=None):
    def evac(m, c, b, pe_t):
        dst = dst_fn(m, c)
        w = dst.shape[-1]
        rows = dst.shape[0]
        if eng == "vector":
            cnt = C.ve.pre(1)
            def f(v, dst=dst, b=b, pe_t=pe_t, w=w, rows=rows):
                v.wait_ge(C.pe.h, pe_t)
                v.tensor_copy(dst, C.bank(b, w=w, rows=rows))\
                    .then_inc(C.ve.h, 1)
            C.emit("vector", f)
            return (C.ve, cnt)
        cnt = C.act.pre(1)
        fn = AF.Copy if func is None else func
        bias = bias_fn(m) if bias_fn is not None else None
        def f(s, dst=dst, b=b, pe_t=pe_t, w=w, fn=fn, scale=scale, bias=bias,
              rows=rows):
            s.wait_ge(C.pe.h, pe_t)
            kw = {"scale": scale}
            if bias is not None:
                kw["bias"] = bias
            s.activation(dst, C.bank(b, w=w, rows=rows), fn, **kw)\
                .then_inc(C.act.h, 1)
        C.emit("scalar", f)
        return (C.act, cnt)
    return evac


def evac_resid(C, dst_fn, res_fn):
    def evac(m, c, b, pe_t):
        cnt = C.ve.pre(1)
        def f(v, m=m, c=c, b=b, pe_t=pe_t):
            v.wait_ge(C.pe.h, pe_t)
            dst = dst_fn(m, c)
            v.scalar_tensor_tensor(dst, C.bank(b, w=dst.shape[-1]), 1.0,
                                   res_fn(m, c), ALU.mult, ALU.add)\
                .then_inc(C.ve.h, 1)
        C.emit("vector", f)
        return (C.ve, cnt)
    return evac



def emit_ln(C, x_tile_fn, dst_fn, nd=ND, eps=1e-5, rms=False, gain_col=None,
            bias_col=None, pre_waits=(), sq_region=None, in_bf16=False):
    invD = 1.0 / (nd * 128)
    SQ = sq_region if sq_region is not None else C.RC
    for c in range(NC):
        for k in range(nd):
            C.act.pre(1)
            pw = tuple(pre_waits) if (c == 0 and k == 0) else ()
            def fsq(s, k=k, c=c, pw=pw):
                for w in pw:
                    wg(s, w)
                s.square(SQ[:, k * 512:(k + 1) * 512],
                         x_tile_fn(k)[:, c * 512:(c + 1) * 512])\
                    .then_inc(C.act.h, 1)
            C.emit("scalar", fsq)
        sq_last = C.act.n
        gd01 = C.snap_guards([0, 1])
        pe_t = C.pe.pre(1)
        ones1 = C.ones_bf[:, 0:1] if in_bf16 else C.ONEF[:, 0:1]
        def fmm(t, c=c, gd01=gd01, sq_last=sq_last, rms=rms, nd=nd, ones1=ones1):
            t.wait_ge(C.act.h, sq_last)
            wg(t, gd01[0])
            if not rms:
                wg(t, gd01[1])
            last = None
            for k in range(nd):
                last = t.matmul(C.ps[0:1, 0:512], C.ones_bf[:, 0:1],
                                SQ[:, k * 512:(k + 1) * 512],
                                start=(k == 0), stop=(k == nd - 1))
            if not rms:
                for k in range(nd):
                    last = t.matmul(C.ps[0:1, 512:1024], ones1,
                                    x_tile_fn(k)[:, c * 512:(c + 1) * 512],
                                    start=(k == 0), stop=(k == nd - 1))
            last.then_inc(C.pe.h, 1)
        C.emit("tensor", fmm)
        if not rms:
            a0 = C.act.pre(1)
            a1 = C.act.pre(1)
            def frow(s, pe_t=pe_t, invD=invD, a0=a0):
                s.wait_ge(C.pe.h, pe_t)
                s.activation(C.RM[0:1, 0:512], C.ps[0:1, 512:1024], AF.Copy,
                             scale=invD).then_inc(C.act.h, 1)
                s.wait_ge(C.act.h, a0)
                s.square(C.RM[0:1, 512:1024], C.RM[0:1, 0:512])\
                    .then_inc(C.act.h, 1)
            C.emit("scalar", frow)
        else:
            a1 = None
        v1 = C.ve.pre(1)
        def frow2(v, pe_t=pe_t, rms=rms, invD=invD, a1=a1):
            v.wait_ge(C.pe.h, pe_t)
            if rms:
                v.tensor_scalar(C.RM[0:1, 1024:1536], C.ps[0:1, 0:512], invD,
                                None, ALU.mult).then_inc(C.ve.h, 1)
            else:
                v.wait_ge(C.act.h, a1)
                v.scalar_tensor_tensor(C.RM[0:1, 1024:1536], C.ps[0:1, 0:512],
                                       invD, C.RM[0:1, 512:1024], ALU.mult,
                                       ALU.subtract).then_inc(C.ve.h, 1)
        C.emit("vector", frow2)
        a2a = C.act.pre(1)
        a2 = C.act.pre(1)
        eps_ap = C.constf(eps)[0:1, :]
        def frow3(s, v1=v1, eps_ap=eps_ap, a2a=a2a):
            s.wait_ge(C.ve.h, v1)
            s.activation(C.RM[0:1, 1536:2048], C.RM[0:1, 1024:1536], AF.Ln,
                         bias=eps_ap).then_inc(C.act.h, 1)
            s.wait_ge(C.act.h, a2a)
            s.activation(C.RM[0:1, 1536:2048], C.RM[0:1, 1536:2048], AF.Exp,
                         scale=-0.5).then_inc(C.act.h, 1)
        C.emit("scalar", frow3)
        v2a = C.ve.pre(1)
        v2 = C.ve.pre(1)
        def fhilo(v, a2=a2, rms=rms, v2a=v2a):
            v.wait_ge(C.act.h, a2)
            if not rms:
                v.tensor_copy(C.RX[0:1, 4096:4608], C.RM[0:1, 0:512])
            v.tensor_copy(C.RX[0:1, 5120:5632], C.RM[0:1, 1536:2048])\
                .then_inc(C.ve.h, 1)
            v.wait_ge(C.ve.h, v2a)
            if not rms:
                v.tensor_tensor(C.RX[0:1, 4608:5120], C.RM[0:1, 0:512],
                                C.RX[0:1, 4096:4608], ALU.subtract)
            v.tensor_tensor(C.RX[0:1, 5632:6144], C.RM[0:1, 1536:2048],
                            C.RX[0:1, 5120:5632], ALU.subtract)\
                .then_inc(C.ve.h, 1)
        C.emit("vector", fhilo)
        gd23 = C.snap_guards([2, 3])
        pe_b = C.pe.pre(1)
        def fbc(t, v2=v2, gd23=gd23, rms=rms):
            t.wait_ge(C.ve.h, v2)
            wg(t, gd23[0])
            wg(t, gd23[1])
            if not rms:
                t.matmul(C.ps[:, 1024:1536], C.ones_bf[0:1, :],
                         C.RX[0:1, 4096:4608], start=True, stop=False)
                t.matmul(C.ps[:, 1024:1536], C.ones_bf[0:1, :],
                         C.RX[0:1, 4608:5120], start=False, stop=True)
            t.matmul(C.ps[:, 1536:2048], C.ones_bf[0:1, :],
                     C.RX[0:1, 5120:5632], start=True, stop=False)
            t.matmul(C.ps[:, 1536:2048], C.ones_bf[0:1, :],
                     C.RX[0:1, 5632:6144], start=False, stop=True)\
                .then_inc(C.pe.h, 1)
        C.emit("tensor", fbc)
        for k in range(nd):
            gc = gain_col(k) if gain_col is not None else None
            bc = bias_col(k) if bias_col is not None else None
            if rms:
                fn_a = C.ve.pre(1)
            elif gc is not None and bc is not None:
                fn_a = C.ve.pre(3) - 2
            else:
                fn_a = C.ve.pre(2) - 1
            def fnorm(v, k=k, c=c, pe_b=pe_b, rms=rms, gc=gc, bc=bc,
                      fn_a=fn_a):
                v.wait_ge(C.pe.h, pe_b)
                xs = x_tile_fn(k)[:, c * 512:(c + 1) * 512]
                dst = dst_fn(k)[:, c * 512:(c + 1) * 512]
                ry = C.RY[:, 0:512]
                if rms:
                    if gc is None:
                        v.tensor_tensor(dst, xs, C.ps[:, 1536:2048], ALU.mult)\
                            .then_inc(C.ve.h, 1)
                    else:
                        v.scalar_tensor_tensor(dst, xs, gc, C.ps[:, 1536:2048],
                                               ALU.mult, ALU.mult)\
                            .then_inc(C.ve.h, 1)
                else:
                    v.tensor_tensor(ry, xs, C.ps[:, 1024:1536], ALU.subtract)\
                        .then_inc(C.ve.h, 1)
                    v.wait_ge(C.ve.h, fn_a)
                    if gc is None:
                        v.tensor_tensor(dst, ry, C.ps[:, 1536:2048], ALU.mult)\
                            .then_inc(C.ve.h, 1)
                    elif bc is None:
                        v.scalar_tensor_tensor(dst, ry, gc, C.ps[:, 1536:2048],
                                               ALU.mult, ALU.mult)\
                            .then_inc(C.ve.h, 1)
                    else:
                        v.scalar_tensor_tensor(ry, ry, gc, C.ps[:, 1536:2048],
                                               ALU.mult, ALU.mult)\
                            .then_inc(C.ve.h, 1)
                        v.wait_ge(C.ve.h, fn_a + 1)
                        v.tensor_scalar(dst, ry, bc, None, ALU.add)\
                            .then_inc(C.ve.h, 1)
            C.emit("vector", fnorm)
        last_v = C.ve.n
        C.guard_bank(0, C.ve, v1)
        if not rms:
            C.guard_bank(1, C.ve, v1)
        C.guard_bank(2, C.ve, last_v)
        C.guard_bank(3, C.ve, last_v)
    return (C.ve, C.ve.n)



def emit_attn64(C, heads, qT_fn, kT_fn, v_lhsT_fn, o_dst_fn,
                shift="passA", kt_hi_fn=None, mask_fn=None, pmask_fn=None,
                ry_off=0, bias_blk=None, den_eps=None):
    nqt = QB // 128
    for h in heads:
        base = 64 * (h % 2)
        for qb in range(NQB):
            kt_hi = kt_hi_fn(qb) if kt_hi_fn is not None else NT
            if shift == "passA":
                for qt in range(nqt):
                    qt_g = qb * nqt + qt
                    kwa = kt_hi * 128 if kt_hi_fn is None \
                        else min(kt_hi, qt_g + 1) * 128
                    nkc = (kwa + 511) // 512
                    gds = C.snap_guards(list(range(nkc)))
                    pe_t = C.pe.pre(1)
                    def fpa(t, h=h, qb=qb, qt=qt, nkc=nkc, kwa=kwa, gds=gds):
                        q = qT_fn(h)
                        k_ = kT_fn(h)
                        last = None
                        for ci in range(nkc):
                            wg(t, gds[ci])
                            w = min(512, kwa - ci * 512)
                            last = t.matmul(
                                C.bank(ci, w=w),
                                q[:, qb * QB + qt * 128: qb * QB + qt * 128 + 128],
                                k_[:, ci * 512: ci * 512 + w],
                                start=True, stop=True)
                        last.then_inc(C.pe.h, 1)
                    C.emit("tensor", fpa)
                    C.ve.pre(1)
                    def fmax(v, qt=qt, pe_t=pe_t, kwa=kwa):
                        v.wait_ge(C.pe.h, pe_t)
                        v.reduce_max(C.RM[:, qt:qt + 1], C.ps[:, 0:kwa], AX.X)\
                            .then_inc(C.ve.h, 1)
                    C.emit("vector", fmax)
                    for ci in range(nkc):
                        C.guard_bank(ci, C.ve, C.ve.n)
                vbk = C.ve.pre(1)
                v_t = C.ve.pre(1)
                def fblk(v, nqt=nqt, vbk=vbk):
                    v.reduce_max(C.RM[:, 8:9], C.RM[:, 0:nqt], AX.X)\
                        .then_inc(C.ve.h, 1)
                    v.wait_ge(C.ve.h, vbk)
                    v.tensor_copy(C.RX[:, 6144:6145], C.RM[:, 8:9])\
                        .then_inc(C.ve.h, 1)
                C.emit("vector", fblk)
                gd6 = C.snap_guards([6])
                pe_r = C.pe.pre(1)
                def frow(t, v_t=v_t, gd6=gd6):
                    t.wait_ge(C.ve.h, v_t)
                    wg(t, gd6[0])
                    t.matmul(C.ps[0:1, 3072:3200], C.RX[:, 6144:6145], C.ident,
                             start=True, stop=True).then_inc(C.pe.h, 1)
                C.emit("tensor", frow)
                vck = C.ve.pre(1)
                v_r = C.ve.pre(1)
                def fcm(v, pe_r=pe_r, vck=vck):
                    v.wait_ge(C.pe.h, pe_r)
                    v.reduce_max(C.RM[0:1, 9:10], C.ps[0:1, 3072:3200], AX.X)\
                        .then_inc(C.ve.h, 1)
                    v.wait_ge(C.ve.h, vck)
                    v.tensor_copy(C.RX[0:1, 6146:6147], C.RM[0:1, 9:10])\
                        .then_inc(C.ve.h, 1)
                C.emit("vector", fcm)
                C.guard_bank(6, C.ve, C.ve.n)
                gd6 = C.snap_guards([6])
                pe_c = C.pe.pre(1)
                def fbcc(t, v_r=v_r, gd6=gd6):
                    t.wait_ge(C.ve.h, v_r)
                    wg(t, gd6[0])
                    t.matmul(C.ps[:, 3072:3073], C.ones_bf[0:1, :],
                             C.RX[0:1, 6146:6147], start=True, stop=True)\
                        .then_inc(C.pe.h, 1)
                C.emit("tensor", fbcc)
                a_c = C.act.pre(1)
                def fnegc(s, pe_c=pe_c):
                    s.wait_ge(C.pe.h, pe_c)
                    s.activation(C.RM[:, 16:17], C.ps[:, 3072:3073], AF.Copy,
                                 scale=-1.0).then_inc(C.act.h, 1)
                C.emit("scalar", fnegc)
                C.guard_bank(6, C.act, a_c)
                bias_ready = (C.act, a_c)
            else:
                bias_ready = None
            if bias_blk is not None:
                bias_fn, bias_ready = bias_blk(h, qb, bias_ready)
            else:
                bias_fn = None

            av_started = False
            for kt in range(kt_hi):
                sb = 4 if (kt % 2 == 0) else 0
                gds = C.snap_guards([sb, sb + 1])
                pe_s = C.pe.pre(1)
                def fst(t, h=h, qb=qb, kt=kt, sb=sb, gds=gds):
                    wg(t, gds[0])
                    wg(t, gds[1])
                    q = qT_fn(h)
                    k_ = kT_fn(h)
                    last = None
                    for half in range(2):
                        last = t.matmul(
                            C.bank(sb + half, w=512),
                            k_[:, kt * 128: kt * 128 + 128],
                            q[:, qb * QB + half * 512: qb * QB + half * 512 + 512],
                            start=True, stop=True)
                    last.then_inc(C.pe.h, 1)
                C.emit("tensor", fst)
                pe_after_mask = (C.pe, pe_s)
                if mask_fn is not None:
                    pe_after_mask = mask_fn(h, qb, kt, sb, pe_s)
                pcols = (kt % 2) * 2048
                a_e = C.act.pre(1)
                bias_ap = bias_fn(kt) if bias_fn is not None else (
                    C.RM[:, 16:17] if shift == "passA"
                    else C.constf(float(shift)))
                def fexp(s, sb=sb, pcols=pcols, pam=pe_after_mask,
                         br=bias_ready, kt=kt, bias_ap=bias_ap):
                    wg(s, pam)
                    if br is not None and kt == 0:
                        wg(s, br)
                    s.activation(C.RX[:, pcols:pcols + 1024],
                                 C.ps[:, sb * 512: sb * 512 + 1024],
                                 AF.Exp, bias=bias_ap).then_inc(C.act.h, 1)
                C.emit("scalar", fexp)
                p_ready = (C.act, a_e)
                if pmask_fn is not None:
                    p_ready = pmask_fn(h, qb, kt, pcols, p_ready)
                gd23 = C.snap_guards([2, 3]) if not av_started else [None, None]
                C.pe.pre(1)
                def fav(t, h=h, kt=kt, pcols=pcols, p_ready=p_ready,
                        gd23=gd23, avs=av_started, last=(kt == kt_hi - 1)):
                    wg(t, p_ready)
                    if not avs:
                        wg(t, gd23[0])
                        wg(t, gd23[1])
                    li = None
                    for half in range(2):
                        li = t.matmul(
                            C.ps[0:65, 1024 + half * 512: 1536 + half * 512],
                            v_lhsT_fn(h, kt),
                            C.RX[:, pcols + half * 512: pcols + half * 512 + 512],
                            start=(not avs), stop=last)
                    li.then_inc(C.pe.h, 1)
                C.emit("tensor", fav)
                av_started = True
                C.guard_bank(sb, C.act, a_e)
                C.guard_bank(sb + 1, C.act, a_e)
            pe_av_done = C.pe.n
            vna = C.ve.pre(1)
            vnb = C.ve.pre(1)
            C.ve.pre(1)
            v_n = C.ve.pre(1)
            def fnorm1(v, pe_av_done=pe_av_done, vna=vna, vnb=vnb):
                v.wait_ge(C.pe.h, pe_av_done)
                if den_eps is not None:
                    v.tensor_scalar(C.RM[64:65, 1024:2048],
                                    C.ps[64:65, 1024:2048], den_eps, None,
                                    ALU.add).then_inc(C.ve.h, 1)
                    v.wait_ge(C.ve.h, vna)
                    v.reciprocal(C.RM[64:65, 1024:2048],
                                 C.RM[64:65, 1024:2048]).then_inc(C.ve.h, 1)
                else:
                    v.reciprocal(C.RM[64:65, 1024:2048],
                                 C.ps[64:65, 1024:2048]).then_inc(C.ve.h, 2)
                v.wait_ge(C.ve.h, vnb)
                v.tensor_copy(C.RX[64:65, 4096:5120], C.RM[64:65, 1024:2048])\
                    .then_inc(C.ve.h, 1)
                v.wait_ge(C.ve.h, vnb + 1)
                v.tensor_tensor(C.RX[64:65, 5120:6144], C.RM[64:65, 1024:2048],
                                C.RX[64:65, 4096:5120], ALU.subtract)\
                    .then_inc(C.ve.h, 1)
            C.emit("vector", fnorm1)
            for half in range(2):
                gd6 = C.snap_guards([6])
                pe_n = C.pe.pre(1)
                def fnorm2(t, v_n=v_n, gd6=gd6, half=half):
                    t.wait_ge(C.ve.h, v_n)
                    wg(t, gd6[0])
                    t.matmul(C.ps[0:64, 3072:3584], C.ones_bf[64:65, 0:64],
                             C.RX[64:65, 4096 + half * 512: 4608 + half * 512],
                             start=True, stop=False)
                    t.matmul(C.ps[0:64, 3072:3584], C.ones_bf[64:65, 0:64],
                             C.RX[64:65, 5120 + half * 512: 5632 + half * 512],
                             start=False, stop=True).then_inc(C.pe.h, 1)
                C.emit("tensor", fnorm2)
                n3a = C.ve.pre(1)
                C.ve.pre(2)
                def fnorm3(v, pe_n=pe_n, h=h, qb=qb, base=base, half=half,
                           n3a=n3a):
                    v.wait_ge(C.pe.h, pe_n)
                    v.tensor_copy(C.RY[0:64, ry_off:ry_off + 512],
                                  C.ps[0:64, 3072:3584]).then_inc(C.ve.h, 1)
                    v.wait_ge(C.ve.h, n3a)
                    dst = o_dst_fn(h, qb)[:, half * 512:half * 512 + 512] \
                        if base == 0 else \
                        C.RX[0:64, 6656 + half * 512: 7168 + half * 512]
                    v.tensor_tensor(
                        dst, C.ps[0:64, 1024 + half * 512: 1536 + half * 512],
                        C.RY[0:64, ry_off:ry_off + 512], ALU.mult)\
                        .then_inc(C.ve.h, 2)
                C.emit("vector", fnorm3)
                C.guard_bank(6, C.ve, C.ve.n)
            C.guard_bank(2, C.ve, C.ve.n)
            C.guard_bank(3, C.ve, C.ve.n)
            if base == 64:
                v_m = C.ve.n
                gds = C.snap_guards([4, 5])
                pe_m = C.pe.pre(1)
                def fmove(t, v_m=v_m, gds=gds):
                    t.wait_ge(C.ve.h, v_m)
                    wg(t, gds[0])
                    wg(t, gds[1])
                    last = None
                    for half in range(2):
                        last = t.matmul(
                            C.ps[64:128, 2048 + half * 512: 2560 + half * 512],
                            C.ident[0:64, 0:64],
                            C.RX[0:64, 6656 + half * 512: 7168 + half * 512],
                            start=True, stop=True)
                    last.then_inc(C.pe.h, 1)
                C.emit("tensor", fmove)
                C.ve.pre(1)
                def fmv2(v, pe_m=pe_m, h=h, qb=qb):
                    v.wait_ge(C.pe.h, pe_m)
                    v.tensor_copy(o_dst_fn(h, qb), C.ps[64:128, 2048:3072])\
                        .then_inc(C.ve.h, 1)
                C.emit("vector", fmv2)
                C.guard_bank(4, C.ve, C.ve.n)
                C.guard_bank(5, C.ve, C.ve.n)
    return (C.ve, C.ve.n)



def emit_l0a(C, W, stop=None):
    RAb = C.RAb
    s_wa = C.sem("l0wa")
    s_wb = C.sem("l0wb")
    cq = lambda k: RAb[:, k * 2048: (k + 1) * 2048]
    ckv = lambda k: RAb[:, 8192 + k * 2048: 8192 + (k + 1) * 2048]
    h_t = lambda k, c: C.RC[:, k * 2048 + c * 512: k * 2048 + c * 512 + 512]

    t_wdq = load_wmat(C, C.RW, 0, W["l0_wdq"], 8, 512, s_wa)
    t_wdkv = load_wmat(C, C.RW, 4096, W["l0_wdkv"], 8, 512, s_wb)
    inner_done = (C.ve, C.ve.n)

    mm_chain(C,
             lambda k, m: C.RW[:, k * 512 + m * 128: k * 512 + m * 128 + 128],
             h_t,
             evac_copy(C, lambda m, c: cq(m)[:, c * 512:(c + 1) * 512]),
             m_tiles=4, k_list=range(8), n_chunks=4,
             pre_waits=((s_wa, t_wdq), inner_done))
    pe_cq = (C.pe, C.pe.n)
    mm_chain(C,
             lambda k, m: C.RW[:, 4096 + k * 512 + m * 128:
                               4096 + k * 512 + m * 128 + 128],
             h_t,
             evac_copy(C, lambda m, c: ckv(m)[:, c * 512:(c + 1) * 512]),
             m_tiles=4, k_list=range(8), n_chunks=4,
             pre_waits=((s_wb, t_wdkv),))
    pe_ckv = (C.pe, C.pe.n)

    if stop == "cq":
        return
    t_upv = load_wmat(C, C.RW, 0, W["l0_wupv"], 4, 1024, s_wa, waits=(pe_cq,))
    t_upq = load_wmat(C, C.RW, 4096, W["l0_wupq"], 4, 1024, s_wb,
                      waits=(pe_ckv,))
    t_upk = load_wmat(C, C.RW, 8192, W["l0_wupk"], 4, 1024, s_wb)

    def v_chain(vhalf, extra_pre):
        po_t = C.po.pre(1)
        def fones(g, extra_pre=extra_pre):
            for w in extra_pre:
                wg(g, w)
            g.memset(RAb[:, 16384:24704], 1.0).then_inc(C.po.h, 1)
        C.emit("gpsimd", fones)
        def v_evac(tt, c, b, pe_t):
            cnt = C.ve.pre(1)
            def f(v, tt=tt, b=b, pe_t=pe_t, po_t=po_t):
                v.wait_ge(C.pe.h, pe_t)
                v.wait_ge(C.po.h, po_t)
                dst = RAb[:, 16384 + tt * 520: 16384 + (tt + 1) * 520]\
                    .rearrange("p (h e) -> p h e", h=8)[:, :, 0:64]
                v.tensor_copy(dst, C.bank(b).rearrange("p (h e) -> p h e", h=8))\
                    .then_inc(C.ve.h, 1)
            C.emit("vector", f)
            return (C.ve, cnt)
        mm_chain(C,
                 lambda k, m: ckv(k)[:, m * 128: m * 128 + 128],
                 lambda k, c: C.RW[:, k * 1024 + vhalf * 512:
                                   k * 1024 + vhalf * 512 + 512],
                 v_evac, m_tiles=16, k_list=range(4), n_chunks=1,
                 pre_waits=((s_wa, t_upv),) + tuple(extra_pre),
                 banks=(0, 1, 2, 3))

    def v_lhsT(h, kt):
        hh = h % 8
        return RAb[:, 16384 + kt * 520 + hh * 65:
                   16384 + kt * 520 + hh * 65 + 65]

    def o_dst(h, qb):
        b = 64 * (h % 2)
        return C.RC[b:b + 64,
                    (h // 2) * 2048 + qb * QB: (h // 2) * 2048 + qb * QB + QB]

    def qT_fn(h):
        b = 64 * (h % 2)
        return RAb[b:b + 64, 24704:24704 + 2048]

    def kT_fn(h):
        b = 64 * (h % 2)
        return RAb[b:b + 64, 26752:26752 + 2048]

    for half in range(2):
        if half == 0:
            v_chain(0, ())
        else:
            v_chain(1, ((C.pe, C.pe.n),))
        for grp in range(4):
            gh = half * 8 + grp * 2
            dtile = gh // 2
            mm_chain(C,
                     lambda k, m, dtile=dtile:
                     C.RW[:, 4096 + k * 1024 + dtile * 128:
                          4096 + k * 1024 + dtile * 128 + 128],
                     lambda k, c: cq(k)[:, c * 512: c * 512 + 512],
                     evac_copy(C, lambda m, c:
                               RAb[:, 24704 + c * 512: 24704 + c * 512 + 512],
                               scale=0.125),
                     m_tiles=1, k_list=range(4), n_chunks=4,
                     pre_waits=((s_wb, t_upk),), banks=(4, 5))
            mm_chain(C,
                     lambda k, m, dtile=dtile:
                     C.RW[:, 8192 + k * 1024 + dtile * 128:
                          8192 + k * 1024 + dtile * 128 + 128],
                     lambda k, c: ckv(k)[:, c * 512: c * 512 + 512],
                     evac_copy(C, lambda m, c:
                               RAb[:, 26752 + c * 512: 26752 + c * 512 + 512]),
                     m_tiles=1, k_list=range(4), n_chunks=4, banks=(4, 5))
            for slot in (24704, 26752):
                a_prev = (C.act, C.act.n)
                gds = C.snap_guards([2, 3, 4, 5])
                pe_sw = C.pe.pre(1)
                def f_sw(t, slot=slot, gds=gds, a_prev=a_prev):
                    wg(t, a_prev)
                    for c4 in range(4):
                        wg(t, gds[c4])
                    last = None
                    for c4 in range(4):
                        last = t.matmul(C.bank(2 + c4), C.pswap,
                                        RAb[:, slot + c4 * 512:
                                            slot + c4 * 512 + 512],
                                        start=True, stop=True)
                    last.then_inc(C.pe.h, 1)
                C.emit("tensor", f_sw)
                C.ve.pre(1)
                def f_t1(v, slot=slot, pe_sw=pe_sw):
                    v.wait_ge(C.pe.h, pe_sw)
                    v.tensor_tensor(RAb[:, slot:slot + 2048],
                                    RAb[:, slot:slot + 2048],
                                    C.ROPE[:, 0:2048], ALU.mult)\
                        .then_inc(C.ve.h, 1)
                C.emit("vector", f_t1)
                for c4 in range(4):
                    C.ve.pre(2)
                    def f_t2(v, slot=slot, c4=c4):
                        v.tensor_tensor(C.RY[:, 0:512], C.bank(2 + c4),
                                        C.ROPE[:, 2048 + c4 * 512:
                                               2048 + c4 * 512 + 512],
                                        ALU.mult)
                        v.tensor_tensor(RAb[:, slot + c4 * 512:
                                            slot + c4 * 512 + 512],
                                        RAb[:, slot + c4 * 512:
                                            slot + c4 * 512 + 512],
                                        C.RY[:, 0:512], ALU.add)\
                            .then_inc(C.ve.h, 2)
                    C.emit("vector", f_t2)
                    C.guard_bank(2 + c4, C.ve, C.ve.n)
            rope_done = (C.ve, C.ve.n)
            C.guard_bank(0, *rope_done)
            C.guard_bank(1, *rope_done)
            if stop == "qkv":
                return
            emit_attn64(C, [gh, gh + 1], qT_fn, kT_fn, v_lhsT, o_dst,
                        shift="passA")
            if stop == "attn1":
                return
    if stop == "attn":
        return

    attn_pe = (C.pe, C.pe.n)
    attn_ve = (C.ve, C.ve.n)
    for halfw in range(2):
        t_wo = load_wmat(C, C.RW, halfw * 4096, W["l0_wout_h%d" % halfw],
                         8, 512, s_wa if halfw == 0 else s_wb,
                         waits=(attn_pe,))
        mm_chain(C,
                 lambda k, m, halfw=halfw:
                 C.RW[:, halfw * 4096 + k * 512 + m * 128:
                      halfw * 4096 + k * 512 + m * 128 + 128],
                 lambda k, c: C.RC[:, k * 2048 + c * 512:
                                   k * 2048 + c * 512 + 512],
                 evac_resid(C,
                            lambda m, c, halfw=halfw:
                            RAb[:, (halfw * 4 + m) * 2048 + c * 512:
                                (halfw * 4 + m) * 2048 + c * 512 + 512],
                            lambda m, c, halfw=halfw:
                            C.RB[:, (halfw * 4 + m) * 2048 + c * 512:
                                 (halfw * 4 + m) * 2048 + c * 512 + 512]),
                 m_tiles=4, k_list=range(8), n_chunks=4,
                 pre_waits=((s_wa if halfw == 0 else s_wb, t_wo), attn_ve),
                 banks=(0, 1, 2, 3))
    return (C.ve, C.ve.n)



def build_l0a_test(stage="full", grps=None):
    nc = bass.Bass("TRN2", target_bir_lowering=False)
    xin = nc.dram_tensor("xin", [ND, 128, T], F32, kind="ExternalInput").ap()
    H = {}
    for nm, shp in [("l0_wdq", [D, 512]), ("l0_wdkv", [D, 512]),
                    ("l0_wupq", [512, D]), ("l0_wupk", [512, D]),
                    ("l0_wupv", [512, D]),
                    ("l0_wout_h0", [D, 512]), ("l0_wout_h1", [D, 512]),
                    ("l0_cols", [128, 16]),
                    ("ropeC", [128, T]), ("ropeS", [128, T]),
                    ("consts", [128, 384])]:
        H[nm] = nc.dram_tensor(nm, shp, F32, kind="ExternalInput").ap()
    s_out = nc.dram_tensor("s_out", [ND, 128, T], F32, kind="ExternalOutput").ap()
    s2_out = nc.dram_tensor("s2_out", [ND, 128, T], F32,
                            kind="ExternalOutput").ap()

    es = ExitStack()
    with es:
        C = Ctx(nc, es)
        blk = es.enter_context(nc.Block())
        s_x = C.sem("x")
        s_k = C.sem("k")
        s_o = C.sem("out")
        for k in range(ND):
            e_dma(C, "sync", C.RA[:, k * 2048:(k + 1) * 2048], xin[k], s_x)
        e_dma(C, "gpsimd", C.CONST[:], H["consts"], s_k)
        e_dma(C, "gpsimd", C.ROPE[:, 0:2048], H["ropeC"], s_k)
        e_dma(C, "gpsimd", C.ROPE[:, 2048:4096], H["ropeS"], s_k)
        e_dma(C, "gpsimd", C.RCOL[:, 0:16], H["l0_cols"], s_k)
        konst_ready = [(s_k, s_k.n), (s_x, s_x.n)]
        ln1 = emit_ln(C, lambda k: C.RA[:, k * 2048:(k + 1) * 2048],
                      lambda k: C.RB[:, k * 2048:(k + 1) * 2048],
                      pre_waits=tuple(konst_ready))
        ln2 = emit_ln(C, lambda k: C.RB[:, k * 2048:(k + 1) * 2048],
                      lambda k: C.RC[:, k * 2048:(k + 1) * 2048],
                      gain_col=lambda k: C.RCOL[:, k:k + 1],
                      bias_col=lambda k: C.RCOL[:, 8 + k:8 + k + 1],
                      pre_waits=(ln1,), sq_region=C.RAb[:, 0:4096],
                      in_bf16=True)
        if stage == "ln2":
            for k in range(ND):
                e_dma(C, "gpsimd", s_out[k], C.RB[:, k * 2048:(k + 1) * 2048],
                      s_o, waits=(ln1,))
            for k in range(ND):
                e_dma(C, "gpsimd", s2_out[k], C.RC[:, k * 2048:(k + 1) * 2048],
                      s_o, waits=(ln2,))
        elif stage == "cq":
            emit_l0a(C, H, stop="cq")
            done = (C.act, C.act.n)
            for k in range(ND):
                e_dma(C, "gpsimd", s_out[k],
                      C.RAb[:, k * 2048:(k + 1) * 2048], s_o, waits=(done,))
        elif stage == "qkv":
            emit_l0a(C, H, stop="qkv")
            done = (C.ve, C.ve.n)
            e_dma(C, "gpsimd", s_out[0], C.RAb[:, 24704:26752], s_o,
                  waits=(done,))
            e_dma(C, "gpsimd", s_out[1], C.RAb[:, 26752:28800], s_o)
            e_dma(C, "gpsimd", s_out[2][:, 0:520],
                  C.RAb[:, 16384:16904], s_o)
            e_dma(C, "gpsimd", s_out[3][:, 0:520],
                  C.RAb[:, 16384 + 520:16384 + 1040], s_o)
        elif stage == "attn1":
            emit_l0a(C, H, stop="attn1")
            done = (C.ve, C.ve.n)
            e_dma(C, "gpsimd", s_out[0], C.RC[:, 0:2048], s_o, waits=(done,))
            e_dma(C, "gpsimd", s_out[1][:, 0:17], C.RM[:, 0:17], s_o)
        elif stage == "attn":
            emit_l0a(C, H, stop="attn")
            done = (C.ve, C.ve.n)
            for k in range(ND):
                e_dma(C, "gpsimd", s_out[k],
                      C.RC[:, k * 2048:(k + 1) * 2048], s_o, waits=(done,))
        else:
            s_done = emit_l0a(C, H)
            for k in range(ND):
                e_dma(C, "gpsimd", s_out[k],
                      C.RAb[:, k * 2048:(k + 1) * 2048], s_o, waits=(s_done,))
        def ffin(g, n=s_o.n):
            g.wait_ge(s_o.h, n)
        C.emit("gpsimd", ffin)
        C.finalize(blk)
    return nc


_BUILD_CACHE = {}


def kernel(**inputs):
    G = stage_shared(inputs)
    stage_l0f(inputs, G)
    stage_l1(inputs, G)
    stage_l2(inputs, G)
    stage_l3(inputs, G)
    xs = stage_x(inputs)
    if "prog" not in _BUILD_CACHE:
        _BUILD_CACHE["prog"] = build_all()
    nc = _BUILD_CACHE["prog"]
    in_maps = []
    for b in range(8):
        m = dict(G, xin=xs[b])
        m["l3_pos"] = stage_l3_pos(inputs, b)
        in_maps.append(m)
    res = run_bass_kernel_spmd(nc, in_maps, core_ids=list(range(8)))
    cur = [np.asarray(res.results[b]["xout"]) for b in range(8)]
    out = np.stack([cur[b].reshape(1024, 2048).T for b in range(8)], 0)
    return np.ascontiguousarray(out.astype(np.float32))



def emit_attn256(C, heads, qT_fn, kT_fn, vc_lhsT, o_dst_fn,
                 pre_waits=()):
    QBS = 512
    nqb = T // QBS
    nqt = QBS // 128
    for h in heads:
        for qb in range(nqb):
            for qt in range(nqt):
                gds = C.snap_guards([0, 1, 2, 3])
                pw = tuple(pre_waits) if (qb == 0 and qt == 0) else ()
                pe_t = C.pe.pre(1)
                def fpa(t, h=h, qb=qb, qt=qt, gds=gds, pw=pw):
                    for w_ in pw:
                        wg(t, w_)
                    last = None
                    for ci in range(4):
                        wg(t, gds[ci])
                        for k2 in range(2):
                            last = t.matmul(
                                C.bank(ci),
                                qT_fn(h, k2)[:, qb * QBS + qt * 128:
                                             qb * QBS + qt * 128 + 128],
                                kT_fn(k2)[:, ci * 512: ci * 512 + 512],
                                start=(k2 == 0), stop=(k2 == 1))
                    last.then_inc(C.pe.h, 1)
                C.emit("tensor", fpa)
                C.ve.pre(1)
                def fmax(v, qt=qt, pe_t=pe_t):
                    v.wait_ge(C.pe.h, pe_t)
                    v.reduce_max(C.RM[:, qt:qt + 1], C.ps[:, 0:2048], AX.X)\
                        .then_inc(C.ve.h, 1)
                C.emit("vector", fmax)
                for ci in range(4):
                    C.guard_bank(ci, C.ve, C.ve.n)
            vbk = C.ve.pre(1)
            v_t = C.ve.pre(1)
            def fblk(v, nqt=nqt, vbk=vbk):
                v.reduce_max(C.RM[:, 8:9], C.RM[:, 0:nqt], AX.X)\
                    .then_inc(C.ve.h, 1)
                v.wait_ge(C.ve.h, vbk)
                v.tensor_copy(C.RX[:, 6144:6145], C.RM[:, 8:9])\
                    .then_inc(C.ve.h, 1)
            C.emit("vector", fblk)
            gd6 = C.snap_guards([6])
            pe_r = C.pe.pre(1)
            def frow(t, v_t=v_t, gd6=gd6):
                t.wait_ge(C.ve.h, v_t)
                wg(t, gd6[0])
                t.matmul(C.ps[0:1, 3072:3200], C.RX[:, 6144:6145], C.ident,
                         start=True, stop=True).then_inc(C.pe.h, 1)
            C.emit("tensor", frow)
            vck = C.ve.pre(1)
            v_r = C.ve.pre(1)
            def fcm(v, pe_r=pe_r, vck=vck):
                v.wait_ge(C.pe.h, pe_r)
                v.reduce_max(C.RM[0:1, 9:10], C.ps[0:1, 3072:3200], AX.X)\
                    .then_inc(C.ve.h, 1)
                v.wait_ge(C.ve.h, vck)
                v.tensor_copy(C.RX[0:1, 6146:6147], C.RM[0:1, 9:10])\
                    .then_inc(C.ve.h, 1)
            C.emit("vector", fcm)
            C.guard_bank(6, C.ve, C.ve.n)
            gd6 = C.snap_guards([6])
            pe_c = C.pe.pre(1)
            def fbcc(t, v_r=v_r, gd6=gd6):
                t.wait_ge(C.ve.h, v_r)
                wg(t, gd6[0])
                t.matmul(C.ps[:, 3072:3073], C.ones_bf[0:1, :],
                         C.RX[0:1, 6146:6147], start=True, stop=True)\
                    .then_inc(C.pe.h, 1)
            C.emit("tensor", fbcc)
            a_c = C.act.pre(1)
            def fnegc(s, pe_c=pe_c):
                s.wait_ge(C.pe.h, pe_c)
                s.activation(C.RM[:, 16:17], C.ps[:, 3072:3073], AF.Copy,
                             scale=-1.0).then_inc(C.act.h, 1)
            C.emit("scalar", fnegc)
            C.guard_bank(6, C.act, a_c)
            bias_ready = (C.act, a_c)

            av_started = False
            for kt in range(NT):
                sb = 4 if kt % 2 == 0 else 5
                gds = C.snap_guards([sb])
                pe_s = C.pe.pre(1)
                def fst(t, h=h, qb=qb, kt=kt, sb=sb, gds=gds):
                    wg(t, gds[0])
                    last = None
                    for k2 in range(2):
                        last = t.matmul(
                            C.bank(sb),
                            kT_fn(k2)[:, kt * 128: kt * 128 + 128],
                            qT_fn(h, k2)[:, qb * QBS: qb * QBS + QBS],
                            start=(k2 == 0), stop=(k2 == 1))
                    last.then_inc(C.pe.h, 1)
                C.emit("tensor", fst)
                pcols = (kt % 2) * 512
                a_e = C.act.pre(1)
                def fexp(s, sb=sb, pcols=pcols, pe_s=pe_s, br=bias_ready,
                         kt=kt):
                    s.wait_ge(C.pe.h, pe_s)
                    if kt == 0:
                        wg(s, br)
                    s.activation(C.RX[:, pcols:pcols + 512], C.bank(sb),
                                 AF.Exp, bias=C.RM[:, 16:17])\
                        .then_inc(C.act.h, 1)
                C.emit("scalar", fexp)
                gdav = C.snap_guards([2, 3, 6]) if not av_started \
                    else [None, None, None]
                C.pe.pre(1)
                def fav(t, kt=kt, pcols=pcols, a_e=a_e, gdav=gdav,
                        avs=av_started, last=(kt == NT - 1)):
                    t.wait_ge(C.act.h, a_e)
                    if not avs:
                        for g_ in gdav:
                            wg(t, g_)
                    p = C.RX[:, pcols:pcols + 512]
                    for mc in range(2):
                        t.matmul(C.ps[0:128, (2 + mc) * 512:(3 + mc) * 512],
                                 vc_lhsT(kt, mc), p,
                                 start=(not avs), stop=last)
                    t.matmul(C.ps[0:1, 3072:3584], vc_lhsT(kt, 2), p,
                             start=(not avs), stop=last).then_inc(C.pe.h, 1)
                C.emit("tensor", fav)
                av_started = True
                C.guard_bank(sb, C.act, a_e)
            pe_av = C.pe.n
            v_n = C.ve.pre(1)
            def fnorm1(v, pe_av=pe_av):
                v.wait_ge(C.pe.h, pe_av)
                v.reciprocal(C.RM[0:1, 1024:1536], C.ps[0:1, 3072:3584])
                v.tensor_copy(C.RX[0:1, 4096:4608], C.RM[0:1, 1024:1536])
                v.tensor_tensor(C.RX[0:1, 4608:5120], C.RM[0:1, 1024:1536],
                                C.RX[0:1, 4096:4608], ALU.subtract)\
                    .then_inc(C.ve.h, 1)
            C.emit("vector", fnorm1)
            gds = C.snap_guards([4, 5])
            pe_n = C.pe.pre(1)
            def fnorm2(t, v_n=v_n, gds=gds):
                t.wait_ge(C.ve.h, v_n)
                wg(t, gds[0])
                t.matmul(C.ps[:, 2048:2560], C.ones_bf[0:1, :],
                         C.RX[0:1, 4096:4608], start=True, stop=False)
                t.matmul(C.ps[:, 2048:2560], C.ones_bf[0:1, :],
                         C.RX[0:1, 4608:5120], start=False, stop=True)\
                    .then_inc(C.pe.h, 1)
            C.emit("tensor", fnorm2)
            v_m = C.ve.pre(1)
            def fnorm3(v, pe_n=pe_n, h=h, qb=qb):
                v.wait_ge(C.pe.h, pe_n)
                v.tensor_copy(C.RY[:, 0:512], C.ps[:, 2048:2560])
                v.tensor_tensor(o_dst_fn(h, 0, qb),
                                C.ps[0:128, 1024:1536],
                                C.RY[:, 0:512], ALU.mult)
                v.tensor_tensor(o_dst_fn(h, 1, qb),
                                C.ps[0:128, 1536:2048],
                                C.RY[:, 0:512], ALU.mult)\
                    .then_inc(C.ve.h, 1)
            C.emit("vector", fnorm3)
            C.guard_bank(2, C.ve, C.ve.n)
            C.guard_bank(3, C.ve, C.ve.n)
            C.guard_bank(4, C.ve, C.ve.n)
            C.guard_bank(6, C.ve, C.ve.n)
    return (C.ve, C.ve.n)



def emit_l0f(C, W, spill_done=None, stop=None):
    RAb = C.RAb
    s_wa = C.sem("fwa")
    s_wb = C.sem("fwb")
    s_wc = C.sem("fwc")
    nx1_t = lambda k: C.RC[:, k * 2048:(k + 1) * 2048]
    rms1 = emit_ln(C, lambda k: C.RB[:, k * 2048:(k + 1) * 2048],
                   nx1_t, rms=True, eps=1e-6,
                   gain_col=lambda k: C.RCOL[:, 16 + k:17 + k],
                   sq_region=RAb[:, 0:4096], in_bf16=True,
                   pre_waits=(spill_done,) if spill_done else ())
    pe0 = (C.pe, C.pe.n)
    t_kc = load_wmat(C, C.RW, 0, W["l0f_wkc"], 8, 256, s_wa, waits=(pe0,))
    t_vc = load_wmat(C, C.RW, 2048, W["l0f_wvc"], 8, 256, s_wa)
    mm_chain(C,
             lambda k, m: C.RW[:, k * 256 + m * 128: k * 256 + m * 128 + 128],
             lambda k, c: C.RC[:, k * 2048 + c * 512: k * 2048 + c * 512 + 512],
             evac_copy(C, lambda m, c:
                       RAb[:, m * 2048 + c * 512: m * 2048 + c * 512 + 512]),
             m_tiles=2, k_list=range(8), n_chunks=4,
             pre_waits=((s_wa, t_kc), rms1), banks=(0, 1))
    po_t = C.po.pre(1)
    def fonesv(g, spill_done=spill_done):
        if spill_done is not None:
            wg(g, spill_done)
        g.memset(RAb[:, 4096:8224], 1.0).then_inc(C.po.h, 1)
    C.emit("gpsimd", fonesv)
    def vc_evac(tt, c, b, pe_t):
        cnt = C.ve.pre(1)
        def f(v, tt=tt, b=b, pe_t=pe_t, po_t=po_t):
            v.wait_ge(C.pe.h, pe_t)
            v.wait_ge(C.po.h, po_t)
            v.tensor_copy(RAb[:, 4096 + tt * 258: 4096 + tt * 258 + 256],
                          C.bank(b, w=256)).then_inc(C.ve.h, 1)
        C.emit("vector", f)
        return (C.ve, cnt)
    mm_chain(C,
             lambda k, m: C.RC[:, k * 2048 + m * 128: k * 2048 + m * 128 + 128],
             lambda k, c: C.RW[:, 2048 + k * 256: 2048 + k * 256 + 256],
             vc_evac, m_tiles=16, k_list=range(8), n_chunks=1,
             pre_waits=((s_wa, t_vc),), banks=(2, 3))
    if stop == "kcvc":
        return None

    def kT_fn(k2):
        return RAb[:, k2 * 2048: (k2 + 1) * 2048]

    def qT_fn(h, k2):
        return RAb[:, 8224 + k2 * 2048: 8224 + (k2 + 1) * 2048]

    def vc_lhsT(kt, mc):
        base = 4096 + kt * 258
        if mc == 2:
            return RAb[:, base + 256: base + 257]
        return RAb[:, base + mc * 128: base + (mc + 1) * 128]

    def o_dst(h, mc, qb):
        tile = 2 * h + mc
        return RAb[:, 12320 + tile * 2048 + qb * 512:
                   12320 + tile * 2048 + qb * 512 + 512]

    wq_guard = (pe0,)
    for h in range(4):
        t_wq = load_wmat(C, C.RW, 4096, W["l0f_wq"][:, 256 * h:256 * h + 256],
                         8, 256, s_wb, waits=wq_guard)
        mm_chain(C,
                 lambda k, m: C.RW[:, 4096 + k * 256 + m * 128:
                                   4096 + k * 256 + m * 128 + 128],
                 lambda k, c: C.RC[:, k * 2048 + c * 512:
                                   k * 2048 + c * 512 + 512],
                 evac_copy(C, lambda m, c:
                           RAb[:, 8224 + m * 2048 + c * 512:
                               8224 + m * 2048 + c * 512 + 512],
                           scale=1.0 / 16.0),
                 m_tiles=2, k_list=range(8), n_chunks=4,
                 pre_waits=((s_wb, t_wq),), banks=(4, 5))
        wq_guard = ((C.pe, C.pe.n),)
        emit_attn256(C, [h], qT_fn, kT_fn, vc_lhsT, o_dst,
                     pre_waits=((C.act, C.act.n),))
    if stop == "attn":
        return None

    attn_pe = (C.pe, C.pe.n)
    attn_ve = (C.ve, C.ve.n)
    for halfw in range(2):
        t_wo = load_wmat(C, C.RW, 8192 if halfw == 0 else 0,
                         W["l0f_wo_h%d" % halfw], 8, 512,
                         s_wc if halfw == 0 else s_wa, waits=(attn_pe,))
        base = 8192 if halfw == 0 else 0
        mm_chain(C,
                 lambda k, m, base=base:
                 C.RW[:, base + k * 512 + m * 128: base + k * 512 + m * 128 + 128],
                 lambda k, c: RAb[:, 12320 + k * 2048 + c * 512:
                                  12320 + k * 2048 + c * 512 + 512],
                 evac_resid(C,
                            lambda m, c, halfw=halfw:
                            C.RC[:, (halfw * 4 + m) * 2048 + c * 512:
                                 (halfw * 4 + m) * 2048 + c * 512 + 512],
                            lambda m, c, halfw=halfw:
                            C.RB[:, (halfw * 4 + m) * 2048 + c * 512:
                                 (halfw * 4 + m) * 2048 + c * 512 + 512]),
                 m_tiles=4, k_list=range(8), n_chunks=4,
                 pre_waits=((s_wc if halfw == 0 else s_wa, t_wo), attn_ve),
                 banks=(0, 1, 2, 3))
    x2_done = (C.ve, C.ve.n)
    if stop == "x2":
        return x2_done

    rms2 = emit_ln(C, lambda k: C.RC[:, k * 2048:(k + 1) * 2048],
                   lambda k: C.RB[:, k * 2048:(k + 1) * 2048],
                   rms=True, eps=1e-6,
                   gain_col=lambda k: C.RCOL[:, 24 + k:25 + k],
                   sq_region=RAb[:, 0:4096], in_bf16=True,
                   pre_waits=(x2_done,))
    if stop == "nx2":
        return rms2
    x2sp = C.nc.dram_tensor("x2sp", [ND, 128, T], BF16, kind="Internal").ap()
    C.x2sp = x2sp
    s_sp = C.sem("x2sp")
    for k in range(ND):
        e_dma(C, "sync", x2sp[k], C.RC[:, k * 2048:(k + 1) * 2048], s_sp,
              waits=(rms2,) if k == 0 else ())
    x2_spilled = (s_sp, s_sp.n)

    t_wr = load_wmat(C, C.RW, 12160, W["l0f_wr"], 8, 16, s_wb,
                     waits=((C.pe, C.pe.n),))
    def router_evac(m, c, b, pe_t):
        cnt0 = C.ve.pre(1)
        cnt = C.ve.pre(1)
        def f(v, c=c, b=b, pe_t=pe_t, cnt0=cnt0):
            v.wait_ge(C.pe.h, pe_t)
            v.tensor_scalar(C.RY[0:8, c * 512:(c + 1) * 512],
                            C.bank(b, rows=8), C.RCOL[0:8, 32:33], None,
                            ALU.add)
            v.tensor_copy(RAb[0:8, c * 512:(c + 1) * 512],
                          C.RY[0:8, c * 512:(c + 1) * 512]).then_inc(C.ve.h, 1)
            v.wait_ge(C.ve.h, cnt0)
            v.tensor_tensor(RAb[0:8, 2048 + c * 512: 2048 + (c + 1) * 512],
                            C.RY[0:8, c * 512:(c + 1) * 512],
                            RAb[0:8, c * 512:(c + 1) * 512], ALU.subtract)\
                .then_inc(C.ve.h, 1)
        C.emit("vector", f)
        return (C.ve, cnt)
    mm_chain(C,
             lambda k, m: C.RW[:, 12160 + (k % 8) * 16 + (k // 8) * 8:
                               12160 + (k % 8) * 16 + (k // 8) * 8 + 8],
             lambda k, c: C.RB[:, (k % 8) * 2048 + c * 512:
                               (k % 8) * 2048 + c * 512 + 512],
             router_evac,
             m_tiles=1, k_list=range(16), n_chunks=4,
             pre_waits=((s_wb, t_wr), rms2, x2_spilled), banks=(4, 5),
             m_rows=8)
    logits_done = (C.ve, C.ve.n)

    for tt in range(NT):
        gd = C.snap_guards([0])
        pe_t = C.pe.pre(1)
        def ftr(t, tt=tt, gd=gd, logits_done=logits_done):
            wg(t, logits_done)
            wg(t, gd[0])
            t.matmul(C.bank(0, w=8), RAb[0:8, tt * 128:(tt + 1) * 128],
                     C.ident[0:8, 0:8], start=True, stop=False)
            t.matmul(C.bank(0, w=8), RAb[0:8, 2048 + tt * 128:
                                         2048 + (tt + 1) * 128],
                     C.ident[0:8, 0:8], start=False, stop=True)\
                .then_inc(C.pe.h, 1)
        C.emit("tensor", ftr)
        v_t = C.ve.pre(1)
        def fsm1(v, pe_t=pe_t):
            v.wait_ge(C.pe.h, pe_t)
            v.reduce_max(C.RM[:, 20:21], C.bank(0, w=8), AX.X)\
                .then_inc(C.ve.h, 1)
        C.emit("vector", fsm1)
        a_m = C.act.pre(1)
        a_t = C.act.pre(1)
        def fsm2(s, v_t=v_t, a_m=a_m):
            s.wait_ge(C.ve.h, v_t)
            s.activation(C.RM[:, 21:22], C.RM[:, 20:21], AF.Copy, scale=-1.0)\
                .then_inc(C.act.h, 1)
            s.wait_ge(C.act.h, a_m)
            s.activation(C.RY[:, 0:8], C.bank(0, w=8), AF.Exp,
                         bias=C.RM[:, 21:22], accum_out=C.RM[:, 22:23])\
                .then_inc(C.act.h, 1)
        C.emit("scalar", fsm2)
        vs = [C.ve.pre(1) for _ in range(6)]
        v_u = vs[-1]
        def fsm3(v, a_t=a_t, vs=vs):
            v.wait_ge(C.act.h, a_t)
            v.reciprocal(C.RM[:, 23:24], C.RM[:, 22:23]).then_inc(C.ve.h, 1)
            v.wait_ge(C.ve.h, vs[0])
            v.tensor_scalar(C.RY[:, 8:16], C.RY[:, 0:8], C.RM[:, 23:24], None,
                            ALU.mult).then_inc(C.ve.h, 1)
            v.wait_ge(C.ve.h, vs[1])
            v.reduce_max(C.RM[:, 24:25], C.RY[:, 8:16], AX.X)\
                .then_inc(C.ve.h, 1)
            v.wait_ge(C.ve.h, vs[2])
            v.tensor_scalar(C.RY[:, 16:24], C.RY[:, 8:16], C.RM[:, 24:25],
                            None, ALU.is_ge).then_inc(C.ve.h, 1)
            v.wait_ge(C.ve.h, vs[3])
            v.scalar_tensor_tensor(C.RY[:, 24:32], C.RY[:, 16:24], -2.0,
                                   C.RY[:, 8:16], ALU.mult, ALU.add)\
                .then_inc(C.ve.h, 1)
            v.wait_ge(C.ve.h, vs[4])
            v.reduce_max(C.RM[:, 25:26], C.RY[:, 24:32], AX.X)\
                .then_inc(C.ve.h, 1)
        C.emit("vector", fsm3)
        a_u = C.act.pre(1)
        def fsm4(s, v_u=v_u):
            s.wait_ge(C.ve.h, v_u)
            s.activation(C.RM[:, 26:27], C.RM[:, 24:25], AF.Exp)
            s.activation(C.RM[:, 27:28], C.RM[:, 25:26], AF.Exp)\
                .then_inc(C.act.h, 1)
        C.emit("scalar", fsm4)
        vs5 = [C.ve.pre(1) for _ in range(5)]
        def fsm5(v, a_u=a_u, tt=tt, vs5=vs5):
            v.wait_ge(C.act.h, a_u)
            v.tensor_tensor(C.RM[:, 28:29], C.RM[:, 26:27], C.RM[:, 27:28],
                            ALU.add).then_inc(C.ve.h, 1)
            v.wait_ge(C.ve.h, vs5[0])
            v.reciprocal(C.RM[:, 29:30], C.RM[:, 28:29]).then_inc(C.ve.h, 1)
            v.wait_ge(C.ve.h, vs5[1])
            v.tensor_tensor(C.RM[:, 30:31], C.RM[:, 26:27], C.RM[:, 29:30],
                            ALU.mult)
            v.tensor_tensor(C.RM[:, 31:32], C.RM[:, 27:28], C.RM[:, 29:30],
                            ALU.mult).then_inc(C.ve.h, 1)
            v.wait_ge(C.ve.h, vs5[2])
            v.tensor_scalar(C.RY[:, 32:40], C.RY[:, 24:32], C.RM[:, 25:26],
                            None, ALU.is_ge)
            v.tensor_scalar(C.RY[:, 40:48], C.RY[:, 16:24], C.RM[:, 30:31],
                            None, ALU.mult).then_inc(C.ve.h, 1)
            v.wait_ge(C.ve.h, vs5[3])
            v.scalar_tensor_tensor(C.RX[:, 6656 + tt * 8: 6664 + tt * 8],
                                   C.RY[:, 32:40], C.RM[:, 31:32],
                                   C.RY[:, 40:48], ALU.mult, ALU.add)\
                .then_inc(C.ve.h, 1)
        C.emit("vector", fsm5)
        C.guard_bank(0, C.ve, C.ve.n)
    comb_done = (C.ve, C.ve.n)
    if stop == "router":
        return comb_done
    return _emit_l0f_moe(C, W, s_wa, s_wb, s_wc, comb_done)


def _emit_l0f_moe(C, W, s_wa, s_wb, s_wc, comb_done):
    RAb = C.RAb
    RAf = C.RA
    slotA, slotB, slotC = 0, 4096, 8192
    guardA = guardB = guardC = ()
    gate_hist = {}
    git = [0]
    for ei, ename in enumerate(["s"] + list(range(8))):
        shared = (ename == "s")
        if not shared:
            e = ename
            gd = C.snap_guards([0, 1, 2, 3])
            pe_t = C.pe.pre(1)
            def ftrk(t, e=e, gd=gd, comb_done=comb_done):
                wg(t, comb_done)
                last = None
                for bk in range(4):
                    wg(t, gd[bk])
                    for j in range(4):
                        tt = bk * 4 + j
                        last = t.matmul(
                            C.ps[0:1, bk * 512 + j * 128: bk * 512 + j * 128 + 128],
                            C.RX[:, 6656 + tt * 8 + e: 6657 + tt * 8 + e],
                            C.ident, start=True, stop=True)
                last.then_inc(C.pe.h, 1)
            C.emit("tensor", ftrk)
            a_t = C.act.pre(1)
            def frowe(s, pe_t=pe_t):
                s.wait_ge(C.pe.h, pe_t)
                s.activation(C.RX[0:1, 4096:4608], C.ps[0:1, 0:512], AF.Copy)
                s.activation(C.RX[0:1, 4608:5120], C.ps[0:1, 512:1024], AF.Copy)
                s.activation(C.RX[0:1, 5120:5632], C.ps[0:1, 1024:1536], AF.Copy)
                s.activation(C.RX[0:1, 5632:6144], C.ps[0:1, 1536:2048], AF.Copy)\
                    .then_inc(C.act.h, 1)
            C.emit("scalar", frowe)
            for bk in range(4):
                C.guard_bank(bk, C.act, a_t)
            for c in range(4):
                gd = C.snap_guards([6])
                pe_b = C.pe.pre(1)
                def fbce(t, c=c, gd=gd, a_t=a_t):
                    t.wait_ge(C.act.h, a_t)
                    wg(t, gd[0])
                    t.matmul(C.ps[:, 3072:3584], C.ones_bf[0:1, :],
                             C.RX[0:1, 4096 + c * 512: 4608 + c * 512],
                             start=True, stop=True).then_inc(C.pe.h, 1)
                C.emit("tensor", fbce)
                C.ve.pre(1)
                def fcpy(v, c=c, pe_b=pe_b):
                    v.wait_ge(C.pe.h, pe_b)
                    v.tensor_copy(C.RY[:, 2048 + c * 512: 2560 + c * 512],
                                  C.ps[:, 3072:3584]).then_inc(C.ve.h, 1)
                C.emit("vector", fcpy)
                C.guard_bank(6, C.ve, C.ve.n)
            bc_ready = (C.ve, C.ve.n)
        else:
            bc_ready = comb_done
        wkey1 = "l0f_sw1" if shared else "l0f_ew1_%d" % ename
        wkey3 = "l0f_sw3" if shared else "l0f_ew3_%d" % ename
        wkey2 = "l0f_sw2" if shared else "l0f_ew2_%d" % ename
        for hm in range(2):
            t1 = load_wmat(C, C.RW, slotA, W[wkey1][:, hm * 512:(hm + 1) * 512],
                           8, 512, s_wa, waits=guardA)
            t3 = load_wmat(C, C.RW, slotB, W[wkey3][:, hm * 512:(hm + 1) * 512],
                           8, 512, s_wb, waits=guardB)
            for mt in range(4):
                for c in range(4):
                    bA = (mt * 4 + c) % 2
                    bB = 2 + (mt * 4 + c) % 2
                    gdA = C.bank_guard[bA]
                    gdB = C.bank_guard[bB]
                    pe_t = C.pe.pre(1)
                    pwa = ((s_wa, t1), (s_wb, t3), bc_ready) \
                        if (mt == 0 and c == 0) else ()
                    def fhmm(t, mt=mt, c=c, bA=bA, bB=bB, gdA=gdA, gdB=gdB,
                             pwa=pwa, slotA=slotA, slotB=slotB):
                        for w_ in pwa:
                            wg(t, w_)
                        wg(t, gdA)
                        wg(t, gdB)
                        last = None
                        for k in range(8):
                            t.matmul(C.bank(bA),
                                     C.RW[:, slotA + k * 512 + mt * 128:
                                          slotA + k * 512 + mt * 128 + 128],
                                     C.RB[:, k * 2048 + c * 512:
                                          k * 2048 + c * 512 + 512],
                                     start=(k == 0), stop=(k == 7))
                        for k in range(8):
                            last = t.matmul(
                                C.bank(bB),
                                C.RW[:, slotB + k * 512 + mt * 128:
                                     slotB + k * 512 + mt * 128 + 128],
                                C.RB[:, k * 2048 + c * 512:
                                     k * 2048 + c * 512 + 512],
                                start=(k == 0), stop=(k == 7))
                        last.then_inc(C.pe.h, 1)
                    C.emit("tensor", fhmm)
                    it = git[0]
                    git[0] += 1
                    sslot = (it % 2) * 512
                    gprev = gate_hist.get(it - 2)
                    a_s = C.act.pre(1)
                    def fsilu(s, bA=bA, pe_t=pe_t, sslot=sslot, gprev=gprev):
                        s.wait_ge(C.pe.h, pe_t)
                        if gprev is not None:
                            s.wait_ge(C.ve.h, gprev)
                        s.activation(C.RX[:, sslot:sslot + 512], C.bank(bA),
                                     AF.Silu).then_inc(C.act.h, 1)
                    C.emit("scalar", fsilu)
                    g_cnt = C.ve.pre(1)
                    def fgate(v, bB=bB, a_s=a_s, mt=mt, c=c, hm=hm,
                              shared=shared, sslot=sslot):
                        v.wait_ge(C.act.h, a_s)
                        htile = hm * 4 + mt
                        dst = C.RC[:, htile * 2048 + c * 512:
                                   htile * 2048 + c * 512 + 512]
                        if shared:
                            v.tensor_tensor(dst, C.RX[:, sslot:sslot + 512],
                                            C.bank(bB), ALU.mult)\
                                .then_inc(C.ve.h, 1)
                        else:
                            v.tensor_tensor(C.RY[:, sslot:sslot + 512],
                                            C.RX[:, sslot:sslot + 512],
                                            C.bank(bB), ALU.mult)
                            v.tensor_tensor(dst, C.RY[:, sslot:sslot + 512],
                                            C.RY[:, 2048 + c * 512:
                                                 2560 + c * 512], ALU.mult)\
                                .then_inc(C.ve.h, 1)
                    C.emit("vector", fgate)
                    gate_hist[it] = g_cnt
                    C.guard_bank(bA, C.act, a_s)
                    C.guard_bank(bB, C.ve, C.ve.n)
            guardA = ((C.pe, C.pe.n),)
            guardB = ((C.pe, C.pe.n),)
        h_done = (C.ve, C.ve.n)
        for hw in range(2):
            t2 = load_wmat(C, C.RW, slotC, W[wkey2][:, hw * 512:(hw + 1) * 512],
                           8, 512, s_wc, waits=guardC)
            def w2_evac(m, c, b, pe_t, hw=hw, init=shared):
                cnt = C.ve.pre(1)
                def f(v, m=m, c=c, b=b, pe_t=pe_t, hw=hw, init=init):
                    v.wait_ge(C.pe.h, pe_t)
                    dtile = hw * 4 + m
                    dst = C.RA[:, dtile * 2048 + c * 512:
                               dtile * 2048 + c * 512 + 512]
                    if init:
                        v.tensor_copy(dst, C.bank(b)).then_inc(C.ve.h, 1)
                    else:
                        v.tensor_tensor(dst, dst, C.bank(b), ALU.add)\
                            .then_inc(C.ve.h, 1)
                C.emit("vector", f)
                return (C.ve, cnt)
            mm_chain(C,
                     lambda k, m: C.RW[:, slotC + k * 512 + m * 128:
                                       slotC + k * 512 + m * 128 + 128],
                     lambda k, c: C.RC[:, k * 2048 + c * 512:
                                       k * 2048 + c * 512 + 512],
                     w2_evac, m_tiles=4, k_list=range(8), n_chunks=4,
                     pre_waits=((s_wc, t2), h_done), banks=(4, 5, 6))
            guardC = ((C.pe, C.pe.n),)
    return (C.ve, C.ve.n)


def emit_l0f_finish(C, x2sp, moe_done):
    s_r = C.sem("x2rl")
    act0 = (C.act, C.act.n)
    tt_hist = {}
    g_done = None
    for k in range(ND):
        waits = [moe_done, act0] if k < 2 else [(C.ve, tt_hist[k - 2])]
        slot = (k % 2) * 2048
        t_l = e_dma(C, "sync", C.RX[:, slot:slot + 2048], x2sp[k], s_r,
                    waits=waits)
        v_t = C.ve.pre(1)
        def ftt(v, k=k, slot=slot, t_l=t_l, moe_done=moe_done):
            v.wait_ge(s_r.h, t_l)
            wg(v, moe_done)
            v.tensor_tensor(C.RY[:, (k % 2) * 2048:(k % 2) * 2048 + 2048],
                            C.RA[:, k * 2048:(k + 1) * 2048],
                            C.RX[:, slot:slot + 2048], ALU.add)\
                .then_inc(C.ve.h, 1)
        C.emit("vector", ftt)
        tt_hist[k] = v_t
        C.act.pre(1)
        def fsil(s, k=k, v_t=v_t):
            s.wait_ge(C.ve.h, v_t)
            s.activation(C.RB[:, k * 2048:(k + 1) * 2048],
                         C.RY[:, (k % 2) * 2048:(k % 2) * 2048 + 2048],
                         AF.Silu).then_inc(C.act.h, 1)
        C.emit("scalar", fsil)
    return (C.act, C.act.n)


def emit_gate(C, x_src, s_sp, xout, g_done):
    s_x = C.sem("gx")
    s_s = C.sem("gs")
    s_o = C.sem("gout")
    prev_tt = {}
    for k in range(ND):
        wx = [g_done] if k == 0 else [(C.ve, prev_tt[k - 1])]
        t_x = e_dma(C, "sync", C.RY[:, 0:2048], x_src[k], s_x, waits=wx)
        slot = 4096 + (k % 2) * 2048
        ws = [g_done] if k < 2 else [(C.ve, prev_tt[k - 2])]
        t_s = e_dma(C, "gpsimd", C.RX[:, slot:slot + 2048], s_sp[k], s_s,
                    waits=ws)
        vta = C.ve.pre(1)
        v_t = C.ve.pre(1)
        def ftt(v, k=k, slot=slot, t_x=t_x, t_s=t_s, vta=vta):
            v.wait_ge(s_x.h, t_x)
            v.wait_ge(s_s.h, t_s)
            v.tensor_tensor(C.RY[:, 2048:4096], C.RX[:, slot:slot + 2048],
                            C.RB[:, k * 2048:(k + 1) * 2048], ALU.mult)\
                .then_inc(C.ve.h, 1)
            v.wait_ge(C.ve.h, vta)
            v.tensor_tensor(C.RY[:, 0:2048], C.RY[:, 0:2048],
                            C.RY[:, 2048:4096], ALU.add).then_inc(C.ve.h, 1)
        C.emit("vector", ftt)
        prev_tt[k] = v_t
        e_dma(C, "sync", xout[k], C.RY[:, 0:2048], s_o, waits=[(C.ve, v_t)])
    def ffin(g, n=s_o.n):
        g.wait_ge(s_o.h, n)
        g.memset(C.RX[0:1, 0:1], 0.0)
    C.emit("gpsimd", ffin)
    return s_o


def stage_l0f(inputs, g):
    g["l0f_wq"] = _t(inputs["L0_ffn_Wq"])
    g["l0f_wkc"] = _t(inputs["L0_ffn_Wkc"])
    g["l0f_wvc"] = _t(inputs["L0_ffn_Wvc"])
    wo = _t(inputs["L0_ffn_Wo"])
    g["l0f_wo_h0"] = np.ascontiguousarray(wo[:, 0:512])
    g["l0f_wo_h1"] = np.ascontiguousarray(wo[:, 512:1024])
    wrt = _t(inputs["L0_ffn_Wr"])
    wr_hi = wrt.astype(ml_dtypes.bfloat16).astype(np.float32)
    g["l0f_wr"] = np.ascontiguousarray(
        np.concatenate([wr_hi, wrt - wr_hi], axis=1))
    g["l0f_sw1"] = _t(inputs["L0_ffn_sW1"])
    g["l0f_sw2"] = _t(inputs["L0_ffn_sW2"])
    g["l0f_sw3"] = _t(inputs["L0_ffn_sW3"])
    for e in range(8):
        g["l0f_ew1_%d" % e] = _t(inputs["L0_ffn_eW1"][e])
        g["l0f_ew2_%d" % e] = _t(inputs["L0_ffn_eW2"][e])
        g["l0f_ew3_%d" % e] = _t(inputs["L0_ffn_eW3"][e])
    cols = np.zeros((128, 17), np.float32)
    cols[:, 0:8] = np.asarray(inputs["L0_ffn_g1"], np.float32).reshape(8, 128).T
    cols[:, 8:16] = np.asarray(inputs["L0_ffn_g2"], np.float32).reshape(8, 128).T
    cols[0:8, 16] = np.asarray(inputs["L0_ffn_expert_bias"], np.float32)
    g["l0f_cols"] = np.ascontiguousarray(cols)
    return g


L0F_SHAPES = [("l0f_wq", [D, D]), ("l0f_wkc", [D, 256]), ("l0f_wvc", [D, 256]),
              ("l0f_wo_h0", [D, 512]), ("l0f_wo_h1", [D, 512]),
              ("l0f_wr", [D, 16]), ("l0f_sw1", [D, D]), ("l0f_sw2", [D, D]),
              ("l0f_sw3", [D, D])] + \
    [("l0f_ew%d_%d" % (j, e), [D, D]) for e in range(8) for j in (1, 2, 3)] + \
    [("l0f_cols", [128, 17])]


def build_l0(stop=None):
    nc = bass.Bass("TRN2", target_bir_lowering=False)
    xin = nc.dram_tensor("xin", [ND, 128, T], F32, kind="ExternalInput").ap()
    H = {}
    for nm, shp in [("l0_wdq", [D, 512]), ("l0_wdkv", [D, 512]),
                    ("l0_wupq", [512, D]), ("l0_wupk", [512, D]),
                    ("l0_wupv", [512, D]),
                    ("l0_wout_h0", [D, 512]), ("l0_wout_h1", [D, 512]),
                    ("l0_cols", [128, 16]),
                    ("ropeC", [128, T]), ("ropeS", [128, T]),
                    ("consts", [128, 384])] + L0F_SHAPES:
        H[nm] = nc.dram_tensor(nm, shp, F32, kind="ExternalInput").ap()
    xout = nc.dram_tensor("xout", [ND, 128, T], F32, kind="ExternalOutput").ap()
    dbg = nc.dram_tensor("dbg", [ND, 128, T], F32, kind="ExternalOutput").ap()
    dbg2 = nc.dram_tensor("dbg2", [ND, 128, T], F32, kind="ExternalOutput").ap()
    dbg3 = nc.dram_tensor("dbg3", [ND, 128, T], F32, kind="ExternalOutput").ap()

    es = ExitStack()
    with es:
        C = Ctx(nc, es)
        blk = es.enter_context(nc.Block())
        s_x = C.sem("x")
        s_k = C.sem("k")
        for k in range(ND):
            e_dma(C, "sync", C.RA[:, k * 2048:(k + 1) * 2048], xin[k], s_x)
        e_dma(C, "gpsimd", C.CONST[:], H["consts"], s_k)
        e_dma(C, "gpsimd", C.ROPE[:, 0:2048], H["ropeC"], s_k)
        e_dma(C, "gpsimd", C.ROPE[:, 2048:4096], H["ropeS"], s_k)
        e_dma(C, "gpsimd", C.RCOL[:, 0:16], H["l0_cols"], s_k)
        e_dma(C, "gpsimd", C.RCOL[:, 16:33], H["l0f_cols"], s_k)
        konst_ready = [(s_k, s_k.n), (s_x, s_x.n)]
        ln1 = emit_ln(C, lambda k: C.RA[:, k * 2048:(k + 1) * 2048],
                      lambda k: C.RB[:, k * 2048:(k + 1) * 2048],
                      pre_waits=tuple(konst_ready))
        ln2 = emit_ln(C, lambda k: C.RB[:, k * 2048:(k + 1) * 2048],
                      lambda k: C.RC[:, k * 2048:(k + 1) * 2048],
                      gain_col=lambda k: C.RCOL[:, k:k + 1],
                      bias_col=lambda k: C.RCOL[:, 8 + k:8 + k + 1],
                      pre_waits=(ln1,), sq_region=C.RAb[:, 0:4096],
                      in_bf16=True)
        s_done = emit_l0a(C, H)
        ssp = nc.dram_tensor("ssp", [ND, 128, T], BF16, kind="Internal").ap()
        s_sp = C.sem("ssp")
        for k in range(ND):
            e_dma(C, "sync", ssp[k], C.RAb[:, k * 2048:(k + 1) * 2048], s_sp,
                  waits=(s_done,) if k == 0 else ())
        spill_done = (s_sp, s_sp.n)
        if stop == "s":
            for k in range(ND):
                e_dma(C, "gpsimd", dbg[k], C.RAb[:, k * 2048:(k + 1) * 2048],
                      C.sem("dbgo"), waits=(s_done,))
            sd = C.dma_sems["dbgo"]
            def ffin(g, n=sd.n):
                g.wait_ge(sd.h, n)
            C.emit("gpsimd", ffin)
            C.finalize(blk)
            return nc
        r = emit_l0f(C, H, spill_done,
                     stop=stop if stop != "macc" else None)
        if stop == "nx2":
            sd = C.sem("dbgo")
            for k in range(ND):
                e_dma(C, "gpsimd", dbg[k], C.RC[:, k * 2048:(k + 1) * 2048],
                      sd, waits=(r,))
                e_dma(C, "gpsimd", dbg2[k], C.RB[:, k * 2048:(k + 1) * 2048],
                      sd)
            def ffin(g, n=sd.n):
                g.wait_ge(sd.h, n)
            C.emit("gpsimd", ffin)
            C.finalize(blk)
            return nc
        if stop in ("kcvc", "attn", "x2", "router"):
            sd = C.sem("dbgo")
            done = r if r is not None else (C.ve, C.ve.n)
            if stop == "x2":
                for k in range(ND):
                    e_dma(C, "gpsimd", dbg[k],
                          C.RC[:, k * 2048:(k + 1) * 2048], sd, waits=(done,))
            else:
                for k in range(ND):
                    if k == 1 and stop == "router":
                        e_dma(C, "gpsimd", dbg[k], C.RX[:, 6144:8192], sd,
                              waits=(done,))
                        continue
                    if k == 2 and stop == "router":
                        e_dma(C, "gpsimd", dbg[k], C.RY[:, 0:2048], sd,
                              waits=(done,))
                        continue
                    if k == 3 and stop == "router":
                        e_dma(C, "gpsimd", dbg[k], C.RM[:, 0:2048], sd,
                              waits=(done,))
                        continue
                    e_dma(C, "gpsimd", dbg[k],
                          C.RAb[:, k * 2048:(k + 1) * 2048], sd, waits=(done,))
            def ffin(g, n=sd.n):
                g.wait_ge(sd.h, n)
            C.emit("gpsimd", ffin)
            C.finalize(blk)
            return nc
        moe_done = r
        if stop == "macc":
            sd = C.sem("dbgo")
            for k in range(ND):
                e_dma(C, "gpsimd", dbg[k], C.RA[:, k * 2048:(k + 1) * 2048],
                      sd, waits=(moe_done,))
            for k in range(ND):
                e_dma(C, "gpsimd", dbg2[k], C.RC[:, k * 2048:(k + 1) * 2048],
                      sd, waits=(moe_done,))
                e_dma(C, "gpsimd", dbg3[k], C.RB[:, k * 2048:(k + 1) * 2048],
                      sd, waits=(moe_done,))
            def ffin(g, n=sd.n):
                g.wait_ge(sd.h, n)
            C.emit("gpsimd", ffin)
            C.finalize(blk)
            return nc
        x2sp = C.x2sp
        g_done = emit_l0f_finish(C, x2sp, moe_done)
        if stop == "g":
            sd = C.sem("dbgo")
            for k in range(ND):
                e_dma(C, "gpsimd", dbg[k], C.RB[:, k * 2048:(k + 1) * 2048],
                      sd, waits=(g_done,))
            def ffin(g, n=sd.n):
                g.wait_ge(sd.h, n)
            C.emit("gpsimd", ffin)
            C.finalize(blk)
            return nc
        emit_gate(C, xin, ssp, xout, g_done)
        C.finalize(blk)
    return nc



def causal_pmask(C, qbs=QB):
    def pmask(h, qb, kt, pcols, p_ready):
        if (kt + 1) * 128 <= qb * qbs:
            return p_ready
        cnt = C.po.pre(1)
        def f(g, kt=kt, qb=qb, pcols=pcols, p_ready=p_ready):
            wg(g, p_ready)
            g.affine_select(C.RX[:, pcols:pcols + qbs],
                            C.RX[:, pcols:pcols + qbs],
                            [[1, qbs]], ALU.is_ge, 0.0,
                            base=qb * qbs - kt * 128, channel_multiplier=-1)\
                .then_inc(C.po.h, 1)
        C.emit("gpsimd", f)
        return (C.po, cnt)
    return pmask



def stage_l1(inputs, g):
    g["l1_w1"] = _t(inputs["L1_ffn_w1_w"])
    g["l1_w2"] = _t(inputs["L1_ffn_w2_w"])
    g["l1_h1"] = _t(inputs["L1_ffn_h1_w"])
    g["l1_h2"] = _t(inputs["L1_ffn_h2_w"])
    g["l1_w3"] = _t(inputs["L1_ffn_w3_w"])
    c2 = np.zeros((128, 128), np.float32)
    c2[:, 0:32] = np.asarray(inputs["L1_ffn_w1_s"], np.float32).reshape(32, 128).T
    c2[:, 32:64] = np.asarray(inputs["L1_ffn_w2_s"], np.float32).reshape(32, 128).T
    c2[:, 64:96] = np.asarray(inputs["L1_ffn_h1_s"], np.float32).reshape(32, 128).T
    c2[:, 96:128] = np.asarray(inputs["L1_ffn_h2_s"], np.float32).reshape(32, 128).T
    g["l1_cols2"] = np.ascontiguousarray(c2)
    c3 = np.zeros((128, 16), np.float32)
    c3[:, 0:8] = np.asarray(inputs["L1_ffn_w3_s"], np.float32).reshape(8, 128).T
    g["l1_cols3"] = np.ascontiguousarray(c3)
    g["l1_wcd"] = np.ascontiguousarray(np.asarray(inputs["L1_attn_Wc_down"],
                                                  np.float32))
    wkv = np.asarray(inputs["L1_attn_Wkv_up"], np.float32)
    g["l1_wkvk"] = np.ascontiguousarray(
        np.concatenate([wkv[:, 0:64], wkv[:, 0:64]], axis=1))
    g["l1_wkvv"] = np.ascontiguousarray(wkv[:, 64:128])
    g["l1_wqu"] = np.ascontiguousarray(np.asarray(inputs["L1_attn_Wq_up"],
                                                  np.float32))
    wp = np.ascontiguousarray(np.asarray(inputs["L1_attn_Wc_proj"], np.float32))
    g["l1_wcp_h0"] = np.ascontiguousarray(wp[:, 0:512])
    g["l1_wcp_h1"] = np.ascontiguousarray(wp[:, 512:1024])
    return g


L1_SHAPES = [("l1_w1", [D, 4096]), ("l1_w2", [D, 4096]),
             ("l1_h1", [4096, 4096]), ("l1_h2", [4096, 4096]),
             ("l1_w3", [4096, D]), ("l1_cols2", [128, 128]),
             ("l1_cols3", [128, 16]),
             ("l1_wcd", [D, 512]), ("l1_wkvk", [512, 128]),
             ("l1_wkvv", [512, 64]), ("l1_wqu", [512, D]),
             ("l1_wcp_h0", [D, 512]), ("l1_wcp_h1", [D, 512])]


def emit_l1(C, W, gsp, stop=None):
    RAb = C.RAb
    s_wa = C.sem("1wa")
    s_wb = C.sem("1wb")
    s_gsp = C.sem("1gsp")
    xnsp = C.nc.dram_tensor("xnsp1", [ND, 128, T], BF16, kind="Internal").ap()
    s_xn = C.sem("1xn")
    xn0 = (C.ve, C.ve.n)
    for k in range(ND):
        e_dma(C, "sync", xnsp[k], C.RB[:, k * 2048:(k + 1) * 2048], s_xn,
              waits=(xn0,) if k == 0 else ())
    guardA = guardB = ()
    gate_hist = {}
    git = [0]
    for tc in range(2):
        for p8 in range(8):
            t1 = load_wmat(C, C.RW, 0, W["l1_w1"][:, p8 * 512:(p8 + 1) * 512],
                           8, 512, s_wa, waits=guardA)
            t2 = load_wmat(C, C.RW, 4096,
                           W["l1_w2"][:, p8 * 512:(p8 + 1) * 512],
                           8, 512, s_wb, waits=guardB)
            for mt in range(4):
                for c2 in range(2):
                    bA = git[0] % 2
                    bB = 2 + git[0] % 2
                    it = git[0]
                    git[0] += 1
                    gdA = C.bank_guard[bA]
                    gdB = C.bank_guard[bB]
                    pe_t = C.pe.pre(1)
                    pwa = ((s_wa, t1), (s_wb, t2)) if (mt == 0 and c2 == 0) \
                        else ()
                    def fmm(t, mt=mt, c2=c2, tc=tc, bA=bA, bB=bB, gdA=gdA,
                            gdB=gdB, pwa=pwa):
                        for w_ in pwa:
                            wg(t, w_)
                        wg(t, gdA)
                        wg(t, gdB)
                        cl = tc * 1024 + c2 * 512
                        last = None
                        for k in range(8):
                            t.matmul(C.bank(bA),
                                     C.RW[:, k * 512 + mt * 128:
                                          k * 512 + mt * 128 + 128],
                                     C.RB[:, k * 2048 + cl: k * 2048 + cl + 512],
                                     start=(k == 0), stop=(k == 7))
                        for k in range(8):
                            last = t.matmul(
                                C.bank(bB),
                                C.RW[:, 4096 + k * 512 + mt * 128:
                                     4096 + k * 512 + mt * 128 + 128],
                                C.RB[:, k * 2048 + cl: k * 2048 + cl + 512],
                                start=(k == 0), stop=(k == 7))
                        last.then_inc(C.pe.h, 1)
                    C.emit("tensor", fmm)
                    sslot = (it % 2) * 512
                    gprev = gate_hist.get(it - 2)
                    a_s = C.act.pre(1)
                    htile = p8 * 4 + mt
                    def fsilu(s, bA=bA, pe_t=pe_t, sslot=sslot, gprev=gprev,
                              htile=htile):
                        s.wait_ge(C.pe.h, pe_t)
                        if gprev is not None:
                            s.wait_ge(C.ve.h, gprev)
                        s.activation(C.RX[:, sslot:sslot + 512], C.bank(bA),
                                     AF.Silu, scale=C.RCOL2[:, htile:htile + 1])\
                            .then_inc(C.act.h, 1)
                    C.emit("scalar", fsilu)
                    g_cnt = C.ve.pre(1)
                    def fgate(v, bB=bB, a_s=a_s, htile=htile, c2=c2,
                              sslot=sslot):
                        v.wait_ge(C.act.h, a_s)
                        dst = RAb[:, htile * 1024 + c2 * 512:
                                  htile * 1024 + c2 * 512 + 512]
                        v.scalar_tensor_tensor(dst, C.bank(bB),
                                               C.RCOL2[:, 32 + htile:33 + htile],
                                               C.RX[:, sslot:sslot + 512],
                                               ALU.mult, ALU.mult)\
                            .then_inc(C.ve.h, 1)
                    C.emit("vector", fgate)
                    gate_hist[it] = g_cnt
                    C.guard_bank(bA, C.act, a_s)
                    C.guard_bank(bB, C.ve, C.ve.n)
            guardA = ((C.pe, C.pe.n),)
            guardB = ((C.pe, C.pe.n),)
        h_done = (C.ve, C.ve.n)
        def big_stage(wkey, scol_base, src_tile, dst_tile, act_fn, sem,
                      pre, guard0):
            guard = guard0
            out = []
            for sl in range(16):
                tl = load_wmat(C, C.RW, 0, W[wkey][:, sl * 256:(sl + 1) * 256],
                               32, 256, sem, waits=guard)
                for mt in range(2):
                    for c2 in range(2):
                        b = (sl * 4 + mt * 2 + c2) % 4
                        gd = C.bank_guard[b]
                        pe_t = C.pe.pre(1)
                        pwa = ((sem, tl),) + (tuple(pre) if sl == 0 and
                                              mt == 0 and c2 == 0 else ())
                        def fmm(t, sl=sl, mt=mt, c2=c2, b=b, gd=gd, pwa=pwa):
                            for w_ in pwa:
                                wg(t, w_)
                            wg(t, gd)
                            last = None
                            for k in range(32):
                                last = t.matmul(
                                    C.bank(b),
                                    C.RW[:, k * 256 + mt * 128:
                                         k * 256 + mt * 128 + 128],
                                    src_tile(k)[:, c2 * 512: c2 * 512 + 512],
                                    start=(k == 0), stop=(k == 31))
                            last.then_inc(C.pe.h, 1)
                        C.emit("tensor", fmm)
                        otile = sl * 2 + mt
                        a_t = C.act.pre(1)
                        def fev(s, b=b, pe_t=pe_t, otile=otile, c2=c2,
                                act_fn=act_fn, scol_base=scol_base):
                            s.wait_ge(C.pe.h, pe_t)
                            s.activation(dst_tile(otile)[:, c2 * 512:
                                                         c2 * 512 + 512],
                                         C.bank(b), act_fn,
                                         scale=C.RCOL2[:, scol_base + otile:
                                                       scol_base + otile + 1])\
                                .then_inc(C.act.h, 1)
                        C.emit("scalar", fev)
                        C.guard_bank(b, C.act, a_t)
                guard = ((C.pe, C.pe.n),)
            return (C.act, C.act.n)
        def h_src(k):
            return RAb[:, k * 1024:(k + 1) * 1024]
        def h2_dst(i):
            if i < 16:
                return C.RB[:, i * 1024:(i + 1) * 1024]
            return C.RC[:, (i - 16) * 1024:(i - 15) * 1024]
        pre2 = [h_done, (s_xn, s_xn.n)]
        d2 = big_stage("l1_h1", 64, h_src, h2_dst, AF.Gelu, s_wa, pre2,
                       ((C.pe, C.pe.n),))
        def h2_src(k):
            return h2_dst(k)
        def h3_dst(i):
            return RAb[:, i * 1024:(i + 1) * 1024]
        d3 = big_stage("l1_h2", 96, h2_src, h3_dst, AF.Gelu, s_wb, [d2],
                       ((C.pe, C.pe.n),))
        guard4 = ((C.pe, C.pe.n),)
        for sl in range(4):
            tl = load_wmat(C, C.RW, 0, W["l1_w3"][:, sl * 256:(sl + 1) * 256],
                           32, 256, s_wa, waits=guard4)
            for mt in range(2):
                for c2 in range(2):
                    b = (sl * 4 + mt * 2 + c2) % 4
                    gd = C.bank_guard[b]
                    pe_t = C.pe.pre(1)
                    pwa = ((s_wa, tl),) + ((d3,) if sl == 0 and mt == 0 and
                                           c2 == 0 else ())
                    def fmm(t, sl=sl, mt=mt, c2=c2, b=b, gd=gd, pwa=pwa):
                        for w_ in pwa:
                            wg(t, w_)
                        wg(t, gd)
                        last = None
                        for k in range(32):
                            last = t.matmul(
                                C.bank(b),
                                C.RW[:, k * 256 + mt * 128:
                                     k * 256 + mt * 128 + 128],
                                RAb[:, k * 1024 + c2 * 512:
                                    k * 1024 + c2 * 512 + 512],
                                start=(k == 0), stop=(k == 31))
                        last.then_inc(C.pe.h, 1)
                    C.emit("tensor", fmm)
                    otile = sl * 2 + mt
                    a_t = C.act.pre(1)
                    def fev(s, b=b, pe_t=pe_t, otile=otile, c2=c2):
                        s.wait_ge(C.pe.h, pe_t)
                        s.activation(C.RX[:, otile * 1024 + c2 * 512:
                                          otile * 1024 + c2 * 512 + 512],
                                     C.bank(b), AF.Silu,
                                     scale=C.RCOL2b[:, otile:otile + 1])\
                            .then_inc(C.act.h, 1)
                    C.emit("scalar", fev)
                    C.guard_bank(b, C.act, a_t)
            guard4 = ((C.pe, C.pe.n),)
        g_done = (C.act, C.act.n)
        for dtile in range(8):
            e_dma(C, "sync", gsp[dtile][:, tc * 1024:(tc + 1) * 1024],
                  C.RX[:, dtile * 1024:(dtile + 1) * 1024], s_gsp,
                  waits=(g_done,) if dtile == 0 else ())
        if tc == 0:
            s_x2 = C.sem("1xr")
            pe_now = (C.pe, C.pe.n)
            for k in range(ND):
                e_dma(C, "gpsimd", C.RB[:, k * 2048:(k + 1) * 2048], xnsp[k],
                      s_x2, waits=(pe_now,) if k == 0 else ())
            guardA = guardA + ((s_x2, s_x2.n),)
            guardB = guardB + ((s_gsp, s_gsp.n),)
    if stop == "ffn":
        return (s_gsp, s_gsp.n)
    s_x3 = C.sem("1xr2")
    pe_now = (C.pe, C.pe.n)
    ve_now = (C.ve, C.ve.n)
    for k in range(ND):
        e_dma(C, "gpsimd", C.RB[:, k * 2048:(k + 1) * 2048], xnsp[k], s_x3,
              waits=(pe_now, (s_gsp, s_gsp.n)) if k == 0 else ())
    xn_rl = (s_x3, s_x3.n)
    t_cd = load_wmat(C, C.RW, 0, W["l1_wcd"], 8, 512, s_wa,
                     waits=((C.pe, C.pe.n),))
    mm_chain(C,
             lambda k, m: C.RW[:, k * 512 + m * 128: k * 512 + m * 128 + 128],
             lambda k, c: C.RB[:, k * 2048 + c * 512: k * 2048 + c * 512 + 512],
             evac_copy(C, lambda m, c:
                       RAb[:, m * 2048 + c * 512: m * 2048 + c * 512 + 512]),
             m_tiles=4, k_list=range(8), n_chunks=4,
             pre_waits=((s_wa, t_cd), xn_rl), banks=(0, 1, 2, 3))
    lat_pe = (C.pe, C.pe.n)
    lat = lambda k: RAb[:, k * 2048:(k + 1) * 2048]
    t_kd = load_wmat(C, C.RW, 4096, W["l1_wkvk"], 4, 128, s_wb,
                     waits=(lat_pe,))
    t_vd = load_wmat(C, C.RW, 4608, W["l1_wkvv"], 4, 64, s_wb)
    mm_chain(C,
             lambda k, m: C.RW[:, 4096 + k * 128: 4096 + k * 128 + 128],
             lambda k, c: lat(k)[:, c * 512: c * 512 + 512],
             evac_copy(C, lambda m, c:
                       RAb[:, 8192 + c * 512: 8192 + c * 512 + 512]),
             m_tiles=1, k_list=range(4), n_chunks=4,
             pre_waits=((s_wb, t_kd),), banks=(4, 5))
    po_t = C.po.pre(1)
    ve_b4 = (C.ve, C.ve.n)
    def fones(g, ve_b4=ve_b4):
        wg(g, ve_b4)
        g.memset(RAb[:, 10240:11280], 1.0).then_inc(C.po.h, 1)
    C.emit("gpsimd", fones)
    def v_evac(tt, c, b, pe_t):
        cnt = C.ve.pre(1)
        def f(v, tt=tt, b=b, pe_t=pe_t, po_t=po_t):
            v.wait_ge(C.pe.h, pe_t)
            v.wait_ge(C.po.h, po_t)
            v.tensor_copy(RAb[:, 10240 + tt * 65: 10240 + tt * 65 + 64],
                          C.bank(b, w=64)).then_inc(C.ve.h, 1)
        C.emit("vector", f)
        return (C.ve, cnt)
    mm_chain(C,
             lambda k, m: lat(k)[:, m * 128: m * 128 + 128],
             lambda k, c: C.RW[:, 4608 + k * 64: 4608 + k * 64 + 64],
             v_evac, m_tiles=16, k_list=range(4), n_chunks=1,
             pre_waits=((s_wb, t_vd),), banks=(0, 1, 2, 3))
    t_qu = load_wmat(C, C.RW, 6144, W["l1_wqu"], 4, 1024, s_wa)

    def qT_fn(h):
        b = 64 * (h % 2)
        return RAb[b:b + 64, 11280:11280 + 2048]

    def kT_fn(h):
        b = 64 * (h % 2)
        return RAb[b:b + 64, 8192:8192 + 2048]

    def v_lhsT(h, kt):
        return RAb[:, 10240 + kt * 65: 10240 + kt * 65 + 65]

    def o_dst(h, qb):
        b = 64 * (h % 2)
        return C.RC[b:b + 64,
                    (h // 2) * 2048 + qb * QB: (h // 2) * 2048 + qb * QB + QB]

    kt_hi = lambda qb: (qb + 1) * (QB // 128)
    pmask = causal_pmask(C)
    for grp in range(8):
        gh = grp * 2
        mm_chain(C,
                 lambda k, m, gh=gh: C.RW[:, 6144 + k * 1024 + (gh // 2) * 128:
                                          6144 + k * 1024 + (gh // 2) * 128 + 128],
                 lambda k, c: lat(k)[:, c * 512: c * 512 + 512],
                 evac_copy(C, lambda m, c:
                           RAb[:, 11280 + c * 512: 11280 + c * 512 + 512],
                           scale=0.125),
                 m_tiles=1, k_list=range(4), n_chunks=4,
                 pre_waits=((s_wa, t_qu),), banks=(4, 5))
        qdone = (C.act, C.act.n)
        C.guard_bank(0, *qdone)
        C.guard_bank(1, *qdone)
        emit_attn64(C, [gh, gh + 1], qT_fn, kT_fn, v_lhsT, o_dst,
                    shift="passA", kt_hi_fn=kt_hi, pmask_fn=pmask)
    attn_pe = (C.pe, C.pe.n)
    attn_ve = (C.ve, C.ve.n)
    for halfw in range(2):
        t_wo = load_wmat(C, C.RW, halfw * 4096, W["l1_wcp_h%d" % halfw],
                         8, 512, s_wb if halfw == 0 else s_wa,
                         waits=(attn_pe,))
        mm_chain(C,
                 lambda k, m, halfw=halfw:
                 C.RW[:, halfw * 4096 + k * 512 + m * 128:
                      halfw * 4096 + k * 512 + m * 128 + 128],
                 lambda k, c: C.RC[:, k * 2048 + c * 512:
                                   k * 2048 + c * 512 + 512],
                 evac_copy(C, lambda m, c, halfw=halfw:
                           RAb[:, 16384 + (halfw * 4 + m) * 2048 + c * 512:
                               16384 + (halfw * 4 + m) * 2048 + c * 512 + 512],
                           eng="vector"),
                 m_tiles=4, k_list=range(8), n_chunks=4,
                 pre_waits=((s_wb if halfw == 0 else s_wa, t_wo), attn_ve),
                 banks=(0, 1, 2, 3))
    return (C.ve, C.ve.n)


def build_l1(stop=None):
    nc = bass.Bass("TRN2", target_bir_lowering=False)
    xin = nc.dram_tensor("xin", [ND, 128, T], F32, kind="ExternalInput").ap()
    H = {}
    for nm, shp in [("consts", [128, 384])] + L1_SHAPES:
        H[nm] = nc.dram_tensor(nm, shp, F32, kind="ExternalInput").ap()
    xout = nc.dram_tensor("xout", [ND, 128, T], F32, kind="ExternalOutput").ap()
    dbg = nc.dram_tensor("dbg", [ND, 128, T], F32, kind="ExternalOutput").ap()
    es = ExitStack()
    with es:
        C = Ctx(nc, es)
        blk = es.enter_context(nc.Block())
        s_x = C.sem("x")
        s_k = C.sem("k")
        for k in range(ND):
            e_dma(C, "sync", C.RA[:, k * 2048:(k + 1) * 2048], xin[k], s_x)
        e_dma(C, "gpsimd", C.CONST[:], H["consts"], s_k)
        e_dma(C, "gpsimd", C.RCOL2[:], H["l1_cols2"], s_k)
        e_dma(C, "gpsimd", C.RCOL2b[:], H["l1_cols3"], s_k)
        konst_ready = [(s_k, s_k.n), (s_x, s_x.n)]
        ln1 = emit_ln(C, lambda k: C.RA[:, k * 2048:(k + 1) * 2048],
                      lambda k: C.RB[:, k * 2048:(k + 1) * 2048],
                      pre_waits=tuple(konst_ready))
        gsp = nc.dram_tensor("gsp", [ND, 128, T], BF16, kind="Internal").ap()
        r = emit_l1(C, H, gsp, stop=stop)
        if stop == "ffn":
            sd = C.sem("dbgo")
            s_gr = C.sem("grl")
            for k in range(ND):
                e_dma(C, "gpsimd", C.RB[:, k * 2048:(k + 1) * 2048], gsp[k],
                      s_gr, waits=(r,) if k == 0 else ())
            for k in range(ND):
                e_dma(C, "gpsimd", dbg[k], C.RB[:, k * 2048:(k + 1) * 2048],
                      sd, waits=((s_gr, s_gr.n),))
            def ffin(g, n=sd.n):
                g.wait_ge(sd.h, n)
            C.emit("gpsimd", ffin)
            C.finalize(blk)
            return nc
        s_done = r
        s_xg = C.sem("gx")
        s_g = C.sem("gg")
        s_o = C.sem("gout")
        prev_tt = {}
        for k in range(ND):
            wx = [s_done] if k == 0 else [(C.ve, prev_tt[k - 1])]
            t_x = e_dma(C, "sync", C.RY[:, 0:2048], xin[k], s_xg, waits=wx)
            slot = (k % 2) * 2048
            ws = [s_done] if k < 2 else [(C.ve, prev_tt[k - 2])]
            t_g = e_dma(C, "gpsimd", C.RX[:, slot:slot + 2048], gsp[k], s_g,
                        waits=ws)
            vta = C.ve.pre(1)
            v_t = C.ve.pre(1)
            def ftt(v, k=k, slot=slot, t_x=t_x, t_g=t_g, vta=vta):
                v.wait_ge(s_xg.h, t_x)
                v.wait_ge(s_g.h, t_g)
                v.tensor_tensor(C.RY[:, 2048:4096],
                                C.RAb[:, 16384 + k * 2048:
                                      16384 + (k + 1) * 2048],
                                C.RX[:, slot:slot + 2048], ALU.mult)\
                    .then_inc(C.ve.h, 1)
                v.wait_ge(C.ve.h, vta)
                v.tensor_tensor(C.RY[:, 0:2048], C.RY[:, 0:2048],
                                C.RY[:, 2048:4096], ALU.add)\
                    .then_inc(C.ve.h, 1)
            C.emit("vector", ftt)
            prev_tt[k] = v_t
            e_dma(C, "sync", xout[k], C.RY[:, 0:2048], s_o,
                  waits=[(C.ve, v_t)])
        def ffin(g, n=s_o.n):
            g.wait_ge(s_o.h, n)
            g.memset(C.RX[0:1, 0:1], 0.0)
        C.emit("gpsimd", ffin)
        C.finalize(blk)
    return nc



def stage_l2(inputs, g):
    Wx = np.asarray(inputs["L2_attn_Wx"], np.float32)
    g["l2_w2x"] = np.ascontiguousarray(
        np.concatenate([Wx[h] for h in range(16)], axis=1))
    bx = np.asarray(inputs["L2_attn_bx"], np.float32).reshape(-1)
    g["l2_bxrow"] = np.ascontiguousarray(bx.reshape(1, 1024))
    wp = np.ascontiguousarray(np.asarray(inputs["L2_attn_Wp"], np.float32))
    g["l2_wp_h0"] = np.ascontiguousarray(wp[:, 0:512])
    g["l2_wp_h1"] = np.ascontiguousarray(wp[:, 512:1024])
    for j in range(6):
        g["l2_qw_%d" % j] = np.ascontiguousarray(
            np.asarray(inputs["L2_ffn_qweights"][j], np.int32).T)
        g["l2_sc_%d" % j] = np.ascontiguousarray(
            np.repeat(np.asarray(inputs["L2_ffn_scales"][j], np.float32),
                      16, axis=1).T)
        g["l2_la_%d" % j] = _t(inputs["L2_ffn_lora_a"][j])
        g["l2_lb_%d" % j] = _t(inputs["L2_ffn_lora_b"][j])
    c2 = np.zeros((128, 128), np.float32)
    for j in range(6):
        c2[:, j * 8:(j + 1) * 8] = np.asarray(
            inputs["L2_ffn_biases"][j], np.float32).reshape(8, 128).T
    for b in range(3):
        c2[:, 48 + b * 8:56 + b * 8] = np.asarray(
            inputs["L2_ffn_ln_gamma"][b], np.float32).reshape(8, 128).T
        c2[:, 72 + b * 8:80 + b * 8] = np.asarray(
            inputs["L2_ffn_ln_beta"][b], np.float32).reshape(8, 128).T
    c2[:, 96:104] = bx.reshape(8, 128).T
    c2[:, 104:112] = np.asarray(inputs["L2_attn_bp"], np.float32)\
        .reshape(8, 128).T
    g["l2_cols"] = np.ascontiguousarray(c2)
    return g


L2_SHAPES = [("l2_w2x", [D, D]), ("l2_bxrow", [1, D]),
             ("l2_wp_h0", [D, 512]), ("l2_wp_h1", [D, 512]),
             ("l2_cols", [128, 128])] + \
    [("l2_qw_%d" % j, [D, D]) for j in range(6)] + \
    [("l2_sc_%d" % j, [D, D]) for j in range(6)] + \
    [("l2_la_%d" % j, [D, 32]) for j in range(6)] + \
    [("l2_lb_%d" % j, [32, D]) for j in range(6)]


def evac_ts_bias(C, dst_fn, bias_fn):
    def evac(m, c, b, pe_t):
        cnt = C.ve.pre(1)
        def f(v, m=m, c=c, b=b, pe_t=pe_t):
            v.wait_ge(C.pe.h, pe_t)
            dst = dst_fn(m, c)
            v.tensor_scalar(dst, C.bank(b, w=dst.shape[-1],
                                        rows=dst.shape[0]),
                            bias_fn(m), None, ALU.add).then_inc(C.ve.h, 1)
        C.emit("vector", f)
        return (C.ve, cnt)
    return evac


def emit_l2(C, W, stop=None):
    RAb = C.RAb
    s_wa = C.sem("2wa")
    s_wb = C.sem("2wb")
    pe0 = (C.pe, C.pe.n)
    t_wx = load_wmat(C, C.RW, 0, W["l2_w2x"], 8, 1024, s_wa, waits=(pe0,))
    t_bx = e_dma(C, "gpsimd", C.RX[0:1, 6656:7680], W["l2_bxrow"], s_wa)
    mm_chain(C,
             lambda k, m: C.RW[:, k * 1024 + m * 128: k * 1024 + m * 128 + 128],
             lambda k, c: C.RB[:, k * 2048 + c * 512: k * 2048 + c * 512 + 512],
             evac_ts_bias(C, lambda m, c:
                          RAb[:, m * 2048 + c * 512: m * 2048 + c * 512 + 512],
                          lambda m: C.RCOL2[:, 96 + m:97 + m]),
             m_tiles=8, k_list=range(8), n_chunks=4,
             pre_waits=((s_wa, t_wx),), banks=(0, 1, 2, 3))
    po_t = C.po.pre(1)
    ve_now = (C.ve, C.ve.n)
    def fones(g, ve_now=ve_now):
        wg(g, ve_now)
        g.memset(RAb[:, 16384:31984], 1.0)
        g.memset(C.RW[:, 8192:9232].bitcast(BF16)[:, 0:1040] if False else
                 C.RW[:, 8192:9232], 1.0).then_inc(C.po.h, 1)
    C.emit("gpsimd", fones)
    def htm_slice(tt):
        if tt < 15:
            return RAb[:, 16384 + tt * 1040: 16384 + (tt + 1) * 1040]
        return C.RW[:, 8192:9232]
    def htm_evac(tt, c, b, pe_t):
        cnt = C.ve.pre(1)
        def f(v, tt=tt, c=c, b=b, pe_t=pe_t, po_t=po_t):
            v.wait_ge(C.pe.h, pe_t)
            v.wait_ge(C.po.h, po_t)
            dst = htm_slice(tt).rearrange("p (h e) -> p h e", h=16)[:, c * 8:
                                                                   c * 8 + 8,
                                                                   0:64]
            v.tensor_copy(dst, C.bank(b).rearrange("p (h e) -> p h e", h=8))\
                .then_inc(C.ve.h, 1)
        C.emit("vector", f)
        return (C.ve, cnt)
    def htm_lhsT(k, m):
        return C.RB[:, k * 2048 + m * 128: k * 2048 + m * 128 + 128]
    for tt in range(16):
        for c in range(2):
            b = (tt * 2 + c) % 4
            gd = C.bank_guard[b]
            pe_t = C.pe.pre(1)
            def fmm(t, tt=tt, c=c, b=b, gd=gd):
                wg(t, gd)
                for k in range(8):
                    t.matmul(C.bank(b), htm_lhsT(k, tt),
                             C.RW[:, k * 1024 + c * 512: k * 1024 + c * 512 + 512],
                             start=(k == 0), stop=False)
                t.matmul(C.bank(b), C.ones_bf[0:1, 0:128],
                         C.RX[0:1, 6656 + c * 512: 6656 + c * 512 + 512],
                         start=False, stop=True).then_inc(C.pe.h, 1)
            C.emit("tensor", fmm)
            C.guard_bank(b, *htm_evac(tt, c, b, pe_t))
    htm_done = (C.ve, C.ve.n)
    if stop == "h":
        return htm_done
    a0 = C.act.pre(0)
    for tt in range(16):
        for h in range(16):
            C.act.pre(1)
            def fsq(s, tt=tt, h=h, htm_done=htm_done):
                if tt == 0 and h == 0:
                    wg(s, htm_done)
                sl = htm_slice(tt)
                s.activation(C.RY[:, 0:64],
                             sl[:, h * 65: h * 65 + 64], AF.Square,
                             accum_out=C.RM[:, 1024 + tt * 16 + h:
                                            1025 + tt * 16 + h])\
                    .then_inc(C.act.h, 1)
            C.emit("scalar", fsq)
    a_n = C.act.pre(1)
    def frn(s):
        s.activation(C.RM[:, 1280:1536], C.RM[:, 1024:1280], AF.Ln)
        s.activation(C.RM[:, 1280:1536], C.RM[:, 1280:1536], AF.Exp,
                     scale=-0.5).then_inc(C.act.h, 1)
    C.emit("scalar", frn)
    v_bf = C.ve.pre(1)
    def fbf(v, a_n=a_n):
        v.wait_ge(C.act.h, a_n)
        v.tensor_copy(C.RX[:, 6144:6400], C.RM[:, 1280:1536])\
            .then_inc(C.ve.h, 1)
    C.emit("vector", fbf)
    for h in range(16):
        gds = C.snap_guards([0, 1, 2, 3])
        pe_t = C.pe.pre(1)
        def ftr(t, h=h, gds=gds, v_bf=v_bf):
            t.wait_ge(C.ve.h, v_bf)
            last = None
            for bk in range(4):
                wg(t, gds[bk])
                for j in range(4):
                    tt = bk * 4 + j
                    last = t.matmul(
                        C.ps[0:1, bk * 512 + j * 128: bk * 512 + j * 128 + 128],
                        C.RX[:, 6144 + tt * 16 + h: 6145 + tt * 16 + h],
                        C.ident, start=True, stop=True)
            last.then_inc(C.pe.h, 1)
        C.emit("tensor", ftr)
        a_r = C.act.pre(1)
        def frow(s, pe_t=pe_t):
            s.wait_ge(C.pe.h, pe_t)
            s.activation(C.RX[0:1, 4096:4608], C.ps[0:1, 0:512], AF.Copy)
            s.activation(C.RX[0:1, 4608:5120], C.ps[0:1, 512:1024], AF.Copy)
            s.activation(C.RX[0:1, 5120:5632], C.ps[0:1, 1024:1536], AF.Copy)
            s.activation(C.RX[0:1, 5632:6144], C.ps[0:1, 1536:2048], AF.Copy)\
                .then_inc(C.act.h, 1)
        C.emit("scalar", frow)
        for bk in range(4):
            C.guard_bank(bk, C.act, a_r)
        base = 64 * (h % 2)
        for c in range(4):
            gd = C.snap_guards([4 + c % 2])
            pe_b = C.pe.pre(1)
            def fbc(t, h=h, c=c, gd=gd, a_r=a_r, base=base):
                t.wait_ge(C.act.h, a_r)
                wg(t, gd[0])
                bnk = 4 + c % 2
                t.matmul(C.ps[base:base + 64, bnk * 512:(bnk + 1) * 512],
                         C.ones_bf[0:1, 0:64],
                         C.RX[0:1, 4096 + c * 512: 4608 + c * 512],
                         start=True, stop=True).then_inc(C.pe.h, 1)
            C.emit("tensor", fbc)
            C.ve.pre(1)
            def fnm(v, h=h, c=c, pe_b=pe_b, base=base):
                v.wait_ge(C.pe.h, pe_b)
                bnk = 4 + c % 2
                sl = RAb[base:base + 64,
                         (h // 2) * 2048 + c * 512: (h // 2) * 2048 + c * 512 + 512]
                v.tensor_tensor(sl, sl,
                                C.ps[base:base + 64, bnk * 512:(bnk + 1) * 512],
                                ALU.mult).then_inc(C.ve.h, 1)
            C.emit("vector", fnm)
            C.guard_bank(4 + c % 2, C.ve, C.ve.n)
    norm_done = (C.ve, C.ve.n)
    if stop == "norm":
        return norm_done

    def qT_fn(h):
        b = 64 * (h % 2)
        return RAb[b:b + 64, (h // 2) * 2048: (h // 2) * 2048 + 2048]

    def v_lhsT(h, kt):
        return htm_slice(kt)[:, h * 65: h * 65 + 65]

    def o_dst(h, qb):
        b = 64 * (h % 2)
        return C.RC[b:b + 64,
                    (h // 2) * 2048 + qb * QB: (h // 2) * 2048 + qb * QB + QB]

    C.guard_bank(0, *norm_done)
    C.guard_bank(1, *norm_done)
    C.guard_bank(4, *norm_done)
    C.guard_bank(5, *norm_done)
    emit_attn64(C, list(range(16)), qT_fn, qT_fn, v_lhsT, o_dst, shift=-1.0)
    attn_pe = (C.pe, C.pe.n)
    attn_ve = (C.ve, C.ve.n)
    for halfw in range(2):
        t_wo = load_wmat(C, C.RW, halfw * 4096, W["l2_wp_h%d" % halfw],
                         8, 512, s_wb if halfw == 0 else s_wa,
                         waits=(attn_pe,))
        mm_chain(C,
                 lambda k, m, halfw=halfw:
                 C.RW[:, halfw * 4096 + k * 512 + m * 128:
                      halfw * 4096 + k * 512 + m * 128 + 128],
                 lambda k, c: C.RC[:, k * 2048 + c * 512:
                                   k * 2048 + c * 512 + 512],
                 evac_ts_bias(C, lambda m, c, halfw=halfw:
                              RAb[:, (halfw * 4 + m) * 2048 + c * 512:
                                  (halfw * 4 + m) * 2048 + c * 512 + 512],
                              lambda m, halfw=halfw:
                              C.RCOL2[:, 104 + halfw * 4 + m:
                                      105 + halfw * 4 + m]),
                 m_tiles=4, k_list=range(8), n_chunks=4,
                 pre_waits=((s_wb if halfw == 0 else s_wa, t_wo), attn_ve),
                 banks=(0, 1, 2, 3))
    s_done = (C.ve, C.ve.n)
    if stop == "attn":
        return s_done
    return emit_l2f(C, W, s_done, stop=stop)


def emit_l2f(C, W, s_done, stop=None):
    RAb = C.RAb
    s_wa = C.sem("2fa")
    s_wb = C.sem("2fb")
    s_wc = C.sem("2fc")
    pe00 = ((C.pe, C.pe.n),)
    guards = {"q": pe00, "s": pe00, "l": pe00}
    for blk_i in range(3):
        for jj in range(2):
            j = 2 * blk_i + jj
            src_t = (lambda k: C.RB[:, k * 2048:(k + 1) * 2048]) if jj == 0 \
                else (lambda k: C.RC[:, k * 2048:(k + 1) * 2048])
            t_la = e_dma(C, "gpsimd",
                         C.RX[:, 1024:1280].rearrange("p (k o) -> p k o", k=8),
                         W["l2_la_%d" % j].rearrange("(k p) o -> p k o", p=128),
                         s_wc, waits=guards["l"])
            t_lb = e_dma(C, "gpsimd", C.RX[0:32, 0:1024],
                         W["l2_lb_%d" % j], s_wc)
            def mid_evac(m, c, b, pe_t):
                cnt = C.ve.pre(1)
                def f(v, c=c, b=b, pe_t=pe_t):
                    v.wait_ge(C.pe.h, pe_t)
                    v.tensor_copy(C.RX[0:32, 6144 + c * 512: 6656 + c * 512],
                                  C.bank(b, rows=32)).then_inc(C.ve.h, 1)
                C.emit("vector", f)
                return (C.ve, cnt)
            mm_chain(C,
                     lambda k, m: C.RX[:, 1024 + k * 32: 1024 + k * 32 + 32],
                     lambda k, c: src_t(k)[:, c * 512: c * 512 + 512],
                     mid_evac, m_tiles=1, k_list=range(8), n_chunks=4,
                     pre_waits=((s_wc, t_la), (s_wc, t_lb), s_done),
                     banks=(4, 5), m_rows=32)
            mid_done = (C.ve, C.ve.n)
            if stop == "mid%d" % j:
                return mid_done
            for hf in range(2):
                t_q = load_wmat(C, C.RW, 0,
                                W["l2_qw_%d" % j][:, hf * 512:(hf + 1) * 512],
                                8, 512, s_wa, waits=guards["q"])
                t_s = load_wmat(C, C.RW, 4096,
                                W["l2_sc_%d" % j][:, hf * 512:(hf + 1) * 512],
                                8, 512, s_wb, waits=guards["s"])
                pe_prev = (C.pe, C.pe.n)
                if stop == "dq1l" and j == 0 and hf == 1:
                    return (s_wb, t_s)
                vq = C.ve.pre(2)
                def fdq(v, t_q=t_q, t_s=t_s, pe_prev=pe_prev, vq=vq):
                    v.wait_ge(s_wa.h, t_q)
                    v.wait_ge(s_wb.h, t_s)
                    wg(v, pe_prev)
                    v.tensor_tensor(C.RW[:, 8192:12288], C.RW[:, 0:4096],
                                    C.RW[:, 4096:8192], ALU.mult)\
                        .then_inc(C.ve.h, 1)
                    v.wait_ge(C.ve.h, vq - 1)
                    v.scalar_tensor_tensor(C.RW[:, 8192:12288],
                                           C.RW[:, 8192:12288], 1.0 / 7.5,
                                           C.RW[:, 4096:8192], ALU.mult,
                                           ALU.subtract).then_inc(C.ve.h, 1)
                C.emit("vector", fdq)
                dq_done = (C.ve, vq)
                if stop == "dq" and j == 0 and hf == 0:
                    return dq_done
                if stop == "dqx" and j == 1 and hf == 0:
                    return dq_done
                if stop == "dq1" and j == 0 and hf == 1:
                    return dq_done
                def ql_evac(m, c, b, bl, pe_t, hf=hf, jj=jj, j=j):
                    c1 = C.ve.pre(1)
                    c2 = C.ve.pre(1)
                    cnt = C.ve.pre(1)
                    def f(v, m=m, c=c, b=b, bl=bl, pe_t=pe_t, hf=hf, jj=jj,
                          j=j, c1=c1, c2=c2):
                        v.wait_ge(C.pe.h, pe_t)
                        mg = hf * 4 + m
                        v.tensor_scalar(C.RY[:, 0:512], C.bank(bl),
                                        C.RCOL2[:, j * 8 + mg:
                                                j * 8 + mg + 1],
                                        None, ALU.add).then_inc(C.ve.h, 1)
                        v.wait_ge(C.ve.h, c1)
                        if jj == 0:
                            dst = C.RC[:, mg * 2048 + c * 512:
                                       mg * 2048 + c * 512 + 512]
                            v.tensor_tensor(dst, C.RY[:, 0:512], C.bank(b),
                                            ALU.add).then_inc(C.ve.h, 2)
                        else:
                            dst = C.RB[:, mg * 2048 + c * 512:
                                       mg * 2048 + c * 512 + 512]
                            v.tensor_tensor(C.RY[:, 512:1024], C.RY[:, 0:512],
                                            C.bank(b), ALU.add)\
                                .then_inc(C.ve.h, 1)
                            v.wait_ge(C.ve.h, c2)
                            v.tensor_tensor(dst, dst, C.RY[:, 512:1024],
                                            ALU.add).then_inc(C.ve.h, 1)
                    C.emit("vector", f)
                    return (C.ve, cnt)
                def ql_lhsT(k, m, hf=hf):
                    return C.RW[:, 8192 + k * 512 + m * 128:
                                8192 + k * 512 + m * 128 + 128]
                first = True
                bi = 0
                for m in range(4):
                    for c in range(4):
                        b = (0, 1, 2, 3)[bi % 4]
                        bl = 4 + bi % 2
                        bi += 1
                        gd = C.bank_guard[b]
                        gdl = C.bank_guard[bl]
                        pw = ((dq_done, mid_done) if first else ())
                        first = False
                        pe_t = C.pe.pre(1)
                        mg = hf * 4 + m
                        def fmm(t, m=m, c=c, b=b, bl=bl, gd=gd, gdl=gdl,
                                pw=pw, mg=mg, src_t=src_t, hf=hf):
                            for w_ in pw:
                                wg(t, w_)
                            wg(t, gd)
                            wg(t, gdl)
                            for k in range(8):
                                t.matmul(C.bank(b), ql_lhsT(k, m, hf),
                                         src_t(k)[:, c * 512: c * 512 + 512],
                                         start=(k == 0), stop=(k == 7))
                            t.matmul(C.bank(bl),
                                     C.RX[0:32, mg * 128: mg * 128 + 128],
                                     C.RX[0:32, 6144 + c * 512: 6656 + c * 512],
                                     start=True, stop=True)\
                                .then_inc(C.pe.h, 1)
                        C.emit("tensor", fmm)
                        gd_ev = ql_evac(m, c, b, bl, pe_t)
                        C.guard_bank(b, *gd_ev)
                        C.guard_bank(bl, *gd_ev)
                guards["q"] = ((C.pe, C.pe.n),)
                guards["s"] = ((C.pe, C.pe.n),)
                if stop == "qlh" and j == 0 and hf == 0:
                    return (C.ve, C.ve.n)
            guards["l"] = ((C.pe, C.pe.n),)
            if stop == "ql%d" % j:
                return (C.ve, C.ve.n)
            if jj == 0:
                lnr = emit_ln(C, lambda k: C.RC[:, k * 2048:(k + 1) * 2048],
                              lambda k: C.RC[:, k * 2048:(k + 1) * 2048],
                              eps=1e-5, in_bf16=True,
                              gain_col=lambda k, b=blk_i:
                              C.RCOL2[:, 48 + b * 8 + k:49 + b * 8 + k],
                              bias_col=lambda k, b=blk_i:
                              C.RCOL2[:, 72 + b * 8 + k:73 + b * 8 + k],
                              pre_waits=((C.ve, C.ve.n),),
                              sq_region=RAb[:, 16384:20480])
                for k in range(ND):
                    C.act.pre(1)
                    def fgl(s, k=k, lnr=lnr):
                        if k == 0:
                            wg(s, lnr)
                        s.activation(C.RC[:, k * 2048:(k + 1) * 2048],
                                     C.RC[:, k * 2048:(k + 1) * 2048],
                                     AF.Gelu).then_inc(C.act.h, 1)
                    C.emit("scalar", fgl)
                s_done = (C.act, C.act.n)
                if stop == "lng" and blk_i == 0:
                    return s_done
            else:
                s_done = (C.ve, C.ve.n)
    ve_fin = (C.ve, C.ve.n)
    for k in range(ND):
        C.act.pre(1)
        def fsl(s, k=k, ve_fin=ve_fin):
            if k == 0:
                wg(s, ve_fin)
            s.activation(C.RB[:, k * 2048:(k + 1) * 2048],
                         C.RB[:, k * 2048:(k + 1) * 2048], AF.Silu)\
                .then_inc(C.act.h, 1)
        C.emit("scalar", fsl)
    return (C.act, C.act.n)


def build_l2(stop=None):
    nc = bass.Bass("TRN2", target_bir_lowering=False)
    xin = nc.dram_tensor("xin", [ND, 128, T], F32, kind="ExternalInput").ap()
    H = {}
    for nm, shp in [("consts", [128, 384])] + L2_SHAPES:
        kind = I32 if nm.startswith("l2_qw") else F32
        H[nm] = nc.dram_tensor(nm, shp, kind, kind="ExternalInput").ap()
    xout = nc.dram_tensor("xout", [ND, 128, T], F32, kind="ExternalOutput").ap()
    dbg = nc.dram_tensor("dbg", [ND, 128, T], F32, kind="ExternalOutput").ap()
    es = ExitStack()
    with es:
        C = Ctx(nc, es)
        blk = es.enter_context(nc.Block())
        s_x = C.sem("x")
        s_k = C.sem("k")
        for k in range(ND):
            e_dma(C, "sync", C.RA[:, k * 2048:(k + 1) * 2048], xin[k], s_x)
        e_dma(C, "gpsimd", C.CONST[:], H["consts"], s_k)
        e_dma(C, "gpsimd", C.RCOL2[:], H["l2_cols"], s_k)
        konst_ready = [(s_k, s_k.n), (s_x, s_x.n)]
        ln1 = emit_ln(C, lambda k: C.RA[:, k * 2048:(k + 1) * 2048],
                      lambda k: C.RB[:, k * 2048:(k + 1) * 2048],
                      pre_waits=tuple(konst_ready))
        r = emit_l2(C, H, stop=stop)
        if stop == "lng":
            sd = C.sem("dbgo")
            for k in range(ND):
                e_dma(C, "gpsimd", dbg[k], C.RC[:, k * 2048:(k + 1) * 2048],
                      sd, waits=(r,))
            def ffin(g, n=sd.n):
                g.wait_ge(sd.h, n)
            C.emit("gpsimd", ffin)
            C.finalize(blk)
            return nc
        if stop == "dq1l":
            sd = C.sem("dbgo")
            e_dma(C, "gpsimd", dbg[0], C.RW[:, 0:2048], sd, waits=(r,))
            def ffin(g, n=sd.n):
                g.wait_ge(sd.h, n)
            C.emit("gpsimd", ffin)
            C.finalize(blk)
            return nc
        if stop == "dq1":
            sd = C.sem("dbgo")
            e_dma(C, "gpsimd", dbg[0], C.RW[:, 8192:10240], sd, waits=(r,))
            def ffin(g, n=sd.n):
                g.wait_ge(sd.h, n)
            C.emit("gpsimd", ffin)
            C.finalize(blk)
            return nc
        if stop == "qlh":
            sd = C.sem("dbgo")
            for k in range(4):
                e_dma(C, "gpsimd", dbg[k], C.RC[:, k * 2048:(k + 1) * 2048],
                      sd, waits=(r,))
            def ffin(g, n=sd.n):
                g.wait_ge(sd.h, n)
            C.emit("gpsimd", ffin)
            C.finalize(blk)
            return nc
        if stop in ("dq", "dqx"):
            sd = C.sem("dbgo")
            for k in range(2):
                e_dma(C, "gpsimd", dbg[k], C.RW[:, 8192 + k * 2048:
                                                8192 + (k + 1) * 2048],
                      sd, waits=(r,))
            def ffin(g, n=sd.n):
                g.wait_ge(sd.h, n)
            C.emit("gpsimd", ffin)
            C.finalize(blk)
            return nc
        if stop is not None and stop.startswith("mid"):
            sd = C.sem("dbgo")
            e_dma(C, "gpsimd", dbg[0][0:32, :], C.RX[0:32, 5120:7168],
                  sd, waits=(r,))
            def ffin(g, n=sd.n):
                g.wait_ge(sd.h, n)
            C.emit("gpsimd", ffin)
            C.finalize(blk)
            return nc
        if stop is not None and stop.startswith("ql"):
            sd = C.sem("dbgo")
            reg = C.RC if stop in ("ql0", "ql2", "ql4") else C.RB
            for k in range(ND):
                e_dma(C, "gpsimd", dbg[k], reg[:, k * 2048:(k + 1) * 2048],
                      sd, waits=(r,))
            reg2 = C.RB if stop in ("ql0", "ql2", "ql4") else C.RC
            for k in range(ND):
                e_dma(C, "gpsimd", xout[k], reg2[:, k * 2048:(k + 1) * 2048],
                      sd, waits=(r,))
            def ffin(g, n=sd.n):
                g.wait_ge(sd.h, n)
            C.emit("gpsimd", ffin)
            C.finalize(blk)
            return nc
        if stop in ("h", "norm", "attn"):
            sd = C.sem("dbgo")
            src_reg = C.RAb if stop != "attn" else C.RAb
            for k in range(ND):
                e_dma(C, "gpsimd", dbg[k],
                      C.RAb[:, k * 2048:(k + 1) * 2048], sd, waits=(r,))
            def ffin(g, n=sd.n):
                g.wait_ge(sd.h, n)
            C.emit("gpsimd", ffin)
            C.finalize(blk)
            return nc
        g_done = r
        s_xg = C.sem("gx")
        s_o = C.sem("gout")
        prev_tt = {}
        for k in range(ND):
            wx = [g_done] if k == 0 else [(C.ve, prev_tt[k - 1])]
            t_x = e_dma(C, "sync", C.RY[:, 0:2048], xin[k], s_xg, waits=wx)
            vta = C.ve.pre(1)
            v_t = C.ve.pre(1)
            def ftt(v, k=k, t_x=t_x, g_done=g_done, vta=vta):
                v.wait_ge(s_xg.h, t_x)
                wg(v, g_done)
                v.tensor_tensor(C.RY[:, 2048:4096],
                                C.RAb[:, k * 2048:(k + 1) * 2048],
                                C.RB[:, k * 2048:(k + 1) * 2048], ALU.mult)\
                    .then_inc(C.ve.h, 1)
                v.wait_ge(C.ve.h, vta)
                v.tensor_tensor(C.RY[:, 0:2048], C.RY[:, 0:2048],
                                C.RY[:, 2048:4096], ALU.add)\
                    .then_inc(C.ve.h, 1)
            C.emit("vector", ftt)
            prev_tt[k] = v_t
            e_dma(C, "sync", xout[k], C.RY[:, 0:2048], s_o,
                  waits=[(C.ve, v_t)])
        def ffin(g, n=s_o.n):
            g.wait_ge(s_o.h, n)
            g.memset(C.RX[0:1, 0:1], 0.0)
        C.emit("gpsimd", ffin)
        C.finalize(blk)
    return nc



def stage_l3(inputs, g):
    Wa = np.asarray(inputs["L3_attn_W_attn"], np.float32)
    g["l3_wq"] = np.ascontiguousarray(Wa[:, 0:1024])
    g["l3_wk"] = np.ascontiguousarray(Wa[:, 1024:2048])
    g["l3_wv"] = np.ascontiguousarray(Wa[:, 2048:3072])
    ba = np.asarray(inputs["L3_attn_b_attn"], np.float32)
    g["l3_bvrow"] = np.ascontiguousarray(ba[2048:3072].reshape(1, 1024))
    wp = np.ascontiguousarray(np.asarray(inputs["L3_attn_W_proj"], np.float32))
    g["l3_wp_h0"] = np.ascontiguousarray(wp[:, 0:512])
    g["l3_wp_h1"] = np.ascontiguousarray(wp[:, 512:1024])
    rwt = np.asarray(inputs["L3_ffn_router_w"],
                     np.float32).reshape(8, 128).T
    rw_hi = rwt.astype(ml_dtypes.bfloat16).astype(np.float32)
    g["l3f_rw"] = np.ascontiguousarray(
        np.concatenate([rw_hi, rwt - rw_hi], axis=1))
    g["l3f_wq"] = np.ascontiguousarray(np.asarray(inputs["L3_ffn_wq"],
                                                  np.float32))
    g["l3f_wk"] = np.ascontiguousarray(np.asarray(inputs["L3_ffn_wk"],
                                                  np.float32))
    g["l3f_wv"] = np.ascontiguousarray(np.asarray(inputs["L3_ffn_wv"],
                                                  np.float32))
    wo = np.ascontiguousarray(np.asarray(inputs["L3_ffn_wo"], np.float32))
    g["l3f_wo_h0"] = np.ascontiguousarray(wo[:, 0:512])
    g["l3f_wo_h1"] = np.ascontiguousarray(wo[:, 512:1024])
    g["l3f_w1"] = np.ascontiguousarray(np.asarray(inputs["L3_ffn_w1"],
                                                  np.float32))
    g["l3f_w2"] = np.ascontiguousarray(np.asarray(inputs["L3_ffn_w2"],
                                                  np.float32))
    cols = np.zeros((128, 64), np.float32)
    cols[:, 0:8] = ba[0:1024].reshape(8, 128).T
    cols[:, 8:16] = ba[1024:2048].reshape(8, 128).T
    cols[:, 16:24] = np.asarray(inputs["L3_attn_b_proj"], np.float32)\
        .reshape(8, 128).T
    cols[:, 24:32] = np.asarray(inputs["L3_ffn_ln1"], np.float32)\
        .reshape(8, 128).T
    cols[:, 32:40] = np.asarray(inputs["L3_ffn_ln2"], np.float32)\
        .reshape(8, 128).T
    cols[0, 40] = float(np.asarray(inputs["L3_ffn_router_b"]).reshape(-1)[0])
    cols[:, 41] = np.where((np.arange(128) % 64) < 32, -1.0, 1.0)
    inv = np.float32(1.0) / np.float32(10000.0) ** (
        np.arange(0, 64, 2, dtype=np.float32) / np.float32(64.0))
    cols[:, 42] = np.tile(inv, 4)
    g["l3_cols"] = np.ascontiguousarray(cols)
    return g


def stage_l3_pos(inputs, b):
    pid = np.asarray(inputs["L3_ffn_position_ids"])
    return np.ascontiguousarray(
        pid[min(b, 3), 0:T].astype(np.float32).reshape(1, T))


L3_SHAPES = [("l3_wq", [D, D]), ("l3_wk", [D, D]), ("l3_wv", [D, D]),
             ("l3_bvrow", [1, D]), ("l3_wp_h0", [D, 512]),
             ("l3_wp_h1", [D, 512]), ("l3f_rw", [128, 16]),
             ("l3f_wq", [D, D]), ("l3f_wk", [D, D]), ("l3f_wv", [D, D]),
             ("l3f_wo_h0", [D, 512]), ("l3f_wo_h1", [D, 512]),
             ("l3f_w1", [D, 4096]), ("l3f_w2", [4096, D]),
             ("l3_cols", [128, 64])]


def emit_l3a(C, W, spill_done=None, stop=None):
    RAb = C.RAb
    s_wa = C.sem("3wa")
    s_wb = C.sem("3wb")
    pe0 = (C.pe, C.pe.n)
    po_t = C.po.pre(1)
    ve_in = (C.ve, C.ve.n)
    def fmast(g, ve_in=ve_in):
        wg(g, ve_in)
        g.iota(C.RA[:, 0:2944].bitcast(I32), [[-1, 2944]], base=896,
               channel_multiplier=1).then_inc(C.po.h, 1)
    C.emit("gpsimd", fmast)
    v_mc = C.ve.pre(1)
    def fmcv(v, po_t=po_t):
        v.wait_ge(C.po.h, po_t)
        v.tensor_copy(C.RY[:, 0:2944], C.RA[:, 0:2944].bitcast(I32))\
            .then_inc(C.ve.h, 1)
    C.emit("vector", fmcv)
    po_t2 = C.po.pre(1)
    def fmast2(g, v_mc=v_mc):
        g.wait_ge(C.ve.h, v_mc)
        g.affine_select(C.RY[:, 0:2944], C.RY[:, 0:2944], [[1, 2944]],
                        ALU.is_ge, -1e9, base=-896, channel_multiplier=-1)\
            .then_inc(C.po.h, 1)
    C.emit("gpsimd", fmast2)
    mast_ready = (C.po, C.po.n)
    t_wv = load_wmat(C, C.RW, 0, W["l3_wv"], 8, 1024, s_wa, waits=(pe0,))
    t_bv = e_dma(C, "gpsimd", C.RX[0:1, 6656:7680], W["l3_bvrow"], s_wa)
    po_v = C.po.pre(1)
    ve0 = (C.ve, C.ve.n)
    def fonesv(g, ve0=ve0):
        wg(g, ve0)
        g.memset(C.RC[:, 0:15600], 1.0)
        g.memset(C.RW[:, 8192:9232], 1.0).then_inc(C.po.h, 1)
    C.emit("gpsimd", fonesv)
    def vtm_slice(tt):
        if tt < 15:
            return C.RC[:, tt * 1040:(tt + 1) * 1040]
        return C.RW[:, 8192:9232]
    for tt in range(16):
        for c in range(2):
            b = (tt * 2 + c) % 4
            gd = C.bank_guard[b]
            pe_t = C.pe.pre(1)
            pwv = ((s_wa, t_wv), (s_wa, t_bv), spill_done) \
                if tt == 0 and c == 0 else ()
            def fmm(t, tt=tt, c=c, b=b, gd=gd, pwv=pwv):
                for w_ in pwv:
                    wg(t, w_)
                wg(t, gd)
                for k in range(8):
                    t.matmul(C.bank(b),
                             C.RB[:, k * 2048 + tt * 128: k * 2048 + tt * 128 + 128],
                             C.RW[:, k * 1024 + c * 512: k * 1024 + c * 512 + 512],
                             start=(k == 0), stop=False)
                t.matmul(C.bank(b), C.ones_bf[0:1, 0:128],
                         C.RX[0:1, 6656 + c * 512: 6656 + c * 512 + 512],
                         start=False, stop=True).then_inc(C.pe.h, 1)
            C.emit("tensor", fmm)
            cnt = C.ve.pre(1)
            def fev(v, tt=tt, c=c, b=b, pe_t=pe_t, po_v=po_v):
                v.wait_ge(C.pe.h, pe_t)
                v.wait_ge(C.po.h, po_v)
                dst = vtm_slice(tt).rearrange("p (h e) -> p h e", h=16)[:, c * 8:
                                                                       c * 8 + 8,
                                                                       0:64]
                v.tensor_copy(dst, C.bank(b).rearrange("p (h e) -> p h e", h=8))\
                    .then_inc(C.ve.h, 1)
            C.emit("vector", fev)
            C.guard_bank(b, C.ve, cnt)
    vtm_done = (C.ve, C.ve.n)
    pe_v = (C.pe, C.pe.n)
    t_wq = load_wmat(C, C.RW, 0, W["l3_wq"], 8, 1024, s_wb, waits=(pe_v,))
    def q_evac(m, c, b, pe_t):
        cnt = C.ve.pre(1)
        def f(v, m=m, c=c, b=b, pe_t=pe_t):
            v.wait_ge(C.pe.h, pe_t)
            v.tensor_scalar(RAb[:, m * 2048 + c * 512: m * 2048 + c * 512 + 512],
                            C.bank(b), C.RCOL[0:128, 0 + m:1 + m], 0.125,
                            ALU.add, ALU.mult).then_inc(C.ve.h, 1)
        C.emit("vector", f)
        return (C.ve, cnt)
    mm_chain(C,
             lambda k, m: C.RW[:, k * 1024 + m * 128: k * 1024 + m * 128 + 128],
             lambda k, c: C.RB[:, k * 2048 + c * 512: k * 2048 + c * 512 + 512],
             q_evac, m_tiles=8, k_list=range(8), n_chunks=4,
             pre_waits=((s_wb, t_wq),), banks=(0, 1, 2, 3))
    pe_q = (C.pe, C.pe.n)
    t_wk = load_wmat(C, C.RW, 0, W["l3_wk"], 8, 1024, s_wa, waits=(pe_q,))
    mm_chain(C,
             lambda k, m: C.RW[:, k * 1024 + m * 128: k * 1024 + m * 128 + 128],
             lambda k, c: C.RB[:, k * 2048 + c * 512: k * 2048 + c * 512 + 512],
             evac_ts_bias(C, lambda m, c:
                          RAb[:, 16384 + m * 2048 + c * 512:
                              16384 + m * 2048 + c * 512 + 512],
                          lambda m: C.RCOL[:, 8 + m:9 + m]),
             m_tiles=8, k_list=range(8), n_chunks=4,
             pre_waits=((s_wa, t_wk),), banks=(0, 1, 2, 3))
    qk_done = (C.ve, C.ve.n)
    if stop == "qkv":
        return qk_done

    def qT_fn(h):
        b = 64 * (h % 2)
        return RAb[b:b + 64, (h // 2) * 2048: (h // 2) * 2048 + 2048]

    def kT_fn(h):
        b = 64 * (h % 2)
        return RAb[b:b + 64, 16384 + (h // 2) * 2048:
                   16384 + (h // 2) * 2048 + 2048]

    def v_lhsT(h, kt):
        return vtm_slice(kt)[:, h * 65: h * 65 + 65]

    def o_dst(h, qb):
        b = 64 * (h % 2)
        return C.RB[b:b + 64,
                    (h // 2) * 2048 + qb * QB: (h // 2) * 2048 + qb * QB + QB]

    def alibi_mask(h, qb, kt, sb, pe_s):
        slope = 1.0 / (2.0 ** (h + 1))
        off = 896 - (kt * 128 - qb * QB)
        cnt = C.ve.pre(1)
        def f(v, sb=sb, off=off, slope=slope, pe_s=pe_s,
              mast_ready=mast_ready):
            v.wait_ge(C.pe.h, pe_s)
            wg(v, mast_ready)
            v.scalar_tensor_tensor(C.ps[:, sb * 512: sb * 512 + 1024],
                                   C.RY[:, off:off + 1024], slope,
                                   C.ps[:, sb * 512: sb * 512 + 1024],
                                   ALU.mult, ALU.add).then_inc(C.ve.h, 1)
        C.emit("vector", f)
        return (C.ve, cnt)

    kt_hi = lambda qb: (qb + 1) * (QB // 128)
    emit_attn64(C, list(range(16)), qT_fn, kT_fn, v_lhsT, o_dst,
                shift=-3.0, kt_hi_fn=kt_hi, mask_fn=alibi_mask,
                ry_off=3072)
    attn_pe = (C.pe, C.pe.n)
    attn_ve = (C.ve, C.ve.n)
    for halfw in range(2):
        t_wo = load_wmat(C, C.RW, halfw * 4096, W["l3_wp_h%d" % halfw],
                         8, 512, s_wb if halfw == 0 else s_wa,
                         waits=(attn_pe,))
        mm_chain(C,
                 lambda k, m, halfw=halfw:
                 C.RW[:, halfw * 4096 + k * 512 + m * 128:
                      halfw * 4096 + k * 512 + m * 128 + 128],
                 lambda k, c: C.RB[:, k * 2048 + c * 512:
                                   k * 2048 + c * 512 + 512],
                 evac_ts_bias(C, lambda m, c, halfw=halfw:
                              RAb[:, (halfw * 4 + m) * 2048 + c * 512:
                                  (halfw * 4 + m) * 2048 + c * 512 + 512],
                              lambda m, halfw=halfw:
                              C.RCOL[:, 16 + halfw * 4 + m:
                                     17 + halfw * 4 + m]),
                 m_tiles=4, k_list=range(8), n_chunks=4,
                 pre_waits=((s_wb if halfw == 0 else s_wa, t_wo), attn_ve),
                 banks=(0, 1, 2, 3))
    return (C.ve, C.ve.n)


def emit_l3f(C, W, xnsp, s_spill_done, stop=None):
    RAb = C.RAb
    s_wa = C.sem("3fa")
    s_wb = C.sem("3fb")
    s_xr = C.sem("3xr")
    pe0 = (C.pe, C.pe.n)
    ve0 = (C.ve, C.ve.n)
    for k in range(ND):
        e_dma(C, "gpsimd", C.RB[:, k * 2048:(k + 1) * 2048], xnsp[k], s_xr,
              waits=(pe0, ve0) if k == 0 else ())
    xn_rl = (s_xr, s_xr.n)
    t_rw = e_dma(C, "gpsimd", C.RW[:, 9232:9248], W["l3f_rw"], s_wa,
                 waits=(pe0,))
    v_w = None
    for rr in range(2):
        gd45 = C.snap_guards([4, 5])
        pe_r = C.pe.pre(1)
        def frout(t, rr=rr, t_rw=t_rw, gd45=gd45, xn_rl=xn_rl):
            if rr == 0:
                wg(t, (s_wa, t_rw))
                wg(t, xn_rl)
            wg(t, gd45[0])
            wg(t, gd45[1])
            last = None
            for ch in range(2):
                c = rr * 2 + ch
                for kk in range(16):
                    k = kk % 8
                    col = 9232 + (kk // 8) * 8 + k
                    last = t.matmul(
                        C.ps[0:1, 2048 + ch * 512: 2048 + ch * 512 + 512],
                        C.RW[:, col: col + 1],
                        C.RB[:, k * 2048 + c * 512: k * 2048 + c * 512 + 512],
                        start=(kk == 0), stop=(kk == 15))
            last.then_inc(C.pe.h, 1)
        C.emit("tensor", frout)
        v_w = C.ve.pre(1)
        def fwev(v, rr=rr, pe_r=pe_r):
            v.wait_ge(C.pe.h, pe_r)
            v.tensor_scalar(C.RY[0:1, rr * 1024: rr * 1024 + 1024],
                            C.ps[0:1, 2048:3072], C.RCOL[0:1, 40:41], None,
                            ALU.add).then_inc(C.ve.h, 1)
        C.emit("vector", fwev)
        C.guard_bank(4, C.ve, C.ve.n)
        C.guard_bank(5, C.ve, C.ve.n)
    vwh = C.ve.pre(2)
    def fwhl(v, v_w=v_w, vwh=vwh):
        v.wait_ge(C.ve.h, v_w)
        v.tensor_copy(C.RX[0:1, 2048:4096], C.RY[0:1, 0:2048])\
            .then_inc(C.ve.h, 1)
        v.wait_ge(C.ve.h, vwh - 1)
        v.tensor_tensor(C.RX[0:1, 4096:6144], C.RY[0:1, 0:2048],
                        C.RX[0:1, 2048:4096], ALU.subtract)\
            .then_inc(C.ve.h, 1)
    C.emit("vector", fwhl)
    gd6w = C.snap_guards([6])
    pe_tr = C.pe.pre(1)
    def ftr16(t, vwh=vwh, gd6w=gd6w):
        t.wait_ge(C.ve.h, vwh)
        wg(t, gd6w[0])
        last = None
        for tt in range(16):
            t.matmul(C.ps[:, 3072 + tt: 3073 + tt],
                     C.RX[0:1, 2048 + tt * 128: 2048 + tt * 128 + 128],
                     C.ones_bf[0:1, 0:1], start=True, stop=False)
            last = t.matmul(C.ps[:, 3072 + tt: 3073 + tt],
                            C.RX[0:1, 4096 + tt * 128: 4096 + tt * 128 + 128],
                            C.ones_bf[0:1, 0:1], start=False, stop=True)
        last.then_inc(C.pe.h, 1)
    C.emit("tensor", ftr16)
    v_16 = C.ve.pre(1)
    def fw16(v, pe_tr=pe_tr):
        v.wait_ge(C.pe.h, pe_tr)
        v.tensor_copy(C.RY[:, 2048:2064], C.ps[:, 3072:3088])\
            .then_inc(C.ve.h, 1)
    C.emit("vector", fw16)
    C.guard_bank(6, C.ve, v_16)
    w16_ready = (C.ve, v_16)
    if stop == "w16":
        return w16_ready
    vb = C.ve.pre(1)
    c_lo = C.constf(-10000.0)
    c_hi = C.constf(10000.0)
    def finit(v, w16_ready=w16_ready, c_lo=c_lo, c_hi=c_hi):
        wg(v, w16_ready)
        v.tensor_copy(C.RY[:, 2064:2065], c_lo)
        v.tensor_copy(C.RY[:, 2065:2066], c_hi).then_inc(C.ve.h, 1)
    C.emit("vector", finit)
    for it in range(48):
        prev_it = C.ve.n
        v1 = C.ve.pre(1)
        v1b = C.ve.pre(1)
        v2 = C.ve.pre(1)
        def fbm(v, v1=v1, v1b=v1b, prev_it=prev_it):
            v.wait_ge(C.ve.h, prev_it)
            v.tensor_scalar(C.RY[:, 2066:2067], C.RY[:, 2064:2065],
                            C.RY[:, 2065:2066], 0.5, ALU.add, ALU.mult)\
                .then_inc(C.ve.h, 1)
            v.wait_ge(C.ve.h, v1)
            v.tensor_scalar(C.RY[:, 2080:2096], C.RY[:, 2048:2064],
                            C.RY[:, 2066:2067], None, ALU.is_gt)\
                .then_inc(C.ve.h, 1)
            v.wait_ge(C.ve.h, v1b)
            v.tensor_scalar(C.RX[:, 2048:2064], C.RY[:, 2080:2096], 1.0,
                            0.0, ALU.mult, ALU.add,
                            accum_out=C.RY[:, 2067:2068]).then_inc(C.ve.h, 1)
        C.emit("vector", fbm)
        v3 = C.ve.pre(1)
        def fcb(v, v2=v2):
            v.wait_ge(C.ve.h, v2)
            v.tensor_copy(C.RX[:, 2064:2065], C.RY[:, 2067:2068])\
                .then_inc(C.ve.h, 1)
        C.emit("vector", fcb)
        gd6 = C.snap_guards([6])
        pe_c = C.pe.pre(1)
        def fct(t, v3=v3, gd6=gd6):
            t.wait_ge(C.ve.h, v3)
            wg(t, gd6[0])
            t.matmul(C.ps[:, 3072:3073], C.ones_bf, C.RX[:, 2064:2065],
                     start=True, stop=True).then_inc(C.pe.h, 1)
        C.emit("tensor", fct)
        v4 = C.ve.pre(1)
        v5 = C.ve.pre(1)
        v6 = C.ve.pre(1)
        v7 = C.ve.pre(1)
        def fup(v, pe_c=pe_c, v4=v4, v5=v5, v6=v6, it=it):
            v.wait_ge(C.pe.h, pe_c)
            if it < 16:
                v.tensor_copy(C.RY[:, 2128 + it: 2129 + it],
                              C.ps[:, 3072:3073])
            v.tensor_scalar(C.RY[:, 2068:2069], C.ps[:, 3072:3073], 255.5,
                            None, ALU.is_ge).then_inc(C.ve.h, 1)
            v.wait_ge(C.ve.h, v4)
            v.tensor_tensor(C.RY[:, 2069:2070], C.RY[:, 2066:2067],
                            C.RY[:, 2064:2065], ALU.subtract)
            v.tensor_tensor(C.RY[:, 2070:2071], C.RY[:, 2066:2067],
                            C.RY[:, 2065:2066], ALU.subtract)
            v.tensor_scalar(C.RY[:, 2071:2072], C.RY[:, 2068:2069], -1.0, 1.0,
                            ALU.mult, ALU.add).then_inc(C.ve.h, 1)
            v.wait_ge(C.ve.h, v5)
            v.tensor_tensor(C.RY[:, 2069:2070], C.RY[:, 2069:2070],
                            C.RY[:, 2068:2069], ALU.mult)
            v.tensor_tensor(C.RY[:, 2070:2071], C.RY[:, 2070:2071],
                            C.RY[:, 2071:2072], ALU.mult).then_inc(C.ve.h, 1)
            v.wait_ge(C.ve.h, v6)
            v.tensor_tensor(C.RY[:, 2064:2065], C.RY[:, 2064:2065],
                            C.RY[:, 2069:2070], ALU.add)
            v.tensor_tensor(C.RY[:, 2065:2066], C.RY[:, 2065:2066],
                            C.RY[:, 2070:2071], ALU.add).then_inc(C.ve.h, 1)
        C.emit("vector", fup)
        C.guard_bank(6, C.ve, C.ve.n)
    prev_n = C.ve.n
    if stop == "bis":
        return (C.ve, prev_n)
    thr_done = C.ve.pre(1)
    def fsel(v, prev_n=prev_n):
        v.wait_ge(C.ve.h, prev_n)
        v.tensor_scalar(C.RY[:, 2080:2096], C.RY[:, 2048:2064],
                        C.RY[:, 2065:2066], None, ALU.is_gt)\
            .then_inc(C.ve.h, 1)
    C.emit("vector", fsel)
    v6 = C.ve.pre(1)
    def fkm(v, thr_done=thr_done):
        v.wait_ge(C.ve.h, thr_done)
        v.tensor_scalar(C.RM[:, 64:80], C.RY[:, 2080:2096], 1e9, -1e9,
                        ALU.mult, ALU.add)
        v.tensor_tensor(C.RY[:, 2096:2112], C.RY[:, 2048:2064],
                        C.RY[:, 2080:2096], ALU.mult).then_inc(C.ve.h, 1)
    C.emit("vector", fkm)
    v7a = C.ve.pre(1)
    v7 = C.ve.pre(1)
    def fbf2(v, v6=v6, v7a=v7a):
        v.wait_ge(C.ve.h, v6)
        v.tensor_copy(C.RX[:, 2064:2080], C.RY[:, 2096:2112])\
            .then_inc(C.ve.h, 1)
        v.wait_ge(C.ve.h, v7a)
        v.tensor_tensor(C.RY[:, 2112:2128], C.RY[:, 2096:2112],
                        C.RX[:, 2064:2080], ALU.subtract).then_inc(C.ve.h, 1)
    C.emit("vector", fbf2)
    v8 = C.ve.pre(1)
    def fbf3(v, v7=v7):
        v.wait_ge(C.ve.h, v7)
        v.tensor_copy(C.RX[:, 2080:2096], C.RY[:, 2112:2128])
        v.tensor_scalar(C.RX[:, 2096:2112], C.RY[:, 2080:2096], -1.0, 1.0,
                        ALU.mult, ALU.add).then_inc(C.ve.h, 1)
    C.emit("vector", fbf3)
    for nm in ("wh", "iv"):
        gds = C.snap_guards([0, 1, 2, 3])
        pe_t = C.pe.pre(1)
        def ftr(t, nm=nm, gds=gds, v8=v8):
            t.wait_ge(C.ve.h, v8)
            last = None
            for bk in range(4):
                wg(t, gds[bk])
                for j in range(4):
                    tt = bk * 4 + j
                    dst = C.ps[0:1, bk * 512 + j * 128: bk * 512 + j * 128 + 128]
                    if nm == "wh":
                        t.matmul(dst, C.RX[:, 2064 + tt: 2065 + tt], C.ident,
                                 start=True, stop=False)
                        last = t.matmul(dst, C.RX[:, 2080 + tt: 2081 + tt],
                                        C.ident, start=False, stop=True)
                    else:
                        last = t.matmul(dst, C.RX[:, 2096 + tt: 2097 + tt],
                                        C.ident, start=True, stop=True)
            last.then_inc(C.pe.h, 1)
        C.emit("tensor", ftr)
        a_t = C.act.pre(1)
        def frow(s, pe_t=pe_t, nm=nm):
            s.wait_ge(C.pe.h, pe_t)
            last = None
            for bk in range(4):
                dst = (C.RY[0:1, bk * 512: bk * 512 + 512] if nm == "wh"
                       else C.RY[0:1, 2048 + bk * 512: 2048 + bk * 512 + 512])
                last = s.activation(dst, C.ps[0:1, bk * 512: bk * 512 + 512],
                                    AF.Copy)
            last.then_inc(C.act.h, 1)
        C.emit("scalar", frow)
        for bk in range(4):
            C.guard_bank(bk, C.act, a_t)
    rows_act = (C.act, C.act.n)
    rsp = C.nc.dram_tensor("rsp", [2, T], F32, kind="Internal").ap()
    C.rsp = rsp
    s_rs = C.sem("3rs")
    t_r1 = e_dma(C, "sync", rsp[0:1], C.RY[0:1, 0:2048], s_rs,
                 waits=[rows_act])
    t_r2 = e_dma(C, "sync", rsp[1:2], C.RY[0:1, 2048:4096], s_rs,
                 waits=[(s_rs, t_r1)])
    rows_done = (s_rs, t_r2)
    if stop == "router":
        return rows_done
    return emit_l3f_attn(C, W, xnsp, rows_done, rsp, s_spill_done, stop)


def emit_l3f_attn(C, W, xnsp, rows_done, rsp, s_spill_done, stop=None):
    RAb = C.RAb
    s_wa = C.sem("3ga")
    s_wb = C.sem("3gb")
    s_ps = C.sem("3ps")
    t_p = e_dma(C, "gpsimd", C.RY[0:1, 2048:4096], W["l3_pos"], s_ps,
                waits=[rows_done])
    vp = C.ve.pre(2)
    def fph(v, t_p=t_p, vp=vp, rows_done=rows_done):
        v.wait_ge(s_ps.h, t_p)
        wg(v, rows_done)
        v.tensor_copy(C.RX[0:1, 2048:4096], C.RY[0:1, 2048:4096])\
            .then_inc(C.ve.h, 1)
        v.wait_ge(C.ve.h, vp - 1)
        v.tensor_tensor(C.RX[0:1, 4096:6144], C.RY[0:1, 2048:4096],
                        C.RX[0:1, 2048:4096], ALU.subtract).then_inc(C.ve.h, 1)
    C.emit("vector", fph)
    for c in range(4):
        gd6 = C.snap_guards([6])
        pe_p = C.pe.pre(1)
        def fpbc(t, c=c, vp=vp, gd6=gd6):
            t.wait_ge(C.ve.h, vp)
            wg(t, gd6[0])
            t.matmul(C.ps[:, 3072:3584], C.ones_bf[0:1, :],
                     C.RX[0:1, 2048 + c * 512: 2048 + c * 512 + 512],
                     start=True, stop=False)
            t.matmul(C.ps[:, 3072:3584], C.ones_bf[0:1, :],
                     C.RX[0:1, 4096 + c * 512: 4096 + c * 512 + 512],
                     start=False, stop=True).then_inc(C.pe.h, 1)
        C.emit("tensor", fpbc)
        vs = [C.ve.pre(1) for _ in range(4)]
        def fth(v, c=c, pe_p=pe_p, vs=vs):
            v.wait_ge(C.pe.h, pe_p)
            v.tensor_scalar(C.RY[:, 3584:4096], C.ps[:, 3072:3584],
                            C.RCOL[:, 42:43], None, ALU.mult)\
                .then_inc(C.ve.h, 1)
            v.wait_ge(C.ve.h, vs[0])
            v.tensor_scalar(C.RY[:, 3072:3584], C.RY[:, 3584:4096],
                            0.15915494309189535, None, ALU.mult)\
                .then_inc(C.ve.h, 1)
            v.wait_ge(C.ve.h, vs[1])
            v.tensor_copy(C.RY[:, 2048:2560].bitcast(I32),
                          C.RY[:, 3072:3584]).then_inc(C.ve.h, 1)
            v.wait_ge(C.ve.h, vs[2])
            v.tensor_copy(C.RY[:, 2560:3072],
                          C.RY[:, 2048:2560].bitcast(I32)).then_inc(C.ve.h, 1)
            v.wait_ge(C.ve.h, vs[3])
            v.scalar_tensor_tensor(C.RY[:, 3072:3584], C.RY[:, 2560:3072],
                                   -6.283185307179586, C.RY[:, 3584:4096],
                                   ALU.mult, ALU.add).then_inc(C.ve.h, 1)
        C.ve.pre(1)
        C.emit("vector", fth)
        v_r = C.ve.n
        acs = [C.act.pre(1) for _ in range(3)]
        c_hpi = C.constf(1.5707963267948966)
        def fsin(s, c=c, v_r=v_r, acs=acs, c_hpi=c_hpi):
            s.wait_ge(C.ve.h, v_r)
            s.activation(C.RY[:, 2048:2560], C.RY[:, 3072:3584], AF.Sin)\
                .then_inc(C.act.h, 1)
            s.activation(C.RY[:, 2560:3072], C.RY[:, 3072:3584], AF.Abs)\
                .then_inc(C.act.h, 1)
            s.wait_ge(C.act.h, acs[1])
            s.activation(C.RY[:, 2560:3072], C.RY[:, 2560:3072], AF.Sin,
                         scale=-1.0, bias=c_hpi).then_inc(C.act.h, 1)
        C.emit("scalar", fsin)
        C.ve.pre(1)
        def ftab(v, c=c, a3=acs[2]):
            v.wait_ge(C.act.h, a3)
            v.tensor_copy(C.ROPE[:, c * 512: c * 512 + 512],
                          C.RY[:, 2560:3072])
            v.tensor_scalar(C.ROPE[:, 2048 + c * 512: 2048 + c * 512 + 512],
                            C.RY[:, 2048:2560], C.RCOL[:, 41:42], None,
                            ALU.mult).then_inc(C.ve.h, 1)
        C.emit("vector", ftab)
        C.guard_bank(6, C.ve, C.ve.n)
    tab_done = (C.ve, C.ve.n)
    if stop == "tab":
        return tab_done
    rms1 = emit_ln(C, lambda k: C.RB[:, k * 2048:(k + 1) * 2048],
                   lambda k: C.RB[:, k * 2048:(k + 1) * 2048],
                   rms=True, eps=1e-6, in_bf16=True,
                   gain_col=lambda k: C.RCOL[:, 24 + k:25 + k],
                   pre_waits=(rows_done, tab_done), sq_region=C.RC[:, 0:4096])
    pe0 = (C.pe, C.pe.n)
    t_wv = load_wmat(C, C.RW, 0, W["l3f_wv"], 8, 1024, s_wa, waits=(pe0,))
    po_v = C.po.pre(1)
    def fonesv(g, rms1=rms1):
        wg(g, rms1)
        g.memset(C.RC[:, 0:15600], 1.0)
        g.memset(C.RW[:, 8192:9232], 1.0).then_inc(C.po.h, 1)
    C.emit("gpsimd", fonesv)
    def vtm_slice(tt):
        if tt < 15:
            return C.RC[:, tt * 1040:(tt + 1) * 1040]
        return C.RW[:, 8192:9232]
    for tt in range(16):
        for c in range(2):
            b = (tt * 2 + c) % 4
            gd = C.bank_guard[b]
            pe_t = C.pe.pre(1)
            pwv = ((s_wa, t_wv), rms1) if tt == 0 and c == 0 else ()
            def fmm(t, tt=tt, c=c, b=b, gd=gd, pwv=pwv):
                for w_ in pwv:
                    wg(t, w_)
                wg(t, gd)
                last = None
                for k in range(8):
                    last = t.matmul(
                        C.bank(b),
                        C.RB[:, k * 2048 + tt * 128: k * 2048 + tt * 128 + 128],
                        C.RW[:, k * 1024 + c * 512: k * 1024 + c * 512 + 512],
                        start=(k == 0), stop=(k == 7))
                last.then_inc(C.pe.h, 1)
            C.emit("tensor", fmm)
            cnt = C.ve.pre(1)
            def fev(v, tt=tt, c=c, b=b, pe_t=pe_t, po_v=po_v):
                v.wait_ge(C.pe.h, pe_t)
                v.wait_ge(C.po.h, po_v)
                dst = vtm_slice(tt).rearrange("p (h e) -> p h e", h=16)[:, c * 8:
                                                                       c * 8 + 8,
                                                                       0:64]
                v.tensor_copy(dst, C.bank(b).rearrange("p (h e) -> p h e", h=8))\
                    .then_inc(C.ve.h, 1)
            C.emit("vector", fev)
            C.guard_bank(b, C.ve, cnt)
    pe_v = (C.pe, C.pe.n)
    t_wq = load_wmat(C, C.RW, 0, W["l3f_wq"], 8, 1024, s_wb, waits=(pe_v,))
    mm_chain(C,
             lambda k, m: C.RW[:, k * 1024 + m * 128: k * 1024 + m * 128 + 128],
             lambda k, c: C.RB[:, k * 2048 + c * 512: k * 2048 + c * 512 + 512],
             evac_copy(C, lambda m, c:
                       RAb[:, m * 2048 + c * 512: m * 2048 + c * 512 + 512],
                       scale=0.125),
             m_tiles=8, k_list=range(8), n_chunks=4,
             pre_waits=((s_wb, t_wq), s_spill_done), banks=(0, 1, 2, 3))
    pe_q = (C.pe, C.pe.n)
    t_wk = load_wmat(C, C.RW, 0, W["l3f_wk"], 8, 1024, s_wa, waits=(pe_q,))
    mm_chain(C,
             lambda k, m: C.RW[:, k * 1024 + m * 128: k * 1024 + m * 128 + 128],
             lambda k, c: C.RB[:, k * 2048 + c * 512: k * 2048 + c * 512 + 512],
             evac_copy(C, lambda m, c:
                       RAb[:, 16384 + m * 2048 + c * 512:
                           16384 + m * 2048 + c * 512 + 512], eng="vector"),
             m_tiles=8, k_list=range(8), n_chunks=4,
             pre_waits=((s_wa, t_wk),), banks=(0, 1, 2, 3))
    qk_ve = (C.ve, C.ve.n)
    qk_act = (C.act, C.act.n)
    for reg_base in (0, 16384):
        for dt in range(8):
            slot = reg_base + dt * 2048
            gds = C.snap_guards([2, 3, 4, 5])
            pe_sw = C.pe.pre(1)
            def f_sw(t, slot=slot, gds=gds, qk_ve=qk_ve, qk_act=qk_act,
                     tab_done=tab_done, first=(reg_base == 0 and dt == 0)):
                if first:
                    wg(t, qk_ve)
                    wg(t, qk_act)
                    wg(t, tab_done)
                for c4 in range(4):
                    wg(t, gds[c4])
                last = None
                for c4 in range(4):
                    last = t.matmul(C.bank(2 + c4), C.pswap,
                                    RAb[:, slot + c4 * 512:
                                        slot + c4 * 512 + 512],
                                    start=True, stop=True)
                last.then_inc(C.pe.h, 1)
            C.emit("tensor", f_sw)
            C.ve.pre(1)
            def f_t1(v, slot=slot, pe_sw=pe_sw):
                v.wait_ge(C.pe.h, pe_sw)
                v.tensor_tensor(RAb[:, slot:slot + 2048],
                                RAb[:, slot:slot + 2048],
                                C.ROPE[:, 0:2048], ALU.mult)\
                    .then_inc(C.ve.h, 1)
            C.emit("vector", f_t1)
            for c4 in range(4):
                t2a = C.ve.pre(1)
                C.ve.pre(1)
                def f_t2(v, slot=slot, c4=c4, t2a=t2a):
                    v.tensor_tensor(C.RY[:, 3584:4096], C.bank(2 + c4),
                                    C.ROPE[:, 2048 + c4 * 512:
                                           2048 + c4 * 512 + 512],
                                    ALU.mult).then_inc(C.ve.h, 1)
                    v.wait_ge(C.ve.h, t2a)
                    v.tensor_tensor(RAb[:, slot + c4 * 512:
                                        slot + c4 * 512 + 512],
                                    RAb[:, slot + c4 * 512:
                                        slot + c4 * 512 + 512],
                                    C.RY[:, 3584:4096], ALU.add)\
                        .then_inc(C.ve.h, 1)
                C.emit("vector", f_t2)
                C.guard_bank(2 + c4, C.ve, C.ve.n)
    rope_done = (C.ve, C.ve.n)
    if stop == "rope":
        return rope_done

    def qT_fn(h):
        b = 64 * (h % 2)
        return RAb[b:b + 64, (h // 2) * 2048: (h // 2) * 2048 + 2048]

    def kT_fn(h):
        b = 64 * (h % 2)
        return RAb[b:b + 64, 16384 + (h // 2) * 2048:
                   16384 + (h // 2) * 2048 + 2048]

    def v_lhsT(h, kt):
        return vtm_slice(kt)[:, h * 65: h * 65 + 65]

    def o_dst(h, qb):
        b = 64 * (h % 2)
        return C.RB[b:b + 64,
                    (h // 2) * 2048 + qb * QB: (h // 2) * 2048 + qb * QB + QB]

    def bias_blk(h, qb, bias_ready):
        cnt = C.ve.pre(1)
        def f(v, bias_ready=bias_ready):
            wg(v, bias_ready)
            v.tensor_scalar(C.RM[:, 48:64], C.RM[:, 64:80], -3.0,
                            None, ALU.add).then_inc(C.ve.h, 1)
        C.emit("vector", f)
        return (lambda kt: C.RM[:, 48 + kt:49 + kt]), (C.ve, cnt)

    kt_hi = lambda qb: (qb + 1) * (QB // 128)
    C.guard_bank(0, *rope_done)
    C.guard_bank(1, *rope_done)
    emit_attn64(C, list(range(16)), qT_fn, kT_fn, v_lhsT, o_dst,
                shift=-3.0, kt_hi_fn=kt_hi, pmask_fn=causal_pmask(C),
                ry_off=3072, bias_blk=bias_blk, den_eps=1e-30)
    attn_pe = (C.pe, C.pe.n)
    attn_ve = (C.ve, C.ve.n)
    if stop == "attn":
        return attn_ve
    s_x2 = C.sem("3x2")
    for k in range(ND):
        e_dma(C, "gpsimd", RAb[:, 16384 + k * 2048: 16384 + (k + 1) * 2048],
              xnsp[k], s_x2, waits=(attn_pe, attn_ve) if k == 0 else ())
    xn2 = (s_x2, s_x2.n)
    for halfw in range(2):
        t_wo = load_wmat(C, C.RW, halfw * 4096, W["l3f_wo_h%d" % halfw],
                         8, 512, s_wb if halfw == 0 else s_wa,
                         waits=(attn_pe,))
        mm_chain(C,
                 lambda k, m, halfw=halfw:
                 C.RW[:, halfw * 4096 + k * 512 + m * 128:
                      halfw * 4096 + k * 512 + m * 128 + 128],
                 lambda k, c: C.RB[:, k * 2048 + c * 512:
                                   k * 2048 + c * 512 + 512],
                 evac_resid(C,
                            lambda m, c, halfw=halfw:
                            C.RC[:, (halfw * 4 + m) * 2048 + c * 512:
                                 (halfw * 4 + m) * 2048 + c * 512 + 512],
                            lambda m, c, halfw=halfw:
                            RAb[:, 16384 + (halfw * 4 + m) * 2048 + c * 512:
                                16384 + (halfw * 4 + m) * 2048 + c * 512 + 512]),
                 m_tiles=4, k_list=range(8), n_chunks=4,
                 pre_waits=((s_wb if halfw == 0 else s_wa, t_wo), attn_ve,
                            xn2),
                 banks=(0, 1, 2, 3))
    x1_done = (C.ve, C.ve.n)
    if stop == "x1":
        return x1_done
    rms2 = emit_ln(C, lambda k: C.RC[:, k * 2048:(k + 1) * 2048],
                   lambda k: C.RB[:, k * 2048:(k + 1) * 2048],
                   rms=True, eps=1e-6, in_bf16=True,
                   gain_col=lambda k: C.RCOL[:, 32 + k:33 + k],
                   pre_waits=(x1_done,), sq_region=RAb[:, 0:4096])
    guard1 = ((C.pe, C.pe.n),)
    prev_fg2 = None
    for tc in range(2):
        for sl in range(8):
            t1 = load_wmat(C, C.RW, 0, W["l3f_w1"][:, sl * 512:(sl + 1) * 512],
                           8, 512, s_wa, waits=guard1)
            for mt in range(4):
                for c2 in range(2):
                    b = (sl * 8 + mt * 2 + c2) % 4
                    gd = C.bank_guard[b]
                    pe_t = C.pe.pre(1)
                    pwa = ((s_wa, t1), rms2) if mt == 0 and c2 == 0 else ()
                    def fmm(t, sl=sl, mt=mt, c2=c2, tc=tc, b=b, gd=gd,
                            pwa=pwa):
                        for w_ in pwa:
                            wg(t, w_)
                        wg(t, gd)
                        cl = tc * 1024 + c2 * 512
                        last = None
                        for k in range(8):
                            last = t.matmul(
                                C.bank(b),
                                C.RW[:, k * 512 + mt * 128:
                                     k * 512 + mt * 128 + 128],
                                C.RB[:, k * 2048 + cl: k * 2048 + cl + 512],
                                start=(k == 0), stop=(k == 7))
                        last.then_inc(C.pe.h, 1)
                    C.emit("tensor", fmm)
                    htile = sl * 4 + mt
                    a_s = C.act.pre(1)
                    def fg1(s, b=b, pe_t=pe_t, prev_fg2=prev_fg2):
                        s.wait_ge(C.pe.h, pe_t)
                        if prev_fg2 is not None:
                            s.wait_ge(C.ve.h, prev_fg2)
                        s.square(C.RY[:, 2048:2560], C.bank(b))\
                            .then_inc(C.act.h, 1)
                    C.emit("scalar", fg1)
                    g2a = C.ve.pre(1)
                    v_1 = C.ve.pre(1)
                    def fg2(v, b=b, a_s=a_s, g2a=g2a):
                        v.wait_ge(C.act.h, a_s)
                        v.tensor_scalar(C.RY[:, 2048:2560], C.RY[:, 2048:2560],
                                        0.044715, 1.0, ALU.mult, ALU.add)\
                            .then_inc(C.ve.h, 1)
                        v.wait_ge(C.ve.h, g2a)
                        v.tensor_tensor(C.RY[:, 2560:3072], C.RY[:, 2048:2560],
                                        C.bank(b), ALU.mult)\
                            .then_inc(C.ve.h, 1)
                    C.emit("vector", fg2)
                    prev_fg2 = v_1
                    a_s2 = C.act.pre(1)
                    def fg3(s, v_1=v_1):
                        s.wait_ge(C.ve.h, v_1)
                        s.activation(C.RY[:, 2560:3072], C.RY[:, 2560:3072],
                                     AF.Sigmoid, scale=1.5957691216057308)\
                            .then_inc(C.act.h, 1)
                    C.emit("scalar", fg3)
                    cnt = C.ve.pre(1)
                    def fg4(v, b=b, a_s2=a_s2, htile=htile, c2=c2):
                        v.wait_ge(C.act.h, a_s2)
                        v.tensor_tensor(RAb[:, htile * 1024 + c2 * 512:
                                            htile * 1024 + c2 * 512 + 512],
                                        C.RY[:, 2560:3072], C.bank(b),
                                        ALU.mult).then_inc(C.ve.h, 1)
                    C.emit("vector", fg4)
                    C.guard_bank(b, C.ve, cnt)
            guard1 = ((C.pe, C.pe.n),)
        h_done = (C.ve, C.ve.n)
        for sl in range(4):
            t2 = load_wmat(C, C.RW, 4096,
                           W["l3f_w2"][:, sl * 256:(sl + 1) * 256],
                           32, 256, s_wb, waits=guard1)
            for mt in range(2):
                for c2 in range(2):
                    b = 4 + (mt * 2 + c2) % 2
                    gd = C.bank_guard[b]
                    pe_t = C.pe.pre(1)
                    pwa = ((s_wb, t2), h_done) if mt == 0 and c2 == 0 else ()
                    def fmm2(t, sl=sl, mt=mt, c2=c2, b=b, gd=gd, pwa=pwa):
                        for w_ in pwa:
                            wg(t, w_)
                        wg(t, gd)
                        last = None
                        for k in range(32):
                            last = t.matmul(
                                C.bank(b),
                                C.RW[:, 4096 + k * 256 + mt * 128:
                                     4096 + k * 256 + mt * 128 + 128],
                                RAb[:, k * 1024 + c2 * 512:
                                    k * 1024 + c2 * 512 + 512],
                                start=(k == 0), stop=(k == 31))
                        last.then_inc(C.pe.h, 1)
                    C.emit("tensor", fmm2)
                    cnt = C.ve.pre(1)
                    def fev2(v, sl=sl, mt=mt, c2=c2, tc=tc, b=b, pe_t=pe_t):
                        v.wait_ge(C.pe.h, pe_t)
                        otile = sl * 2 + mt
                        dst = C.RC[:, otile * 2048 + tc * 1024 + c2 * 512:
                                   otile * 2048 + tc * 1024 + c2 * 512 + 512]
                        v.tensor_tensor(dst, dst, C.bank(b), ALU.add)\
                            .then_inc(C.ve.h, 1)
                    C.emit("vector", fev2)
                    C.guard_bank(b, C.ve, cnt)
            guard1 = ((C.pe, C.pe.n),)
    x2_done = (C.ve, C.ve.n)
    if stop == "x2":
        return x2_done
    s_rr = C.sem("3rr")
    t_w1r = e_dma(C, "gpsimd", C.RY[0:1, 2048:4096], rsp[0:1], s_rr,
                  waits=[x2_done])
    t_w2r = e_dma(C, "gpsimd", C.RX[0:1, 6144:8192], rsp[1:2], s_rr,
                  waits=[(s_rr, t_w1r)])
    v_hs = C.ve.pre(2)
    def fsplit(v, t_w1r=t_w1r, v_hs=v_hs):
        v.wait_ge(s_rr.h, t_w1r)
        v.tensor_copy(C.RX[0:1, 2048:4096], C.RY[0:1, 2048:4096])\
            .then_inc(C.ve.h, 1)
        v.wait_ge(C.ve.h, v_hs - 1)
        v.tensor_tensor(C.RX[0:1, 4096:6144], C.RY[0:1, 2048:4096],
                        C.RX[0:1, 2048:4096], ALU.subtract).then_inc(C.ve.h, 1)
    C.emit("vector", fsplit)
    split_done = v_hs
    for c in range(4):
        gd6 = C.snap_guards([6])
        pe_b = C.pe.pre(1)
        def fbc(t, c=c, split_done=split_done, gd6=gd6):
            t.wait_ge(C.ve.h, split_done)
            wg(t, gd6[0])
            t.matmul(C.ps[:, 3072:3584], C.ones_bf[0:1, :],
                     C.RX[0:1, 2048 + c * 512: 2048 + c * 512 + 512],
                     start=True, stop=False)
            t.matmul(C.ps[:, 3072:3584], C.ones_bf[0:1, :],
                     C.RX[0:1, 4096 + c * 512: 4096 + c * 512 + 512],
                     start=False, stop=True).then_inc(C.pe.h, 1)
        C.emit("tensor", fbc)
        C.ve.pre(1)
        def fcp(v, c=c, pe_b=pe_b):
            v.wait_ge(C.pe.h, pe_b)
            v.tensor_copy(C.RY[:, 2048 + c * 512: 2048 + c * 512 + 512],
                          C.ps[:, 3072:3584]).then_inc(C.ve.h, 1)
        C.emit("vector", fcp)
        C.guard_bank(6, C.ve, C.ve.n)
        gd6b = C.snap_guards([6])
        pe_b2 = C.pe.pre(1)
        def fbc2(t, c=c, t_w2r=t_w2r, gd6b=gd6b):
            t.wait_ge(s_rr.h, t_w2r)
            wg(t, gd6b[0])
            t.matmul(C.ps[:, 3072:3584], C.ones_bf[0:1, :],
                     C.RX[0:1, 6144 + c * 512: 6144 + c * 512 + 512],
                     start=True, stop=True).then_inc(C.pe.h, 1)
        C.emit("tensor", fbc2)
        C.ve.pre(1)
        def fcp2(v, c=c, pe_b2=pe_b2):
            v.wait_ge(C.pe.h, pe_b2)
            v.tensor_copy(C.RX[:, c * 512: c * 512 + 512], C.ps[:, 3072:3584])\
                .then_inc(C.ve.h, 1)
        C.emit("vector", fcp2)
        C.guard_bank(6, C.ve, C.ve.n)
    bc_done = (C.ve, C.ve.n)
    s_x3 = C.sem("3x3")
    prev = {}
    prev_sil = None
    for k in range(ND):
        slot = 4096 + (k % 2) * 2048
        ws = [bc_done] if k < 2 else [(C.ve, prev[k - 2])]
        t_x = e_dma(C, "gpsimd", C.RX[:, slot:slot + 2048], xnsp[k], s_x3,
                    waits=ws)
        cba = C.ve.pre(1)
        v_t = C.ve.pre(1)
        def fcomb(v, k=k, slot=slot, t_x=t_x, bc_done=bc_done,
                  prev_sil=prev_sil, cba=cba):
            v.wait_ge(s_x3.h, t_x)
            wg(v, bc_done)
            if prev_sil is not None:
                v.wait_ge(C.act.h, prev_sil)
            v.tensor_tensor(C.RY[:, 0:2048], C.RC[:, k * 2048:(k + 1) * 2048],
                            C.RY[:, 2048:4096], ALU.mult)
            v.tensor_tensor(C.RX[:, slot:slot + 2048],
                            C.RX[:, slot:slot + 2048],
                            C.RX[:, 0:2048], ALU.mult).then_inc(C.ve.h, 1)
            v.wait_ge(C.ve.h, cba)
            v.tensor_tensor(C.RY[:, 0:2048], C.RY[:, 0:2048],
                            C.RX[:, slot:slot + 2048], ALU.add)\
                .then_inc(C.ve.h, 1)
        C.emit("vector", fcomb)
        prev[k] = v_t
        a_sil = C.act.pre(1)
        def fsil(s, k=k, v_t=v_t):
            s.wait_ge(C.ve.h, v_t)
            s.activation(C.RB[:, k * 2048:(k + 1) * 2048], C.RY[:, 0:2048],
                         AF.Silu).then_inc(C.act.h, 1)
        C.emit("scalar", fsil)
        prev_sil = a_sil
    return (C.act, C.act.n)


def build_l3(stop=None):
    nc = bass.Bass("TRN2", target_bir_lowering=False)
    xin = nc.dram_tensor("xin", [ND, 128, T], F32, kind="ExternalInput").ap()
    H = {}
    for nm, shp in [("consts", [128, 384]), ("l3_pos", [1, T])] + L3_SHAPES:
        H[nm] = nc.dram_tensor(nm, shp, F32, kind="ExternalInput").ap()
    xout = nc.dram_tensor("xout", [ND, 128, T], F32, kind="ExternalOutput").ap()
    dbg = nc.dram_tensor("dbg", [ND, 128, T], F32, kind="ExternalOutput").ap()
    es = ExitStack()
    with es:
        C = Ctx(nc, es)
        blk = es.enter_context(nc.Block())
        s_x = C.sem("x")
        s_k = C.sem("k")
        for k in range(ND):
            e_dma(C, "sync", C.RA[:, k * 2048:(k + 1) * 2048], xin[k], s_x)
        e_dma(C, "gpsimd", C.CONST[:], H["consts"], s_k)
        e_dma(C, "gpsimd", C.RCOL[:, 0:48], H["l3_cols"][:, 0:48], s_k)
        konst_ready = [(s_k, s_k.n), (s_x, s_x.n)]
        ln1 = emit_ln(C, lambda k: C.RA[:, k * 2048:(k + 1) * 2048],
                      lambda k: C.RB[:, k * 2048:(k + 1) * 2048],
                      pre_waits=tuple(konst_ready))
        xnsp = nc.dram_tensor("xnsp", [ND, 128, T], BF16, kind="Internal").ap()
        s_sp = C.sem("xnsp")
        for k in range(ND):
            e_dma(C, "sync", xnsp[k], C.RB[:, k * 2048:(k + 1) * 2048], s_sp,
                  waits=(ln1,) if k == 0 else ())
        spill_done = (s_sp, s_sp.n)
        r_a = emit_l3a(C, H, spill_done=spill_done, stop=stop)

        def dump(tiles, wait, rows=128):
            sd = C.sem("dbgo")
            for k, src in enumerate(tiles):
                e_dma(C, "gpsimd", dbg[k][0:rows, :], src, sd,
                      waits=(wait,) if k == 0 else ())
            def ffin(g, n=sd.n):
                g.wait_ge(sd.h, n)
            C.emit("gpsimd", ffin)
            C.finalize(blk)
            return nc

        if stop in ("qkv", "s"):
            return dump([C.RAb[:, k * 2048:(k + 1) * 2048] for k in range(ND)],
                        r_a)
        ssp = nc.dram_tensor("ssp", [ND, 128, T], BF16, kind="Internal").ap()
        s_ss = C.sem("ssp")
        for k in range(ND):
            e_dma(C, "sync", ssp[k], C.RAb[:, k * 2048:(k + 1) * 2048], s_ss,
                  waits=(r_a,) if k == 0 else ())
        s_spd = (s_ss, s_ss.n)
        r = emit_l3f(C, H, xnsp, s_spd, stop=stop)
        if stop == "bis":
            sd = C.sem("dbgo")
            e_dma(C, "gpsimd", dbg[0][:, 0:64], C.RY[:, 2048:2112], sd,
                  waits=(r,))
            def ffin(g, n=sd.n):
                g.wait_ge(sd.h, n)
            C.emit("gpsimd", ffin)
            C.finalize(blk)
            return nc
        if stop == "w16":
            sd = C.sem("dbgo")
            e_dma(C, "gpsimd", dbg[0][:, 0:64], C.RY[:, 2048:2112], sd,
                  waits=(r,))
            e_dma(C, "gpsimd", dbg[1][0:1, :], C.RY[0:1, 0:2048], sd)
            def ffin(g, n=sd.n):
                g.wait_ge(sd.h, n)
            C.emit("gpsimd", ffin)
            C.finalize(blk)
            return nc
        if stop == "router":
            sd = C.sem("dbgo")
            e_dma(C, "gpsimd", dbg[0][0:1, :], C.RY[0:1, 0:2048], sd,
                  waits=(r,))
            e_dma(C, "gpsimd", dbg[1][0:1, :], C.RY[0:1, 2048:4096], sd)
            def ffin(g, n=sd.n):
                g.wait_ge(sd.h, n)
            C.emit("gpsimd", ffin)
            C.finalize(blk)
            return nc
        if stop in ("tab", "rope"):
            return dump([C.ROPE[:, 0:2048], C.ROPE[:, 2048:4096]], r)
        if stop == "qk":
            return dump([C.RAb[:, k * 2048:(k + 1) * 2048] for k in range(ND)],
                        r)
        if stop == "attn":
            return dump([C.RB[:, k * 2048:(k + 1) * 2048] for k in range(ND)],
                        r)
        if stop in ("x1", "x2"):
            return dump([C.RC[:, k * 2048:(k + 1) * 2048] for k in range(ND)],
                        r)
        if stop == "g":
            return dump([C.RB[:, k * 2048:(k + 1) * 2048] for k in range(ND)],
                        r)
        emit_gate(C, xin, ssp, xout, r)
        C.finalize(blk)
    return nc


def build_all():
    nc = bass.Bass("TRN2", target_bir_lowering=False)
    xin = nc.dram_tensor("xin", [ND, 128, T], F32, kind="ExternalInput").ap()
    H = {}
    all_shapes = ([("l0_wdq", [D, 512]), ("l0_wdkv", [D, 512]),
                   ("l0_wupq", [512, D]), ("l0_wupk", [512, D]),
                   ("l0_wupv", [512, D]),
                   ("l0_wout_h0", [D, 512]), ("l0_wout_h1", [D, 512]),
                   ("l0_cols", [128, 16]),
                   ("ropeC", [128, T]), ("ropeS", [128, T]),
                   ("consts", [128, 384]), ("l3_pos", [1, T])] +
                  L0F_SHAPES + L1_SHAPES + L2_SHAPES + L3_SHAPES)
    for nm, shp in all_shapes:
        kind = I32 if nm.startswith("l2_qw") else F32
        H[nm] = nc.dram_tensor(nm, shp, kind, kind="ExternalInput").ap()
    xout = nc.dram_tensor("xout", [ND, 128, T], F32, kind="ExternalOutput").ap()
    xc1 = nc.dram_tensor("xc1", [ND, 128, T], F32, kind="Internal").ap()
    xc2 = nc.dram_tensor("xc2", [ND, 128, T], F32, kind="Internal").ap()
    xc3 = nc.dram_tensor("xc3", [ND, 128, T], F32, kind="Internal").ap()
    es = ExitStack()
    with es:
        C = Ctx(nc, es)
        blk = es.enter_context(nc.Block())

        def barrier():
            return ((C.pe, C.pe.n), (C.ve, C.ve.n), (C.act, C.act.n),
                    (C.po, C.po.n))

        s_x = C.sem("x")
        s_k = C.sem("k")
        for k in range(ND):
            e_dma(C, "sync", C.RA[:, k * 2048:(k + 1) * 2048], xin[k], s_x)
        e_dma(C, "gpsimd", C.CONST[:], H["consts"], s_k)
        e_dma(C, "gpsimd", C.ROPE[:, 0:2048], H["ropeC"], s_k)
        e_dma(C, "gpsimd", C.ROPE[:, 2048:4096], H["ropeS"], s_k)
        e_dma(C, "gpsimd", C.RCOL[:, 0:16], H["l0_cols"], s_k)
        e_dma(C, "gpsimd", C.RCOL[:, 16:33], H["l0f_cols"], s_k)
        konst_ready = [(s_k, s_k.n), (s_x, s_x.n)]
        ln1 = emit_ln(C, lambda k: C.RA[:, k * 2048:(k + 1) * 2048],
                      lambda k: C.RB[:, k * 2048:(k + 1) * 2048],
                      pre_waits=tuple(konst_ready))
        ln2 = emit_ln(C, lambda k: C.RB[:, k * 2048:(k + 1) * 2048],
                      lambda k: C.RC[:, k * 2048:(k + 1) * 2048],
                      gain_col=lambda k: C.RCOL[:, k:k + 1],
                      bias_col=lambda k: C.RCOL[:, 8 + k:8 + k + 1],
                      pre_waits=(ln1,), sq_region=C.RAb[:, 0:4096],
                      in_bf16=True)
        s_done = emit_l0a(C, H)
        ssp0 = nc.dram_tensor("ssp0", [ND, 128, T], BF16, kind="Internal").ap()
        s_sp = C.sem("ssp")
        for k in range(ND):
            e_dma(C, "sync", ssp0[k], C.RAb[:, k * 2048:(k + 1) * 2048], s_sp,
                  waits=(s_done,) if k == 0 else ())
        spill_done = (s_sp, s_sp.n)
        moe_done = emit_l0f(C, H, spill_done)
        g_done = emit_l0f_finish(C, C.x2sp, moe_done)
        so0 = emit_gate(C, xin, ssp0, xc1, g_done)

        bar = barrier()
        s_x1 = C.sem("x")
        s_k1 = C.sem("k1")
        for k in range(ND):
            e_dma(C, "sync", C.RA[:, k * 2048:(k + 1) * 2048], xc1[k], s_x1,
                  waits=((so0, so0.n),) + bar if k == 0 else ())
        e_dma(C, "gpsimd", C.RCOL2[:], H["l1_cols2"], s_k1,
              waits=((so0, so0.n),))
        e_dma(C, "gpsimd", C.RCOL2b[:], H["l1_cols3"], s_k1)
        kr1 = [(s_k1, s_k1.n), (s_x1, s_x1.n)] + list(bar)
        ln1_1 = emit_ln(C, lambda k: C.RA[:, k * 2048:(k + 1) * 2048],
                        lambda k: C.RB[:, k * 2048:(k + 1) * 2048],
                        pre_waits=tuple(kr1))
        gsp = nc.dram_tensor("gsp", [ND, 128, T], BF16, kind="Internal").ap()
        r1 = emit_l1(C, H, gsp)
        s_xg = C.sem("gx")
        s_g = C.sem("gg")
        s_o1 = C.sem("gout")
        prev_tt = {}
        for k in range(ND):
            wx = [r1] if k == 0 else [(C.ve, prev_tt[k - 1])]
            t_x = e_dma(C, "sync", C.RY[:, 0:2048], xc1[k], s_xg, waits=wx)
            slot = (k % 2) * 2048
            ws = [r1] if k < 2 else [(C.ve, prev_tt[k - 2])]
            t_g = e_dma(C, "gpsimd", C.RX[:, slot:slot + 2048], gsp[k], s_g,
                        waits=ws)
            vta = C.ve.pre(1)
            v_t = C.ve.pre(1)
            def ftt(v, k=k, slot=slot, t_x=t_x, t_g=t_g, vta=vta):
                v.wait_ge(s_xg.h, t_x)
                v.wait_ge(s_g.h, t_g)
                v.tensor_tensor(C.RY[:, 2048:4096],
                                C.RAb[:, 16384 + k * 2048:
                                      16384 + (k + 1) * 2048],
                                C.RX[:, slot:slot + 2048], ALU.mult)\
                    .then_inc(C.ve.h, 1)
                v.wait_ge(C.ve.h, vta)
                v.tensor_tensor(C.RY[:, 0:2048], C.RY[:, 0:2048],
                                C.RY[:, 2048:4096], ALU.add)\
                    .then_inc(C.ve.h, 1)
            C.emit("vector", ftt)
            prev_tt[k] = v_t
            e_dma(C, "sync", xc2[k], C.RY[:, 0:2048], s_o1,
                  waits=[(C.ve, v_t)])

        bar = barrier()
        s_x2 = C.sem("x")
        s_k2 = C.sem("k2")
        for k in range(ND):
            e_dma(C, "sync", C.RA[:, k * 2048:(k + 1) * 2048], xc2[k], s_x2,
                  waits=((s_o1, s_o1.n),) + bar if k == 0 else ())
        e_dma(C, "gpsimd", C.RCOL2[:], H["l2_cols"], s_k2,
              waits=((s_o1, s_o1.n),))
        kr2 = [(s_k2, s_k2.n), (s_x2, s_x2.n)] + list(bar)
        ln1_2 = emit_ln(C, lambda k: C.RA[:, k * 2048:(k + 1) * 2048],
                        lambda k: C.RB[:, k * 2048:(k + 1) * 2048],
                        pre_waits=tuple(kr2))
        r2 = emit_l2(C, H)
        s_xg2 = C.sem("gx")
        s_o2 = C.sem("gout")
        prev_tt = {}
        for k in range(ND):
            wx = [r2] if k == 0 else [(C.ve, prev_tt[k - 1])]
            t_x = e_dma(C, "sync", C.RY[:, 0:2048], xc2[k], s_xg2, waits=wx)
            vta = C.ve.pre(1)
            v_t = C.ve.pre(1)
            def ftt2(v, k=k, t_x=t_x, r2=r2, vta=vta):
                v.wait_ge(s_xg2.h, t_x)
                wg(v, r2)
                v.tensor_tensor(C.RY[:, 2048:4096],
                                C.RAb[:, k * 2048:(k + 1) * 2048],
                                C.RB[:, k * 2048:(k + 1) * 2048], ALU.mult)\
                    .then_inc(C.ve.h, 1)
                v.wait_ge(C.ve.h, vta)
                v.tensor_tensor(C.RY[:, 0:2048], C.RY[:, 0:2048],
                                C.RY[:, 2048:4096], ALU.add)\
                    .then_inc(C.ve.h, 1)
            C.emit("vector", ftt2)
            prev_tt[k] = v_t
            e_dma(C, "sync", xc3[k], C.RY[:, 0:2048], s_o2,
                  waits=[(C.ve, v_t)])

        bar = barrier()
        s_x3 = C.sem("x")
        s_k3 = C.sem("k3")
        for k in range(ND):
            e_dma(C, "sync", C.RA[:, k * 2048:(k + 1) * 2048], xc3[k], s_x3,
                  waits=((s_o2, s_o2.n),) + bar if k == 0 else ())
        e_dma(C, "gpsimd", C.RCOL[:, 0:48], H["l3_cols"][:, 0:48], s_k3,
              waits=((s_o2, s_o2.n),))
        kr3 = [(s_k3, s_k3.n), (s_x3, s_x3.n)] + list(bar)
        ln1_3 = emit_ln(C, lambda k: C.RA[:, k * 2048:(k + 1) * 2048],
                        lambda k: C.RB[:, k * 2048:(k + 1) * 2048],
                        pre_waits=tuple(kr3))
        xnsp3 = nc.dram_tensor("xnsp3", [ND, 128, T], BF16,
                               kind="Internal").ap()
        s_sp3 = C.sem("xnsp")
        for k in range(ND):
            e_dma(C, "sync", xnsp3[k], C.RB[:, k * 2048:(k + 1) * 2048],
                  s_sp3, waits=(ln1_3,) if k == 0 else ())
        spill3 = (s_sp3, s_sp3.n)
        r_a = emit_l3a(C, H, spill_done=spill3)
        ssp3 = nc.dram_tensor("ssp3", [ND, 128, T], BF16,
                              kind="Internal").ap()
        s_ss3 = C.sem("ssp3")
        for k in range(ND):
            e_dma(C, "sync", ssp3[k], C.RAb[:, k * 2048:(k + 1) * 2048],
                  s_ss3, waits=(r_a,) if k == 0 else ())
        r3 = emit_l3f(C, H, xnsp3, (s_ss3, s_ss3.n))
        emit_gate(C, xc3, ssp3, xout, r3)
        C.finalize(blk)
    return nc
